# version 05
# baseline (speedup 1.0000x reference)
import contextlib
import math

import numpy as np

import concourse.bass as bass
import concourse.mybir as mybir
from concourse.bass_utils import run_bass_kernel_spmd

F32 = mybir.dt.float32
BF16 = mybir.dt.bfloat16
ACT = mybir.ActivationFunctionType
ALU = mybir.AluOpType
AX = mybir.AxisListType

DM = 1024
SEQ = 4096
NSEQ = 4
P = 2048
HALO = 1280
XROWS = HALO + P
NT = XROWS // 128
OWN0 = HALO // 128

FH = 8
AH = 16
DH = 64
HID = 2730
NHC = 22
NGRP = 11
FCTX = 5
ACTX = 3
MASKVAL = -1.0e30
LN32 = math.log(32.0)
LN8 = math.log(8.0)


def _chunks():
    c0 = dict(tiles=list(range(0, 10)), sbase=5, xnbase=1,
              b_kv=[1, 2, 3, 4], b_full=[5, 6, 7, 8, 9],
              c_tiles=[5, 6, 7, 8, 9],
              d_kv=[6, 7, 8, 9], d_full=[],
              e_tiles=[],
              f_kv=[8, 9], f_full=[])
    def own(t0):
        t = list(range(t0, t0 + 8))
        return dict(tiles=t, sbase=t0, xnbase=t0, b_kv=[], b_full=t,
                    c_tiles=t, d_kv=[], d_full=t, e_tiles=t, f_kv=[], f_full=t)
    return [c0, own(10), own(18)]


CHUNKS = _chunks()


def _rope_tables_for_core(half):
    n = SEQ
    hi = np.arange(0, DH, 2, dtype=np.float64)
    inv_freq = 10000.0 ** (-hi / DH)
    base = (hi + 0.4 * DH) / (1.4 * DH)
    out = np.zeros((3, 128, 1152), dtype=np.float64)
    for ci, ch in enumerate(CHUNKS):
        wave = sorted(set(ch["b_kv"] + ch["b_full"]))
        for sj, r in enumerate(wave):
            g0 = r * 128 - HALO + (0 if half == 0 else P)
            if half == 0 and g0 < 0:
                continue
            t = g0 + np.arange(128, dtype=np.float64)
            fr = t[:, None] * inv_freq[None, :]
            power = (t - n // 2) / 256.0
            scale = base[None, :] ** power[:, None]
            cs, sn = np.cos(fr), np.sin(fr)
            for ti, tab in enumerate((cs * scale, sn * scale,
                                      cs / scale, sn / scale)):
                out[ci, :, ti * 288 + sj * 32: ti * 288 + sj * 32 + 32] = tab
    return np.ascontiguousarray(out.astype(np.float32))


def _masks_for_core(half):
    qi = np.arange(128)[:, None]

    def mk(nblk, win, nvar):
        ctxw = nblk * 128
        j = np.arange(ctxw)[None, :]
        keep = (j >= qi) & (j <= qi + win)
        m = np.zeros((128, nvar * ctxw), dtype=np.float32)
        for v in range(nvar):
            mm = keep.copy()
            if half == 0 and (nvar - 1 - v) > 0:
                mm[:, :(nvar - 1 - v) * 128] = False
            m[:, v * ctxw:(v + 1) * ctxw] = np.where(mm, 0.0, MASKVAL)
        return np.ascontiguousarray(m)

    return mk(FCTX, 512, 5), mk(ACTX, 256, 3)


def _const_inputs(q_scale, k_scale, ln_g, ln_b, b1_, b2_, b_gates, gamma):
    d = {}
    sc = np.zeros((2, 2, 64), np.float32)
    for l in range(2):
        sc[l] = np.stack([np.asarray(q_scale[l], np.float32),
                          np.asarray(k_scale[l], np.float32)])
    d["scales"] = np.ascontiguousarray(
        np.broadcast_to(sc.reshape(1, 256), (128, 256)))
    lnp = np.zeros((128, 2, 2, 8), np.float32)
    for l in range(2):
        lnp[:, l, 0, :] = np.asarray(ln_g[l], np.float32).reshape(8, 128).T
        lnp[:, l, 1, :] = np.asarray(ln_b[l], np.float32).reshape(8, 128).T
    d["lnp"] = np.ascontiguousarray(lnp.reshape(128, 32))
    b2 = np.zeros((128, 2, 8), np.float32)
    for l in range(2):
        b2[:, l, :] = np.asarray(b2_[l], np.float32).reshape(8, 128).T
    d["b2c"] = np.ascontiguousarray(b2.reshape(128, 16))
    b1a = np.zeros((128, 2, NHC), np.float32)
    b1g = np.zeros((128, 2, NHC), np.float32)
    for l in range(2):
        b1 = np.asarray(b1_[l], np.float32)
        for arr, seg in ((b1a, b1[:HID]), (b1g, b1[HID:])):
            pad = np.zeros(NHC * 128, np.float32)
            pad[:HID] = seg
            arr[:, l, :] = pad.reshape(NHC, 128).T
    d["b1a"] = np.ascontiguousarray(b1a.reshape(128, 2 * NHC))
    d["b1g"] = np.ascontiguousarray(b1g.reshape(128, 2 * NHC))
    d["bgv"] = np.ascontiguousarray(
        np.broadcast_to(np.asarray(b_gates, np.float32).reshape(1, 16), (128, 16)))
    d["gmc"] = np.ascontiguousarray(np.asarray(gamma, np.float32).reshape(8, 128).T)
    d["ident"] = np.eye(128, dtype=np.float32)
    return d


class Rec:
    def __init__(self, ctx, sem_name):
        self.ctx = ctx
        self.sem = sem_name
        self.ops = []

    def wait(self, tok):
        if tok is None:
            return
        sem, val = tok
        if val > 0:
            self.ops.append(("w", sem, val))

    def waits(self, toks):
        best = {}
        for t in toks:
            if t is None:
                continue
            s, v = t
            best[s] = max(best.get(s, 0), v)
        for s, v in best.items():
            self.wait((s, v))

    def op(self, method, *a, **kw):
        r = [method, a, kw, []]
        self.ops.append(("o", r))
        return _OpH(self, r)

    def tok(self):
        return (self.sem, self.ctx.count[self.sem])


class _OpH:
    def __init__(self, rec, r):
        self.rec = rec
        self.r = r

    def inc(self, n=1, sem=None):
        sem = sem or self.rec.sem
        self.r[3].append((sem, n))
        self.rec.ctx.count[sem] += n
        return (sem, self.rec.ctx.count[sem])


class TB:

    def __init__(self, ap=None, name=""):
        self.ap = ap
        self.name = name
        self.wr = None
        self.rd = []


def bw(rec, *tbs):
    for t in tbs:
        rec.waits(t.rd)


def ew(tok, *tbs):
    for t in tbs:
        t.wr = tok
        t.rd = []


def br(rec, *tbs):
    for t in tbs:
        rec.wait(t.wr)


def er(tok, *tbs):
    for t in tbs:
        t.rd.append(tok)


DMA_SEMS = ("yi", "yx0", "yx1", "yr", "yo0", "yo1",
            "ga0", "ga1", "ga2", "ga3", "gw0", "gw1", "go0", "go1")


class Ctx:
    def __init__(self, nc):
        self.nc = nc
        self.count = {k: 0 for k in ("t", "v", "s", "y", "g") + DMA_SEMS}
        self.T = Rec(self, "t")
        self.V = Rec(self, "v")
        self.S = Rec(self, "s")
        self.Y = Rec(self, "y")
        self.G = Rec(self, "g")
        self.sem = {}


def shift_view(buf, kc, col, n=128):
    off = col - 1 if kc >= 4 else col
    return buf[:, kc, off:off + n]


def build_program(nphase=99, taps=()):
    nc = bass.Bass("TRN2", target_bir_lowering=False,
                   detect_race_conditions=False)
    c = Ctx(nc)
    T, V, S, Y, G = c.T, c.V, c.S, c.Y, c.G

    xin = nc.dram_tensor("xin", [XROWS, DM], F32, kind="ExternalInput").ap()
    yout = nc.dram_tensor("yout", [P, DM], F32, kind="ExternalOutput").ap()
    rope_in = nc.dram_tensor("rope_in", [3, 128, 1152], F32, kind="ExternalInput").ap()
    maskf_in = nc.dram_tensor("maskf_in", [128, 5 * 640], F32, kind="ExternalInput").ap()
    maska_in = nc.dram_tensor("maska_in", [128, 3 * 384], F32, kind="ExternalInput").ap()
    ident_in = nc.dram_tensor("ident", [128, 128], F32, kind="ExternalInput").ap()
    scales_in = nc.dram_tensor("scales", [128, 256], F32, kind="ExternalInput").ap()
    lnp_in = nc.dram_tensor("lnp", [128, 32], F32, kind="ExternalInput").ap()
    b2c_in = nc.dram_tensor("b2c", [128, 16], F32, kind="ExternalInput").ap()
    b1a_in = nc.dram_tensor("b1a", [128, 2 * NHC], F32, kind="ExternalInput").ap()
    b1g_in = nc.dram_tensor("b1g", [128, 2 * NHC], F32, kind="ExternalInput").ap()
    bgv_in = nc.dram_tensor("bgv", [128, 16], F32, kind="ExternalInput").ap()
    gmc_in = nc.dram_tensor("gmc", [128, 8], F32, kind="ExternalInput").ap()
    w_aqkv = nc.dram_tensor("w_aqkv", [128, 8, 3072], F32, kind="ExternalInput").ap()
    w_agat = nc.dram_tensor("w_agat", [128, 8, 16], F32, kind="ExternalInput").ap()
    w_aout = nc.dram_tensor("w_aout", [128, 8, DM], F32, kind="ExternalInput").ap()
    w_fqkv = nc.dram_tensor("w_fqkv", [2, 128, 8, 1536], F32, kind="ExternalInput").ap()
    w_fo = nc.dram_tensor("w_fo", [2, 128, 4, DM], F32, kind="ExternalInput").ap()
    w_f1 = nc.dram_tensor("w_f1", [2, NGRP, 128, 8, 512], F32, kind="ExternalInput").ap()
    w_f2 = nc.dram_tensor("w_f2", [2, NGRP, 128, 2, DM], F32, kind="ExternalInput").ap()

    A = nc.alloc_sbuf_tensor
    xn_bf = A("xn_bf", [128, 8, 9 * 128 + 1], BF16).ap()
    strm = A("strm", [128, 8, 8 * 128 + 1], F32).ap()
    y1arena = A("y1arena", [128, 4100], F32).ap()
    y1a = y1arena.bitcast(BF16).rearrange("p (a b) -> p a b", b=1025)
    kt1 = A("kt1", [128, 4, FCTX * 128], BF16).ap()
    v1r = A("v1r", [128, FCTX, FH * 65], BF16).ap()
    kt2 = A("kt2", [128, 4, FCTX * 128], BF16).ap()
    v2r = A("v2r", [128, FCTX, FH * 65], BF16).ap()
    kta = A("kta", [128, 8, ACTX * 128], BF16).ap()
    vat = A("vat", [128, ACTX, AH * 65], BF16).ap()
    rope = A("rope", [128, 4, 9, 32], F32).ap()
    maskf = A("maskf", [128, 5, 640], BF16).ap()
    maska = A("maska", [128, 3, 384], BF16).ap()
    id_f = A("id_f", [128, 128], F32).ap()
    id_b = A("id_b", [128, 128], BF16).ap()
    scl = A("scl", [128, 2, 2, 64], F32).ap()
    lnp = A("lnp_sb", [128, 2, 2, 8], F32).ap()
    b2c = A("b2c_sb", [128, 2, 8], F32).ap()
    b1a = A("b1a_sb", [128, 2, NHC], F32).ap()
    b1g = A("b1g_sb", [128, 2, NHC], F32).ap()
    bgv = A("bgv_sb", [128, 16], F32).ap()
    gmc = A("gmc_sb", [128, 8], F32).ap()
    wag = A("wag", [128, 8, 16], BF16).ap()
    inv32 = A("inv32", [128, NT], F32).ap()
    xn_cr = A("xn_cr", [128, 8], BF16).ap()
    y1b_cr = A("y1b_cr", [128, 8], BF16).ap()
    x1f_cr = A("x1f_cr", [128, 8], F32).ap()
    y1f_cr = A("y1f_cr", [128, 8], F32).ap()
    xt0 = A("xt0", [128, DM], F32).ap()
    xt1 = A("xt1", [128, DM], F32).ap()
    znt = A("znt", [128, DM], F32).ap()
    msc0 = A("msc0", [128, 640], F32).ap()
    msc1 = A("msc1", [128, 640], F32).ap()
    prb0 = A("prb0", [128, 640], BF16).ap()
    prb1 = A("prb1", [128, 640], BF16).ap()
    ptb0 = prb0.rearrange("p (a n) -> p a n", n=128)
    ptb1 = prb1.rearrange("p (a n) -> p a n", n=128)
    qtb = A("qtb", [128, 8, 128], BF16).ap()
    otk = A("otk", [128, DM], BF16).ap()
    otb = A("otb", [128, 8, 128], BF16).ap()
    krot = A("krot", [128, 512], BF16).ap()
    qrot = A("qrot", [128, 512], BF16).ap()
    sml = A("sml", [128, 10, 16], F32).ap()
    gsc = A("gsc", [128, 3, 16], F32).ap()
    eps5 = A("eps5", [128, 1], F32).ap()
    tiny = A("tiny", [128, 1], F32).ap()
    y1f_pv = A("y1f_pv", [128, 8], F32).ap()
    ninv32 = A("ninv32", [128, NT], F32).ap()
    WA = A("WA", [128, 8, 3072], BF16).ap()
    W2 = A("W2", [128, 8, DM], BF16).ap()
    ostg = y1arena[:, 0:2048].rearrange("p (a b) -> p a b", a=2)
    rem = nc.sbuf_bytes_remaining
    assert rem >= 0, f"SBUF over budget by {-rem} bytes"

    PS = nc.alloc_psum_tensor
    PSA = PS("PSA", [128, 512], F32).ap()
    PSTR = PS("PSTR", [128, 1024], BF16).ap()
    PSB = PS("PSB", [128, 1024], F32).ap()
    PSC = PS("PSC", [128, 1024], F32).ap()
    PSD = PS("PSD", [128, 512], F32).ap()
    PSE = PS("PSE", [128, 512], F32).ap()
    assert nc.psum_banks_remaining >= 0
    pa = TB(PSA, "pa")
    ptr = TB(PSTR, "ptr")
    pb0, pb1 = TB(None, "pb0"), TB(None, "pb1")
    pc0, pc1 = TB(None, "pc0"), TB(None, "pc1")
    pd = TB(PSD, "pd")
    pe = TB(PSE, "pe")

    tb = {n: TB(None, n) for n in
          ["xt0", "xt1", "znt", "msc0", "msc1", "prb0", "prb1",
           "qtb", "otk", "otb", "krot", "qrot", "ubf",
           "ostg0", "ostg1", "gsc", "lnh", "y1bf", "rope"]}
    WAr = [TB(None, f"WAr{i}") for i in range(4)]
    W2wo = [TB(None, "W2wo0"), TB(None, "W2wo1")]
    W2g = [TB(None, "W2g0"), TB(None, "W2g1")]
    W2flat = W2.rearrange("p a b -> p (a b)")
    ubf_l = [W2flat[:, 0:2050].rearrange("p (a b) -> p a b", a=2),
             W2flat[:, 4096:4096 + 2050].rearrange("p (a b) -> p a b", a=2)]
    phase_toks = {"xnbf": None, "strm": None}

    def emit_init():
        Y.op("dma_start", out=id_f, in_=ident_in).inc(16, "yi")
        Y.op("dma_start", out=scl, in_=scales_in.rearrange(
            "p (l t d) -> p l t d", l=2, t=2)).inc(16, "yi")
        Y.op("dma_start", out=lnp, in_=lnp_in.rearrange(
            "p (l i c) -> p l i c", l=2, i=2)).inc(16, "yi")
        Y.op("dma_start", out=b2c, in_=b2c_in.rearrange("p (l c) -> p l c", l=2)).inc(16, "yi")
        Y.op("dma_start", out=b1a, in_=b1a_in.rearrange("p (l c) -> p l c", l=2)).inc(16, "yi")
        Y.op("dma_start", out=b1g, in_=b1g_in.rearrange("p (l c) -> p l c", l=2)).inc(16, "yi")
        Y.op("dma_start", out=bgv, in_=bgv_in).inc(16, "yi")
        Y.op("dma_start", out=gmc, in_=gmc_in).inc(16, "yi")
        wa_f32 = WA.rearrange("p a b -> p (a b)").bitcast(F32)
        mf_stage = wa_f32[:, 0:3200]
        ma_stage = wa_f32[:, 3200:3200 + 1152]
        Y.op("dma_start", out=mf_stage, in_=maskf_in).inc(16, "yi")
        tok_m = Y.op("dma_start", out=ma_stage, in_=maska_in).inc(16, "yi")
        V.wait(tok_m)
        V.op("tensor_copy", out=maskf.rearrange("p a b -> p (a b)"), in_=mf_stage)
        V.op("tensor_copy", out=maska.rearrange("p a b -> p (a b)"), in_=ma_stage)
        V.op("tensor_copy", out=id_b, in_=id_f)
        for ap in (kt1, kt2, kta):
            V.op("memset", ap, 0.0)
        for ap in (v1r, v2r, vat):
            v4 = ap.rearrange("p s (h e) -> p s h e", e=65)
            V.op("memset", v4[:, :, :, 0:64], 0.0)
            V.op("memset", v4[:, :, :, 64:65], 1.0)
        for ap in (xn_cr, y1b_cr, x1f_cr, y1f_cr, y1f_pv, inv32):
            V.op("memset", ap, 0.0)
        V.op("memset", eps5, 1e-5)
        V.op("memset", tiny, 1e-27)
        vtok = V.op("memset", sml, 0.0).inc()
        T.wait(vtok)
        S.wait(vtok)
        G.wait(vtok)
        ytok = ("yi", c.count["yi"])
        T.wait(ytok)
        S.wait(ytok)
        V.wait(ytok)
        for r in WAr:
            r.rd = [vtok]

    def load_wqkv_ffn(layer):
        bw(G, WAr[0], WAr[1])
        tok = G.op("dma_start", out=WA[:, :, 0:1536],
                   in_=w_fqkv[layer]).inc(16, "ga0")
        ew(tok, WAr[0], WAr[1])
        return tok

    def load_wo_ffn(layer):
        reg = W2wo[layer]
        bw(G, reg, W2g[0], W2g[1])
        dst = W2flat[:, layer * 4096:(layer + 1) * 4096].rearrange(
            "p (k n) -> p k n", k=4)
        tok = G.op("dma_start", out=dst,
                   in_=w_fo[layer]).inc(16, ("go0", "go1")[layer])
        ew(tok, reg)
        return tok

    def load_wgrp(layer, g):
        reg = WAr[g % 4]
        bw(G, reg)
        c0_ = 2 * g * 128
        cw = min(256, HID - c0_)
        base = WA[:, :, 768 * (g % 4):768 * (g % 4) + 768]
        tok1 = G.op("dma_start", out=base[:, :, 0:512],
                    in_=w_f1[layer, g]).inc(16, f"ga{g % 4}")
        ew(tok1, reg)
        w2reg = W2g[g % 2]
        bw(G, w2reg, W2wo[1 - layer])
        half0 = 4096 * (1 - layer)
        w2slot = W2flat[:, half0 + (g % 2) * 2048: half0 + (g % 2) * 2048 + 2048]
        w2slot = w2slot.rearrange("p (k n) -> p k n", k=2)
        tok2 = G.op("dma_start", out=w2slot,
                    in_=w_f2[layer, g]).inc(16, f"gw{g % 2}")
        ew(tok2, w2reg)
        return reg, w2reg, base, w2slot, cw

    def load_watt():
        bw(G, *WAr)
        tok = G.op("dma_start", out=WA, in_=w_aqkv).inc(16, "ga0")
        ew(tok, *WAr)
        tokg = G.op("dma_start", out=wag, in_=w_agat).inc(16, "ga1")
        return tok, tokg

    def load_wout():
        bw(G, W2wo[0], W2wo[1], W2g[0], W2g[1])
        dst = W2flat.rearrange("p (k n) -> p k n", k=8)
        tok = G.op("dma_start", out=dst, in_=w_aout).inc(16, "go0")
        ew(tok, W2wo[0], W2wo[1], W2g[0], W2g[1])
        return tok

    def emit_A(ch):
        for i, r in enumerate(ch["tiles"]):
            xt = (xt0, xt1)[i % 2]
            xtb = tb[("xt0", "xt1")[i % 2]]
            bw(Y, xtb)
            ytok = Y.op("dma_start", out=xt,
                        in_=xin[r * 128:(r + 1) * 128, :]).inc(
                            16, ("yx0", "yx1")[i % 2])
            ew(ytok, xtb)
            V.wait(ytok)
            bw(V, tb["znt"])
            V.op("bn_stats", out=sml[:, 0, 0:6], in_=xt[:, 0:512])
            vtok = V.op("bn_stats", out=sml[:, 0, 6:12], in_=xt[:, 512:1024]).inc()
            V.wait(vtok)
            vtok = V.op("bn_aggr", out=sml[:, 1, 0:2], in_=sml[:, 0, 0:12]).inc()
            S.wait(vtok)
            stok = S.op("activation", out=sml[:, 1, 2:3], in_=sml[:, 1, 1:2],
                        func=ACT.Ln, bias=eps5).inc()
            S.wait(stok)
            stok = S.op("activation", out=sml[:, 1, 3:4], in_=sml[:, 1, 2:3],
                        func=ACT.Exp, scale=-0.5).inc()
            V.wait(stok)
            vtok = V.op("tensor_scalar", out=sml[:, 1, 4:5], in0=sml[:, 1, 0:1],
                        scalar1=sml[:, 1, 3:4], scalar2=-1.0,
                        op0=ALU.mult, op1=ALU.mult).inc()
            bw(S, tb["znt"])
            S.wait(vtok)
            stok = S.op("activation", out=znt, in_=xt, func=ACT.Identity,
                        bias=sml[:, 1, 4:5], scale=sml[:, 1, 3:4]).inc()
            ew(stok, tb["znt"])
            er(stok, xtb)
            V.wait(stok)
            V.op("bn_stats", out=sml[:, 2, 0:6], in_=znt[:, 0:512])
            vtok = V.op("bn_stats", out=sml[:, 2, 6:12], in_=znt[:, 512:1024]).inc()
            V.wait(vtok)
            vtok = V.op("bn_aggr", out=sml[:, 3, 0:2], in_=sml[:, 2, 0:12]).inc()
            S.wait(vtok)
            stok = S.op("activation", out=sml[:, 3, 2:3], in_=sml[:, 3, 1:2],
                        func=ACT.Ln, bias=tiny).inc()
            S.wait(stok)
            stok = S.op("activation", out=inv32[:, r:r + 1], in_=sml[:, 3, 2:3],
                        func=ACT.Exp, scale=-0.5).inc()
            V.wait(stok)
            V.op("tensor_scalar_mul", out=ninv32[:, r:r + 1],
                 in0=inv32[:, r:r + 1], scalar1=-1.0).inc()
            bw(T, pb0, pb1)
            T.wait(stok)
            for kc in range(8):
                h = T.op("transpose", PSB[:, kc * 128:(kc + 1) * 128],
                         znt[:, kc * 128:(kc + 1) * 128], id_f)
            ttok = h.inc()
            ew(ttok, pb0, pb1)
            er(ttok, tb["znt"])
            pb8 = PSB.rearrange("p (a b) -> p a b", a=8)
            if r >= ch["xnbase"]:
                col = 1 + (r - ch["xnbase"]) * 128
                br(S, pb0)
                stok = S.op("copy", out=xn_bf[:, :, col:col + 128], in_=pb8).inc()
                er(stok, pb0, pb1)
            elif r == ch["xnbase"] - 1:
                br(S, pb0)
                stok = S.op("copy", out=xn_bf[:, :, 0:1],
                            in_=pb8[:, :, 127:128]).inc()
                er(stok, pb0, pb1)
            if r >= ch["sbase"]:
                scol = 1 + (r - ch["sbase"]) * 128
                br(V, pb0)
                V.wait(stok)
                vtok = V.op("tensor_copy", out=strm[:, :, scol:scol + 128],
                            in_=pb8).inc()
                er(vtok, pb0, pb1)
        if ch["tiles"][0] >= OWN0:
            S.op("copy", out=xn_bf[:, :, 0:1], in_=xn_cr.unsqueeze(2))
        last_col = 1 + (ch["tiles"][-1] - ch["xnbase"]) * 128 + 127
        S.wait(S.tok())
        stok = S.op("copy", out=xn_cr.unsqueeze(2),
                    in_=xn_bf[:, :, last_col:last_col + 1]).inc()
        phase_toks["xnbf"] = stok
        phase_toks["strm"] = c.V.tok()

    def emit_wave_ffn(ch, layer, wq_tok, wo_tok):
        kv_tiles = ch["b_kv"] if layer == 0 else ch["d_kv"]
        full_tiles = ch["b_full"] if layer == 0 else ch["d_full"]
        all_tiles = sorted(set(kv_tiles + full_tiles))
        if not all_tiles:
            return
        src = xn_bf if layer == 0 else y1a
        src_base = ch["xnbase"] if layer == 0 else ch["sbase"]
        src_tok = phase_toks["xnbf"] if layer == 0 else tb["y1bf"].wr
        ktr, vrr = (kt1, v1r) if layer == 0 else (kt2, v2r)
        kt3 = ktr.rearrange("p a (s n) -> p a s n", n=128)
        v4 = vrr.rearrange("p s (h e) -> p s h e", e=65)
        wq = WA[:, :, 0:1536]
        wo4 = W2flat[:, layer * 4096:(layer + 1) * 4096].rearrange(
            "p (k n) -> p k n", k=4)
        wave_slots = {r: j for j, r in enumerate(
            sorted(set(ch["b_kv"] + ch["b_full"])))}
        bw(Y, tb["rope"])
        V.wait(tb["rope"].wr)
        rtok = Y.op("dma_start", out=rope.rearrange("p a b e -> p (a b e)"),
                    in_=rope_in[CHUNKS.index(ch)]).inc(16, "yr")
        ew(rtok, tb["rope"])
        T.wait(src_tok)

        last_qkv_tok = None
        last_wo_tok = None
        for r in all_tiles:
            full = r in full_tiles
            col = 1 + (r - src_base) * 128
            slot = r % FCTX
            rj = wave_slots[r]
            mvi = min(max(r - OWN0, 0), 4)
            bw(T, pa)
            T.wait(wq_tok)
            for kc in range(8):
                h = T.op("matmul", PSA, shift_view(src, kc, col),
                         wq[:, kc, 512:1024], start=(kc == 0), stop=(kc == 7))
            tok_k = h.inc()
            ew(tok_k, pa)
            last_qkv_tok = tok_k
            bw(S, tb["xt0"])
            br(S, pa)
            stok = S.op("copy", out=xt0[:, 0:512], in_=PSA).inc()
            er(stok, pa)
            ew(stok, tb["xt0"])
            bw(T, pa)
            for kc in range(8):
                h = T.op("matmul", PSA, shift_view(src, kc, col),
                         wq[:, kc, 1024:1536], start=(kc == 0), stop=(kc == 7))
            tok_v = h.inc()
            ew(tok_v, pa)
            last_qkv_tok = tok_v
            br(S, pa)
            stok = S.op("copy", out=v4[:, slot, :, 0:64],
                        in_=PSA.rearrange("p (h e) -> p h e", e=64)).inc()
            er(stok, pa)
            tok_vring = stok
            if full:
                bw(T, pa)
                for kc in range(8):
                    h = T.op("matmul", PSA, shift_view(src, kc, col),
                             wq[:, kc, 0:512], start=(kc == 0), stop=(kc == 7))
                tok_q = h.inc()
                ew(tok_q, pa)
                last_qkv_tok = tok_q
                bw(S, tb["xt1"])
                br(S, pa)
                stok = S.op("copy", out=xt1[:, 0:512], in_=PSA).inc()
                er(stok, pa)
                ew(stok, tb["xt1"])
            kf = xt0[:, 0:512]
            kf3 = kf.rearrange("p (h d) -> p h d", d=64)
            kf4 = kf.rearrange("p (h s d) -> p h s d", s=2, d=32)
            V.wait(tb["xt0"].wr)
            bw(V, tb["znt"])
            V.op("tensor_mul", out=znt[:, 0:512], in0=kf, in1=kf)
            vtok = V.op("reduce_sum", out=sml[:, 1, 0:8],
                        in_=znt[:, 0:512].rearrange("p (h d) -> p h d", d=64),
                        axis=AX.X).inc()
            S.wait(vtok)
            stok = S.op("activation", out=sml[:, 1, 8:16], in_=sml[:, 1, 0:8],
                        func=ACT.Ln, bias=tiny).inc()
            S.wait(stok)
            stok = S.op("activation", out=sml[:, 1, 8:16], in_=sml[:, 1, 8:16],
                        func=ACT.Exp, scale=-0.5).inc()
            V.wait(stok)
            inv_b = sml[:, 1, 8:16].unsqueeze(2).broadcast_to(
                [128, 8, 64])
            V.op("tensor_tensor", out=kf3, in0=kf3, in1=inv_b, op=ALU.mult)
            ks_b = scl[:, layer, 1, :].unsqueeze(1).broadcast_to(
                [128, 8, 64])
            vtok = V.op("tensor_tensor", out=kf3, in0=kf3, in1=ks_b,
                        op=ALU.mult).inc()
            V.wait(vtok)
            V.wait(tb["rope"].wr)
            sh = znt[:, 0:512].rearrange("p (h s d) -> p h s d", s=2, d=32)
            V.op("tensor_copy", out=sh[:, :, 0, :], in_=kf4[:, :, 1, :])
            V.op("tensor_copy", out=sh[:, :, 1, :], in_=kf4[:, :, 0, :])
            cs_b = rope[:, 2, rj, :].unsqueeze(1).unsqueeze(1).broadcast_to(
                [128, 8, 2, 32])
            sn_b = rope[:, 3, rj, :].unsqueeze(1).unsqueeze(1).broadcast_to(
                [128, 8, 2, 32])
            V.op("tensor_tensor", out=kf4, in0=kf4, in1=cs_b, op=ALU.mult)
            V.op("tensor_tensor", out=sh, in0=sh, in1=sn_b, op=ALU.mult)
            bw(V, tb["krot"])
            kr4 = krot.rearrange("p (h s d) -> p h s d", s=2, d=32)
            V.op("tensor_sub", out=kr4[:, :, 0, :], in0=kf4[:, :, 0, :],
                 in1=sh[:, :, 0, :])
            vtok = V.op("tensor_add", out=kr4[:, :, 1, :], in0=kf4[:, :, 1, :],
                        in1=sh[:, :, 1, :]).inc()
            ew(vtok, tb["krot"])
            ew(vtok, tb["znt"])
            er(vtok, tb["xt0"], tb["rope"])
            bw(T, ptr)
            T.wait(vtok)
            for j in range(4):
                h = T.op("transpose", PSTR[:, j * 128:(j + 1) * 128],
                         krot[:, j * 128:(j + 1) * 128], id_b)
            ttok = h.inc()
            ew(ttok, ptr)
            er(ttok, tb["krot"])
            br(S, ptr)
            stok = S.op("copy", out=kt3[:, :, slot, :],
                        in_=PSTR[:, 0:512].rearrange("p (a n) -> p a n",
                                                     n=128)).inc()
            er(stok, ptr)
            tok_kring = stok
            if not full:
                continue
            qf = xt1[:, 0:512]
            qf3 = qf.rearrange("p (h d) -> p h d", d=64)
            qf4 = qf.rearrange("p (h s d) -> p h s d", s=2, d=32)
            V.wait(tb["xt1"].wr)
            bw(V, tb["znt"])
            V.op("tensor_mul", out=znt[:, 0:512], in0=qf, in1=qf)
            vtok = V.op("reduce_sum", out=sml[:, 2, 0:8],
                        in_=znt[:, 0:512].rearrange("p (h d) -> p h d", d=64),
                        axis=AX.X).inc()
            S.wait(vtok)
            stok = S.op("activation", out=sml[:, 2, 8:16], in_=sml[:, 2, 0:8],
                        func=ACT.Ln, scale=1.0 / 64.0, bias=tiny).inc()
            S.wait(stok)
            tok_invq8 = S.op("activation", out=sml[:, 2, 8:16],
                             in_=sml[:, 2, 8:16], func=ACT.Exp, scale=-0.5).inc()
            qs_b = scl[:, layer, 0, :].unsqueeze(1).broadcast_to(
                [128, 8, 64])
            vtok = V.op("tensor_tensor", out=qf3, in0=qf3, in1=qs_b,
                        op=ALU.mult).inc()
            V.wait(vtok)
            sh2 = znt[:, 0:512].rearrange("p (h s d) -> p h s d", s=2, d=32)
            V.op("tensor_copy", out=sh2[:, :, 0, :], in_=qf4[:, :, 1, :])
            V.op("tensor_copy", out=sh2[:, :, 1, :], in_=qf4[:, :, 0, :])
            cs_q = rope[:, 0, rj, :].unsqueeze(1).unsqueeze(1).broadcast_to(
                [128, 8, 2, 32])
            sn_q = rope[:, 1, rj, :].unsqueeze(1).unsqueeze(1).broadcast_to(
                [128, 8, 2, 32])
            V.op("tensor_tensor", out=qf4, in0=qf4, in1=cs_q, op=ALU.mult)
            V.op("tensor_tensor", out=sh2, in0=sh2, in1=sn_q, op=ALU.mult)
            bw(V, tb["qrot"])
            qr4 = qrot.rearrange("p (h s d) -> p h s d", s=2, d=32)
            V.op("tensor_sub", out=qr4[:, :, 0, :], in0=qf4[:, :, 0, :],
                 in1=sh2[:, :, 0, :])
            vtok = V.op("tensor_add", out=qr4[:, :, 1, :], in0=qf4[:, :, 1, :],
                        in1=sh2[:, :, 1, :]).inc()
            ew(vtok, tb["qrot"])
            ew(vtok, tb["znt"])
            er(vtok, tb["xt1"], tb["rope"])
            bw(T, ptr)
            T.wait(vtok)
            for j in range(4):
                h = T.op("transpose", PSTR[:, 512 + j * 128:512 + (j + 1) * 128],
                         qrot[:, j * 128:(j + 1) * 128], id_b)
            ttok = h.inc()
            ew(ttok, ptr)
            er(ttok, tb["qrot"])
            bw(S, tb["qtb"])
            br(S, ptr)
            stok = S.op("copy", out=qtb[:, 0:4, :],
                        in_=PSTR[:, 512:1024].rearrange("p (a n) -> p a n",
                                                        n=128)).inc()
            er(stok, ptr)
            ew(stok, tb["qtb"])
            tok_qT = stok
            ctx_tiles = list(range(r - 4, r + 1))
            hs = [dict() for _ in range(FH)]

            def _sc(hh):
                par = hh % 2
                sc_ap = (PSB, PSC)[par]
                sc_tbs_ = ((pb0, pb1), (pc0, pc1))[par]
                pr = hh // 2
                lo = 64 * (hh % 2)
                bw(T, *sc_tbs_)
                T.wait(tok_qT)
                T.wait(tok_kring)
                for j, cr in enumerate(ctx_tiles):
                    cslot = cr % FCTX
                    T.op("matmul", sc_ap[:, j * 128:(j + 1) * 128],
                         qtb[lo:lo + 64, pr, :], kt3[lo:lo + 64, pr, cslot, :],
                         start=True, stop=False)
                    h = T.op("matmul", sc_ap[:, j * 128:(j + 1) * 128], id_b,
                             maskf[:, mvi, j * 128:(j + 1) * 128],
                             start=False, stop=True)
                hs[hh]["sc"] = h.inc()
                ew(hs[hh]["sc"], *sc_tbs_)

            def _mx(hh):
                par = hh % 2
                sc_ap = (PSB, PSC)[par]
                sc_tbs_ = ((pb0, pb1), (pc0, pc1))[par]
                V.wait(hs[hh]["sc"])
                h = V.op("reduce_max", out=sml[:, 3, hh:hh + 1],
                         in_=sc_ap[:, 0:640], axis=AX.X, negate=True)
                hs[hh]["mx"] = h.inc()
                er(hs[hh]["mx"], *sc_tbs_)

            def _nm(hh):
                h = V.op("tensor_scalar_mul", out=sml[:, 4, hh:hh + 1],
                         in0=sml[:, 3, hh:hh + 1],
                         scalar1=sml[:, 2, 8 + hh:9 + hh])
                hs[hh]["nm"] = h.inc()

            def _ex(hh):
                par = hh % 2
                sc_ap = (PSB, PSC)[par]
                sc_tbs_ = ((pb0, pb1), (pc0, pc1))[par]
                prb = (prb0, prb1)[par]
                prtb = tb[("prb0", "prb1")[par]]
                bw(S, prtb)
                S.wait(hs[hh]["nm"])
                S.wait(tok_invq8)
                h = S.op("activation", out=prb, in_=sc_ap[:, 0:640], func=ACT.Exp,
                         bias=sml[:, 4, hh:hh + 1],
                         scale=sml[:, 2, 8 + hh:9 + hh])
                hs[hh]["ex"] = h.inc()
                er(hs[hh]["ex"], *sc_tbs_)
                ew(hs[hh]["ex"], prtb)

            def _pt(hh):
                par = hh % 2
                prb = (prb0, prb1)[par]
                prtb = tb[("prb0", "prb1")[par]]
                bw(T, ptr)
                T.wait(hs[hh]["ex"])
                for j in range(5):
                    h = T.op("transpose", PSTR[:, j * 128:(j + 1) * 128],
                             prb[:, j * 128:(j + 1) * 128], id_b)
                hs[hh]["pt"] = h.inc()
                ew(hs[hh]["pt"], ptr)
                er(hs[hh]["pt"], prtb)

            def _pe(hh):
                par = hh % 2
                ptbuf = (ptb0, ptb1)[par]
                prtb = tb[("prb0", "prb1")[par]]
                br(S, ptr)
                h = S.op("copy", out=ptbuf,
                         in_=PSTR[:, 0:640].rearrange("p (a n) -> p a n", n=128))
                hs[hh]["pe"] = h.inc()
                er(hs[hh]["pe"], ptr)
                ew(hs[hh]["pe"], prtb)

            def _av(hh):
                par = hh % 2
                ptbuf = (ptb0, ptb1)[par]
                prtb = tb[("prb0", "prb1")[par]]
                avtb = (pd, pe)[par]
                bw(T, avtb)
                T.wait(hs[hh]["pe"])
                T.wait(tok_vring)
                for j, cr in enumerate(ctx_tiles):
                    cslot = cr % FCTX
                    h = T.op("matmul", avtb.ap[:, 0:65], ptbuf[:, j, :],
                             vrr[:, cslot, hh * 65:hh * 65 + 65],
                             start=(j == 0), stop=(j == 4))
                hs[hh]["av"] = h.inc()
                ew(hs[hh]["av"], avtb)
                er(hs[hh]["av"], prtb)

            def _rc(hh):
                avtb = (pd, pe)[hh % 2]
                br(V, avtb)
                V.wait(hs[hh]["av"])
                h = V.op("reciprocal", out=sml[:, 5, hh:hh + 1],
                         in_=avtb.ap[:, 64:65])
                hs[hh]["rc"] = h.inc()
                er(hs[hh]["rc"], avtb)

            def _oe(hh):
                avtb = (pd, pe)[hh % 2]
                bw(S, tb["otk"])
                br(S, avtb)
                S.wait(hs[hh]["rc"])
                h = S.op("activation", out=otk[:, hh * 64:(hh + 1) * 64],
                         in_=avtb.ap[:, 0:64], func=ACT.Identity,
                         scale=sml[:, 5, hh:hh + 1])
                hs[hh]["oe"] = h.inc()
                er(hs[hh]["oe"], avtb)
                ew(hs[hh]["oe"], tb["otk"])

            _sc(0)
            _sc(1)
            _mx(0)
            _mx(1)
            for hh in range(FH):
                _nm(hh)
                _ex(hh)
                _pt(hh)
                _pe(hh)
                _av(hh)
                _rc(hh)
                _oe(hh)
                if hh + 2 < FH:
                    _sc(hh + 2)
                    _mx(hh + 2)
            bw(T, ptr)
            T.wait(tb["otk"].wr)
            for j in range(4):
                h = T.op("transpose", PSTR[:, j * 128:(j + 1) * 128],
                         otk[:, j * 128:(j + 1) * 128], id_b)
            ttok = h.inc()
            ew(ttok, ptr)
            er(ttok, tb["otk"])
            bw(S, tb["otb"])
            br(S, ptr)
            stok = S.op("copy", out=otb[:, 0:4, :],
                        in_=PSTR[:, 0:512].rearrange("p (a n) -> p a n",
                                                     n=128)).inc()
            er(stok, ptr)
            ew(stok, tb["otb"])
            tok_oT = stok
            scol = 1 + (r - ch["sbase"]) * 128
            for rnd in range(2):
                bw(T, pe)
                T.wait(tok_oT)
                T.wait(wo_tok)
                for dmc in range(4):
                    dm = rnd * 4 + dmc
                    for ic in range(4):
                        h = T.op("matmul", PSE[:, dmc * 128:(dmc + 1) * 128],
                                 wo4[:, ic, dm * 128:(dm + 1) * 128],
                                 otb[:, ic, :], start=(ic == 0), stop=(ic == 3))
                tok_wo = h.inc()
                ew(tok_wo, pe)
                last_wo_tok = tok_wo
                br(V, pe)
                V.wait(tok_wo)
                for dmc in range(4):
                    dm = rnd * 4 + dmc
                    h = V.op("tensor_tensor", out=strm[:, dm, scol:scol + 128],
                             in0=PSE[:, dmc * 128:(dmc + 1) * 128],
                             in1=strm[:, dm, scol:scol + 128], op=ALU.add)
                vtok = h.inc()
                er(vtok, pe)
                er(vtok, tb["otb"])
        er(last_qkv_tok, WAr[0], WAr[1])
        if last_wo_tok is not None:
            er(last_wo_tok, W2wo[layer])
        phase_toks["strm"] = c.V.tok()

    def emit_ffn(ch, layer):
        tiles = ch["c_tiles"] if layer == 0 else ch["e_tiles"]
        if not tiles:
            return
        ncols = len(tiles) * 128
        cr_ = x1f_cr if layer == 0 else y1f_pv
        if layer == 0:
            h0 = V.op("tensor_copy", out=y1f_pv.unsqueeze(2),
                      in_=y1f_cr.unsqueeze(2))
            V.wait(h0.inc())
        if layer == 1:
            V.wait(tb["y1bf"].wr)
        h = V.op("tensor_copy", out=strm[:, :, 0:1],
                 in_=cr_.unsqueeze(2))
        last = 1 + (tiles[-1] - ch["sbase"]) * 128 + 127
        if layer == 0:
            h = V.op("tensor_copy", out=cr_.unsqueeze(2),
                     in_=strm[:, :, last:last + 1])
        crumb_tok = h.inc()
        stok = None
        for r in tiles:
            col = 1 + (r - ch["sbase"]) * 128
            lcol = (r - tiles[0]) * 128
            bw(T, pb0, pb1)
            T.wait(crumb_tok)
            T.wait(phase_toks["strm"])
            for kc in range(8):
                h = T.op("transpose", PSB[:, kc * 128:(kc + 1) * 128],
                         shift_view(strm, kc, col), id_f)
            ttok = h.inc()
            ew(ttok, pb0, pb1)
            bw(S, tb["xt0"])
            br(S, pb0)
            stok = S.op("copy", out=xt0, in_=PSB).inc()
            er(stok, pb0, pb1)
            ew(stok, tb["xt0"])
            V.wait(stok)
            bw(V, tb["znt"])
            V.op("bn_stats", out=sml[:, 6, 0:6], in_=xt0[:, 0:512])
            vtok = V.op("bn_stats", out=sml[:, 6, 6:12],
                        in_=xt0[:, 512:1024]).inc()
            V.wait(vtok)
            vtok = V.op("bn_aggr", out=sml[:, 7, 0:2], in_=sml[:, 6, 0:12]).inc()
            S.wait(vtok)
            stok = S.op("activation", out=sml[:, 7, 2:3], in_=sml[:, 7, 1:2],
                        func=ACT.Ln, bias=eps5).inc()
            S.wait(stok)
            stok = S.op("activation", out=sml[:, 7, 3:4], in_=sml[:, 7, 2:3],
                        func=ACT.Exp, scale=-0.5).inc()
            V.wait(stok)
            vtok = V.op("tensor_scalar", out=sml[:, 7, 4:5], in0=sml[:, 7, 0:1],
                        scalar1=sml[:, 7, 3:4], scalar2=-1.0,
                        op0=ALU.mult, op1=ALU.mult).inc()
            bw(S, tb["znt"])
            S.wait(vtok)
            stok = S.op("activation", out=znt, in_=xt0, func=ACT.Identity,
                        bias=sml[:, 7, 4:5], scale=sml[:, 7, 3:4]).inc()
            ew(stok, tb["znt"])
            er(stok, tb["xt0"])
            bw(T, pc0, pc1)
            T.wait(stok)
            for kc in range(8):
                h = T.op("transpose", PSC[:, kc * 128:(kc + 1) * 128],
                         znt[:, kc * 128:(kc + 1) * 128], id_f)
            ttok = h.inc()
            ew(ttok, pc0, pc1)
            er(ttok, tb["znt"])
            br(S, pc0)
            for kc in range(8):
                h = S.op("activation", out=y1a[:, kc, 1 + lcol:1 + lcol + 128],
                         in_=PSC[:, kc * 128:(kc + 1) * 128], func=ACT.Identity,
                         scale=lnp[:, layer, 0, kc:kc + 1],
                         bias=lnp[:, layer, 1, kc:kc + 1])
            stok = h.inc()
            er(stok, pc0, pc1)
        ew(stok, tb["lnh"])
        wins = []
        off, left = 1, ncols
        while left > 0:
            w = min(512, left)
            wins.append((off, w))
            off += w
            left -= w
        vtok = None
        ps_y_par = 0
        ubf = ubf_l[layer]
        tb["ubf"].rd = tb["ubf"].rd + list(W2wo[layer].rd)
        vtok_last_u = None
        for g in range(NGRP):
            reg, w2reg, base, w2slot, cw = load_wgrp(layer, g)
            nch = 2 if cw > 128 else 1
            for hc in range(nch):
                ccw = min(128, cw - hc * 128)
                for (wo_, W) in wins:
                    bw(T, pd, pe)
                    T.wait(reg.wr)
                    T.wait(tb["lnh"].wr)
                    for kc in range(8):
                        T.op("matmul", PSD[0:ccw, 0:W],
                             base[:, kc, hc * 128:hc * 128 + ccw],
                             y1a[:, kc, wo_:wo_ + W],
                             start=(kc == 0), stop=(kc == 7))
                    for kc in range(8):
                        h = T.op("matmul", PSE[0:ccw, 0:W],
                                 base[:, kc, 256 + hc * 128:256 + hc * 128 + ccw],
                                 y1a[:, kc, wo_:wo_ + W],
                                 start=(kc == 0), stop=(kc == 7))
                    tok_ag = h.inc()
                    ew(tok_ag, pd, pe)
                    er(tok_ag, reg)
                    bw(S, tb["znt"])
                    br(S, pe)
                    S.wait(tok_ag)
                    stok = S.op("activation", out=znt[0:ccw, 0:W],
                                in_=PSE[0:ccw, 0:W], func=ACT.Gelu,
                                bias=b1g[0:ccw, layer,
                                         2 * g + hc:2 * g + hc + 1]).inc()
                    er(stok, pe)
                    ew(stok, tb["znt"])
                    bw(V, tb["ubf"])
                    br(V, pd)
                    V.wait(stok)
                    vtok = V.op("scalar_tensor_tensor",
                                out=ubf[0:ccw, hc, wo_:wo_ + W],
                                in0=PSD[0:ccw, 0:W],
                                scalar=b1a[0:ccw, layer,
                                           2 * g + hc:2 * g + hc + 1],
                                in1=znt[0:ccw, 0:W], op0=ALU.add,
                                op1=ALU.mult).inc()
                    er(vtok, pd)
                    er(vtok, tb["znt"])
                    ew(vtok, tb["ubf"])
            for dmc in range(8):
                for (wo_, W) in wins:
                    pcb = (pc0, pc1)[ps_y_par]
                    pslice = PSC[:, ps_y_par * 512:ps_y_par * 512 + W]
                    bw(T, pcb)
                    T.wait(w2reg.wr)
                    T.wait(tb["ubf"].wr)
                    for hc in range(nch):
                        chw = min(128, cw - hc * 128)
                        h = T.op("matmul", pslice,
                                 w2slot[0:chw, hc, dmc * 128:(dmc + 1) * 128],
                                 ubf[0:chw, hc, wo_:wo_ + W],
                                 start=(hc == 0), stop=(hc == nch - 1))
                    tok_y = h.inc()
                    ew(tok_y, pcb)
                    er(tok_y, w2reg)
                    er(tok_y, tb["ubf"])
                    br(V, pcb)
                    V.wait(tok_y)
                    if g == 0:
                        h = V.op("scalar_tensor_tensor",
                                 out=strm[:, dmc, wo_:wo_ + W], in0=pslice,
                                 scalar=b2c[:, layer, dmc:dmc + 1],
                                 in1=strm[:, dmc, wo_:wo_ + W],
                                 op0=ALU.add, op1=ALU.add)
                    else:
                        h = V.op("tensor_tensor", out=strm[:, dmc, wo_:wo_ + W],
                                 in0=pslice, in1=strm[:, dmc, wo_:wo_ + W],
                                 op=ALU.add)
                    vtok = h.inc()
                    er(vtok, pcb)
                    ps_y_par ^= 1
        er(vtok, W2wo[layer])
        phase_toks["strm"] = vtok
        if layer == 0:
            S.wait(vtok)
            S.op("copy", out=y1a[:, :, 0:1], in_=y1b_cr.unsqueeze(2))
            for (wo_, W) in wins:
                h = S.op("copy", out=y1a[:, :, wo_:wo_ + W],
                         in_=strm[:, :, wo_:wo_ + W])
            S.wait(h.inc())
            S.op("copy", out=y1b_cr.unsqueeze(2),
                 in_=y1a[:, :, last:last + 1])
            stok = S.op("copy", out=y1f_cr.unsqueeze(2),
                        in_=strm[:, :, last:last + 1]).inc()
            ew(stok, tb["y1bf"])

    def emit_att(ch, wq_tok, wg_tok, wout_tok):
        kv_tiles, full_tiles = ch["f_kv"], ch["f_full"]
        all_tiles = sorted(set(kv_tiles + full_tiles))
        if not all_tiles:
            return
        kt3 = kta.rearrange("p a (s n) -> p a s n", n=128)
        va4 = vat.rearrange("p s (h e) -> p s h e", e=65)
        wo8 = W2flat.rearrange("p (k n) -> p k n", k=8)
        seedT, seedS = c.T.tok(), c.S.tok()
        for n_ in ("ostg0", "ostg1"):
            tb[n_].rd = tb[n_].rd + [seedT, seedS]
        V.wait(wq_tok)
        V.wait(wg_tok)
        gb = gmc.unsqueeze(2).broadcast_to([128, 8, 3072])
        V.op("tensor_tensor", out=WA, in0=WA, in1=gb, op=ALU.mult)
        gb2 = gmc.unsqueeze(2).broadcast_to([128, 8, 16])
        gtok = V.op("tensor_tensor", out=wag, in0=wag, in1=gb2, op=ALU.mult).inc()
        T.wait(gtok)
        T.wait(phase_toks["xnbf"])
        last_qkv_tok = None
        last_wout_tok = None
        for r in all_tiles:
            full = r in full_tiles
            slot = r % ACTX
            mvi = min(max(r - OWN0, 0), 2)
            col = 1 + (r - ch["xnbase"]) * 128
            for hf in range(2):
                bw(T, pa)
                for kc in range(8):
                    h = T.op("matmul", PSA, xn_bf[:, kc, col:col + 128],
                             WA[:, kc, 1024 + hf * 512:1024 + (hf + 1) * 512],
                             start=(kc == 0), stop=(kc == 7))
                tok = h.inc()
                ew(tok, pa)
                last_qkv_tok = tok
                bw(S, tb["krot"])
                br(S, pa)
                stok = S.op("activation", out=krot, in_=PSA, func=ACT.Identity,
                            scale=inv32[:, r:r + 1]).inc()
                er(stok, pa)
                ew(stok, tb["krot"])
                bw(T, ptr)
                T.wait(stok)
                for j in range(4):
                    h = T.op("transpose",
                             PSTR[:, hf * 512 + j * 128:hf * 512 + (j + 1) * 128],
                             krot[:, j * 128:(j + 1) * 128], id_b)
                ttok = h.inc()
                ew(ttok, ptr)
                er(ttok, tb["krot"])
                br(S, ptr)
                stok = S.op("copy", out=kt3[:, 4 * hf:4 * hf + 4, slot, :],
                            in_=PSTR[:, hf * 512:(hf + 1) * 512].rearrange(
                                "p (a n) -> p a n", n=128)).inc()
                er(stok, ptr)
            tok_kring = stok
            for hf in range(2):
                bw(T, pa)
                for kc in range(8):
                    h = T.op("matmul", PSA, xn_bf[:, kc, col:col + 128],
                             WA[:, kc, 2048 + hf * 512:2048 + (hf + 1) * 512],
                             start=(kc == 0), stop=(kc == 7))
                tok = h.inc()
                ew(tok, pa)
                last_qkv_tok = tok
                br(S, pa)
                stok = S.op("activation",
                            out=va4[:, slot, 8 * hf:8 * hf + 8, 0:64],
                            in_=PSA.rearrange("p (h e) -> p h e", e=64),
                            func=ACT.Identity, scale=inv32[:, r:r + 1]).inc()
                er(stok, pa)
            tok_vring = stok
            if not full:
                continue
            for hf in range(2):
                bw(T, pa)
                for kc in range(8):
                    h = T.op("matmul", PSA, xn_bf[:, kc, col:col + 128],
                             WA[:, kc, hf * 512:(hf + 1) * 512],
                             start=(kc == 0), stop=(kc == 7))
                tok = h.inc()
                ew(tok, pa)
                last_qkv_tok = tok
                bw(S, tb["otk"])
                br(S, pa)
                stok = S.op("activation", out=otk[:, hf * 512:(hf + 1) * 512],
                            in_=PSA, func=ACT.Identity,
                            scale=inv32[:, r:r + 1]).inc()
                er(stok, pa)
                ew(stok, tb["otk"])
            bw(T, pd)
            for kc in range(8):
                h = T.op("matmul", PSD[:, 0:16], xn_bf[:, kc, col:col + 128],
                         wag[:, kc, :], start=(kc == 0), stop=(kc == 7))
            tok_g = h.inc()
            ew(tok_g, pd)
            last_qkv_tok = tok_g
            bw(S, tb["gsc"])
            br(S, pd)
            stok = S.op("activation", out=gsc[:, 0, :], in_=PSD[:, 0:16],
                        func=ACT.Identity, scale=ninv32[:, r:r + 1]).inc()
            er(stok, pd)
            V.wait(stok)
            vtok = V.op("tensor_sub", out=gsc[:, 0, :], in0=gsc[:, 0, :],
                        in1=bgv).inc()
            S.wait(vtok)
            stok = S.op("activation", out=gsc[:, 1, :], in_=gsc[:, 0, :],
                        func=ACT.Exp).inc()
            V.wait(stok)
            vtok = V.op("tensor_scalar_add", out=gsc[:, 1, :], in0=gsc[:, 1, :],
                        scalar1=1.0).inc()
            V.wait(vtok)
            vtok = V.op("reciprocal", out=gsc[:, 1, :], in_=gsc[:, 1, :]).inc()
            ew(vtok, tb["gsc"])
            bw(T, ptr)
            T.wait(tb["otk"].wr)
            for j in range(8):
                h = T.op("transpose", PSTR[:, j * 128:(j + 1) * 128],
                         otk[:, j * 128:(j + 1) * 128], id_b)
            ttok = h.inc()
            ew(ttok, ptr)
            er(ttok, tb["otk"])
            bw(S, tb["qtb"])
            br(S, ptr)
            stok = S.op("copy", out=qtb,
                        in_=PSTR.rearrange("p (a n) -> p a n", n=128)).inc()
            er(stok, ptr)
            ew(stok, tb["qtb"])
            tok_qT = stok
            ctx_tiles = list(range(r - 2, r + 1))
            hs = [dict() for _ in range(AH)]

            def _sc(hh):
                par = hh % 2
                sc_tb = (pb0, pb1)[par]
                pr = hh // 2
                lo = 64 * (hh % 2)
                bw(T, sc_tb)
                T.wait(tok_qT)
                T.wait(tok_kring)
                for j, cr_t in enumerate(ctx_tiles):
                    cslot = cr_t % ACTX
                    T.op("matmul",
                         PSB[:, par * 512 + j * 128:par * 512 + (j + 1) * 128],
                         qtb[lo:lo + 64, pr, :], kt3[lo:lo + 64, pr, cslot, :],
                         start=True, stop=False)
                    h = T.op("matmul",
                             PSB[:, par * 512 + j * 128:par * 512 + (j + 1) * 128],
                             id_b, maska[:, mvi, j * 128:(j + 1) * 128],
                             start=False, stop=True)
                hs[hh]["sc"] = h.inc()
                ew(hs[hh]["sc"], sc_tb)

            def _mx(hh):
                par = hh % 2
                sc_tb = (pb0, pb1)[par]
                V.wait(hs[hh]["sc"])
                h = V.op("reduce_max", out=sml[:, 7, hh:hh + 1],
                         in_=PSB[:, par * 512:par * 512 + 384], axis=AX.X,
                         negate=True)
                hs[hh]["mx"] = h.inc()
                er(hs[hh]["mx"], sc_tb)

            def _nm(hh):
                h = V.op("tensor_scalar_mul", out=sml[:, 8, hh:hh + 1],
                         in0=sml[:, 7, hh:hh + 1], scalar1=0.125)
                hs[hh]["nm"] = h.inc()

            def _ex(hh):
                par = hh % 2
                sc_tb = (pb0, pb1)[par]
                prb = (prb0, prb1)[par]
                prtb = tb[("prb0", "prb1")[par]]
                bw(S, prtb)
                S.wait(hs[hh]["nm"])
                h = S.op("activation", out=prb[:, 0:384],
                         in_=PSB[:, par * 512:par * 512 + 384], func=ACT.Exp,
                         bias=sml[:, 8, hh:hh + 1], scale=0.125)
                hs[hh]["ex"] = h.inc()
                er(hs[hh]["ex"], sc_tb)
                ew(hs[hh]["ex"], prtb)

            def _pt(hh):
                par = hh % 2
                prb = (prb0, prb1)[par]
                prtb = tb[("prb0", "prb1")[par]]
                bw(T, ptr)
                T.wait(hs[hh]["ex"])
                for j in range(3):
                    h = T.op("transpose", PSTR[:, j * 128:(j + 1) * 128],
                             prb[:, j * 128:(j + 1) * 128], id_b)
                hs[hh]["pt"] = h.inc()
                ew(hs[hh]["pt"], ptr)
                er(hs[hh]["pt"], prtb)

            def _pe(hh):
                par = hh % 2
                ptbuf = (ptb0, ptb1)[par]
                prtb = tb[("prb0", "prb1")[par]]
                br(S, ptr)
                h = S.op("copy", out=ptbuf[:, 0:3, :],
                         in_=PSTR[:, 0:384].rearrange("p (a n) -> p a n", n=128))
                hs[hh]["pe"] = h.inc()
                er(hs[hh]["pe"], ptr)
                ew(hs[hh]["pe"], prtb)

            def _av(hh):
                par = hh % 2
                ptbuf = (ptb0, ptb1)[par]
                prtb = tb[("prb0", "prb1")[par]]
                avtb = (pd, pe)[par]
                bw(T, avtb)
                T.wait(hs[hh]["pe"])
                T.wait(tok_vring)
                for j, cr_t in enumerate(ctx_tiles):
                    cslot = cr_t % ACTX
                    h = T.op("matmul", avtb.ap[:, 0:65], ptbuf[:, j, :],
                             vat[:, cslot, hh * 65:hh * 65 + 65],
                             start=(j == 0), stop=(j == 2))
                hs[hh]["av"] = h.inc()
                ew(hs[hh]["av"], avtb)
                er(hs[hh]["av"], prtb)

            def _rc1(hh):
                avtb = (pd, pe)[hh % 2]
                br(V, avtb)
                V.wait(hs[hh]["av"])
                V.wait(tb["gsc"].wr)
                h = V.op("reciprocal", out=sml[:, 9, hh:hh + 1],
                         in_=avtb.ap[:, 64:65])
                hs[hh]["rc1"] = h.inc()
                er(hs[hh]["rc1"], avtb)

            def _rc2(hh):
                h = V.op("tensor_mul", out=sml[:, 9, hh:hh + 1],
                         in0=sml[:, 9, hh:hh + 1], in1=gsc[:, 1, hh:hh + 1])
                hs[hh]["rc2"] = h.inc()

            def _oe(hh):
                avtb = (pd, pe)[hh % 2]
                bw(S, tb["otk"])
                br(S, avtb)
                S.wait(hs[hh]["rc2"])
                h = S.op("activation", out=otk[:, hh * 64:(hh + 1) * 64],
                         in_=avtb.ap[:, 0:64], func=ACT.Identity,
                         scale=sml[:, 9, hh:hh + 1])
                hs[hh]["oe"] = h.inc()
                er(hs[hh]["oe"], avtb)
                ew(hs[hh]["oe"], tb["otk"])

            _sc(0)
            _sc(1)
            _mx(0)
            _mx(1)
            for hh in range(AH):
                _nm(hh)
                _ex(hh)
                _pt(hh)
                _pe(hh)
                _av(hh)
                _rc1(hh)
                if hh + 2 < AH:
                    _sc(hh + 2)
                    _mx(hh + 2)
                if hh >= 1:
                    _rc2(hh - 1)
                    _oe(hh - 1)
            V.wait(V.tok())
            _rc2(AH - 1)
            _oe(AH - 1)
            bw(T, ptr)
            T.wait(tb["otk"].wr)
            for j in range(8):
                h = T.op("transpose", PSTR[:, j * 128:(j + 1) * 128],
                         otk[:, j * 128:(j + 1) * 128], id_b)
            ttok = h.inc()
            ew(ttok, ptr)
            er(ttok, tb["otk"])
            bw(S, tb["otb"])
            br(S, ptr)
            stok = S.op("copy", out=otb,
                        in_=PSTR.rearrange("p (a n) -> p a n", n=128)).inc()
            er(stok, ptr)
            ew(stok, tb["otb"])
            tok_oT = stok
            bw(T, pc0, pc1)
            T.wait(tok_oT)
            T.wait(wout_tok)
            for dmc in range(8):
                for ic in range(8):
                    h = T.op("matmul", PSC[:, dmc * 128:(dmc + 1) * 128],
                             wo8[:, ic, dmc * 128:(dmc + 1) * 128],
                             otb[:, ic, :], start=(ic == 0), stop=(ic == 7))
            tok_wout = h.inc()
            ew(tok_wout, pc0, pc1)
            er(tok_wout, tb["otb"])
            last_wout_tok = tok_wout
            scol = 1 + (r - ch["sbase"]) * 128
            bw(V, tb["znt"])
            br(V, pc0)
            V.wait(tok_wout)
            for dmc in range(8):
                h = V.op("tensor_tensor", out=znt[:, dmc * 128:(dmc + 1) * 128],
                         in0=PSC[:, dmc * 128:(dmc + 1) * 128],
                         in1=strm[:, dmc, scol:scol + 128], op=ALU.add)
            vtok = h.inc()
            er(vtok, pc0, pc1)
            ew(vtok, tb["znt"])
            bw(T, pb0, pb1)
            T.wait(vtok)
            for kc in range(8):
                h = T.op("transpose", PSB[:, kc * 128:(kc + 1) * 128],
                         znt[:, kc * 128:(kc + 1) * 128], id_f)
            ttok = h.inc()
            ew(ttok, pb0, pb1)
            er(ttok, tb["znt"])
            ib = r % 2
            stg = ostg[:, ib, :]
            stb = tb[("ostg0", "ostg1")[ib]]
            xtb = tb[("xt0", "xt1")[ib]]
            bw(Y, xtb)
            ytok = Y.op("dma_start", out=(xt0, xt1)[ib],
                        in_=xin[r * 128:(r + 1) * 128, :]).inc(
                            16, ("yx0", "yx1")[ib])
            ew(ytok, xtb)
            bw(V, stb)
            br(V, pb0)
            V.wait(ytok)
            vtok = V.op("tensor_add", out=stg, in0=PSB, in1=(xt0, xt1)[ib]).inc()
            er(vtok, pb0, pb1)
            er(vtok, xtb)
            ew(vtok, stb)
            Y.wait(vtok)
            orow = (r - OWN0) * 128
            ytok = Y.op("dma_start", out=yout[orow:orow + 128, :],
                        in_=stg).inc(16, ("yo0", "yo1")[ib])
            er(ytok, stb)
        er(last_qkv_tok, *WAr)
        if last_wout_tok is not None:
            er(last_wout_tok, W2wo[0], W2wo[1], W2g[0], W2g[1])

    def do_taps(label):
        here = dict(strm=strm, xn_bf=xn_bf, y1a=y1a, inv32=inv32, kt1=kt1,
                    v1r=v1r, kt2=kt2, v2r=v2r, kta=kta, vat=vat, otk=otk,
                    qtb=qtb, ubf0=ubf_l[0], ubf1=ubf_l[1], krot=krot, gsc=gsc, ninv32=ninv32,
                    qrot=qrot, sml=sml, rope=rope, WA=WA, W2=W2)
        for lab, bname in taps:
            if lab != label:
                continue
            ap = here[bname]
            tname = f"tap_{lab}_{bname}"
            shp = list(ap.shape)
            tout = nc.dram_tensor(tname, shp, F32, kind="ExternalOutput").ap()
            for e_ in (T, V, S):
                pass
            G.wait(("t", c.count["t"]))
            G.wait(("v", c.count["v"]))
            G.wait(("s", c.count["s"]))
            ttok = G.op("dma_start", out=tout, in_=ap).inc(16, "yi")
            for e_ in (T, V, S, Y):
                e_.wait(ttok)

    emit_init()
    for ch in CHUNKS:
        if nphase >= 2:
            wq0 = load_wqkv_ffn(0)
            wo0 = load_wo_ffn(0)
        ci = CHUNKS.index(ch)
        if nphase >= 1:
            emit_A(ch)
            do_taps(f"A{ci}")
        if nphase >= 2:
            emit_wave_ffn(ch, 0, wq0, wo0)
            do_taps(f"B{ci}")
        if nphase >= 3:
            emit_ffn(ch, 0)
            do_taps(f"C{ci}")
        if nphase >= 4:
            wq1 = load_wqkv_ffn(1)
            wo1 = load_wo_ffn(1)
            emit_wave_ffn(ch, 1, wq1, wo1)
            do_taps(f"D{ci}")
        if nphase >= 5:
            emit_ffn(ch, 1)
            do_taps(f"E{ci}")
        if nphase >= 6:
            watok, wgtok = load_watt()
            wouttok = load_wout()
            emit_att(ch, watok, wgtok, wouttok)
            do_taps(f"F{ci}")
    for s_ in ("yo0", "yo1"):
        Y.wait((s_, c.count[s_]))
        V.wait((s_, c.count[s_]))

    with contextlib.ExitStack() as st:
        for name in ("t", "v", "s") + DMA_SEMS:
            c.sem[name] = st.enter_context(nc.semaphore(f"sem_{name}"))
        block = st.enter_context(nc.Block())

        def mk_replay(rec):
            def f(eng):
                for item in rec.ops:
                    if item[0] == "w":
                        eng.wait_ge(c.sem[item[1]], item[2])
                    else:
                        m, a, kw, incs = item[1]
                        inst = getattr(eng, m)(*a, **kw)
                        for s, n in incs:
                            inst = inst.then_inc(c.sem[s], n)
            return f

        block.tensor(mk_replay(c.T))
        block.vector(mk_replay(c.V))
        block.scalar(mk_replay(c.S))
        block.sync(mk_replay(c.Y))
        block.gpsimd(mk_replay(c.G))
    return nc


def _lhs_chunks(w):
    w = np.ascontiguousarray(np.asarray(w, np.float32))
    K, N = w.shape
    return np.ascontiguousarray(w.reshape(K // 128, 128, N).transpose(1, 0, 2))


def _prep_weights(waqkv, wagat, waout, wfqkv, wfo, wf1, wf2):
    d = {}
    d["w_aqkv"] = _lhs_chunks(waqkv)
    d["w_agat"] = _lhs_chunks(wagat)
    d["w_aout"] = _lhs_chunks(waout)
    d["w_fqkv"] = np.stack([_lhs_chunks(np.asarray(wfqkv, np.float32)[l])
                            for l in range(2)])
    d["w_fo"] = np.stack([_lhs_chunks(np.asarray(wfo, np.float32)[l])
                          for l in range(2)])
    w1 = np.asarray(wf1, np.float32)
    w2 = np.asarray(wf2, np.float32)
    w1g = np.zeros((2, NGRP, 128, 8, 512), np.float32)
    w2g = np.zeros((2, NGRP, 128, 2, DM), np.float32)
    for l in range(2):
        w1c = _lhs_chunks(w1[l])
        for g in range(NGRP):
            c0 = 256 * g
            cw = min(256, HID - c0)
            w1g[l, g, :, :, 0:cw] = w1c[:, :, c0:c0 + cw]
            w1g[l, g, :, :, 256:256 + cw] = w1c[:, :, HID + c0:HID + c0 + cw]
            r0 = 2 * g * 128
            rw = min(256, HID - r0)
            blk = w2[l, r0:r0 + rw]
            pad = np.zeros((256, DM), np.float32)
            pad[:rw] = blk
            w2g[l, g] = pad.reshape(2, 128, DM).transpose(1, 0, 2)
    d["w_f1"] = np.ascontiguousarray(w1g)
    d["w_f2"] = np.ascontiguousarray(w2g)
    return d


def make_in_maps(x, consts, weights):
    rope_h = [_rope_tables_for_core(h) for h in (0, 1)]
    masks_h = [_masks_for_core(h) for h in (0, 1)]
    in_maps = []
    for core in range(8):
        seq, half = core // 2, core % 2
        if half == 0:
            xi = np.concatenate([np.zeros((HALO, DM), np.float32),
                                 x[seq, 0:P]], axis=0)
        else:
            xi = np.ascontiguousarray(x[seq, P - HALO:SEQ])
        mf, ma = masks_h[half]
        im = dict(xin=xi, rope_in=rope_h[half], maskf_in=mf, maska_in=ma,
                  **consts, **weights)
        in_maps.append(im)
    return in_maps


_NC_CACHE = {}


def _get_program():
    if "nc" not in _NC_CACHE:
        _NC_CACHE["nc"] = build_program()
    return _NC_CACHE["nc"]


def kernel(L0_attn_x, L0_attn_gamma, L0_attn_W_qkv, L0_attn_W_gates,
           L0_attn_b_gates, L0_attn_W_out, L0_ffn_wqkv, L0_ffn_q_scale,
           L0_ffn_k_scale, L0_ffn_wo, L0_ffn_ln_g, L0_ffn_ln_b, L0_ffn_w1,
           L0_ffn_b1, L0_ffn_w2, L0_ffn_b2):
    x = np.ascontiguousarray(np.asarray(L0_attn_x, dtype=np.float32))
    consts = _const_inputs(L0_ffn_q_scale, L0_ffn_k_scale, L0_ffn_ln_g,
                           L0_ffn_ln_b, L0_ffn_b1, L0_ffn_b2, L0_attn_b_gates,
                           L0_attn_gamma)
    weights = _prep_weights(L0_attn_W_qkv, L0_attn_W_gates, L0_attn_W_out,
                            L0_ffn_wqkv, L0_ffn_wo, L0_ffn_w1, L0_ffn_w2)
    in_maps = make_in_maps(x, consts, weights)
    nc = _get_program()
    res = run_bass_kernel_spmd(nc, in_maps, core_ids=list(range(8)))
    out = np.empty((NSEQ, SEQ, DM), dtype=np.float32)
    for core in range(8):
        seq, half = core // 2, core % 2
        out[seq, half * P:(half + 1) * P] = np.asarray(res.results[core]["yout"])
    return out


# version 07
# speedup vs baseline: 1.1908x; 1.1908x over previous
import contextlib
import math

import numpy as np

import concourse.bass as bass
import concourse.mybir as mybir
from concourse.bass_utils import run_bass_kernel_spmd

F32 = mybir.dt.float32
BF16 = mybir.dt.bfloat16
ACT = mybir.ActivationFunctionType
ALU = mybir.AluOpType
AX = mybir.AxisListType

DM = 1024
SEQ = 4096
NSEQ = 4
P = 2048
HALO = 1280
XROWS = HALO + P
NT = XROWS // 128
OWN0 = HALO // 128

FH = 8
AH = 16
DH = 64
HID = 2730
NHC = 22
NGRP = 11
FCTX = 5
RING = 6
ACTX = 3
MASKVAL = -1.0e30
LN32 = math.log(32.0)
LN8 = math.log(8.0)


def _chunks():
    c0 = dict(tiles=list(range(0, 10)), sbase=5, xnbase=1,
              b_kv=[1, 2, 3, 4], b_full=[5, 6, 7, 8, 9],
              c_tiles=[5, 6, 7, 8, 9],
              d_kv=[6, 7, 8, 9], d_full=[],
              e_tiles=[],
              f_kv=[8, 9], f_full=[])
    def own(t0):
        t = list(range(t0, t0 + 8))
        return dict(tiles=t, sbase=t0, xnbase=t0, b_kv=[], b_full=t,
                    c_tiles=t, d_kv=[], d_full=t, e_tiles=t, f_kv=[], f_full=t)
    return [c0, own(10), own(18)]


CHUNKS = _chunks()


def _rope_tables_for_core(half):
    n = SEQ
    hi = np.arange(0, DH, 2, dtype=np.float64)
    inv_freq = 10000.0 ** (-hi / DH)
    base = (hi + 0.4 * DH) / (1.4 * DH)
    out = np.zeros((3, 128, 1152), dtype=np.float64)
    for ci, ch in enumerate(CHUNKS):
        wave = sorted(set(ch["b_kv"] + ch["b_full"]))
        for sj, r in enumerate(wave):
            g0 = r * 128 - HALO + (0 if half == 0 else P)
            if half == 0 and g0 < 0:
                continue
            t = g0 + np.arange(128, dtype=np.float64)
            fr = t[:, None] * inv_freq[None, :]
            power = (t - n // 2) / 256.0
            scale = base[None, :] ** power[:, None]
            cs, sn = np.cos(fr), np.sin(fr)
            for ti, tab in enumerate((cs * scale, sn * scale,
                                      cs / scale, sn / scale)):
                out[ci, :, ti * 288 + sj * 32: ti * 288 + sj * 32 + 32] = tab
    return np.ascontiguousarray(out.astype(np.float32))


def _masks_for_core(half):
    qi = np.arange(128)[:, None]

    def mk(nblk, win, nvar):
        ctxw = nblk * 128
        j = np.arange(ctxw)[None, :]
        keep = (j >= qi) & (j <= qi + win)
        m = np.zeros((128, nvar * ctxw), dtype=np.float32)
        for v in range(nvar):
            mm = keep.copy()
            if half == 0 and (nvar - 1 - v) > 0:
                mm[:, :(nvar - 1 - v) * 128] = False
            m[:, v * ctxw:(v + 1) * ctxw] = np.where(mm, 0.0, MASKVAL)
        return np.ascontiguousarray(m)

    return mk(FCTX, 512, 5), mk(ACTX, 256, 3)


def _const_inputs(q_scale, k_scale, ln_g, ln_b, b1_, b2_, b_gates, gamma):
    d = {}
    sc = np.zeros((2, 2, 64), np.float32)
    for l in range(2):
        sc[l] = np.stack([np.asarray(q_scale[l], np.float32),
                          np.asarray(k_scale[l], np.float32)])
    d["scales"] = np.ascontiguousarray(
        np.broadcast_to(sc.reshape(1, 256), (128, 256)))
    lnp = np.zeros((128, 2, 2, 8), np.float32)
    for l in range(2):
        lnp[:, l, 0, :] = np.asarray(ln_g[l], np.float32).reshape(8, 128).T
        lnp[:, l, 1, :] = np.asarray(ln_b[l], np.float32).reshape(8, 128).T
    d["lnp"] = np.ascontiguousarray(lnp.reshape(128, 32))
    b2 = np.zeros((128, 2, 8), np.float32)
    for l in range(2):
        b2[:, l, :] = np.asarray(b2_[l], np.float32).reshape(8, 128).T
    d["b2c"] = np.ascontiguousarray(b2.reshape(128, 16))
    b1a = np.zeros((128, 2, NHC), np.float32)
    b1g = np.zeros((128, 2, NHC), np.float32)
    for l in range(2):
        b1 = np.asarray(b1_[l], np.float32)
        for arr, seg in ((b1a, b1[:HID]), (b1g, b1[HID:])):
            pad = np.zeros(NHC * 128, np.float32)
            pad[:HID] = seg
            arr[:, l, :] = pad.reshape(NHC, 128).T
    d["b1a"] = np.ascontiguousarray(b1a.reshape(128, 2 * NHC))
    d["b1g"] = np.ascontiguousarray(b1g.reshape(128, 2 * NHC))
    d["bgv"] = np.ascontiguousarray(
        np.broadcast_to(np.asarray(b_gates, np.float32).reshape(1, 16), (128, 16)))
    d["gmc"] = np.ascontiguousarray(np.asarray(gamma, np.float32).reshape(8, 128).T)
    d["ident"] = np.eye(128, dtype=np.float32)
    return d


class Rec:
    def __init__(self, ctx, sem_name):
        self.ctx = ctx
        self.sem = sem_name
        self.ops = []

    def wait(self, tok):
        if tok is None:
            return
        sem, val = tok
        if val > 0:
            self.ops.append(("w", sem, val))

    def waits(self, toks):
        best = {}
        for t in toks:
            if t is None:
                continue
            s, v = t
            best[s] = max(best.get(s, 0), v)
        for s, v in best.items():
            self.wait((s, v))

    def op(self, method, *a, **kw):
        r = [method, a, kw, []]
        self.ops.append(("o", r))
        return _OpH(self, r)

    def tok(self):
        return (self.sem, self.ctx.count[self.sem])


class _OpH:
    def __init__(self, rec, r):
        self.rec = rec
        self.r = r

    def inc(self, n=1, sem=None):
        sem = sem or self.rec.sem
        self.r[3].append((sem, n))
        self.rec.ctx.count[sem] += n
        return (sem, self.rec.ctx.count[sem])


class TB:

    def __init__(self, ap=None, name=""):
        self.ap = ap
        self.name = name
        self.wr = None
        self.rd = []


def bw(rec, *tbs):
    for t in tbs:
        rec.waits(t.rd)


def ew(tok, *tbs):
    for t in tbs:
        t.wr = tok
        t.rd = []


def br(rec, *tbs):
    for t in tbs:
        rec.wait(t.wr)


def er(tok, *tbs):
    for t in tbs:
        t.rd.append(tok)


DMA_SEMS = ("yi", "yx0", "yx1", "yr", "yo0", "yo1",
            "ga0", "ga1", "ga2", "ga3", "gw0", "gw1", "go0", "go1")


class Ctx:
    def __init__(self, nc):
        self.nc = nc
        self.count = {k: 0 for k in ("t", "v", "s", "y", "g") + DMA_SEMS}
        self.T = Rec(self, "t")
        self.V = Rec(self, "v")
        self.S = Rec(self, "s")
        self.Y = Rec(self, "y")
        self.G = Rec(self, "g")
        self.sem = {}


def shift_view(buf, kc, col, n=128):
    off = col - 1 if kc >= 4 else col
    return buf[:, kc, off:off + n]


def build_program(nphase=99, taps=()):
    nc = bass.Bass("TRN2", target_bir_lowering=False,
                   detect_race_conditions=False)
    c = Ctx(nc)
    T, V, S, Y, G = c.T, c.V, c.S, c.Y, c.G

    xin = nc.dram_tensor("xin", [XROWS, DM], F32, kind="ExternalInput").ap()
    yout = nc.dram_tensor("yout", [P, DM], F32, kind="ExternalOutput").ap()
    rope_in = nc.dram_tensor("rope_in", [3, 128, 1152], F32, kind="ExternalInput").ap()
    maskf_in = nc.dram_tensor("maskf_in", [128, 5 * 640], F32, kind="ExternalInput").ap()
    maska_in = nc.dram_tensor("maska_in", [128, 3 * 384], F32, kind="ExternalInput").ap()
    ident_in = nc.dram_tensor("ident", [128, 128], F32, kind="ExternalInput").ap()
    scales_in = nc.dram_tensor("scales", [128, 256], F32, kind="ExternalInput").ap()
    lnp_in = nc.dram_tensor("lnp", [128, 32], F32, kind="ExternalInput").ap()
    b2c_in = nc.dram_tensor("b2c", [128, 16], F32, kind="ExternalInput").ap()
    b1a_in = nc.dram_tensor("b1a", [128, 2 * NHC], F32, kind="ExternalInput").ap()
    b1g_in = nc.dram_tensor("b1g", [128, 2 * NHC], F32, kind="ExternalInput").ap()
    bgv_in = nc.dram_tensor("bgv", [128, 16], F32, kind="ExternalInput").ap()
    gmc_in = nc.dram_tensor("gmc", [128, 8], F32, kind="ExternalInput").ap()
    w_aqkv = nc.dram_tensor("w_aqkv", [128, 8, 3072], F32, kind="ExternalInput").ap()
    w_agat = nc.dram_tensor("w_agat", [128, 8, 16], F32, kind="ExternalInput").ap()
    w_aout = nc.dram_tensor("w_aout", [128, 8, DM], F32, kind="ExternalInput").ap()
    w_fqkv = nc.dram_tensor("w_fqkv", [2, 128, 8, 1536], F32, kind="ExternalInput").ap()
    w_fo = nc.dram_tensor("w_fo", [2, 128, 4, DM], F32, kind="ExternalInput").ap()
    w_f1 = nc.dram_tensor("w_f1", [2, NGRP, 128, 8, 512], F32, kind="ExternalInput").ap()
    w_f2 = nc.dram_tensor("w_f2", [2, NGRP, 128, 2, DM], F32, kind="ExternalInput").ap()

    A = nc.alloc_sbuf_tensor
    xn_bf = A("xn_bf", [128, 8, 9 * 128 + 1], BF16).ap()
    strm = A("strm", [128, 8, 8 * 128 + 1], F32).ap()
    y1arena = A("y1arena", [128, 4100], F32).ap()
    y1a = y1arena.bitcast(BF16).rearrange("p (a b) -> p a b", b=1025)
    kt1 = A("kt1", [128, 4, RING * 128], BF16).ap()
    v1r = A("v1r", [128, RING, FH * 65], BF16).ap()
    kt2 = A("kt2", [128, 4, RING * 128], BF16).ap()
    v2r = A("v2r", [128, RING, FH * 65], BF16).ap()
    kta = A("kta", [128, 8, ACTX * 128], BF16).ap()
    vat = A("vat", [128, ACTX, AH * 65], BF16).ap()
    rope = A("rope", [128, 4, 9, 32], F32).ap()
    maskf = A("maskf", [128, 5, 640], BF16).ap()
    maska = A("maska", [128, 3, 384], BF16).ap()
    id_f = A("id_f", [128, 128], F32).ap()
    id_b = A("id_b", [128, 128], BF16).ap()
    scl = A("scl", [128, 2, 2, 64], F32).ap()
    lnp = A("lnp_sb", [128, 2, 2, 8], F32).ap()
    b2c = A("b2c_sb", [128, 2, 8], F32).ap()
    b1a = A("b1a_sb", [128, 2, NHC], F32).ap()
    b1g = A("b1g_sb", [128, 2, NHC], F32).ap()
    bgv = A("bgv_sb", [128, 16], F32).ap()
    gmc = A("gmc_sb", [128, 8], F32).ap()
    wag = A("wag", [128, 8, 16], BF16).ap()
    inv32 = A("inv32", [128, NT], F32).ap()
    xn_cr = A("xn_cr", [128, 8], BF16).ap()
    y1b_cr = A("y1b_cr", [128, 8], BF16).ap()
    x1f_cr = A("x1f_cr", [128, 8], F32).ap()
    y1f_cr = A("y1f_cr", [128, 8], F32).ap()
    xt0 = A("xt0", [128, DM], F32).ap()
    xt1 = A("xt1", [128, DM], F32).ap()
    znt = A("znt", [128, DM], F32).ap()
    prb0 = A("prb0", [128, 640], BF16).ap()
    prb1 = A("prb1", [128, 640], BF16).ap()
    ptb0 = prb0.rearrange("p (a n) -> p a n", n=128)
    ptb1 = prb1.rearrange("p (a n) -> p a n", n=128)
    qtb = A("qtb", [128, 8, 128], BF16).ap()
    otk = A("otk", [128, DM], BF16).ap()
    otb = A("otb", [128, 8, 128], BF16).ap()
    krot = A("krot", [128, 512], BF16).ap()
    qrot = A("qrot", [128, 512], BF16).ap()
    sml = A("sml", [128, 10, 16], F32).ap()
    gsc = A("gsc", [128, 3, 16], F32).ap()
    eps5 = A("eps5", [128, 1], F32).ap()
    tiny = A("tiny", [128, 1], F32).ap()
    y1f_pv = A("y1f_pv", [128, 8], F32).ap()
    ninv32 = A("ninv32", [128, NT], F32).ap()
    inv32q = A("inv32q", [128, NT], F32).ap()
    WA = A("WA", [128, 8, 3072], BF16).ap()
    W2 = A("W2", [128, 8, DM], BF16).ap()
    ostg = y1arena[:, 0:2048].rearrange("p (a b) -> p a b", a=2)
    rem = nc.sbuf_bytes_remaining
    assert rem >= 0, f"SBUF over budget by {-rem} bytes"

    PS = nc.alloc_psum_tensor
    PSA = PS("PSA", [128, 512], F32).ap()
    PSTR = PS("PSTR", [128, 1024], BF16).ap()
    PSB = PS("PSB", [128, 1024], F32).ap()
    PSC = PS("PSC", [128, 1024], F32).ap()
    PSD = PS("PSD", [128, 512], F32).ap()
    PSE = PS("PSE", [128, 512], F32).ap()
    assert nc.psum_banks_remaining >= 0
    pa = TB(PSA, "pa")
    ptr = TB(PSTR, "ptr")
    pb0, pb1 = TB(None, "pb0"), TB(None, "pb1")
    pc0, pc1 = TB(None, "pc0"), TB(None, "pc1")
    pd = TB(PSD, "pd")
    pe = TB(PSE, "pe")

    tb = {n: TB(None, n) for n in
          ["xt0", "xt1", "znt", "prb0", "prb1",
           "qtb", "qtb0", "qtb1", "otk", "otb", "krot", "qrot", "ubf",
           "ostg0", "ostg1", "gsc", "lnh", "y1bf", "rope"]}
    WAr = [TB(None, f"WAr{i}") for i in range(4)]
    W2wo = [TB(None, "W2wo0"), TB(None, "W2wo1")]
    W2g = [TB(None, "W2g0"), TB(None, "W2g1")]
    W2flat = W2.rearrange("p a b -> p (a b)")
    ubf_l = [W2flat[:, 0:2050].rearrange("p (a b) -> p a b", a=2),
             W2flat[:, 4096:4096 + 2050].rearrange("p (a b) -> p a b", a=2)]
    phase_toks = {"xnbf": None, "strm": None}

    def emit_init():
        Y.op("dma_start", out=id_f, in_=ident_in).inc(16, "yi")
        Y.op("dma_start", out=scl, in_=scales_in.rearrange(
            "p (l t d) -> p l t d", l=2, t=2)).inc(16, "yi")
        Y.op("dma_start", out=lnp, in_=lnp_in.rearrange(
            "p (l i c) -> p l i c", l=2, i=2)).inc(16, "yi")
        Y.op("dma_start", out=b2c, in_=b2c_in.rearrange("p (l c) -> p l c", l=2)).inc(16, "yi")
        Y.op("dma_start", out=b1a, in_=b1a_in.rearrange("p (l c) -> p l c", l=2)).inc(16, "yi")
        Y.op("dma_start", out=b1g, in_=b1g_in.rearrange("p (l c) -> p l c", l=2)).inc(16, "yi")
        Y.op("dma_start", out=bgv, in_=bgv_in).inc(16, "yi")
        Y.op("dma_start", out=gmc, in_=gmc_in).inc(16, "yi")
        wa_f32 = WA.rearrange("p a b -> p (a b)").bitcast(F32)
        mf_stage = wa_f32[:, 0:3200]
        ma_stage = wa_f32[:, 3200:3200 + 1152]
        Y.op("dma_start", out=mf_stage, in_=maskf_in).inc(16, "yi")
        tok_m = Y.op("dma_start", out=ma_stage, in_=maska_in).inc(16, "yi")
        V.wait(tok_m)
        V.op("tensor_copy", out=maskf.rearrange("p a b -> p (a b)"), in_=mf_stage)
        V.op("tensor_copy", out=maska.rearrange("p a b -> p (a b)"), in_=ma_stage)
        V.op("tensor_copy", out=id_b, in_=id_f)
        for ap in (kt1, kt2, kta):
            V.op("memset", ap, 0.0)
        for ap in (v1r, v2r, vat):
            v4 = ap.rearrange("p s (h e) -> p s h e", e=65)
            V.op("memset", v4[:, :, :, 0:64], 0.0)
            V.op("memset", v4[:, :, :, 64:65], 1.0)
        for ap in (xn_cr, y1b_cr, x1f_cr, y1f_cr, y1f_pv, inv32):
            V.op("memset", ap, 0.0)
        V.op("memset", eps5, 1e-5)
        V.op("memset", tiny, 1e-27)
        vtok = V.op("memset", sml, 0.0).inc()
        T.wait(vtok)
        S.wait(vtok)
        G.wait(vtok)
        ytok = ("yi", c.count["yi"])
        T.wait(ytok)
        S.wait(ytok)
        V.wait(ytok)
        for r in WAr:
            r.rd = [vtok]

    def load_wqkv_ffn(layer):
        bw(G, WAr[0], WAr[1])
        tok = G.op("dma_start", out=WA[:, :, 0:1536],
                   in_=w_fqkv[layer]).inc(16, "ga0")
        ew(tok, WAr[0], WAr[1])
        return tok

    def load_wo_ffn(layer):
        reg = W2wo[layer]
        bw(G, reg, W2g[0], W2g[1])
        dst = W2flat[:, layer * 4096:(layer + 1) * 4096].rearrange(
            "p (k n) -> p k n", k=4)
        tok = G.op("dma_start", out=dst,
                   in_=w_fo[layer]).inc(16, ("go0", "go1")[layer])
        ew(tok, reg)
        return tok

    def load_wgrp(layer, g):
        reg = WAr[g % 4]
        bw(G, reg)
        c0_ = 2 * g * 128
        cw = min(256, HID - c0_)
        base = WA[:, :, 768 * (g % 4):768 * (g % 4) + 768]
        tok1 = G.op("dma_start", out=base[:, :, 0:512],
                    in_=w_f1[layer, g]).inc(16, f"ga{g % 4}")
        ew(tok1, reg)
        w2reg = W2g[g % 2]
        bw(G, w2reg, W2wo[1 - layer])
        half0 = 4096 * (1 - layer)
        w2slot = W2flat[:, half0 + (g % 2) * 2048: half0 + (g % 2) * 2048 + 2048]
        w2slot = w2slot.rearrange("p (k n) -> p k n", k=2)
        tok2 = G.op("dma_start", out=w2slot,
                    in_=w_f2[layer, g]).inc(16, f"gw{g % 2}")
        ew(tok2, w2reg)
        return reg, w2reg, base, w2slot, cw

    def load_watt():
        bw(G, *WAr)
        tok = G.op("dma_start", out=WA, in_=w_aqkv).inc(16, "ga0")
        ew(tok, *WAr)
        tokg = G.op("dma_start", out=wag, in_=w_agat).inc(16, "ga1")
        return tok, tokg

    def load_wout():
        bw(G, W2wo[0], W2wo[1], W2g[0], W2g[1])
        dst = W2flat.rearrange("p (k n) -> p k n", k=8)
        tok = G.op("dma_start", out=dst, in_=w_aout).inc(16, "go0")
        ew(tok, W2wo[0], W2wo[1], W2g[0], W2g[1])
        return tok

    def emit_A(ch):
        for i, r in enumerate(ch["tiles"]):
            xt = (xt0, xt1)[i % 2]
            xtb = tb[("xt0", "xt1")[i % 2]]
            bw(Y, xtb)
            ytok = Y.op("dma_start", out=xt,
                        in_=xin[r * 128:(r + 1) * 128, :]).inc(
                            16, ("yx0", "yx1")[i % 2])
            ew(ytok, xtb)
            V.wait(ytok)
            bw(V, tb["znt"])
            V.op("bn_stats", out=sml[:, 0, 0:6], in_=xt[:, 0:512])
            vtok = V.op("bn_stats", out=sml[:, 0, 6:12], in_=xt[:, 512:1024]).inc()
            V.wait(vtok)
            vtok = V.op("bn_aggr", out=sml[:, 1, 0:2], in_=sml[:, 0, 0:12]).inc()
            S.wait(vtok)
            stok = S.op("activation", out=sml[:, 1, 2:3], in_=sml[:, 1, 1:2],
                        func=ACT.Ln, bias=eps5).inc()
            S.wait(stok)
            stok = S.op("activation", out=sml[:, 1, 3:4], in_=sml[:, 1, 2:3],
                        func=ACT.Exp, scale=-0.5).inc()
            V.wait(stok)
            vtok = V.op("tensor_scalar", out=sml[:, 1, 4:5], in0=sml[:, 1, 0:1],
                        scalar1=sml[:, 1, 3:4], scalar2=-1.0,
                        op0=ALU.mult, op1=ALU.mult).inc()
            bw(S, tb["znt"])
            S.wait(vtok)
            stok = S.op("activation", out=znt, in_=xt, func=ACT.Identity,
                        bias=sml[:, 1, 4:5], scale=sml[:, 1, 3:4]).inc()
            ew(stok, tb["znt"])
            er(stok, xtb)
            V.wait(stok)
            V.op("bn_stats", out=sml[:, 2, 0:6], in_=znt[:, 0:512])
            vtok = V.op("bn_stats", out=sml[:, 2, 6:12], in_=znt[:, 512:1024]).inc()
            V.wait(vtok)
            vtok = V.op("bn_aggr", out=sml[:, 3, 0:2], in_=sml[:, 2, 0:12]).inc()
            S.wait(vtok)
            stok = S.op("activation", out=sml[:, 3, 2:3], in_=sml[:, 3, 1:2],
                        func=ACT.Ln, bias=tiny).inc()
            S.wait(stok)
            stok = S.op("activation", out=inv32[:, r:r + 1], in_=sml[:, 3, 2:3],
                        func=ACT.Exp, scale=-0.5).inc()
            V.wait(stok)
            V.op("tensor_scalar_mul", out=ninv32[:, r:r + 1],
                 in0=inv32[:, r:r + 1], scalar1=-1.0)
            V.op("tensor_scalar_mul", out=inv32q[:, r:r + 1],
                 in0=inv32[:, r:r + 1], scalar1=0.125).inc()
            bw(T, pb0, pb1)
            T.wait(stok)
            for kc in range(8):
                h = T.op("transpose", PSB[:, kc * 128:(kc + 1) * 128],
                         znt[:, kc * 128:(kc + 1) * 128], id_f)
            ttok = h.inc()
            ew(ttok, pb0, pb1)
            er(ttok, tb["znt"])
            pb8 = PSB.rearrange("p (a b) -> p a b", a=8)
            if r >= ch["xnbase"]:
                col = 1 + (r - ch["xnbase"]) * 128
                br(S, pb0)
                stok = S.op("copy", out=xn_bf[:, :, col:col + 128], in_=pb8).inc()
                er(stok, pb0, pb1)
            elif r == ch["xnbase"] - 1:
                br(S, pb0)
                stok = S.op("copy", out=xn_bf[:, :, 0:1],
                            in_=pb8[:, :, 127:128]).inc()
                er(stok, pb0, pb1)
            if r >= ch["sbase"]:
                scol = 1 + (r - ch["sbase"]) * 128
                br(V, pb0)
                V.wait(stok)
                vtok = V.op("tensor_copy", out=strm[:, :, scol:scol + 128],
                            in_=pb8).inc()
                er(vtok, pb0, pb1)
        if ch["tiles"][0] >= OWN0:
            S.op("copy", out=xn_bf[:, :, 0:1], in_=xn_cr.unsqueeze(2))
        last_col = 1 + (ch["tiles"][-1] - ch["xnbase"]) * 128 + 127
        S.wait(S.tok())
        stok = S.op("copy", out=xn_cr.unsqueeze(2),
                    in_=xn_bf[:, :, last_col:last_col + 1]).inc()
        phase_toks["xnbf"] = stok
        phase_toks["strm"] = c.V.tok()

    def emit_wave_ffn(ch, layer, wq_tok, wo_tok):
        kv_tiles = ch["b_kv"] if layer == 0 else ch["d_kv"]
        full_tiles = ch["b_full"] if layer == 0 else ch["d_full"]
        all_tiles = sorted(set(kv_tiles + full_tiles))
        if not all_tiles:
            return
        src_ = xn_bf if layer == 0 else y1a
        src_base = ch["xnbase"] if layer == 0 else ch["sbase"]
        src_tok = phase_toks["xnbf"] if layer == 0 else tb["y1bf"].wr
        ktr, vrr = (kt1, v1r) if layer == 0 else (kt2, v2r)
        kt3 = ktr.rearrange("p a (s n) -> p a s n", n=128)
        v4 = vrr.rearrange("p s (h e) -> p s h e", e=65)
        wq = WA[:, :, 0:1536]
        wo4 = W2flat[:, layer * 4096:(layer + 1) * 4096].rearrange(
            "p (k n) -> p k n", k=4)
        wave_slots = {r: j for j, r in enumerate(
            sorted(set(ch["b_kv"] + ch["b_full"])))}
        bw(Y, tb["rope"])
        rtok = Y.op("dma_start", out=rope.rearrange("p a b e -> p (a b e)"),
                    in_=rope_in[CHUNKS.index(ch)]).inc(16, "yr")
        ew(rtok, tb["rope"])
        T.wait(src_tok)

        ts_ = {r: {} for r in all_tiles}

        def prep_mm(r):
            d = ts_[r]
            full = r in full_tiles
            col = 1 + (r - src_base) * 128
            slot = r % RING
            bw(T, pa)
            T.wait(wq_tok)
            for kc in range(8):
                h = T.op("matmul", PSA, shift_view(src_, kc, col),
                         wq[:, kc, 512:1024], start=(kc == 0), stop=(kc == 7))
            tok = h.inc()
            ew(tok, pa)
            d["qkv_last"] = tok
            bw(S, tb["xt0"])
            br(S, pa)
            stok = S.op("copy", out=xt0[:, 0:512], in_=PSA).inc()
            er(stok, pa)
            ew(stok, tb["xt0"])
            bw(T, pa)
            for kc in range(8):
                h = T.op("matmul", PSA, shift_view(src_, kc, col),
                         wq[:, kc, 1024:1536], start=(kc == 0), stop=(kc == 7))
            tok = h.inc()
            ew(tok, pa)
            d["qkv_last"] = tok
            br(S, pa)
            stok = S.op("copy", out=v4[:, slot, :, 0:64],
                        in_=PSA.rearrange("p (h e) -> p h e", e=64)).inc()
            er(stok, pa)
            d["vring"] = stok
            if full:
                bw(T, pa)
                for kc in range(8):
                    h = T.op("matmul", PSA, shift_view(src_, kc, col),
                             wq[:, kc, 0:512], start=(kc == 0), stop=(kc == 7))
                tok = h.inc()
                ew(tok, pa)
                d["qkv_last"] = tok
                bw(S, tb["xt1"])
                br(S, pa)
                stok = S.op("copy", out=xt1[:, 0:512], in_=PSA).inc()
                er(stok, pa)
                ew(stok, tb["xt1"])

        def prep_vec(r):
            d = ts_[r]
            full = r in full_tiles
            slot = r % RING
            rj = wave_slots[r]
            ksl = 1 if (r - all_tiles[0]) % 2 == 0 else 6
            qsl = ksl + 1
            kf = xt0[:, 0:512]
            kf3 = kf.rearrange("p (h d) -> p h d", d=64)
            kf4 = kf.rearrange("p (h s d) -> p h s d", s=2, d=32)
            V.wait(tb["xt0"].wr)
            bw(V, tb["znt"])
            V.op("tensor_mul", out=znt[:, 0:512], in0=kf, in1=kf)
            vtok = V.op("reduce_sum", out=sml[:, ksl, 0:8],
                        in_=znt[:, 0:512].rearrange("p (h d) -> p h d", d=64),
                        axis=AX.X).inc()
            S.wait(vtok)
            stok = S.op("activation", out=sml[:, ksl, 8:16], in_=sml[:, ksl, 0:8],
                        func=ACT.Ln, bias=tiny).inc()
            S.wait(stok)
            stok = S.op("activation", out=sml[:, ksl, 8:16], in_=sml[:, ksl, 8:16],
                        func=ACT.Exp, scale=-0.5).inc()
            V.wait(stok)
            inv_b = sml[:, ksl, 8:16].unsqueeze(2).broadcast_to([128, 8, 64])
            V.op("tensor_tensor", out=kf3, in0=kf3, in1=inv_b, op=ALU.mult)
            ks_b = scl[:, layer, 1, :].unsqueeze(1).broadcast_to([128, 8, 64])
            vtok = V.op("tensor_tensor", out=kf3, in0=kf3, in1=ks_b,
                        op=ALU.mult).inc()
            V.wait(vtok)
            V.wait(tb["rope"].wr)
            sh = znt[:, 0:512].rearrange("p (h s d) -> p h s d", s=2, d=32)
            V.op("tensor_copy", out=sh[:, :, 0, :], in_=kf4[:, :, 1, :])
            V.op("tensor_copy", out=sh[:, :, 1, :], in_=kf4[:, :, 0, :])
            cs_b = rope[:, 2, rj, :].unsqueeze(1).unsqueeze(1).broadcast_to(
                [128, 8, 2, 32])
            sn_b = rope[:, 3, rj, :].unsqueeze(1).unsqueeze(1).broadcast_to(
                [128, 8, 2, 32])
            V.op("tensor_tensor", out=kf4, in0=kf4, in1=cs_b, op=ALU.mult)
            V.op("tensor_tensor", out=sh, in0=sh, in1=sn_b, op=ALU.mult)
            bw(V, tb["krot"])
            kr4 = krot.rearrange("p (h s d) -> p h s d", s=2, d=32)
            V.op("tensor_sub", out=kr4[:, :, 0, :], in0=kf4[:, :, 0, :],
                 in1=sh[:, :, 0, :])
            vtok = V.op("tensor_add", out=kr4[:, :, 1, :], in0=kf4[:, :, 1, :],
                        in1=sh[:, :, 1, :]).inc()
            ew(vtok, tb["krot"])
            ew(vtok, tb["znt"])
            er(vtok, tb["xt0"], tb["rope"])
            bw(T, ptr)
            T.wait(vtok)
            for j in range(4):
                h = T.op("transpose", PSTR[:, j * 128:(j + 1) * 128],
                         krot[:, j * 128:(j + 1) * 128], id_b)
            ttok = h.inc()
            ew(ttok, ptr)
            er(ttok, tb["krot"])
            br(S, ptr)
            stok = S.op("copy", out=kt3[:, :, slot, :],
                        in_=PSTR[:, 0:512].rearrange("p (a n) -> p a n",
                                                     n=128)).inc()
            er(stok, ptr)
            d["kring"] = stok
            if not full:
                return
            qpar = (r - all_tiles[0]) % 2
            qtb_tb = tb[("qtb0", "qtb1")[qpar]]
            qtv = qtb[:, 4 * qpar:4 * qpar + 4, :]
            d["qsl"] = qsl
            qf = xt1[:, 0:512]
            qf3 = qf.rearrange("p (h d) -> p h d", d=64)
            qf4 = qf.rearrange("p (h s d) -> p h s d", s=2, d=32)
            V.wait(tb["xt1"].wr)
            bw(V, tb["znt"])
            V.op("tensor_mul", out=znt[:, 0:512], in0=qf, in1=qf)
            vtok = V.op("reduce_sum", out=sml[:, qsl, 0:8],
                        in_=znt[:, 0:512].rearrange("p (h d) -> p h d", d=64),
                        axis=AX.X).inc()
            S.wait(vtok)
            stok = S.op("activation", out=sml[:, qsl, 8:16], in_=sml[:, qsl, 0:8],
                        func=ACT.Ln, scale=1.0 / 64.0, bias=tiny).inc()
            S.wait(stok)
            stok = S.op("activation", out=sml[:, qsl, 8:16],
                        in_=sml[:, qsl, 8:16], func=ACT.Exp, scale=-0.5).inc()
            V.wait(stok)
            invq_b = sml[:, qsl, 8:16].unsqueeze(2).broadcast_to([128, 8, 64])
            V.op("tensor_tensor", out=qf3, in0=qf3, in1=invq_b, op=ALU.mult)
            qs_b = scl[:, layer, 0, :].unsqueeze(1).broadcast_to([128, 8, 64])
            vtok = V.op("tensor_tensor", out=qf3, in0=qf3, in1=qs_b,
                        op=ALU.mult).inc()
            V.wait(vtok)
            sh2 = znt[:, 0:512].rearrange("p (h s d) -> p h s d", s=2, d=32)
            V.op("tensor_copy", out=sh2[:, :, 0, :], in_=qf4[:, :, 1, :])
            V.op("tensor_copy", out=sh2[:, :, 1, :], in_=qf4[:, :, 0, :])
            cs_q = rope[:, 0, rj, :].unsqueeze(1).unsqueeze(1).broadcast_to(
                [128, 8, 2, 32])
            sn_q = rope[:, 1, rj, :].unsqueeze(1).unsqueeze(1).broadcast_to(
                [128, 8, 2, 32])
            V.op("tensor_tensor", out=qf4, in0=qf4, in1=cs_q, op=ALU.mult)
            V.op("tensor_tensor", out=sh2, in0=sh2, in1=sn_q, op=ALU.mult)
            bw(V, tb["qrot"])
            qr4 = qrot.rearrange("p (h s d) -> p h s d", s=2, d=32)
            V.op("tensor_sub", out=qr4[:, :, 0, :], in0=qf4[:, :, 0, :],
                 in1=sh2[:, :, 0, :])
            vtok = V.op("tensor_add", out=qr4[:, :, 1, :], in0=qf4[:, :, 1, :],
                        in1=sh2[:, :, 1, :]).inc()
            ew(vtok, tb["qrot"])
            ew(vtok, tb["znt"])
            er(vtok, tb["xt1"], tb["rope"])
            bw(T, ptr)
            T.wait(vtok)
            for j in range(4):
                h = T.op("transpose", PSTR[:, 512 + j * 128:512 + (j + 1) * 128],
                         qrot[:, j * 128:(j + 1) * 128], id_b)
            ttok = h.inc()
            ew(ttok, ptr)
            er(ttok, tb["qrot"])
            bw(S, qtb_tb)
            br(S, ptr)
            stok = S.op("copy", out=qtv,
                        in_=PSTR[:, 512:1024].rearrange("p (a n) -> p a n",
                                                        n=128)).inc()
            er(stok, ptr)
            ew(stok, qtb_tb)
            d["qT"] = stok

        def heads(r):
            d = ts_[r]
            col = 1 + (r - src_base) * 128
            mvi = min(max(r - OWN0, 0), 4)
            qpar = (r - all_tiles[0]) % 2
            qtv = qtb[:, 4 * qpar:4 * qpar + 4, :]
            qtb_tb = tb[("qtb0", "qtb1")[qpar]]
            qsl = d["qsl"]
            ctx_tiles = list(range(r - 4, r + 1))
            hs = [dict() for _ in range(FH)]
            v_ = min(max(r - OWN0, 0), 4)
            mask_regions = [0, 4] + [j for j in (1, 2, 3) if j < 4 - v_]

            def _scp(pp):
                toks = []
                for par in (0, 1):
                    sc_tbs_ = ((pb0, pb1), (pc0, pc1))[par]
                    bw(T, *sc_tbs_)
                T.wait(d["qT"])
                T.wait(d["kring"])
                for j, cr in enumerate(ctx_tiles):
                    cslot = cr % RING
                    masked = j in mask_regions
                    for par in (0, 1):
                        sc_ap = (PSB, PSC)[par]
                        lo = 64 * par
                        T.op("matmul", sc_ap[:, j * 128:(j + 1) * 128],
                             qtv[lo:lo + 64, pp, :],
                             kt3[lo:lo + 64, pp, cslot, :],
                             start=True, stop=not masked)
                    if masked:
                        for par in (0, 1):
                            sc_ap = (PSB, PSC)[par]
                            h = T.op("matmul", sc_ap[:, j * 128:(j + 1) * 128],
                                     id_b, maskf[:, v_, j * 128:(j + 1) * 128],
                                     start=False, stop=True)
                tok = h.inc()
                hs[2 * pp]["sc"] = tok
                hs[2 * pp + 1]["sc"] = tok
                ew(tok, pb0, pb1, pc0, pc1)

            def _mx(hh):
                par = hh % 2
                sc_ap = (PSB, PSC)[par]
                sc_tbs_ = ((pb0, pb1), (pc0, pc1))[par]
                V.wait(hs[hh]["sc"])
                h = V.op("reduce_max", out=sml[:, 3, hh:hh + 1],
                         in_=sc_ap[:, 0:640], axis=AX.X, negate=True)
                hs[hh]["mx"] = h.inc()
                er(hs[hh]["mx"], *sc_tbs_)

            def _ex(hh):
                par = hh % 2
                sc_ap = (PSB, PSC)[par]
                sc_tbs_ = ((pb0, pb1), (pc0, pc1))[par]
                prb = (prb0, prb1)[par]
                prtb = tb[("prb0", "prb1")[par]]
                bw(S, prtb)
                S.wait(hs[hh]["mx"])
                h = S.op("activation", out=prb, in_=sc_ap[:, 0:640], func=ACT.Exp,
                         bias=sml[:, 3, hh:hh + 1])
                hs[hh]["ex"] = h.inc()
                er(hs[hh]["ex"], *sc_tbs_)
                ew(hs[hh]["ex"], prtb)

            def _pt(hh):
                par = hh % 2
                prb = (prb0, prb1)[par]
                prtb = tb[("prb0", "prb1")[par]]
                bw(T, ptr)
                T.wait(hs[hh]["ex"])
                for j in range(5):
                    h = T.op("transpose", PSTR[:, j * 128:(j + 1) * 128],
                             prb[:, j * 128:(j + 1) * 128], id_b)
                hs[hh]["pt"] = h.inc()
                ew(hs[hh]["pt"], ptr)
                er(hs[hh]["pt"], prtb)

            def _pe(hh):
                par = hh % 2
                ptbuf = (ptb0, ptb1)[par]
                prtb = tb[("prb0", "prb1")[par]]
                br(V, ptr)
                h = V.op("tensor_copy", out=ptbuf,
                         in_=PSTR[:, 0:640].rearrange("p (a n) -> p a n", n=128))
                hs[hh]["pe"] = h.inc()
                er(hs[hh]["pe"], ptr)
                ew(hs[hh]["pe"], prtb)

            def _av(hh):
                par = hh % 2
                ptbuf = (ptb0, ptb1)[par]
                prtb = tb[("prb0", "prb1")[par]]
                avtb = (pd, pe)[par]
                bw(T, avtb)
                T.wait(hs[hh]["pe"])
                T.wait(d["vring"])
                for j, cr in enumerate(ctx_tiles):
                    cslot = cr % RING
                    h = T.op("matmul", avtb.ap[:, 0:65], ptbuf[:, j, :],
                             vrr[:, cslot, hh * 65:hh * 65 + 65],
                             start=(j == 0), stop=(j == 4))
                hs[hh]["av"] = h.inc()
                ew(hs[hh]["av"], avtb)
                er(hs[hh]["av"], prtb)

            def _rc(hh):
                avtb = (pd, pe)[hh % 2]
                br(V, avtb)
                V.wait(hs[hh]["av"])
                h = V.op("reciprocal", out=sml[:, 5, hh:hh + 1],
                         in_=avtb.ap[:, 64:65])
                hs[hh]["rc"] = h.inc()
                er(hs[hh]["rc"], avtb)

            def _oe(hh):
                avtb = (pd, pe)[hh % 2]
                bw(S, tb["otk"])
                br(S, avtb)
                S.wait(hs[hh]["rc"])
                h = S.op("activation", out=otk[:, hh * 64:(hh + 1) * 64],
                         in_=avtb.ap[:, 0:64], func=ACT.Identity,
                         scale=sml[:, 5, hh:hh + 1])
                hs[hh]["oe"] = h.inc()
                er(hs[hh]["oe"], avtb)
                ew(hs[hh]["oe"], tb["otk"])

            _scp(0)
            _mx(0)
            _mx(1)
            for pp in range(FH // 2):
                h0, h1 = 2 * pp, 2 * pp + 1
                _ex(h0)
                _pt(h0)
                _ex(h1)
                _pe(h0)
                _av(h0)
                _pt(h1)
                _pe(h1)
                if pp + 1 < FH // 2:
                    _scp(pp + 1)
                _rc(h0)
                _av(h1)
                _oe(h0)
                if pp + 1 < FH // 2:
                    _mx(h0 + 2)
                    _mx(h1 + 2)
                _rc(h1)
                _oe(h1)
            er(c.T.tok(), qtb_tb)

        def tail(r):
            scol = 1 + (r - ch["sbase"]) * 128
            bw(T, ptr)
            T.wait(tb["otk"].wr)
            for j in range(4):
                h = T.op("transpose", PSTR[:, j * 128:(j + 1) * 128],
                         otk[:, j * 128:(j + 1) * 128], id_b)
            ttok = h.inc()
            ew(ttok, ptr)
            er(ttok, tb["otk"])
            bw(S, tb["otb"])
            br(S, ptr)
            stok = S.op("copy", out=otb[:, 0:4, :],
                        in_=PSTR[:, 0:512].rearrange("p (a n) -> p a n",
                                                     n=128)).inc()
            er(stok, ptr)
            ew(stok, tb["otb"])
            tok_oT = stok
            for rnd in range(2):
                bw(T, pe)
                T.wait(tok_oT)
                T.wait(wo_tok)
                for dmc in range(4):
                    dm = rnd * 4 + dmc
                    for ic in range(4):
                        h = T.op("matmul", PSE[:, dmc * 128:(dmc + 1) * 128],
                                 wo4[:, ic, dm * 128:(dm + 1) * 128],
                                 otb[:, ic, :], start=(ic == 0), stop=(ic == 3))
                tok_wo = h.inc()
                ew(tok_wo, pe)
                ts_[r]["wo"] = tok_wo
                br(V, pe)
                V.wait(tok_wo)
                for dmc in range(4):
                    dm = rnd * 4 + dmc
                    h = V.op("tensor_tensor", out=strm[:, dm, scol:scol + 128],
                             in0=PSE[:, dmc * 128:(dmc + 1) * 128],
                             in1=strm[:, dm, scol:scol + 128], op=ALU.add)
                vtok = h.inc()
                er(vtok, pe)
                er(vtok, tb["otb"])

        prep_mm(all_tiles[0])
        prep_vec(all_tiles[0])
        last_qkv = None
        last_wo = None
        for i, r in enumerate(all_tiles):
            if i + 1 < len(all_tiles):
                prep_mm(all_tiles[i + 1])
            if r in full_tiles:
                heads(r)
            if i + 1 < len(all_tiles):
                prep_vec(all_tiles[i + 1])
            if r in full_tiles:
                tail(r)
        for r in all_tiles:
            if "qkv_last" in ts_[r]:
                last_qkv = ts_[r]["qkv_last"]
            if "wo" in ts_[r]:
                last_wo = ts_[r]["wo"]
        er(last_qkv, WAr[0], WAr[1])
        if last_wo is not None:
            er(last_wo, W2wo[layer])
        phase_toks["strm"] = c.V.tok()


    def emit_ffn(ch, layer):
        tiles = ch["c_tiles"] if layer == 0 else ch["e_tiles"]
        if not tiles:
            return
        ncols = len(tiles) * 128
        cr_ = x1f_cr if layer == 0 else y1f_pv
        if layer == 0:
            h0 = V.op("tensor_copy", out=y1f_pv.unsqueeze(2),
                      in_=y1f_cr.unsqueeze(2))
            V.wait(h0.inc())
        if layer == 1:
            V.wait(tb["y1bf"].wr)
        h = V.op("tensor_copy", out=strm[:, :, 0:1],
                 in_=cr_.unsqueeze(2))
        last = 1 + (tiles[-1] - ch["sbase"]) * 128 + 127
        if layer == 0:
            h = V.op("tensor_copy", out=cr_.unsqueeze(2),
                     in_=strm[:, :, last:last + 1])
        crumb_tok = h.inc()
        stok = None
        for r in tiles:
            col = 1 + (r - ch["sbase"]) * 128
            lcol = (r - tiles[0]) * 128
            bw(T, pb0, pb1)
            T.wait(crumb_tok)
            T.wait(phase_toks["strm"])
            for kc in range(8):
                h = T.op("transpose", PSB[:, kc * 128:(kc + 1) * 128],
                         shift_view(strm, kc, col), id_f)
            ttok = h.inc()
            ew(ttok, pb0, pb1)
            bw(S, tb["xt0"])
            br(S, pb0)
            stok = S.op("copy", out=xt0, in_=PSB).inc()
            er(stok, pb0, pb1)
            ew(stok, tb["xt0"])
            V.wait(stok)
            bw(V, tb["znt"])
            V.op("bn_stats", out=sml[:, 6, 0:6], in_=xt0[:, 0:512])
            vtok = V.op("bn_stats", out=sml[:, 6, 6:12],
                        in_=xt0[:, 512:1024]).inc()
            V.wait(vtok)
            vtok = V.op("bn_aggr", out=sml[:, 7, 0:2], in_=sml[:, 6, 0:12]).inc()
            S.wait(vtok)
            stok = S.op("activation", out=sml[:, 7, 2:3], in_=sml[:, 7, 1:2],
                        func=ACT.Ln, bias=eps5).inc()
            S.wait(stok)
            stok = S.op("activation", out=sml[:, 7, 3:4], in_=sml[:, 7, 2:3],
                        func=ACT.Exp, scale=-0.5).inc()
            V.wait(stok)
            vtok = V.op("tensor_scalar", out=sml[:, 7, 4:5], in0=sml[:, 7, 0:1],
                        scalar1=sml[:, 7, 3:4], scalar2=-1.0,
                        op0=ALU.mult, op1=ALU.mult).inc()
            bw(S, tb["znt"])
            S.wait(vtok)
            stok = S.op("activation", out=znt, in_=xt0, func=ACT.Identity,
                        bias=sml[:, 7, 4:5], scale=sml[:, 7, 3:4]).inc()
            ew(stok, tb["znt"])
            er(stok, tb["xt0"])
            bw(T, pc0, pc1)
            T.wait(stok)
            for kc in range(8):
                h = T.op("transpose", PSC[:, kc * 128:(kc + 1) * 128],
                         znt[:, kc * 128:(kc + 1) * 128], id_f)
            ttok = h.inc()
            ew(ttok, pc0, pc1)
            er(ttok, tb["znt"])
            br(S, pc0)
            for kc in range(8):
                h = S.op("activation", out=y1a[:, kc, 1 + lcol:1 + lcol + 128],
                         in_=PSC[:, kc * 128:(kc + 1) * 128], func=ACT.Identity,
                         scale=lnp[:, layer, 0, kc:kc + 1],
                         bias=lnp[:, layer, 1, kc:kc + 1])
            stok = h.inc()
            er(stok, pc0, pc1)
        ew(stok, tb["lnh"])
        wins = []
        off, left = 1, ncols
        while left > 0:
            w = min(512, left)
            wins.append((off, w))
            off += w
            left -= w
        vtok = None
        ps_y_par = 0
        ubf = ubf_l[layer]
        tb["ubf"].rd = tb["ubf"].rd + list(W2wo[layer].rd)
        vtok_last_u = None
        for g in range(NGRP):
            reg, w2reg, base, w2slot, cw = load_wgrp(layer, g)
            nch = 2 if cw > 128 else 1
            for hc in range(nch):
                ccw = min(128, cw - hc * 128)
                for (wo_, W) in wins:
                    bw(T, pd, pe)
                    T.wait(reg.wr)
                    T.wait(tb["lnh"].wr)
                    for kc in range(8):
                        T.op("matmul", PSD[0:ccw, 0:W],
                             base[:, kc, hc * 128:hc * 128 + ccw],
                             y1a[:, kc, wo_:wo_ + W],
                             start=(kc == 0), stop=(kc == 7))
                    for kc in range(8):
                        h = T.op("matmul", PSE[0:ccw, 0:W],
                                 base[:, kc, 256 + hc * 128:256 + hc * 128 + ccw],
                                 y1a[:, kc, wo_:wo_ + W],
                                 start=(kc == 0), stop=(kc == 7))
                    tok_ag = h.inc()
                    ew(tok_ag, pd, pe)
                    er(tok_ag, reg)
                    bw(S, tb["znt"])
                    br(S, pe)
                    S.wait(tok_ag)
                    stok = S.op("activation", out=znt[0:ccw, 0:W],
                                in_=PSE[0:ccw, 0:W], func=ACT.Gelu,
                                bias=b1g[0:ccw, layer,
                                         2 * g + hc:2 * g + hc + 1]).inc()
                    er(stok, pe)
                    ew(stok, tb["znt"])
                    bw(V, tb["ubf"])
                    br(V, pd)
                    V.wait(stok)
                    vtok = V.op("scalar_tensor_tensor",
                                out=ubf[0:ccw, hc, wo_:wo_ + W],
                                in0=PSD[0:ccw, 0:W],
                                scalar=b1a[0:ccw, layer,
                                           2 * g + hc:2 * g + hc + 1],
                                in1=znt[0:ccw, 0:W], op0=ALU.add,
                                op1=ALU.mult).inc()
                    er(vtok, pd)
                    er(vtok, tb["znt"])
                    ew(vtok, tb["ubf"])
            for dmc in range(8):
                for (wo_, W) in wins:
                    pcb = (pc0, pc1)[ps_y_par]
                    pslice = PSC[:, ps_y_par * 512:ps_y_par * 512 + W]
                    bw(T, pcb)
                    T.wait(w2reg.wr)
                    T.wait(tb["ubf"].wr)
                    for hc in range(nch):
                        chw = min(128, cw - hc * 128)
                        h = T.op("matmul", pslice,
                                 w2slot[0:chw, hc, dmc * 128:(dmc + 1) * 128],
                                 ubf[0:chw, hc, wo_:wo_ + W],
                                 start=(hc == 0), stop=(hc == nch - 1))
                    tok_y = h.inc()
                    ew(tok_y, pcb)
                    er(tok_y, w2reg)
                    er(tok_y, tb["ubf"])
                    br(V, pcb)
                    V.wait(tok_y)
                    if g == 0:
                        h = V.op("scalar_tensor_tensor",
                                 out=strm[:, dmc, wo_:wo_ + W], in0=pslice,
                                 scalar=b2c[:, layer, dmc:dmc + 1],
                                 in1=strm[:, dmc, wo_:wo_ + W],
                                 op0=ALU.add, op1=ALU.add)
                    else:
                        h = V.op("tensor_tensor", out=strm[:, dmc, wo_:wo_ + W],
                                 in0=pslice, in1=strm[:, dmc, wo_:wo_ + W],
                                 op=ALU.add)
                    vtok = h.inc()
                    er(vtok, pcb)
                    ps_y_par ^= 1
        er(vtok, W2wo[layer])
        phase_toks["strm"] = vtok
        if layer == 0:
            S.wait(vtok)
            S.op("copy", out=y1a[:, :, 0:1], in_=y1b_cr.unsqueeze(2))
            for (wo_, W) in wins:
                h = S.op("copy", out=y1a[:, :, wo_:wo_ + W],
                         in_=strm[:, :, wo_:wo_ + W])
            S.wait(h.inc())
            S.op("copy", out=y1b_cr.unsqueeze(2),
                 in_=y1a[:, :, last:last + 1])
            stok = S.op("copy", out=y1f_cr.unsqueeze(2),
                        in_=strm[:, :, last:last + 1]).inc()
            ew(stok, tb["y1bf"])

    def emit_att(ch, wq_tok, wg_tok, wout_tok):
        kv_tiles, full_tiles = ch["f_kv"], ch["f_full"]
        all_tiles = sorted(set(kv_tiles + full_tiles))
        if not all_tiles:
            return
        kt3 = kta.rearrange("p a (s n) -> p a s n", n=128)
        va4 = vat.rearrange("p s (h e) -> p s h e", e=65)
        wo8 = W2flat.rearrange("p (k n) -> p k n", k=8)
        seedT, seedS = c.T.tok(), c.S.tok()
        for n_ in ("ostg0", "ostg1"):
            tb[n_].rd = tb[n_].rd + [seedT, seedS]
        V.wait(wq_tok)
        V.wait(wg_tok)
        gb = gmc.unsqueeze(2).broadcast_to([128, 8, 3072])
        V.op("tensor_tensor", out=WA, in0=WA, in1=gb, op=ALU.mult)
        gb2 = gmc.unsqueeze(2).broadcast_to([128, 8, 16])
        gtok = V.op("tensor_tensor", out=wag, in0=wag, in1=gb2, op=ALU.mult).inc()
        T.wait(gtok)
        T.wait(phase_toks["xnbf"])
        last_qkv_tok = None
        last_wout_tok = None
        for r in all_tiles:
            full = r in full_tiles
            slot = r % ACTX
            mvi = min(max(r - OWN0, 0), 2)
            col = 1 + (r - ch["xnbase"]) * 128
            for hf in range(2):
                bw(T, pa)
                for kc in range(8):
                    h = T.op("matmul", PSA, xn_bf[:, kc, col:col + 128],
                             WA[:, kc, 1024 + hf * 512:1024 + (hf + 1) * 512],
                             start=(kc == 0), stop=(kc == 7))
                tok = h.inc()
                ew(tok, pa)
                last_qkv_tok = tok
                bw(S, tb["krot"])
                br(S, pa)
                stok = S.op("activation", out=krot, in_=PSA, func=ACT.Identity,
                            scale=inv32[:, r:r + 1]).inc()
                er(stok, pa)
                ew(stok, tb["krot"])
                bw(T, ptr)
                T.wait(stok)
                for j in range(4):
                    h = T.op("transpose",
                             PSTR[:, hf * 512 + j * 128:hf * 512 + (j + 1) * 128],
                             krot[:, j * 128:(j + 1) * 128], id_b)
                ttok = h.inc()
                ew(ttok, ptr)
                er(ttok, tb["krot"])
                br(S, ptr)
                stok = S.op("copy", out=kt3[:, 4 * hf:4 * hf + 4, slot, :],
                            in_=PSTR[:, hf * 512:(hf + 1) * 512].rearrange(
                                "p (a n) -> p a n", n=128)).inc()
                er(stok, ptr)
            tok_kring = stok
            for hf in range(2):
                bw(T, pa)
                for kc in range(8):
                    h = T.op("matmul", PSA, xn_bf[:, kc, col:col + 128],
                             WA[:, kc, 2048 + hf * 512:2048 + (hf + 1) * 512],
                             start=(kc == 0), stop=(kc == 7))
                tok = h.inc()
                ew(tok, pa)
                last_qkv_tok = tok
                br(S, pa)
                stok = S.op("activation",
                            out=va4[:, slot, 8 * hf:8 * hf + 8, 0:64],
                            in_=PSA.rearrange("p (h e) -> p h e", e=64),
                            func=ACT.Identity, scale=inv32[:, r:r + 1]).inc()
                er(stok, pa)
            tok_vring = stok
            if not full:
                continue
            for hf in range(2):
                bw(T, pa)
                for kc in range(8):
                    h = T.op("matmul", PSA, xn_bf[:, kc, col:col + 128],
                             WA[:, kc, hf * 512:(hf + 1) * 512],
                             start=(kc == 0), stop=(kc == 7))
                tok = h.inc()
                ew(tok, pa)
                last_qkv_tok = tok
                bw(S, tb["otk"])
                br(S, pa)
                stok = S.op("activation", out=otk[:, hf * 512:(hf + 1) * 512],
                            in_=PSA, func=ACT.Identity,
                            scale=inv32q[:, r:r + 1]).inc()
                er(stok, pa)
                ew(stok, tb["otk"])
            bw(T, pd)
            for kc in range(8):
                h = T.op("matmul", PSD[:, 0:16], xn_bf[:, kc, col:col + 128],
                         wag[:, kc, :], start=(kc == 0), stop=(kc == 7))
            tok_g = h.inc()
            ew(tok_g, pd)
            last_qkv_tok = tok_g
            bw(S, tb["gsc"])
            br(S, pd)
            stok = S.op("activation", out=gsc[:, 0, :], in_=PSD[:, 0:16],
                        func=ACT.Identity, scale=ninv32[:, r:r + 1]).inc()
            er(stok, pd)
            V.wait(stok)
            vtok = V.op("tensor_sub", out=gsc[:, 0, :], in0=gsc[:, 0, :],
                        in1=bgv).inc()
            S.wait(vtok)
            stok = S.op("activation", out=gsc[:, 1, :], in_=gsc[:, 0, :],
                        func=ACT.Exp).inc()
            V.wait(stok)
            vtok = V.op("tensor_scalar_add", out=gsc[:, 1, :], in0=gsc[:, 1, :],
                        scalar1=1.0).inc()
            V.wait(vtok)
            vtok = V.op("reciprocal", out=gsc[:, 1, :], in_=gsc[:, 1, :]).inc()
            ew(vtok, tb["gsc"])
            bw(T, ptr)
            T.wait(tb["otk"].wr)
            for j in range(8):
                h = T.op("transpose", PSTR[:, j * 128:(j + 1) * 128],
                         otk[:, j * 128:(j + 1) * 128], id_b)
            ttok = h.inc()
            ew(ttok, ptr)
            er(ttok, tb["otk"])
            bw(S, tb["qtb"])
            br(S, ptr)
            stok = S.op("copy", out=qtb,
                        in_=PSTR.rearrange("p (a n) -> p a n", n=128)).inc()
            er(stok, ptr)
            ew(stok, tb["qtb"])
            tok_qT = stok
            ctx_tiles = list(range(r - 2, r + 1))
            hs = [dict() for _ in range(AH)]
            v_ = min(max(r - OWN0, 0), 2)
            mask_regions = [0, 2] + ([1] if v_ == 0 else [])

            def _scp(pp):
                for par in (0, 1):
                    bw(T, (pb0, pb1)[par])
                T.wait(tok_qT)
                T.wait(tok_kring)
                for j, cr_t in enumerate(ctx_tiles):
                    cslot = cr_t % ACTX
                    masked = j in mask_regions
                    for par in (0, 1):
                        lo = 64 * par
                        T.op("matmul",
                             PSB[:, par * 512 + j * 128:par * 512 + (j + 1) * 128],
                             qtb[lo:lo + 64, pp, :], kt3[lo:lo + 64, pp, cslot, :],
                             start=True, stop=not masked)
                    if masked:
                        for par in (0, 1):
                            h = T.op("matmul",
                                     PSB[:, par * 512 + j * 128:par * 512 + (j + 1) * 128],
                                     id_b, maska[:, v_, j * 128:(j + 1) * 128],
                                     start=False, stop=True)
                tok = h.inc()
                hs[2 * pp]["sc"] = tok
                hs[2 * pp + 1]["sc"] = tok
                ew(tok, pb0, pb1)

            def _mx(hh):
                par = hh % 2
                sc_tb = (pb0, pb1)[par]
                V.wait(hs[hh]["sc"])
                h = V.op("reduce_max", out=sml[:, 7, hh:hh + 1],
                         in_=PSB[:, par * 512:par * 512 + 384], axis=AX.X,
                         negate=True)
                hs[hh]["mx"] = h.inc()
                er(hs[hh]["mx"], sc_tb)

            def _ex(hh):
                par = hh % 2
                sc_tb = (pb0, pb1)[par]
                prb = (prb0, prb1)[par]
                prtb = tb[("prb0", "prb1")[par]]
                bw(S, prtb)
                S.wait(hs[hh]["mx"])
                h = S.op("activation", out=prb[:, 0:384],
                         in_=PSB[:, par * 512:par * 512 + 384], func=ACT.Exp,
                         bias=sml[:, 7, hh:hh + 1])
                hs[hh]["ex"] = h.inc()
                er(hs[hh]["ex"], sc_tb)
                ew(hs[hh]["ex"], prtb)

            def _pt(hh):
                par = hh % 2
                prb = (prb0, prb1)[par]
                prtb = tb[("prb0", "prb1")[par]]
                bw(T, ptr)
                T.wait(hs[hh]["ex"])
                for j in range(3):
                    h = T.op("transpose", PSTR[:, j * 128:(j + 1) * 128],
                             prb[:, j * 128:(j + 1) * 128], id_b)
                hs[hh]["pt"] = h.inc()
                ew(hs[hh]["pt"], ptr)
                er(hs[hh]["pt"], prtb)

            def _pe(hh):
                par = hh % 2
                ptbuf = (ptb0, ptb1)[par]
                prtb = tb[("prb0", "prb1")[par]]
                br(V, ptr)
                h = V.op("tensor_copy", out=ptbuf[:, 0:3, :],
                         in_=PSTR[:, 0:384].rearrange("p (a n) -> p a n", n=128))
                hs[hh]["pe"] = h.inc()
                er(hs[hh]["pe"], ptr)
                ew(hs[hh]["pe"], prtb)

            def _av(hh):
                par = hh % 2
                ptbuf = (ptb0, ptb1)[par]
                prtb = tb[("prb0", "prb1")[par]]
                avtb = (pd, pe)[par]
                bw(T, avtb)
                T.wait(hs[hh]["pe"])
                T.wait(tok_vring)
                for j, cr_t in enumerate(ctx_tiles):
                    cslot = cr_t % ACTX
                    h = T.op("matmul", avtb.ap[:, 0:65], ptbuf[:, j, :],
                             vat[:, cslot, hh * 65:hh * 65 + 65],
                             start=(j == 0), stop=(j == 2))
                hs[hh]["av"] = h.inc()
                ew(hs[hh]["av"], avtb)
                er(hs[hh]["av"], prtb)

            def _rc1(hh):
                avtb = (pd, pe)[hh % 2]
                br(V, avtb)
                V.wait(hs[hh]["av"])
                V.wait(tb["gsc"].wr)
                h = V.op("reciprocal", out=sml[:, 9, hh:hh + 1],
                         in_=avtb.ap[:, 64:65])
                hs[hh]["rc1"] = h.inc()
                er(hs[hh]["rc1"], avtb)

            def _rc2(hh):
                V.wait(hs[hh]["rc1"])
                h = V.op("tensor_mul", out=sml[:, 9, hh:hh + 1],
                         in0=sml[:, 9, hh:hh + 1], in1=gsc[:, 1, hh:hh + 1])
                hs[hh]["rc2"] = h.inc()

            def _oe(hh):
                avtb = (pd, pe)[hh % 2]
                bw(S, tb["otk"])
                br(S, avtb)
                S.wait(hs[hh]["rc2"])
                h = S.op("activation", out=otk[:, hh * 64:(hh + 1) * 64],
                         in_=avtb.ap[:, 0:64], func=ACT.Identity,
                         scale=sml[:, 9, hh:hh + 1])
                hs[hh]["oe"] = h.inc()
                er(hs[hh]["oe"], avtb)
                ew(hs[hh]["oe"], tb["otk"])

            _scp(0)
            _mx(0)
            _mx(1)
            for pp in range(AH // 2):
                h0, h1 = 2 * pp, 2 * pp + 1
                _ex(h0)
                _pt(h0)
                _ex(h1)
                _pe(h0)
                _av(h0)
                _pt(h1)
                _pe(h1)
                if pp + 1 < AH // 2:
                    _scp(pp + 1)
                _rc1(h0)
                _av(h1)
                _rc1(h1)
                if pp + 1 < AH // 2:
                    _mx(h0 + 2)
                _rc2(h0)
                _oe(h0)
                if pp + 1 < AH // 2:
                    _mx(h1 + 2)
                _rc2(h1)
                _oe(h1)
            bw(T, ptr)
            T.wait(tb["otk"].wr)
            for j in range(8):
                h = T.op("transpose", PSTR[:, j * 128:(j + 1) * 128],
                         otk[:, j * 128:(j + 1) * 128], id_b)
            ttok = h.inc()
            ew(ttok, ptr)
            er(ttok, tb["otk"])
            bw(S, tb["otb"])
            br(S, ptr)
            stok = S.op("copy", out=otb,
                        in_=PSTR.rearrange("p (a n) -> p a n", n=128)).inc()
            er(stok, ptr)
            ew(stok, tb["otb"])
            tok_oT = stok
            bw(T, pc0, pc1)
            T.wait(tok_oT)
            T.wait(wout_tok)
            for dmc in range(8):
                for ic in range(8):
                    h = T.op("matmul", PSC[:, dmc * 128:(dmc + 1) * 128],
                             wo8[:, ic, dmc * 128:(dmc + 1) * 128],
                             otb[:, ic, :], start=(ic == 0), stop=(ic == 7))
            tok_wout = h.inc()
            ew(tok_wout, pc0, pc1)
            er(tok_wout, tb["otb"])
            last_wout_tok = tok_wout
            scol = 1 + (r - ch["sbase"]) * 128
            bw(V, tb["znt"])
            br(V, pc0)
            V.wait(tok_wout)
            for dmc in range(8):
                h = V.op("tensor_tensor", out=znt[:, dmc * 128:(dmc + 1) * 128],
                         in0=PSC[:, dmc * 128:(dmc + 1) * 128],
                         in1=strm[:, dmc, scol:scol + 128], op=ALU.add)
            vtok = h.inc()
            er(vtok, pc0, pc1)
            ew(vtok, tb["znt"])
            bw(T, pb0, pb1)
            T.wait(vtok)
            for kc in range(8):
                h = T.op("transpose", PSB[:, kc * 128:(kc + 1) * 128],
                         znt[:, kc * 128:(kc + 1) * 128], id_f)
            ttok = h.inc()
            ew(ttok, pb0, pb1)
            er(ttok, tb["znt"])
            ib = r % 2
            stg = ostg[:, ib, :]
            stb = tb[("ostg0", "ostg1")[ib]]
            xtb = tb[("xt0", "xt1")[ib]]
            bw(Y, xtb)
            ytok = Y.op("dma_start", out=(xt0, xt1)[ib],
                        in_=xin[r * 128:(r + 1) * 128, :]).inc(
                            16, ("yx0", "yx1")[ib])
            ew(ytok, xtb)
            bw(V, stb)
            br(V, pb0)
            V.wait(ytok)
            vtok = V.op("tensor_add", out=stg, in0=PSB, in1=(xt0, xt1)[ib]).inc()
            er(vtok, pb0, pb1)
            er(vtok, xtb)
            ew(vtok, stb)
            Y.wait(vtok)
            orow = (r - OWN0) * 128
            ytok = Y.op("dma_start", out=yout[orow:orow + 128, :],
                        in_=stg).inc(16, ("yo0", "yo1")[ib])
            er(ytok, stb)
        er(last_qkv_tok, *WAr)
        if last_wout_tok is not None:
            er(last_wout_tok, W2wo[0], W2wo[1], W2g[0], W2g[1])

    def do_taps(label):
        here = dict(strm=strm, xn_bf=xn_bf, y1a=y1a, inv32=inv32, kt1=kt1,
                    v1r=v1r, kt2=kt2, v2r=v2r, kta=kta, vat=vat, otk=otk,
                    qtb=qtb, ubf0=ubf_l[0], ubf1=ubf_l[1], krot=krot, gsc=gsc, ninv32=ninv32,
                    qrot=qrot, sml=sml, rope=rope, WA=WA, W2=W2)
        for lab, bname in taps:
            if lab != label:
                continue
            ap = here[bname]
            tname = f"tap_{lab}_{bname}"
            shp = list(ap.shape)
            tout = nc.dram_tensor(tname, shp, F32, kind="ExternalOutput").ap()
            for e_ in (T, V, S):
                pass
            G.wait(("t", c.count["t"]))
            G.wait(("v", c.count["v"]))
            G.wait(("s", c.count["s"]))
            ttok = G.op("dma_start", out=tout, in_=ap).inc(16, "yi")
            for e_ in (T, V, S, Y):
                e_.wait(ttok)

    emit_init()
    for ch in CHUNKS:
        if nphase >= 2:
            wq0 = load_wqkv_ffn(0)
            wo0 = load_wo_ffn(0)
        ci = CHUNKS.index(ch)
        if nphase >= 1:
            emit_A(ch)
            do_taps(f"A{ci}")
        if nphase >= 2:
            emit_wave_ffn(ch, 0, wq0, wo0)
            do_taps(f"B{ci}")
        if nphase >= 3:
            emit_ffn(ch, 0)
            do_taps(f"C{ci}")
        if nphase >= 4:
            wq1 = load_wqkv_ffn(1)
            wo1 = load_wo_ffn(1)
            emit_wave_ffn(ch, 1, wq1, wo1)
            do_taps(f"D{ci}")
        if nphase >= 5:
            emit_ffn(ch, 1)
            do_taps(f"E{ci}")
        if nphase >= 6:
            watok, wgtok = load_watt()
            wouttok = load_wout()
            emit_att(ch, watok, wgtok, wouttok)
            do_taps(f"F{ci}")
    for s_ in ("yo0", "yo1"):
        Y.wait((s_, c.count[s_]))
        V.wait((s_, c.count[s_]))

    with contextlib.ExitStack() as st:
        for name in ("t", "v", "s") + DMA_SEMS:
            c.sem[name] = st.enter_context(nc.semaphore(f"sem_{name}"))
        block = st.enter_context(nc.Block())

        def mk_replay(rec):
            def f(eng):
                for item in rec.ops:
                    if item[0] == "w":
                        eng.wait_ge(c.sem[item[1]], item[2])
                    else:
                        m, a, kw, incs = item[1]
                        inst = getattr(eng, m)(*a, **kw)
                        for s, n in incs:
                            inst = inst.then_inc(c.sem[s], n)
            return f

        block.tensor(mk_replay(c.T))
        block.vector(mk_replay(c.V))
        block.scalar(mk_replay(c.S))
        block.sync(mk_replay(c.Y))
        block.gpsimd(mk_replay(c.G))
    return nc


def _lhs_chunks(w):
    w = np.ascontiguousarray(np.asarray(w, np.float32))
    K, N = w.shape
    return np.ascontiguousarray(w.reshape(K // 128, 128, N).transpose(1, 0, 2))


def _prep_weights(waqkv, wagat, waout, wfqkv, wfo, wf1, wf2):
    d = {}
    d["w_aqkv"] = _lhs_chunks(waqkv)
    d["w_agat"] = _lhs_chunks(wagat)
    d["w_aout"] = _lhs_chunks(waout)
    d["w_fqkv"] = np.stack([_lhs_chunks(np.asarray(wfqkv, np.float32)[l])
                            for l in range(2)])
    d["w_fo"] = np.stack([_lhs_chunks(np.asarray(wfo, np.float32)[l])
                          for l in range(2)])
    w1 = np.asarray(wf1, np.float32)
    w2 = np.asarray(wf2, np.float32)
    w1g = np.zeros((2, NGRP, 128, 8, 512), np.float32)
    w2g = np.zeros((2, NGRP, 128, 2, DM), np.float32)
    for l in range(2):
        w1c = _lhs_chunks(w1[l])
        for g in range(NGRP):
            c0 = 256 * g
            cw = min(256, HID - c0)
            w1g[l, g, :, :, 0:cw] = w1c[:, :, c0:c0 + cw]
            w1g[l, g, :, :, 256:256 + cw] = w1c[:, :, HID + c0:HID + c0 + cw]
            r0 = 2 * g * 128
            rw = min(256, HID - r0)
            blk = w2[l, r0:r0 + rw]
            pad = np.zeros((256, DM), np.float32)
            pad[:rw] = blk
            w2g[l, g] = pad.reshape(2, 128, DM).transpose(1, 0, 2)
    d["w_f1"] = np.ascontiguousarray(w1g)
    d["w_f2"] = np.ascontiguousarray(w2g)
    return d


def make_in_maps(x, consts, weights):
    rope_h = [_rope_tables_for_core(h) for h in (0, 1)]
    masks_h = [_masks_for_core(h) for h in (0, 1)]
    in_maps = []
    for core in range(8):
        seq, half = core // 2, core % 2
        if half == 0:
            xi = np.concatenate([np.zeros((HALO, DM), np.float32),
                                 x[seq, 0:P]], axis=0)
        else:
            xi = np.ascontiguousarray(x[seq, P - HALO:SEQ])
        mf, ma = masks_h[half]
        im = dict(xin=xi, rope_in=rope_h[half], maskf_in=mf, maska_in=ma,
                  **consts, **weights)
        in_maps.append(im)
    return in_maps


_NC_CACHE = {}


def _get_program():
    if "nc" not in _NC_CACHE:
        _NC_CACHE["nc"] = build_program()
    return _NC_CACHE["nc"]


def kernel(L0_attn_x, L0_attn_gamma, L0_attn_W_qkv, L0_attn_W_gates,
           L0_attn_b_gates, L0_attn_W_out, L0_ffn_wqkv, L0_ffn_q_scale,
           L0_ffn_k_scale, L0_ffn_wo, L0_ffn_ln_g, L0_ffn_ln_b, L0_ffn_w1,
           L0_ffn_b1, L0_ffn_w2, L0_ffn_b2):
    x = np.ascontiguousarray(np.asarray(L0_attn_x, dtype=np.float32))
    consts = _const_inputs(L0_ffn_q_scale, L0_ffn_k_scale, L0_ffn_ln_g,
                           L0_ffn_ln_b, L0_ffn_b1, L0_ffn_b2, L0_attn_b_gates,
                           L0_attn_gamma)
    weights = _prep_weights(L0_attn_W_qkv, L0_attn_W_gates, L0_attn_W_out,
                            L0_ffn_wqkv, L0_ffn_wo, L0_ffn_w1, L0_ffn_w2)
    in_maps = make_in_maps(x, consts, weights)
    nc = _get_program()
    res = run_bass_kernel_spmd(nc, in_maps, core_ids=list(range(8)))
    out = np.empty((NSEQ, SEQ, DM), dtype=np.float32)
    for core in range(8):
        seq, half = core // 2, core % 2
        out[seq, half * P:(half + 1) * P] = np.asarray(res.results[core]["yout"])
    return out


# version 08
# speedup vs baseline: 1.2157x; 1.0209x over previous
import contextlib
import math

import numpy as np

import concourse.bass as bass
import concourse.mybir as mybir
from concourse.bass_utils import run_bass_kernel_spmd

F32 = mybir.dt.float32
BF16 = mybir.dt.bfloat16
ACT = mybir.ActivationFunctionType
ALU = mybir.AluOpType
AX = mybir.AxisListType

DM = 1024
SEQ = 4096
NSEQ = 4
P = 2048
HALO = 1280
XROWS = HALO + P
NT = XROWS // 128
OWN0 = HALO // 128

FH = 8
AH = 16
DH = 64
HID = 2730
NHC = 22
NGRP = 11
FCTX = 5
RING = 6
ACTX = 3
MASKVAL = -1.0e30
LN32 = math.log(32.0)
LN8 = math.log(8.0)


def _chunks():
    c0 = dict(tiles=list(range(0, 10)), sbase=5, xnbase=1,
              b_kv=[1, 2, 3, 4], b_full=[5, 6, 7, 8, 9],
              c_tiles=[5, 6, 7, 8, 9],
              d_kv=[6, 7, 8, 9], d_full=[],
              e_tiles=[],
              f_kv=[8, 9], f_full=[])
    def own(t0):
        t = list(range(t0, t0 + 8))
        return dict(tiles=t, sbase=t0, xnbase=t0, b_kv=[], b_full=t,
                    c_tiles=t, d_kv=[], d_full=t, e_tiles=t, f_kv=[], f_full=t)
    return [c0, own(10), own(18)]


CHUNKS = _chunks()


def _rope_tables_for_core(half):
    n = SEQ
    hi = np.arange(0, DH, 2, dtype=np.float64)
    inv_freq = 10000.0 ** (-hi / DH)
    base = (hi + 0.4 * DH) / (1.4 * DH)
    out = np.zeros((3, 128, 1152), dtype=np.float64)
    for ci, ch in enumerate(CHUNKS):
        wave = sorted(set(ch["b_kv"] + ch["b_full"]))
        for sj, r in enumerate(wave):
            g0 = r * 128 - HALO + (0 if half == 0 else P)
            if half == 0 and g0 < 0:
                continue
            t = g0 + np.arange(128, dtype=np.float64)
            fr = t[:, None] * inv_freq[None, :]
            power = (t - n // 2) / 256.0
            scale = base[None, :] ** power[:, None]
            cs, sn = np.cos(fr), np.sin(fr)
            for ti, tab in enumerate((cs * scale, sn * scale,
                                      cs / scale, sn / scale)):
                out[ci, :, ti * 288 + sj * 32: ti * 288 + sj * 32 + 32] = tab
    return np.ascontiguousarray(out.astype(np.float32))


def _masks_for_core(half):
    qi = np.arange(128)[:, None]

    def mk(nblk, win, nvar):
        ctxw = nblk * 128
        j = np.arange(ctxw)[None, :]
        keep = (j >= qi) & (j <= qi + win)
        m = np.zeros((128, nvar * ctxw), dtype=np.float32)
        for v in range(nvar):
            mm = keep.copy()
            if half == 0 and (nvar - 1 - v) > 0:
                mm[:, :(nvar - 1 - v) * 128] = False
            m[:, v * ctxw:(v + 1) * ctxw] = np.where(mm, 0.0, MASKVAL)
        return np.ascontiguousarray(m)

    return mk(FCTX, 512, 5), mk(ACTX, 256, 3)


def _const_inputs(q_scale, k_scale, ln_g, ln_b, b1_, b2_, b_gates, gamma):
    d = {}
    sc = np.zeros((2, 2, 64), np.float32)
    for l in range(2):
        sc[l] = np.stack([np.asarray(q_scale[l], np.float32),
                          np.asarray(k_scale[l], np.float32)])
    d["scales"] = np.ascontiguousarray(
        np.broadcast_to(sc.reshape(1, 256), (128, 256)))
    lnp = np.zeros((128, 2, 2, 8), np.float32)
    for l in range(2):
        lnp[:, l, 0, :] = np.asarray(ln_g[l], np.float32).reshape(8, 128).T
        lnp[:, l, 1, :] = np.asarray(ln_b[l], np.float32).reshape(8, 128).T
    d["lnp"] = np.ascontiguousarray(lnp.reshape(128, 32))
    b2 = np.zeros((128, 2, 8), np.float32)
    for l in range(2):
        b2[:, l, :] = np.asarray(b2_[l], np.float32).reshape(8, 128).T
    d["b2c"] = np.ascontiguousarray(b2.reshape(128, 16))
    b1a = np.zeros((128, 2, NHC), np.float32)
    b1g = np.zeros((128, 2, NHC), np.float32)
    for l in range(2):
        b1 = np.asarray(b1_[l], np.float32)
        for arr, seg in ((b1a, b1[:HID]), (b1g, b1[HID:])):
            pad = np.zeros(NHC * 128, np.float32)
            pad[:HID] = seg
            arr[:, l, :] = pad.reshape(NHC, 128).T
    d["b1a"] = np.ascontiguousarray(b1a.reshape(128, 2 * NHC))
    d["b1g"] = np.ascontiguousarray(b1g.reshape(128, 2 * NHC))
    d["bgv"] = np.ascontiguousarray(
        np.broadcast_to(np.asarray(b_gates, np.float32).reshape(1, 16), (128, 16)))
    d["gmc"] = np.ascontiguousarray(np.asarray(gamma, np.float32).reshape(8, 128).T)
    d["ident"] = np.eye(128, dtype=np.float32)
    return d


class Rec:
    def __init__(self, ctx, sem_name):
        self.ctx = ctx
        self.sem = sem_name
        self.ops = []

    def wait(self, tok):
        if tok is None:
            return
        sem, val = tok
        if val > 0:
            self.ops.append(("w", sem, val))

    def waits(self, toks):
        best = {}
        for t in toks:
            if t is None:
                continue
            s, v = t
            best[s] = max(best.get(s, 0), v)
        for s, v in best.items():
            self.wait((s, v))

    def op(self, method, *a, **kw):
        r = [method, a, kw, []]
        self.ops.append(("o", r))
        return _OpH(self, r)

    def tok(self):
        return (self.sem, self.ctx.count[self.sem])


class _OpH:
    def __init__(self, rec, r):
        self.rec = rec
        self.r = r

    def inc(self, n=1, sem=None):
        sem = sem or self.rec.sem
        self.r[3].append((sem, n))
        self.rec.ctx.count[sem] += n
        return (sem, self.rec.ctx.count[sem])


class TB:

    def __init__(self, ap=None, name=""):
        self.ap = ap
        self.name = name
        self.wr = None
        self.rd = []


def bw(rec, *tbs):
    for t in tbs:
        rec.waits(t.rd)


def ew(tok, *tbs):
    for t in tbs:
        t.wr = tok
        t.rd = []


def br(rec, *tbs):
    for t in tbs:
        rec.wait(t.wr)


def er(tok, *tbs):
    for t in tbs:
        t.rd.append(tok)


DMA_SEMS = ("yi", "yx0", "yx1", "yr", "yo0", "yo1",
            "ga0", "ga1", "ga2", "ga3", "gw0", "gw1", "go0", "go1")


class Ctx:
    def __init__(self, nc):
        self.nc = nc
        self.count = {k: 0 for k in ("t", "v", "s", "y", "g") + DMA_SEMS}
        self.T = Rec(self, "t")
        self.V = Rec(self, "v")
        self.S = Rec(self, "s")
        self.Y = Rec(self, "y")
        self.G = Rec(self, "g")
        self.sem = {}


def shift_view(buf, kc, col, n=128):
    off = col - 1 if kc >= 4 else col
    return buf[:, kc, off:off + n]


def build_program(nphase=99, taps=()):
    nc = bass.Bass("TRN2", target_bir_lowering=False,
                   detect_race_conditions=False)
    c = Ctx(nc)
    T, V, S, Y, G = c.T, c.V, c.S, c.Y, c.G

    xin = nc.dram_tensor("xin", [XROWS, DM], F32, kind="ExternalInput").ap()
    yout = nc.dram_tensor("yout", [P, DM], F32, kind="ExternalOutput").ap()
    rope_in = nc.dram_tensor("rope_in", [3, 128, 1152], F32, kind="ExternalInput").ap()
    maskf_in = nc.dram_tensor("maskf_in", [128, 5 * 640], F32, kind="ExternalInput").ap()
    maska_in = nc.dram_tensor("maska_in", [128, 3 * 384], F32, kind="ExternalInput").ap()
    ident_in = nc.dram_tensor("ident", [128, 128], F32, kind="ExternalInput").ap()
    scales_in = nc.dram_tensor("scales", [128, 256], F32, kind="ExternalInput").ap()
    lnp_in = nc.dram_tensor("lnp", [128, 32], F32, kind="ExternalInput").ap()
    b2c_in = nc.dram_tensor("b2c", [128, 16], F32, kind="ExternalInput").ap()
    b1a_in = nc.dram_tensor("b1a", [128, 2 * NHC], F32, kind="ExternalInput").ap()
    b1g_in = nc.dram_tensor("b1g", [128, 2 * NHC], F32, kind="ExternalInput").ap()
    bgv_in = nc.dram_tensor("bgv", [128, 16], F32, kind="ExternalInput").ap()
    gmc_in = nc.dram_tensor("gmc", [128, 8], F32, kind="ExternalInput").ap()
    w_aqkv = nc.dram_tensor("w_aqkv", [128, 8, 3072], F32, kind="ExternalInput").ap()
    w_agat = nc.dram_tensor("w_agat", [128, 8, 16], F32, kind="ExternalInput").ap()
    w_aout = nc.dram_tensor("w_aout", [128, 8, DM], F32, kind="ExternalInput").ap()
    w_fqkv = nc.dram_tensor("w_fqkv", [2, 128, 8, 1536], F32, kind="ExternalInput").ap()
    w_fo = nc.dram_tensor("w_fo", [2, 128, 4, DM], F32, kind="ExternalInput").ap()
    w_f1 = nc.dram_tensor("w_f1", [2, NGRP, 128, 8, 512], F32, kind="ExternalInput").ap()
    w_f2 = nc.dram_tensor("w_f2", [2, NGRP, 128, 2, DM], F32, kind="ExternalInput").ap()

    A = nc.alloc_sbuf_tensor
    xn_bf = A("xn_bf", [128, 8, 9 * 128 + 1], BF16).ap()
    strm = A("strm", [128, 8, 8 * 128 + 1], F32).ap()
    y1arena = A("y1arena", [128, 4100], F32).ap()
    y1a = y1arena.bitcast(BF16).rearrange("p (a b) -> p a b", b=1025)
    kt1 = A("kt1", [128, 4, RING * 128], BF16).ap()
    v1r = A("v1r", [128, RING, FH * 65], BF16).ap()
    kt2 = A("kt2", [128, 4, RING * 128], BF16).ap()
    v2r = A("v2r", [128, RING, FH * 65], BF16).ap()
    kta = A("kta", [128, 8, ACTX * 128], BF16).ap()
    vat = A("vat", [128, ACTX, AH * 65], BF16).ap()
    rope = A("rope", [128, 4, 9, 32], F32).ap()
    maskf = A("maskf", [128, 5, 640], BF16).ap()
    maska = A("maska", [128, 3, 384], BF16).ap()
    id_f = A("id_f", [128, 128], F32).ap()
    id_b = A("id_b", [128, 128], BF16).ap()
    scl = A("scl", [128, 2, 2, 64], F32).ap()
    lnp = A("lnp_sb", [128, 2, 2, 8], F32).ap()
    b2c = A("b2c_sb", [128, 2, 8], F32).ap()
    b1a = A("b1a_sb", [128, 2, NHC], F32).ap()
    b1g = A("b1g_sb", [128, 2, NHC], F32).ap()
    bgv = A("bgv_sb", [128, 16], F32).ap()
    gmc = A("gmc_sb", [128, 8], F32).ap()
    wag = A("wag", [128, 8, 16], BF16).ap()
    inv32 = A("inv32", [128, NT], F32).ap()
    xn_cr = A("xn_cr", [128, 8], BF16).ap()
    y1b_cr = A("y1b_cr", [128, 8], BF16).ap()
    x1f_cr = A("x1f_cr", [128, 8], F32).ap()
    y1f_cr = A("y1f_cr", [128, 8], F32).ap()
    xt0 = A("xt0", [128, DM], F32).ap()
    xt1 = A("xt1", [128, DM], F32).ap()
    znt = A("znt", [128, DM], F32).ap()
    prb0 = A("prb0", [128, 640], BF16).ap()
    prb1 = A("prb1", [128, 640], BF16).ap()
    ptb0 = prb0.rearrange("p (a n) -> p a n", n=128)
    ptb1 = prb1.rearrange("p (a n) -> p a n", n=128)
    qtb = A("qtb", [128, 8, 128], BF16).ap()
    otk = A("otk", [128, DM], BF16).ap()
    otb = A("otb", [128, 8, 128], BF16).ap()
    krot = A("krot", [128, 512], BF16).ap()
    qrot = A("qrot", [128, 512], BF16).ap()
    sml = A("sml", [128, 10, 16], F32).ap()
    gsc = A("gsc", [128, 3, 16], F32).ap()
    eps5 = A("eps5", [128, 1], F32).ap()
    tiny = A("tiny", [128, 1], F32).ap()
    y1f_pv = A("y1f_pv", [128, 8], F32).ap()
    ninv32 = A("ninv32", [128, NT], F32).ap()
    inv32q = A("inv32q", [128, NT], F32).ap()
    WA = A("WA", [128, 8, 3072], BF16).ap()
    W2 = A("W2", [128, 8, DM], BF16).ap()
    ostg = y1arena[:, 0:2048].rearrange("p (a b) -> p a b", a=2)
    rem = nc.sbuf_bytes_remaining
    assert rem >= 0, f"SBUF over budget by {-rem} bytes"

    PS = nc.alloc_psum_tensor
    PSA = PS("PSA", [128, 512], F32).ap()
    PSTR = PS("PSTR", [128, 1024], BF16).ap()
    PSB = PS("PSB", [128, 1024], F32).ap()
    PSC = PS("PSC", [128, 1024], F32).ap()
    PSD = PS("PSD", [128, 512], F32).ap()
    PSE = PS("PSE", [128, 512], F32).ap()
    assert nc.psum_banks_remaining >= 0
    pa = TB(PSA, "pa")
    ptr = TB(PSTR, "ptr")
    pb0, pb1 = TB(None, "pb0"), TB(None, "pb1")
    pc0, pc1 = TB(None, "pc0"), TB(None, "pc1")
    pd = TB(PSD, "pd")
    pe = TB(PSE, "pe")

    tb = {n: TB(None, n) for n in
          ["xt0", "xt1", "znt", "prb0", "prb1",
           "qtb", "qtb0", "qtb1", "otk", "otb", "krot", "qrot", "ubf",
           "ostg0", "ostg1", "gsc", "lnh", "y1bf", "rope"]}
    WAr = [TB(None, f"WAr{i}") for i in range(4)]
    W2wo = [TB(None, "W2wo0"), TB(None, "W2wo1")]
    W2g = [TB(None, "W2g0"), TB(None, "W2g1")]
    W2flat = W2.rearrange("p a b -> p (a b)")
    ubf_l = [W2flat[:, 0:2050].rearrange("p (a b) -> p a b", a=2),
             W2flat[:, 4096:4096 + 2050].rearrange("p (a b) -> p a b", a=2)]
    phase_toks = {"xnbf": None, "strm": None}

    def emit_init():
        Y.op("dma_start", out=id_f, in_=ident_in).inc(16, "yi")
        Y.op("dma_start", out=scl, in_=scales_in.rearrange(
            "p (l t d) -> p l t d", l=2, t=2)).inc(16, "yi")
        Y.op("dma_start", out=lnp, in_=lnp_in.rearrange(
            "p (l i c) -> p l i c", l=2, i=2)).inc(16, "yi")
        Y.op("dma_start", out=b2c, in_=b2c_in.rearrange("p (l c) -> p l c", l=2)).inc(16, "yi")
        Y.op("dma_start", out=b1a, in_=b1a_in.rearrange("p (l c) -> p l c", l=2)).inc(16, "yi")
        Y.op("dma_start", out=b1g, in_=b1g_in.rearrange("p (l c) -> p l c", l=2)).inc(16, "yi")
        Y.op("dma_start", out=bgv, in_=bgv_in).inc(16, "yi")
        Y.op("dma_start", out=gmc, in_=gmc_in).inc(16, "yi")
        wa_f32 = WA.rearrange("p a b -> p (a b)").bitcast(F32)
        mf_stage = wa_f32[:, 0:3200]
        ma_stage = wa_f32[:, 3200:3200 + 1152]
        Y.op("dma_start", out=mf_stage, in_=maskf_in).inc(16, "yi")
        tok_m = Y.op("dma_start", out=ma_stage, in_=maska_in).inc(16, "yi")
        V.wait(tok_m)
        V.op("tensor_copy", out=maskf.rearrange("p a b -> p (a b)"), in_=mf_stage)
        V.op("tensor_copy", out=maska.rearrange("p a b -> p (a b)"), in_=ma_stage)
        V.op("tensor_copy", out=id_b, in_=id_f)
        for ap in (kt1, kt2, kta):
            V.op("memset", ap, 0.0)
        for ap in (v1r, v2r, vat):
            v4 = ap.rearrange("p s (h e) -> p s h e", e=65)
            V.op("memset", v4[:, :, :, 0:64], 0.0)
            V.op("memset", v4[:, :, :, 64:65], 1.0)
        for ap in (xn_cr, y1b_cr, x1f_cr, y1f_cr, y1f_pv, inv32):
            V.op("memset", ap, 0.0)
        V.op("memset", eps5, 1e-5)
        V.op("memset", tiny, 1e-27)
        vtok = V.op("memset", sml, 0.0).inc()
        T.wait(vtok)
        S.wait(vtok)
        G.wait(vtok)
        ytok = ("yi", c.count["yi"])
        T.wait(ytok)
        S.wait(ytok)
        V.wait(ytok)
        for r in WAr:
            r.rd = [vtok]

    def load_wqkv_ffn(layer):
        bw(G, WAr[0], WAr[1])
        tok = G.op("dma_start", out=WA[:, :, 0:1536],
                   in_=w_fqkv[layer]).inc(16, "ga0")
        ew(tok, WAr[0], WAr[1])
        return tok

    def load_wo_ffn(layer):
        reg = W2wo[layer]
        bw(G, reg, W2g[0], W2g[1])
        dst = W2flat[:, layer * 4096:(layer + 1) * 4096].rearrange(
            "p (k n) -> p k n", k=4)
        tok = G.op("dma_start", out=dst,
                   in_=w_fo[layer]).inc(16, ("go0", "go1")[layer])
        ew(tok, reg)
        return tok

    def load_wgrp(layer, g):
        reg = WAr[g % 4]
        bw(G, reg)
        c0_ = 2 * g * 128
        cw = min(256, HID - c0_)
        base = WA[:, :, 768 * (g % 4):768 * (g % 4) + 768]
        tok1 = G.op("dma_start", out=base[:, :, 0:512],
                    in_=w_f1[layer, g]).inc(16, f"ga{g % 4}")
        ew(tok1, reg)
        w2reg = W2g[g % 2]
        bw(G, w2reg, W2wo[1 - layer])
        half0 = 4096 * (1 - layer)
        w2slot = W2flat[:, half0 + (g % 2) * 2048: half0 + (g % 2) * 2048 + 2048]
        w2slot = w2slot.rearrange("p (k n) -> p k n", k=2)
        tok2 = G.op("dma_start", out=w2slot,
                    in_=w_f2[layer, g]).inc(16, f"gw{g % 2}")
        ew(tok2, w2reg)
        return reg, w2reg, base, w2slot, cw

    def load_watt():
        bw(G, *WAr)
        tok = G.op("dma_start", out=WA, in_=w_aqkv).inc(16, "ga0")
        ew(tok, *WAr)
        tokg = G.op("dma_start", out=wag, in_=w_agat).inc(16, "ga1")
        return tok, tokg

    def load_wout():
        bw(G, W2wo[0], W2wo[1], W2g[0], W2g[1])
        dst = W2flat.rearrange("p (k n) -> p k n", k=8)
        tok = G.op("dma_start", out=dst, in_=w_aout).inc(16, "go0")
        ew(tok, W2wo[0], W2wo[1], W2g[0], W2g[1])
        return tok

    def emit_A(ch):
        for i, r in enumerate(ch["tiles"]):
            xt = (xt0, xt1)[i % 2]
            xtb = tb[("xt0", "xt1")[i % 2]]
            bw(Y, xtb)
            ytok = Y.op("dma_start", out=xt,
                        in_=xin[r * 128:(r + 1) * 128, :]).inc(
                            16, ("yx0", "yx1")[i % 2])
            ew(ytok, xtb)
            V.wait(ytok)
            bw(V, tb["znt"])
            V.op("bn_stats", out=sml[:, 0, 0:6], in_=xt[:, 0:512])
            vtok = V.op("bn_stats", out=sml[:, 0, 6:12], in_=xt[:, 512:1024]).inc()
            V.wait(vtok)
            vtok = V.op("bn_aggr", out=sml[:, 1, 0:2], in_=sml[:, 0, 0:12]).inc()
            S.wait(vtok)
            stok = S.op("activation", out=sml[:, 1, 2:3], in_=sml[:, 1, 1:2],
                        func=ACT.Ln, bias=eps5).inc()
            S.wait(stok)
            stok = S.op("activation", out=sml[:, 1, 3:4], in_=sml[:, 1, 2:3],
                        func=ACT.Exp, scale=-0.5).inc()
            V.wait(stok)
            vtok = V.op("tensor_scalar", out=sml[:, 1, 4:5], in0=sml[:, 1, 0:1],
                        scalar1=sml[:, 1, 3:4], scalar2=-1.0,
                        op0=ALU.mult, op1=ALU.mult).inc()
            bw(S, tb["znt"])
            S.wait(vtok)
            stok = S.op("activation", out=znt, in_=xt, func=ACT.Identity,
                        bias=sml[:, 1, 4:5], scale=sml[:, 1, 3:4]).inc()
            ew(stok, tb["znt"])
            er(stok, xtb)
            V.wait(stok)
            V.op("bn_stats", out=sml[:, 2, 0:6], in_=znt[:, 0:512])
            vtok = V.op("bn_stats", out=sml[:, 2, 6:12], in_=znt[:, 512:1024]).inc()
            V.wait(vtok)
            vtok = V.op("bn_aggr", out=sml[:, 3, 0:2], in_=sml[:, 2, 0:12]).inc()
            S.wait(vtok)
            stok = S.op("activation", out=sml[:, 3, 2:3], in_=sml[:, 3, 1:2],
                        func=ACT.Ln, bias=tiny).inc()
            S.wait(stok)
            stok = S.op("activation", out=inv32[:, r:r + 1], in_=sml[:, 3, 2:3],
                        func=ACT.Exp, scale=-0.5).inc()
            V.wait(stok)
            V.op("tensor_scalar_mul", out=ninv32[:, r:r + 1],
                 in0=inv32[:, r:r + 1], scalar1=-1.0)
            V.op("tensor_scalar_mul", out=inv32q[:, r:r + 1],
                 in0=inv32[:, r:r + 1], scalar1=0.125).inc()
            bw(T, pb0, pb1)
            T.wait(stok)
            for kc in range(8):
                h = T.op("transpose", PSB[:, kc * 128:(kc + 1) * 128],
                         znt[:, kc * 128:(kc + 1) * 128], id_f)
            ttok = h.inc()
            ew(ttok, pb0, pb1)
            er(ttok, tb["znt"])
            pb8 = PSB.rearrange("p (a b) -> p a b", a=8)
            if r >= ch["xnbase"]:
                col = 1 + (r - ch["xnbase"]) * 128
                br(S, pb0)
                stok = S.op("copy", out=xn_bf[:, :, col:col + 128], in_=pb8).inc()
                er(stok, pb0, pb1)
            elif r == ch["xnbase"] - 1:
                br(S, pb0)
                stok = S.op("copy", out=xn_bf[:, :, 0:1],
                            in_=pb8[:, :, 127:128]).inc()
                er(stok, pb0, pb1)
            if r >= ch["sbase"]:
                scol = 1 + (r - ch["sbase"]) * 128
                br(V, pb0)
                V.wait(stok)
                vtok = V.op("tensor_copy", out=strm[:, :, scol:scol + 128],
                            in_=pb8).inc()
                er(vtok, pb0, pb1)
        if ch["tiles"][0] >= OWN0:
            S.op("copy", out=xn_bf[:, :, 0:1], in_=xn_cr.unsqueeze(2))
        last_col = 1 + (ch["tiles"][-1] - ch["xnbase"]) * 128 + 127
        S.wait(S.tok())
        stok = S.op("copy", out=xn_cr.unsqueeze(2),
                    in_=xn_bf[:, :, last_col:last_col + 1]).inc()
        phase_toks["xnbf"] = stok
        phase_toks["strm"] = c.V.tok()

    def emit_wave_ffn(ch, layer, wq_tok, wo_tok):
        kv_tiles = ch["b_kv"] if layer == 0 else ch["d_kv"]
        full_tiles = ch["b_full"] if layer == 0 else ch["d_full"]
        all_tiles = sorted(set(kv_tiles + full_tiles))
        if not all_tiles:
            return
        src_ = xn_bf if layer == 0 else y1a
        src_base = ch["xnbase"] if layer == 0 else ch["sbase"]
        src_tok = phase_toks["xnbf"] if layer == 0 else tb["y1bf"].wr
        ktr, vrr = (kt1, v1r) if layer == 0 else (kt2, v2r)
        kt3 = ktr.rearrange("p a (s n) -> p a s n", n=128)
        v4 = vrr.rearrange("p s (h e) -> p s h e", e=65)
        wq = WA[:, :, 0:1536]
        wo4 = W2flat[:, layer * 4096:(layer + 1) * 4096].rearrange(
            "p (k n) -> p k n", k=4)
        wave_slots = {r: j for j, r in enumerate(
            sorted(set(ch["b_kv"] + ch["b_full"])))}
        bw(Y, tb["rope"])
        rtok = Y.op("dma_start", out=rope.rearrange("p a b e -> p (a b e)"),
                    in_=rope_in[CHUNKS.index(ch)]).inc(16, "yr")
        ew(rtok, tb["rope"])
        T.wait(src_tok)

        ts_ = {r: {} for r in all_tiles}

        def prep_mm(r):
            d = ts_[r]
            full = r in full_tiles
            col = 1 + (r - src_base) * 128
            slot = r % RING
            psc5 = PSC[:, 0:512]
            bw(T, pa)
            T.wait(wq_tok)
            for kc in range(8):
                h = T.op("matmul", PSA, shift_view(src_, kc, col),
                         wq[:, kc, 512:1024], start=(kc == 0), stop=(kc == 7))
            tok = h.inc()
            ew(tok, pa)
            d["qkv_last"] = tok
            bw(S, tb["xt0"])
            br(S, pa)
            stok = S.op("copy", out=xt0[:, 0:512], in_=PSA).inc()
            er(stok, pa)
            ew(stok, tb["xt0"])
            bw(T, pc0)
            for kc in range(8):
                h = T.op("matmul", psc5, shift_view(src_, kc, col),
                         wq[:, kc, 1024:1536], start=(kc == 0), stop=(kc == 7))
            tok = h.inc()
            ew(tok, pc0)
            d["qkv_last"] = tok
            br(S, pc0)
            stok = S.op("copy", out=v4[:, slot, :, 0:64],
                        in_=psc5.rearrange("p (h e) -> p h e", e=64)).inc()
            er(stok, pc0)
            d["vring"] = stok
            if full:
                bw(T, pa)
                for kc in range(8):
                    h = T.op("matmul", PSA, shift_view(src_, kc, col),
                             wq[:, kc, 0:512], start=(kc == 0), stop=(kc == 7))
                tok = h.inc()
                ew(tok, pa)
                d["qkv_last"] = tok
                bw(S, tb["xt1"])
                br(S, pa)
                stok = S.op("copy", out=xt1[:, 0:512], in_=PSA).inc()
                er(stok, pa)
                ew(stok, tb["xt1"])

        def prep_vec(r):
            d = ts_[r]
            full = r in full_tiles
            slot = r % RING
            rj = wave_slots[r]
            ksl = 1 if (r - all_tiles[0]) % 2 == 0 else 6
            qsl = ksl + 1
            kf = xt0[:, 0:512]
            kf3 = kf.rearrange("p (h d) -> p h d", d=64)
            kf4 = kf.rearrange("p (h s d) -> p h s d", s=2, d=32)
            V.wait(tb["xt0"].wr)
            bw(V, tb["znt"])
            V.op("tensor_mul", out=znt[:, 0:512], in0=kf, in1=kf)
            vtok = V.op("reduce_sum", out=sml[:, ksl, 0:8],
                        in_=znt[:, 0:512].rearrange("p (h d) -> p h d", d=64),
                        axis=AX.X).inc()
            S.wait(vtok)
            stok = S.op("activation", out=sml[:, ksl, 8:16], in_=sml[:, ksl, 0:8],
                        func=ACT.Ln, bias=tiny).inc()
            S.wait(stok)
            stok = S.op("activation", out=sml[:, ksl, 8:16], in_=sml[:, ksl, 8:16],
                        func=ACT.Exp, scale=-0.5).inc()
            V.wait(stok)
            inv_b = sml[:, ksl, 8:16].unsqueeze(2).broadcast_to([128, 8, 64])
            V.op("tensor_tensor", out=kf3, in0=kf3, in1=inv_b, op=ALU.mult)
            ks_b = scl[:, layer, 1, :].unsqueeze(1).broadcast_to([128, 8, 64])
            vtok = V.op("tensor_tensor", out=kf3, in0=kf3, in1=ks_b,
                        op=ALU.mult).inc()
            V.wait(vtok)
            V.wait(tb["rope"].wr)
            sh = znt[:, 0:512].rearrange("p (h s d) -> p h s d", s=2, d=32)
            V.op("tensor_copy", out=sh[:, :, 0, :], in_=kf4[:, :, 1, :])
            V.op("tensor_copy", out=sh[:, :, 1, :], in_=kf4[:, :, 0, :])
            cs_b = rope[:, 2, rj, :].unsqueeze(1).unsqueeze(1).broadcast_to(
                [128, 8, 2, 32])
            sn_b = rope[:, 3, rj, :].unsqueeze(1).unsqueeze(1).broadcast_to(
                [128, 8, 2, 32])
            V.op("tensor_tensor", out=kf4, in0=kf4, in1=cs_b, op=ALU.mult)
            V.op("tensor_tensor", out=sh, in0=sh, in1=sn_b, op=ALU.mult)
            bw(V, tb["krot"])
            kr4 = krot.rearrange("p (h s d) -> p h s d", s=2, d=32)
            V.op("tensor_sub", out=kr4[:, :, 0, :], in0=kf4[:, :, 0, :],
                 in1=sh[:, :, 0, :])
            vtok = V.op("tensor_add", out=kr4[:, :, 1, :], in0=kf4[:, :, 1, :],
                        in1=sh[:, :, 1, :]).inc()
            ew(vtok, tb["krot"])
            ew(vtok, tb["znt"])
            er(vtok, tb["xt0"], tb["rope"])
            bw(T, ptr)
            T.wait(vtok)
            for j in range(4):
                h = T.op("transpose", PSTR[:, j * 128:(j + 1) * 128],
                         krot[:, j * 128:(j + 1) * 128], id_b)
            ttok = h.inc()
            ew(ttok, ptr)
            er(ttok, tb["krot"])
            br(S, ptr)
            stok = S.op("copy", out=kt3[:, :, slot, :],
                        in_=PSTR[:, 0:512].rearrange("p (a n) -> p a n",
                                                     n=128)).inc()
            er(stok, ptr)
            d["kring"] = stok
            if not full:
                return
            qpar = (r - all_tiles[0]) % 2
            qtb_tb = tb[("qtb0", "qtb1")[qpar]]
            qtv = qtb[:, 4 * qpar:4 * qpar + 4, :]
            d["qsl"] = qsl
            qf = xt1[:, 0:512]
            qf3 = qf.rearrange("p (h d) -> p h d", d=64)
            qf4 = qf.rearrange("p (h s d) -> p h s d", s=2, d=32)
            V.wait(tb["xt1"].wr)
            bw(V, tb["znt"])
            V.op("tensor_mul", out=znt[:, 0:512], in0=qf, in1=qf)
            vtok = V.op("reduce_sum", out=sml[:, qsl, 0:8],
                        in_=znt[:, 0:512].rearrange("p (h d) -> p h d", d=64),
                        axis=AX.X).inc()
            S.wait(vtok)
            stok = S.op("activation", out=sml[:, qsl, 8:16], in_=sml[:, qsl, 0:8],
                        func=ACT.Ln, scale=1.0 / 64.0, bias=tiny).inc()
            S.wait(stok)
            stok = S.op("activation", out=sml[:, qsl, 8:16],
                        in_=sml[:, qsl, 8:16], func=ACT.Exp, scale=-0.5).inc()
            V.wait(stok)
            invq_b = sml[:, qsl, 8:16].unsqueeze(2).broadcast_to([128, 8, 64])
            V.op("tensor_tensor", out=qf3, in0=qf3, in1=invq_b, op=ALU.mult)
            qs_b = scl[:, layer, 0, :].unsqueeze(1).broadcast_to([128, 8, 64])
            vtok = V.op("tensor_tensor", out=qf3, in0=qf3, in1=qs_b,
                        op=ALU.mult).inc()
            V.wait(vtok)
            sh2 = znt[:, 0:512].rearrange("p (h s d) -> p h s d", s=2, d=32)
            V.op("tensor_copy", out=sh2[:, :, 0, :], in_=qf4[:, :, 1, :])
            V.op("tensor_copy", out=sh2[:, :, 1, :], in_=qf4[:, :, 0, :])
            cs_q = rope[:, 0, rj, :].unsqueeze(1).unsqueeze(1).broadcast_to(
                [128, 8, 2, 32])
            sn_q = rope[:, 1, rj, :].unsqueeze(1).unsqueeze(1).broadcast_to(
                [128, 8, 2, 32])
            V.op("tensor_tensor", out=qf4, in0=qf4, in1=cs_q, op=ALU.mult)
            V.op("tensor_tensor", out=sh2, in0=sh2, in1=sn_q, op=ALU.mult)
            bw(V, tb["qrot"])
            qr4 = qrot.rearrange("p (h s d) -> p h s d", s=2, d=32)
            V.op("tensor_sub", out=qr4[:, :, 0, :], in0=qf4[:, :, 0, :],
                 in1=sh2[:, :, 0, :])
            vtok = V.op("tensor_add", out=qr4[:, :, 1, :], in0=qf4[:, :, 1, :],
                        in1=sh2[:, :, 1, :]).inc()
            ew(vtok, tb["qrot"])
            ew(vtok, tb["znt"])
            er(vtok, tb["xt1"], tb["rope"])
            bw(T, ptr)
            T.wait(vtok)
            for j in range(4):
                h = T.op("transpose", PSTR[:, 512 + j * 128:512 + (j + 1) * 128],
                         qrot[:, j * 128:(j + 1) * 128], id_b)
            ttok = h.inc()
            ew(ttok, ptr)
            er(ttok, tb["qrot"])
            bw(S, qtb_tb)
            br(S, ptr)
            stok = S.op("copy", out=qtv,
                        in_=PSTR[:, 512:1024].rearrange("p (a n) -> p a n",
                                                        n=128)).inc()
            er(stok, ptr)
            ew(stok, qtb_tb)
            d["qT"] = stok

        def heads(r):
            d = ts_[r]
            col = 1 + (r - src_base) * 128
            mvi = min(max(r - OWN0, 0), 4)
            qpar = (r - all_tiles[0]) % 2
            qtv = qtb[:, 4 * qpar:4 * qpar + 4, :]
            qtb_tb = tb[("qtb0", "qtb1")[qpar]]
            qsl = d["qsl"]
            ctx_tiles = list(range(r - 4, r + 1))
            hs = [dict() for _ in range(FH)]
            v_ = min(max(r - OWN0, 0), 4)
            mask_regions = [0, 4] + [j for j in (1, 2, 3) if j < 4 - v_]

            def _scp(pp):
                toks = []
                for par in (0, 1):
                    sc_tbs_ = ((pb0, pb1), (pc0, pc1))[par]
                    bw(T, *sc_tbs_)
                T.wait(d["qT"])
                T.wait(d["kring"])
                for j, cr in enumerate(ctx_tiles):
                    cslot = cr % RING
                    masked = j in mask_regions
                    for par in (0, 1):
                        sc_ap = (PSB, PSC)[par]
                        lo = 64 * par
                        T.op("matmul", sc_ap[:, j * 128:(j + 1) * 128],
                             qtv[lo:lo + 64, pp, :],
                             kt3[lo:lo + 64, pp, cslot, :],
                             start=True, stop=not masked)
                    if masked:
                        for par in (0, 1):
                            sc_ap = (PSB, PSC)[par]
                            h = T.op("matmul", sc_ap[:, j * 128:(j + 1) * 128],
                                     id_b, maskf[:, v_, j * 128:(j + 1) * 128],
                                     start=False, stop=True)
                tok = h.inc()
                hs[2 * pp]["sc"] = tok
                hs[2 * pp + 1]["sc"] = tok
                ew(tok, pb0, pb1, pc0, pc1)

            def _mx(hh):
                par = hh % 2
                sc_ap = (PSB, PSC)[par]
                sc_tbs_ = ((pb0, pb1), (pc0, pc1))[par]
                V.wait(hs[hh]["sc"])
                h = V.op("reduce_max", out=sml[:, 3, hh:hh + 1],
                         in_=sc_ap[:, 0:640], axis=AX.X, negate=True)
                hs[hh]["mx"] = h.inc()
                er(hs[hh]["mx"], *sc_tbs_)

            def _ex(hh):
                par = hh % 2
                sc_ap = (PSB, PSC)[par]
                sc_tbs_ = ((pb0, pb1), (pc0, pc1))[par]
                prb = (prb0, prb1)[par]
                prtb = tb[("prb0", "prb1")[par]]
                bw(S, prtb)
                S.wait(hs[hh]["mx"])
                h = S.op("activation", out=prb, in_=sc_ap[:, 0:640], func=ACT.Exp,
                         bias=sml[:, 3, hh:hh + 1])
                hs[hh]["ex"] = h.inc()
                er(hs[hh]["ex"], *sc_tbs_)
                ew(hs[hh]["ex"], prtb)

            def _pt(hh):
                par = hh % 2
                prb = (prb0, prb1)[par]
                prtb = tb[("prb0", "prb1")[par]]
                bw(T, ptr)
                T.wait(hs[hh]["ex"])
                for j in range(5):
                    h = T.op("transpose", PSTR[:, j * 128:(j + 1) * 128],
                             prb[:, j * 128:(j + 1) * 128], id_b)
                hs[hh]["pt"] = h.inc()
                ew(hs[hh]["pt"], ptr)
                er(hs[hh]["pt"], prtb)

            def _pe(hh):
                par = hh % 2
                ptbuf = (ptb0, ptb1)[par]
                prtb = tb[("prb0", "prb1")[par]]
                br(V, ptr)
                h = V.op("tensor_copy", out=ptbuf,
                         in_=PSTR[:, 0:640].rearrange("p (a n) -> p a n", n=128))
                hs[hh]["pe"] = h.inc()
                er(hs[hh]["pe"], ptr)
                ew(hs[hh]["pe"], prtb)

            def _av(hh):
                par = hh % 2
                ptbuf = (ptb0, ptb1)[par]
                prtb = tb[("prb0", "prb1")[par]]
                avtb = (pd, pe)[par]
                bw(T, avtb)
                T.wait(hs[hh]["pe"])
                T.wait(d["vring"])
                for j, cr in enumerate(ctx_tiles):
                    cslot = cr % RING
                    h = T.op("matmul", avtb.ap[:, 0:65], ptbuf[:, j, :],
                             vrr[:, cslot, hh * 65:hh * 65 + 65],
                             start=(j == 0), stop=(j == 4))
                hs[hh]["av"] = h.inc()
                ew(hs[hh]["av"], avtb)
                er(hs[hh]["av"], prtb)

            def _rc(hh):
                avtb = (pd, pe)[hh % 2]
                br(V, avtb)
                V.wait(hs[hh]["av"])
                h = V.op("reciprocal", out=sml[:, 5, hh:hh + 1],
                         in_=avtb.ap[:, 64:65])
                hs[hh]["rc"] = h.inc()
                er(hs[hh]["rc"], avtb)

            def _oe(hh):
                avtb = (pd, pe)[hh % 2]
                bw(S, tb["otk"])
                br(S, avtb)
                S.wait(hs[hh]["rc"])
                h = S.op("activation", out=otk[:, hh * 64:(hh + 1) * 64],
                         in_=avtb.ap[:, 0:64], func=ACT.Identity,
                         scale=sml[:, 5, hh:hh + 1])
                hs[hh]["oe"] = h.inc()
                er(hs[hh]["oe"], avtb)
                ew(hs[hh]["oe"], tb["otk"])

            _scp(0)
            _mx(0)
            _mx(1)
            for pp in range(FH // 2):
                h0, h1 = 2 * pp, 2 * pp + 1
                _ex(h0)
                _pt(h0)
                _ex(h1)
                _pe(h0)
                _av(h0)
                _pt(h1)
                _pe(h1)
                if pp + 1 < FH // 2:
                    _scp(pp + 1)
                _rc(h0)
                _av(h1)
                _oe(h0)
                if pp + 1 < FH // 2:
                    _mx(h0 + 2)
                    _mx(h1 + 2)
                _rc(h1)
                _oe(h1)
            er(c.T.tok(), qtb_tb)

        def tail(r):
            scol = 1 + (r - ch["sbase"]) * 128
            bw(T, ptr)
            T.wait(tb["otk"].wr)
            for j in range(4):
                h = T.op("transpose", PSTR[:, j * 128:(j + 1) * 128],
                         otk[:, j * 128:(j + 1) * 128], id_b)
            ttok = h.inc()
            ew(ttok, ptr)
            er(ttok, tb["otk"])
            bw(S, tb["otb"])
            br(S, ptr)
            stok = S.op("copy", out=otb[:, 0:4, :],
                        in_=PSTR[:, 0:512].rearrange("p (a n) -> p a n",
                                                     n=128)).inc()
            er(stok, ptr)
            ew(stok, tb["otb"])
            tok_oT = stok
            for rnd in range(2):
                bw(T, pe)
                T.wait(tok_oT)
                T.wait(wo_tok)
                for dmc in range(4):
                    dm = rnd * 4 + dmc
                    for ic in range(4):
                        h = T.op("matmul", PSE[:, dmc * 128:(dmc + 1) * 128],
                                 wo4[:, ic, dm * 128:(dm + 1) * 128],
                                 otb[:, ic, :], start=(ic == 0), stop=(ic == 3))
                tok_wo = h.inc()
                ew(tok_wo, pe)
                ts_[r]["wo"] = tok_wo
                br(V, pe)
                V.wait(tok_wo)
                for dmc in range(4):
                    dm = rnd * 4 + dmc
                    h = V.op("tensor_tensor", out=strm[:, dm, scol:scol + 128],
                             in0=PSE[:, dmc * 128:(dmc + 1) * 128],
                             in1=strm[:, dm, scol:scol + 128], op=ALU.add)
                vtok = h.inc()
                er(vtok, pe)
                er(vtok, tb["otb"])

        prep_mm(all_tiles[0])
        prep_vec(all_tiles[0])
        last_qkv = None
        last_wo = None
        for i, r in enumerate(all_tiles):
            if i + 1 < len(all_tiles):
                prep_mm(all_tiles[i + 1])
            if r in full_tiles:
                heads(r)
            if i + 1 < len(all_tiles):
                prep_vec(all_tiles[i + 1])
            if r in full_tiles:
                tail(r)
        for r in all_tiles:
            if "qkv_last" in ts_[r]:
                last_qkv = ts_[r]["qkv_last"]
            if "wo" in ts_[r]:
                last_wo = ts_[r]["wo"]
        er(last_qkv, WAr[0], WAr[1])
        if last_wo is not None:
            er(last_wo, W2wo[layer])
        phase_toks["strm"] = c.V.tok()


    def emit_ffn(ch, layer):
        tiles = ch["c_tiles"] if layer == 0 else ch["e_tiles"]
        if not tiles:
            return
        ncols = len(tiles) * 128
        cr_ = x1f_cr if layer == 0 else y1f_pv
        if layer == 0:
            h0 = V.op("tensor_copy", out=y1f_pv.unsqueeze(2),
                      in_=y1f_cr.unsqueeze(2))
            V.wait(h0.inc())
        if layer == 1:
            V.wait(tb["y1bf"].wr)
        h = V.op("tensor_copy", out=strm[:, :, 0:1],
                 in_=cr_.unsqueeze(2))
        last = 1 + (tiles[-1] - ch["sbase"]) * 128 + 127
        if layer == 0:
            h = V.op("tensor_copy", out=cr_.unsqueeze(2),
                     in_=strm[:, :, last:last + 1])
        crumb_tok = h.inc()
        cst = {r: {} for r in tiles}

        def c1_tr1(r):
            col = 1 + (r - ch["sbase"]) * 128
            bw(T, pb0, pb1)
            T.wait(crumb_tok)
            T.wait(phase_toks["strm"])
            for kc in range(8):
                h = T.op("transpose", PSB[:, kc * 128:(kc + 1) * 128],
                         shift_view(strm, kc, col), id_f)
            ttok = h.inc()
            ew(ttok, pb0, pb1)

        def c1_stats(r, i):
            xt = (xt0, xt1)[i % 2]
            xtb = tb[("xt0", "xt1")[i % 2]]
            sa = 6 + 2 * (i % 2)
            sb = sa + 1
            bw(S, xtb)
            br(S, pb0)
            stok = S.op("copy", out=xt, in_=PSB).inc()
            er(stok, pb0, pb1)
            ew(stok, xtb)
            V.wait(stok)
            V.op("bn_stats", out=sml[:, sa, 0:6], in_=xt[:, 0:512])
            vtok = V.op("bn_stats", out=sml[:, sa, 6:12], in_=xt[:, 512:1024]).inc()
            V.wait(vtok)
            vtok = V.op("bn_aggr", out=sml[:, sb, 0:2], in_=sml[:, sa, 0:12]).inc()
            S.wait(vtok)
            stok = S.op("activation", out=sml[:, sb, 2:3], in_=sml[:, sb, 1:2],
                        func=ACT.Ln, bias=eps5).inc()
            S.wait(stok)
            stok = S.op("activation", out=sml[:, sb, 3:4], in_=sml[:, sb, 2:3],
                        func=ACT.Exp, scale=-0.5).inc()
            V.wait(stok)
            vtok = V.op("tensor_scalar", out=sml[:, sb, 4:5], in0=sml[:, sb, 0:1],
                        scalar1=sml[:, sb, 3:4], scalar2=-1.0,
                        op0=ALU.mult, op1=ALU.mult).inc()
            bw(S, tb["znt"])
            S.wait(vtok)
            stok = S.op("activation", out=znt, in_=xt, func=ACT.Identity,
                        bias=sml[:, sb, 4:5], scale=sml[:, sb, 3:4]).inc()
            ew(stok, tb["znt"])
            er(stok, xtb)
            cst[r]["aff"] = stok

        def c1_tr2(r, i):
            lcol = (r - tiles[0]) * 128
            bw(T, pc0, pc1)
            T.wait(cst[r]["aff"])
            for kc in range(8):
                h = T.op("transpose", PSC[:, kc * 128:(kc + 1) * 128],
                         znt[:, kc * 128:(kc + 1) * 128], id_f)
            ttok = h.inc()
            ew(ttok, pc0, pc1)
            er(ttok, tb["znt"])
            br(S, pc0)
            for kc in range(8):
                h = S.op("activation", out=y1a[:, kc, 1 + lcol:1 + lcol + 128],
                         in_=PSC[:, kc * 128:(kc + 1) * 128], func=ACT.Identity,
                         scale=lnp[:, layer, 0, kc:kc + 1],
                         bias=lnp[:, layer, 1, kc:kc + 1])
            stok = h.inc()
            er(stok, pc0, pc1)
            cst[r]["lnh"] = stok

        c1_tr1(tiles[0])
        for i, r in enumerate(tiles):
            c1_stats(r, i)
            if i + 1 < len(tiles):
                c1_tr1(tiles[i + 1])
            c1_tr2(r, i)
        stok = cst[tiles[-1]]["lnh"]
        ew(stok, tb["lnh"])
        wins = []
        off, left = 1, ncols
        while left > 0:
            w = min(512, left)
            wins.append((off, w))
            off += w
            left -= w
        vtok = None
        ps_y_par = 0
        ubf = ubf_l[layer]
        tb["ubf"].rd = tb["ubf"].rd + list(W2wo[layer].rd)
        vtok_last_u = None
        for g in range(NGRP):
            reg, w2reg, base, w2slot, cw = load_wgrp(layer, g)
            nch = 2 if cw > 128 else 1
            for hc in range(nch):
                ccw = min(128, cw - hc * 128)
                for (wo_, W) in wins:
                    bw(T, pd, pe)
                    T.wait(reg.wr)
                    T.wait(tb["lnh"].wr)
                    for kc in range(8):
                        T.op("matmul", PSD[0:ccw, 0:W],
                             base[:, kc, hc * 128:hc * 128 + ccw],
                             y1a[:, kc, wo_:wo_ + W],
                             start=(kc == 0), stop=(kc == 7))
                    for kc in range(8):
                        h = T.op("matmul", PSE[0:ccw, 0:W],
                                 base[:, kc, 256 + hc * 128:256 + hc * 128 + ccw],
                                 y1a[:, kc, wo_:wo_ + W],
                                 start=(kc == 0), stop=(kc == 7))
                    tok_ag = h.inc()
                    ew(tok_ag, pd, pe)
                    er(tok_ag, reg)
                    bw(S, tb["znt"])
                    br(S, pe)
                    S.wait(tok_ag)
                    stok = S.op("activation", out=znt[0:ccw, 0:W],
                                in_=PSE[0:ccw, 0:W], func=ACT.Gelu,
                                bias=b1g[0:ccw, layer,
                                         2 * g + hc:2 * g + hc + 1]).inc()
                    er(stok, pe)
                    ew(stok, tb["znt"])
                    bw(V, tb["ubf"])
                    br(V, pd)
                    V.wait(stok)
                    vtok = V.op("scalar_tensor_tensor",
                                out=ubf[0:ccw, hc, wo_:wo_ + W],
                                in0=PSD[0:ccw, 0:W],
                                scalar=b1a[0:ccw, layer,
                                           2 * g + hc:2 * g + hc + 1],
                                in1=znt[0:ccw, 0:W], op0=ALU.add,
                                op1=ALU.mult).inc()
                    er(vtok, pd)
                    er(vtok, tb["znt"])
                    ew(vtok, tb["ubf"])
            for dmc in range(8):
                for (wo_, W) in wins:
                    pcb = (pc0, pc1)[ps_y_par]
                    pslice = PSC[:, ps_y_par * 512:ps_y_par * 512 + W]
                    bw(T, pcb)
                    T.wait(w2reg.wr)
                    T.wait(tb["ubf"].wr)
                    for hc in range(nch):
                        chw = min(128, cw - hc * 128)
                        h = T.op("matmul", pslice,
                                 w2slot[0:chw, hc, dmc * 128:(dmc + 1) * 128],
                                 ubf[0:chw, hc, wo_:wo_ + W],
                                 start=(hc == 0), stop=(hc == nch - 1))
                    tok_y = h.inc()
                    ew(tok_y, pcb)
                    er(tok_y, w2reg)
                    er(tok_y, tb["ubf"])
                    br(V, pcb)
                    V.wait(tok_y)
                    if g == 0:
                        h = V.op("scalar_tensor_tensor",
                                 out=strm[:, dmc, wo_:wo_ + W], in0=pslice,
                                 scalar=b2c[:, layer, dmc:dmc + 1],
                                 in1=strm[:, dmc, wo_:wo_ + W],
                                 op0=ALU.add, op1=ALU.add)
                    else:
                        h = V.op("tensor_tensor", out=strm[:, dmc, wo_:wo_ + W],
                                 in0=pslice, in1=strm[:, dmc, wo_:wo_ + W],
                                 op=ALU.add)
                    vtok = h.inc()
                    er(vtok, pcb)
                    ps_y_par ^= 1
        er(vtok, W2wo[layer])
        phase_toks["strm"] = vtok
        if layer == 0:
            S.wait(vtok)
            S.op("copy", out=y1a[:, :, 0:1], in_=y1b_cr.unsqueeze(2))
            for (wo_, W) in wins:
                h = S.op("copy", out=y1a[:, :, wo_:wo_ + W],
                         in_=strm[:, :, wo_:wo_ + W])
            S.wait(h.inc())
            S.op("copy", out=y1b_cr.unsqueeze(2),
                 in_=y1a[:, :, last:last + 1])
            stok = S.op("copy", out=y1f_cr.unsqueeze(2),
                        in_=strm[:, :, last:last + 1]).inc()
            ew(stok, tb["y1bf"])

    def emit_att(ch, wq_tok, wg_tok, wout_tok):
        kv_tiles, full_tiles = ch["f_kv"], ch["f_full"]
        all_tiles = sorted(set(kv_tiles + full_tiles))
        if not all_tiles:
            return
        kt3 = kta.rearrange("p a (s n) -> p a s n", n=128)
        va4 = vat.rearrange("p s (h e) -> p s h e", e=65)
        wo8 = W2flat.rearrange("p (k n) -> p k n", k=8)
        seedT, seedS = c.T.tok(), c.S.tok()
        for n_ in ("ostg0", "ostg1"):
            tb[n_].rd = tb[n_].rd + [seedT, seedS]
        V.wait(wq_tok)
        V.wait(wg_tok)
        gb = gmc.unsqueeze(2).broadcast_to([128, 8, 3072])
        V.op("tensor_tensor", out=WA, in0=WA, in1=gb, op=ALU.mult)
        gb2 = gmc.unsqueeze(2).broadcast_to([128, 8, 16])
        gtok = V.op("tensor_tensor", out=wag, in0=wag, in1=gb2, op=ALU.mult).inc()
        T.wait(gtok)
        T.wait(phase_toks["xnbf"])
        last_qkv_tok = None
        last_wout_tok = None
        for r in all_tiles:
            full = r in full_tiles
            slot = r % ACTX
            mvi = min(max(r - OWN0, 0), 2)
            col = 1 + (r - ch["xnbase"]) * 128
            for hf in range(2):
                pax = (pa, pc0)[hf]
                psx = (PSA, PSC[:, 0:512])[hf]
                bw(T, pax)
                for kc in range(8):
                    h = T.op("matmul", psx, xn_bf[:, kc, col:col + 128],
                             WA[:, kc, 1024 + hf * 512:1024 + (hf + 1) * 512],
                             start=(kc == 0), stop=(kc == 7))
                tok = h.inc()
                ew(tok, pax)
                last_qkv_tok = tok
                bw(S, tb["krot"])
                br(S, pax)
                stok = S.op("activation", out=krot, in_=psx, func=ACT.Identity,
                            scale=inv32[:, r:r + 1]).inc()
                er(stok, pax)
                ew(stok, tb["krot"])
                bw(T, ptr)
                T.wait(stok)
                for j in range(4):
                    h = T.op("transpose",
                             PSTR[:, hf * 512 + j * 128:hf * 512 + (j + 1) * 128],
                             krot[:, j * 128:(j + 1) * 128], id_b)
                ttok = h.inc()
                ew(ttok, ptr)
                er(ttok, tb["krot"])
                br(S, ptr)
                stok = S.op("copy", out=kt3[:, 4 * hf:4 * hf + 4, slot, :],
                            in_=PSTR[:, hf * 512:(hf + 1) * 512].rearrange(
                                "p (a n) -> p a n", n=128)).inc()
                er(stok, ptr)
            tok_kring = stok
            for hf in range(2):
                pax = (pa, pc0)[hf]
                psx = (PSA, PSC[:, 0:512])[hf]
                bw(T, pax)
                for kc in range(8):
                    h = T.op("matmul", psx, xn_bf[:, kc, col:col + 128],
                             WA[:, kc, 2048 + hf * 512:2048 + (hf + 1) * 512],
                             start=(kc == 0), stop=(kc == 7))
                tok = h.inc()
                ew(tok, pax)
                last_qkv_tok = tok
                br(S, pax)
                stok = S.op("activation",
                            out=va4[:, slot, 8 * hf:8 * hf + 8, 0:64],
                            in_=psx.rearrange("p (h e) -> p h e", e=64),
                            func=ACT.Identity, scale=inv32[:, r:r + 1]).inc()
                er(stok, pax)
            tok_vring = stok
            if not full:
                continue
            for hf in range(2):
                pax = (pa, pc0)[hf]
                psx = (PSA, PSC[:, 0:512])[hf]
                bw(T, pax)
                for kc in range(8):
                    h = T.op("matmul", psx, xn_bf[:, kc, col:col + 128],
                             WA[:, kc, hf * 512:(hf + 1) * 512],
                             start=(kc == 0), stop=(kc == 7))
                tok = h.inc()
                ew(tok, pax)
                last_qkv_tok = tok
                bw(S, tb["otk"])
                br(S, pax)
                stok = S.op("activation", out=otk[:, hf * 512:(hf + 1) * 512],
                            in_=psx, func=ACT.Identity,
                            scale=inv32q[:, r:r + 1]).inc()
                er(stok, pax)
                ew(stok, tb["otk"])
            bw(T, pd)
            for kc in range(8):
                h = T.op("matmul", PSD[:, 0:16], xn_bf[:, kc, col:col + 128],
                         wag[:, kc, :], start=(kc == 0), stop=(kc == 7))
            tok_g = h.inc()
            ew(tok_g, pd)
            last_qkv_tok = tok_g
            bw(S, tb["gsc"])
            br(S, pd)
            stok = S.op("activation", out=gsc[:, 0, :], in_=PSD[:, 0:16],
                        func=ACT.Identity, scale=ninv32[:, r:r + 1]).inc()
            er(stok, pd)
            V.wait(stok)
            vtok = V.op("tensor_sub", out=gsc[:, 0, :], in0=gsc[:, 0, :],
                        in1=bgv).inc()
            S.wait(vtok)
            stok = S.op("activation", out=gsc[:, 1, :], in_=gsc[:, 0, :],
                        func=ACT.Exp).inc()
            V.wait(stok)
            vtok = V.op("tensor_scalar_add", out=gsc[:, 1, :], in0=gsc[:, 1, :],
                        scalar1=1.0).inc()
            V.wait(vtok)
            vtok = V.op("reciprocal", out=gsc[:, 1, :], in_=gsc[:, 1, :]).inc()
            ew(vtok, tb["gsc"])
            bw(T, ptr)
            T.wait(tb["otk"].wr)
            for j in range(8):
                h = T.op("transpose", PSTR[:, j * 128:(j + 1) * 128],
                         otk[:, j * 128:(j + 1) * 128], id_b)
            ttok = h.inc()
            ew(ttok, ptr)
            er(ttok, tb["otk"])
            bw(S, tb["qtb"])
            br(S, ptr)
            stok = S.op("copy", out=qtb,
                        in_=PSTR.rearrange("p (a n) -> p a n", n=128)).inc()
            er(stok, ptr)
            ew(stok, tb["qtb"])
            tok_qT = stok
            ctx_tiles = list(range(r - 2, r + 1))
            hs = [dict() for _ in range(AH)]
            v_ = min(max(r - OWN0, 0), 2)
            mask_regions = [0, 2] + ([1] if v_ == 0 else [])

            def _scp(pp):
                for par in (0, 1):
                    bw(T, (pb0, pb1)[par])
                T.wait(tok_qT)
                T.wait(tok_kring)
                for j, cr_t in enumerate(ctx_tiles):
                    cslot = cr_t % ACTX
                    masked = j in mask_regions
                    for par in (0, 1):
                        lo = 64 * par
                        T.op("matmul",
                             PSB[:, par * 512 + j * 128:par * 512 + (j + 1) * 128],
                             qtb[lo:lo + 64, pp, :], kt3[lo:lo + 64, pp, cslot, :],
                             start=True, stop=not masked)
                    if masked:
                        for par in (0, 1):
                            h = T.op("matmul",
                                     PSB[:, par * 512 + j * 128:par * 512 + (j + 1) * 128],
                                     id_b, maska[:, v_, j * 128:(j + 1) * 128],
                                     start=False, stop=True)
                tok = h.inc()
                hs[2 * pp]["sc"] = tok
                hs[2 * pp + 1]["sc"] = tok
                ew(tok, pb0, pb1)

            def _mx(hh):
                par = hh % 2
                sc_tb = (pb0, pb1)[par]
                V.wait(hs[hh]["sc"])
                h = V.op("reduce_max", out=sml[:, 7, hh:hh + 1],
                         in_=PSB[:, par * 512:par * 512 + 384], axis=AX.X,
                         negate=True)
                hs[hh]["mx"] = h.inc()
                er(hs[hh]["mx"], sc_tb)

            def _ex(hh):
                par = hh % 2
                sc_tb = (pb0, pb1)[par]
                prb = (prb0, prb1)[par]
                prtb = tb[("prb0", "prb1")[par]]
                bw(S, prtb)
                S.wait(hs[hh]["mx"])
                h = S.op("activation", out=prb[:, 0:384],
                         in_=PSB[:, par * 512:par * 512 + 384], func=ACT.Exp,
                         bias=sml[:, 7, hh:hh + 1])
                hs[hh]["ex"] = h.inc()
                er(hs[hh]["ex"], sc_tb)
                ew(hs[hh]["ex"], prtb)

            def _pt(hh):
                par = hh % 2
                prb = (prb0, prb1)[par]
                prtb = tb[("prb0", "prb1")[par]]
                bw(T, ptr)
                T.wait(hs[hh]["ex"])
                for j in range(3):
                    h = T.op("transpose", PSTR[:, j * 128:(j + 1) * 128],
                             prb[:, j * 128:(j + 1) * 128], id_b)
                hs[hh]["pt"] = h.inc()
                ew(hs[hh]["pt"], ptr)
                er(hs[hh]["pt"], prtb)

            def _pe(hh):
                par = hh % 2
                ptbuf = (ptb0, ptb1)[par]
                prtb = tb[("prb0", "prb1")[par]]
                br(V, ptr)
                h = V.op("tensor_copy", out=ptbuf[:, 0:3, :],
                         in_=PSTR[:, 0:384].rearrange("p (a n) -> p a n", n=128))
                hs[hh]["pe"] = h.inc()
                er(hs[hh]["pe"], ptr)
                ew(hs[hh]["pe"], prtb)

            def _av(hh):
                par = hh % 2
                ptbuf = (ptb0, ptb1)[par]
                prtb = tb[("prb0", "prb1")[par]]
                avtb = (pd, pe)[par]
                bw(T, avtb)
                T.wait(hs[hh]["pe"])
                T.wait(tok_vring)
                for j, cr_t in enumerate(ctx_tiles):
                    cslot = cr_t % ACTX
                    h = T.op("matmul", avtb.ap[:, 0:65], ptbuf[:, j, :],
                             vat[:, cslot, hh * 65:hh * 65 + 65],
                             start=(j == 0), stop=(j == 2))
                hs[hh]["av"] = h.inc()
                ew(hs[hh]["av"], avtb)
                er(hs[hh]["av"], prtb)

            def _rc1(hh):
                avtb = (pd, pe)[hh % 2]
                br(V, avtb)
                V.wait(hs[hh]["av"])
                V.wait(tb["gsc"].wr)
                h = V.op("reciprocal", out=sml[:, 9, hh:hh + 1],
                         in_=avtb.ap[:, 64:65])
                hs[hh]["rc1"] = h.inc()
                er(hs[hh]["rc1"], avtb)

            def _rc2(hh):
                V.wait(hs[hh]["rc1"])
                h = V.op("tensor_mul", out=sml[:, 9, hh:hh + 1],
                         in0=sml[:, 9, hh:hh + 1], in1=gsc[:, 1, hh:hh + 1])
                hs[hh]["rc2"] = h.inc()

            def _oe(hh):
                avtb = (pd, pe)[hh % 2]
                bw(S, tb["otk"])
                br(S, avtb)
                S.wait(hs[hh]["rc2"])
                h = S.op("activation", out=otk[:, hh * 64:(hh + 1) * 64],
                         in_=avtb.ap[:, 0:64], func=ACT.Identity,
                         scale=sml[:, 9, hh:hh + 1])
                hs[hh]["oe"] = h.inc()
                er(hs[hh]["oe"], avtb)
                ew(hs[hh]["oe"], tb["otk"])

            _scp(0)
            _mx(0)
            _mx(1)
            for pp in range(AH // 2):
                h0, h1 = 2 * pp, 2 * pp + 1
                _ex(h0)
                _pt(h0)
                _ex(h1)
                _pe(h0)
                _av(h0)
                _pt(h1)
                _pe(h1)
                if pp + 1 < AH // 2:
                    _scp(pp + 1)
                _rc1(h0)
                _av(h1)
                _rc1(h1)
                if pp + 1 < AH // 2:
                    _mx(h0 + 2)
                _rc2(h0)
                _oe(h0)
                if pp + 1 < AH // 2:
                    _mx(h1 + 2)
                _rc2(h1)
                _oe(h1)
            bw(T, ptr)
            T.wait(tb["otk"].wr)
            for j in range(8):
                h = T.op("transpose", PSTR[:, j * 128:(j + 1) * 128],
                         otk[:, j * 128:(j + 1) * 128], id_b)
            ttok = h.inc()
            ew(ttok, ptr)
            er(ttok, tb["otk"])
            bw(S, tb["otb"])
            br(S, ptr)
            stok = S.op("copy", out=otb,
                        in_=PSTR.rearrange("p (a n) -> p a n", n=128)).inc()
            er(stok, ptr)
            ew(stok, tb["otb"])
            tok_oT = stok
            bw(T, pc0, pc1)
            T.wait(tok_oT)
            T.wait(wout_tok)
            for dmc in range(8):
                for ic in range(8):
                    h = T.op("matmul", PSC[:, dmc * 128:(dmc + 1) * 128],
                             wo8[:, ic, dmc * 128:(dmc + 1) * 128],
                             otb[:, ic, :], start=(ic == 0), stop=(ic == 7))
            tok_wout = h.inc()
            ew(tok_wout, pc0, pc1)
            er(tok_wout, tb["otb"])
            last_wout_tok = tok_wout
            scol = 1 + (r - ch["sbase"]) * 128
            bw(V, tb["znt"])
            br(V, pc0)
            V.wait(tok_wout)
            for dmc in range(8):
                h = V.op("tensor_tensor", out=znt[:, dmc * 128:(dmc + 1) * 128],
                         in0=PSC[:, dmc * 128:(dmc + 1) * 128],
                         in1=strm[:, dmc, scol:scol + 128], op=ALU.add)
            vtok = h.inc()
            er(vtok, pc0, pc1)
            ew(vtok, tb["znt"])
            bw(T, pb0, pb1)
            T.wait(vtok)
            for kc in range(8):
                h = T.op("transpose", PSB[:, kc * 128:(kc + 1) * 128],
                         znt[:, kc * 128:(kc + 1) * 128], id_f)
            ttok = h.inc()
            ew(ttok, pb0, pb1)
            er(ttok, tb["znt"])
            ib = r % 2
            stg = ostg[:, ib, :]
            stb = tb[("ostg0", "ostg1")[ib]]
            xtb = tb[("xt0", "xt1")[ib]]
            bw(Y, xtb)
            ytok = Y.op("dma_start", out=(xt0, xt1)[ib],
                        in_=xin[r * 128:(r + 1) * 128, :]).inc(
                            16, ("yx0", "yx1")[ib])
            ew(ytok, xtb)
            bw(V, stb)
            br(V, pb0)
            V.wait(ytok)
            vtok = V.op("tensor_add", out=stg, in0=PSB, in1=(xt0, xt1)[ib]).inc()
            er(vtok, pb0, pb1)
            er(vtok, xtb)
            ew(vtok, stb)
            Y.wait(vtok)
            orow = (r - OWN0) * 128
            ytok = Y.op("dma_start", out=yout[orow:orow + 128, :],
                        in_=stg).inc(16, ("yo0", "yo1")[ib])
            er(ytok, stb)
        er(last_qkv_tok, *WAr)
        if last_wout_tok is not None:
            er(last_wout_tok, W2wo[0], W2wo[1], W2g[0], W2g[1])

    def do_taps(label):
        here = dict(strm=strm, xn_bf=xn_bf, y1a=y1a, inv32=inv32, kt1=kt1,
                    v1r=v1r, kt2=kt2, v2r=v2r, kta=kta, vat=vat, otk=otk,
                    qtb=qtb, ubf0=ubf_l[0], ubf1=ubf_l[1], krot=krot, gsc=gsc, ninv32=ninv32,
                    qrot=qrot, sml=sml, rope=rope, WA=WA, W2=W2)
        for lab, bname in taps:
            if lab != label:
                continue
            ap = here[bname]
            tname = f"tap_{lab}_{bname}"
            shp = list(ap.shape)
            tout = nc.dram_tensor(tname, shp, F32, kind="ExternalOutput").ap()
            for e_ in (T, V, S):
                pass
            G.wait(("t", c.count["t"]))
            G.wait(("v", c.count["v"]))
            G.wait(("s", c.count["s"]))
            ttok = G.op("dma_start", out=tout, in_=ap).inc(16, "yi")
            for e_ in (T, V, S, Y):
                e_.wait(ttok)

    emit_init()
    for ch in CHUNKS:
        if nphase >= 2:
            wq0 = load_wqkv_ffn(0)
            wo0 = load_wo_ffn(0)
        ci = CHUNKS.index(ch)
        if nphase >= 1:
            emit_A(ch)
            do_taps(f"A{ci}")
        if nphase >= 2:
            emit_wave_ffn(ch, 0, wq0, wo0)
            do_taps(f"B{ci}")
        if nphase >= 3:
            emit_ffn(ch, 0)
            do_taps(f"C{ci}")
        if nphase >= 4:
            wq1 = load_wqkv_ffn(1)
            wo1 = load_wo_ffn(1)
            emit_wave_ffn(ch, 1, wq1, wo1)
            do_taps(f"D{ci}")
        if nphase >= 5:
            emit_ffn(ch, 1)
            do_taps(f"E{ci}")
        if nphase >= 6:
            watok, wgtok = load_watt()
            wouttok = load_wout()
            emit_att(ch, watok, wgtok, wouttok)
            do_taps(f"F{ci}")
    for s_ in ("yo0", "yo1"):
        Y.wait((s_, c.count[s_]))
        V.wait((s_, c.count[s_]))

    with contextlib.ExitStack() as st:
        for name in ("t", "v", "s") + DMA_SEMS:
            c.sem[name] = st.enter_context(nc.semaphore(f"sem_{name}"))
        block = st.enter_context(nc.Block())

        def mk_replay(rec):
            def f(eng):
                for item in rec.ops:
                    if item[0] == "w":
                        eng.wait_ge(c.sem[item[1]], item[2])
                    else:
                        m, a, kw, incs = item[1]
                        inst = getattr(eng, m)(*a, **kw)
                        for s, n in incs:
                            inst = inst.then_inc(c.sem[s], n)
            return f

        block.tensor(mk_replay(c.T))
        block.vector(mk_replay(c.V))
        block.scalar(mk_replay(c.S))
        block.sync(mk_replay(c.Y))
        block.gpsimd(mk_replay(c.G))
    return nc


def _lhs_chunks(w):
    w = np.ascontiguousarray(np.asarray(w, np.float32))
    K, N = w.shape
    return np.ascontiguousarray(w.reshape(K // 128, 128, N).transpose(1, 0, 2))


def _prep_weights(waqkv, wagat, waout, wfqkv, wfo, wf1, wf2):
    d = {}
    d["w_aqkv"] = _lhs_chunks(waqkv)
    d["w_agat"] = _lhs_chunks(wagat)
    d["w_aout"] = _lhs_chunks(waout)
    d["w_fqkv"] = np.stack([_lhs_chunks(np.asarray(wfqkv, np.float32)[l])
                            for l in range(2)])
    d["w_fo"] = np.stack([_lhs_chunks(np.asarray(wfo, np.float32)[l])
                          for l in range(2)])
    w1 = np.asarray(wf1, np.float32)
    w2 = np.asarray(wf2, np.float32)
    w1g = np.zeros((2, NGRP, 128, 8, 512), np.float32)
    w2g = np.zeros((2, NGRP, 128, 2, DM), np.float32)
    for l in range(2):
        w1c = _lhs_chunks(w1[l])
        for g in range(NGRP):
            c0 = 256 * g
            cw = min(256, HID - c0)
            w1g[l, g, :, :, 0:cw] = w1c[:, :, c0:c0 + cw]
            w1g[l, g, :, :, 256:256 + cw] = w1c[:, :, HID + c0:HID + c0 + cw]
            r0 = 2 * g * 128
            rw = min(256, HID - r0)
            blk = w2[l, r0:r0 + rw]
            pad = np.zeros((256, DM), np.float32)
            pad[:rw] = blk
            w2g[l, g] = pad.reshape(2, 128, DM).transpose(1, 0, 2)
    d["w_f1"] = np.ascontiguousarray(w1g)
    d["w_f2"] = np.ascontiguousarray(w2g)
    return d


def make_in_maps(x, consts, weights):
    rope_h = [_rope_tables_for_core(h) for h in (0, 1)]
    masks_h = [_masks_for_core(h) for h in (0, 1)]
    in_maps = []
    for core in range(8):
        seq, half = core // 2, core % 2
        if half == 0:
            xi = np.concatenate([np.zeros((HALO, DM), np.float32),
                                 x[seq, 0:P]], axis=0)
        else:
            xi = np.ascontiguousarray(x[seq, P - HALO:SEQ])
        mf, ma = masks_h[half]
        im = dict(xin=xi, rope_in=rope_h[half], maskf_in=mf, maska_in=ma,
                  **consts, **weights)
        in_maps.append(im)
    return in_maps


_NC_CACHE = {}


def _get_program():
    if "nc" not in _NC_CACHE:
        _NC_CACHE["nc"] = build_program()
    return _NC_CACHE["nc"]


def kernel(L0_attn_x, L0_attn_gamma, L0_attn_W_qkv, L0_attn_W_gates,
           L0_attn_b_gates, L0_attn_W_out, L0_ffn_wqkv, L0_ffn_q_scale,
           L0_ffn_k_scale, L0_ffn_wo, L0_ffn_ln_g, L0_ffn_ln_b, L0_ffn_w1,
           L0_ffn_b1, L0_ffn_w2, L0_ffn_b2):
    x = np.ascontiguousarray(np.asarray(L0_attn_x, dtype=np.float32))
    consts = _const_inputs(L0_ffn_q_scale, L0_ffn_k_scale, L0_ffn_ln_g,
                           L0_ffn_ln_b, L0_ffn_b1, L0_ffn_b2, L0_attn_b_gates,
                           L0_attn_gamma)
    weights = _prep_weights(L0_attn_W_qkv, L0_attn_W_gates, L0_attn_W_out,
                            L0_ffn_wqkv, L0_ffn_wo, L0_ffn_w1, L0_ffn_w2)
    in_maps = make_in_maps(x, consts, weights)
    nc = _get_program()
    res = run_bass_kernel_spmd(nc, in_maps, core_ids=list(range(8)))
    out = np.empty((NSEQ, SEQ, DM), dtype=np.float32)
    for core in range(8):
        seq, half = core // 2, core % 2
        out[seq, half * P:(half + 1) * P] = np.asarray(res.results[core]["yout"])
    return out


# version 09
# speedup vs baseline: 1.2544x; 1.0318x over previous
import contextlib
import math

import numpy as np

import concourse.bass as bass
import concourse.mybir as mybir
from concourse.bass_utils import run_bass_kernel_spmd

F32 = mybir.dt.float32
BF16 = mybir.dt.bfloat16
ACT = mybir.ActivationFunctionType
ALU = mybir.AluOpType
AX = mybir.AxisListType

DM = 1024
SEQ = 4096
NSEQ = 4
P = 2048
HALO = 1280
XROWS = HALO + P
NT = XROWS // 128
OWN0 = HALO // 128

FH = 8
AH = 16
DH = 64
HID = 2730
NHC = 22
NGRP = 11
FCTX = 5
RING = 6
ACTX = 3
MASKVAL = -1.0e30
LN32 = math.log(32.0)
LN8 = math.log(8.0)


def _chunks():
    c0 = dict(tiles=list(range(0, 10)), sbase=5, xnbase=1,
              b_kv=[1, 2, 3, 4], b_full=[5, 6, 7, 8, 9],
              c_tiles=[5, 6, 7, 8, 9],
              d_kv=[6, 7, 8, 9], d_full=[],
              e_tiles=[],
              f_kv=[8, 9], f_full=[])
    def own(t0):
        t = list(range(t0, t0 + 8))
        return dict(tiles=t, sbase=t0, xnbase=t0, b_kv=[], b_full=t,
                    c_tiles=t, d_kv=[], d_full=t, e_tiles=t, f_kv=[], f_full=t)
    return [c0, own(10), own(18)]


CHUNKS = _chunks()


def _rope_tables_for_core(half):
    n = SEQ
    hi = np.arange(0, DH, 2, dtype=np.float64)
    inv_freq = 10000.0 ** (-hi / DH)
    base = (hi + 0.4 * DH) / (1.4 * DH)
    out = np.zeros((3, 128, 1152), dtype=np.float64)
    for ci, ch in enumerate(CHUNKS):
        wave = sorted(set(ch["b_kv"] + ch["b_full"]))
        for sj, r in enumerate(wave):
            g0 = r * 128 - HALO + (0 if half == 0 else P)
            if half == 0 and g0 < 0:
                continue
            t = g0 + np.arange(128, dtype=np.float64)
            fr = t[:, None] * inv_freq[None, :]
            power = (t - n // 2) / 256.0
            scale = base[None, :] ** power[:, None]
            cs, sn = np.cos(fr), np.sin(fr)
            for ti, tab in enumerate((cs * scale, sn * scale,
                                      cs / scale, sn / scale)):
                out[ci, :, ti * 288 + sj * 32: ti * 288 + sj * 32 + 32] = tab
    return np.ascontiguousarray(out.astype(np.float32))


def _masks_for_core(half):
    qi = np.arange(128)[:, None]

    def mk(nblk, win, nvar):
        ctxw = nblk * 128
        j = np.arange(ctxw)[None, :]
        keep = (j >= qi) & (j <= qi + win)
        m = np.zeros((128, nvar * ctxw), dtype=np.float32)
        for v in range(nvar):
            mm = keep.copy()
            if half == 0 and (nvar - 1 - v) > 0:
                mm[:, :(nvar - 1 - v) * 128] = False
            m[:, v * ctxw:(v + 1) * ctxw] = np.where(mm, 0.0, MASKVAL)
        return np.ascontiguousarray(m)

    return mk(FCTX, 512, 5), mk(ACTX, 256, 3)


def _const_inputs(q_scale, k_scale, ln_g, ln_b, b1_, b2_, b_gates, gamma):
    d = {}
    sc = np.zeros((2, 2, 64), np.float32)
    for l in range(2):
        sc[l] = np.stack([np.asarray(q_scale[l], np.float32),
                          np.asarray(k_scale[l], np.float32)])
    d["scales"] = np.ascontiguousarray(
        np.broadcast_to(sc.reshape(1, 256), (128, 256)))
    lnp = np.zeros((128, 2, 2, 8), np.float32)
    for l in range(2):
        lnp[:, l, 0, :] = np.asarray(ln_g[l], np.float32).reshape(8, 128).T
        lnp[:, l, 1, :] = np.asarray(ln_b[l], np.float32).reshape(8, 128).T
    d["lnp"] = np.ascontiguousarray(lnp.reshape(128, 32))
    b2 = np.zeros((128, 2, 8), np.float32)
    for l in range(2):
        b2[:, l, :] = np.asarray(b2_[l], np.float32).reshape(8, 128).T
    d["b2c"] = np.ascontiguousarray(b2.reshape(128, 16))
    b1a = np.zeros((128, 2, NHC), np.float32)
    b1g = np.zeros((128, 2, NHC), np.float32)
    for l in range(2):
        b1 = np.asarray(b1_[l], np.float32)
        for arr, seg in ((b1a, b1[:HID]), (b1g, b1[HID:])):
            pad = np.zeros(NHC * 128, np.float32)
            pad[:HID] = seg
            arr[:, l, :] = pad.reshape(NHC, 128).T
    d["b1a"] = np.ascontiguousarray(b1a.reshape(128, 2 * NHC))
    d["b1g"] = np.ascontiguousarray(b1g.reshape(128, 2 * NHC))
    d["bgv"] = np.ascontiguousarray(
        np.broadcast_to(np.asarray(b_gates, np.float32).reshape(1, 16), (128, 16)))
    d["gmc"] = np.ascontiguousarray(np.asarray(gamma, np.float32).reshape(8, 128).T)
    d["ident"] = np.eye(128, dtype=np.float32)
    return d


class Rec:
    def __init__(self, ctx, sem_name):
        self.ctx = ctx
        self.sem = sem_name
        self.ops = []

    def wait(self, tok):
        if tok is None:
            return
        sem, val = tok
        if val > 0:
            self.ops.append(("w", sem, val))

    def waits(self, toks):
        best = {}
        for t in toks:
            if t is None:
                continue
            s, v = t
            best[s] = max(best.get(s, 0), v)
        for s, v in best.items():
            self.wait((s, v))

    def op(self, method, *a, **kw):
        r = [method, a, kw, []]
        self.ops.append(("o", r))
        return _OpH(self, r)

    def tok(self):
        return (self.sem, self.ctx.count[self.sem])


class _OpH:
    def __init__(self, rec, r):
        self.rec = rec
        self.r = r

    def inc(self, n=1, sem=None):
        sem = sem or self.rec.sem
        self.r[3].append((sem, n))
        self.rec.ctx.count[sem] += n
        return (sem, self.rec.ctx.count[sem])


class TB:

    def __init__(self, ap=None, name=""):
        self.ap = ap
        self.name = name
        self.wr = None
        self.rd = []


def bw(rec, *tbs):
    for t in tbs:
        rec.waits(t.rd)


def ew(tok, *tbs):
    for t in tbs:
        t.wr = tok
        t.rd = []


def br(rec, *tbs):
    for t in tbs:
        rec.wait(t.wr)


def er(tok, *tbs):
    for t in tbs:
        t.rd.append(tok)


DMA_SEMS = ("yi", "yx0", "yx1", "yr", "yo0", "yo1",
            "ga0", "ga1", "ga2", "ga3", "gw0", "gw1", "go0", "go1")


class Ctx:
    def __init__(self, nc):
        self.nc = nc
        self.count = {k: 0 for k in ("t", "v", "s", "y", "g") + DMA_SEMS}
        self.T = Rec(self, "t")
        self.V = Rec(self, "v")
        self.S = Rec(self, "s")
        self.Y = Rec(self, "y")
        self.G = Rec(self, "g")
        self.sem = {}


def shift_view(buf, kc, col, n=128):
    off = col - 1 if kc >= 4 else col
    return buf[:, kc, off:off + n]


def build_program(nphase=99, taps=()):
    nc = bass.Bass("TRN2", target_bir_lowering=False,
                   detect_race_conditions=False)
    c = Ctx(nc)
    T, V, S, Y, G = c.T, c.V, c.S, c.Y, c.G

    xin = nc.dram_tensor("xin", [XROWS, DM], F32, kind="ExternalInput").ap()
    yout = nc.dram_tensor("yout", [P, DM], F32, kind="ExternalOutput").ap()
    rope_in = nc.dram_tensor("rope_in", [3, 128, 1152], F32, kind="ExternalInput").ap()
    maskf_in = nc.dram_tensor("maskf_in", [128, 5 * 640], F32, kind="ExternalInput").ap()
    maska_in = nc.dram_tensor("maska_in", [128, 3 * 384], F32, kind="ExternalInput").ap()
    ident_in = nc.dram_tensor("ident", [128, 128], F32, kind="ExternalInput").ap()
    scales_in = nc.dram_tensor("scales", [128, 256], F32, kind="ExternalInput").ap()
    lnp_in = nc.dram_tensor("lnp", [128, 32], F32, kind="ExternalInput").ap()
    b2c_in = nc.dram_tensor("b2c", [128, 16], F32, kind="ExternalInput").ap()
    b1a_in = nc.dram_tensor("b1a", [128, 2 * NHC], F32, kind="ExternalInput").ap()
    b1g_in = nc.dram_tensor("b1g", [128, 2 * NHC], F32, kind="ExternalInput").ap()
    bgv_in = nc.dram_tensor("bgv", [128, 16], F32, kind="ExternalInput").ap()
    gmc_in = nc.dram_tensor("gmc", [128, 8], F32, kind="ExternalInput").ap()
    w_aqkv = nc.dram_tensor("w_aqkv", [128, 8, 3072], F32, kind="ExternalInput").ap()
    w_agat = nc.dram_tensor("w_agat", [128, 8, 16], F32, kind="ExternalInput").ap()
    w_aout = nc.dram_tensor("w_aout", [128, 8, DM], F32, kind="ExternalInput").ap()
    w_fqkv = nc.dram_tensor("w_fqkv", [2, 128, 8, 1536], F32, kind="ExternalInput").ap()
    w_fo = nc.dram_tensor("w_fo", [2, 128, 4, DM], F32, kind="ExternalInput").ap()
    w_f1 = nc.dram_tensor("w_f1", [2, NGRP, 128, 8, 512], F32, kind="ExternalInput").ap()
    w_f2 = nc.dram_tensor("w_f2", [2, NGRP, 128, 2, DM], F32, kind="ExternalInput").ap()

    A = nc.alloc_sbuf_tensor
    xn_bf = A("xn_bf", [128, 8, 9 * 128 + 1], BF16).ap()
    strm = A("strm", [128, 8, 8 * 128 + 1], F32).ap()
    y1arena = A("y1arena", [128, 4100], F32).ap()
    y1a = y1arena.bitcast(BF16).rearrange("p (a b) -> p a b", b=1025)
    kt1 = A("kt1", [128, 4, RING * 128], BF16).ap()
    v1r = A("v1r", [128, RING, FH * 65], BF16).ap()
    kt2 = A("kt2", [128, 4, RING * 128], BF16).ap()
    v2r = A("v2r", [128, RING, FH * 65], BF16).ap()
    kta = A("kta", [128, 8, ACTX * 128], BF16).ap()
    vat = A("vat", [128, ACTX, AH * 65], BF16).ap()
    rope = A("rope", [128, 4, 9, 32], F32).ap()
    maskf = A("maskf", [128, 5, 640], BF16).ap()
    maska = A("maska", [128, 3, 384], BF16).ap()
    id_f = A("id_f", [128, 128], F32).ap()
    id_b = A("id_b", [128, 128], BF16).ap()
    scl = A("scl", [128, 2, 2, 64], F32).ap()
    lnp = A("lnp_sb", [128, 2, 2, 8], F32).ap()
    b2c = A("b2c_sb", [128, 2, 8], F32).ap()
    b1a = A("b1a_sb", [128, 2, NHC], F32).ap()
    b1g = A("b1g_sb", [128, 2, NHC], F32).ap()
    bgv = A("bgv_sb", [128, 16], F32).ap()
    gmc = A("gmc_sb", [128, 8], F32).ap()
    wag = A("wag", [128, 8, 16], BF16).ap()
    inv32 = A("inv32", [128, NT], F32).ap()
    xn_cr = A("xn_cr", [128, 8], BF16).ap()
    y1b_cr = A("y1b_cr", [128, 8], BF16).ap()
    x1f_cr = A("x1f_cr", [128, 8], F32).ap()
    y1f_cr = A("y1f_cr", [128, 8], F32).ap()
    xt0 = A("xt0", [128, DM], F32).ap()
    xt1 = A("xt1", [128, DM], F32).ap()
    znt = A("znt", [128, DM], F32).ap()
    prb0 = A("prb0", [128, 640], BF16).ap()
    prb1 = A("prb1", [128, 640], BF16).ap()
    ptb0 = prb0.rearrange("p (a n) -> p a n", n=128)
    ptb1 = prb1.rearrange("p (a n) -> p a n", n=128)
    qtb = A("qtb", [128, 8, 128], BF16).ap()
    otk = A("otk", [128, DM], BF16).ap()
    otb = A("otb", [128, 8, 128], BF16).ap()
    krot = A("krot", [128, 512], BF16).ap()
    qrot = A("qrot", [128, 512], BF16).ap()
    sml = A("sml", [128, 10, 16], F32).ap()
    gsc = A("gsc", [128, 3, 16], F32).ap()
    eps5 = A("eps5", [128, 1], F32).ap()
    tiny = A("tiny", [128, 1], F32).ap()
    y1f_pv = A("y1f_pv", [128, 8], F32).ap()
    ninv32 = A("ninv32", [128, NT], F32).ap()
    inv32q = A("inv32q", [128, NT], F32).ap()
    WA = A("WA", [128, 8, 3072], BF16).ap()
    W2 = A("W2", [128, 8, DM], BF16).ap()
    ostg = y1arena[:, 0:2048].rearrange("p (a b) -> p a b", a=2)
    rem = nc.sbuf_bytes_remaining
    assert rem >= 0, f"SBUF over budget by {-rem} bytes"

    PS = nc.alloc_psum_tensor
    PSA = PS("PSA", [128, 512], F32).ap()
    PSTR = PS("PSTR", [128, 1024], BF16).ap()
    PSB = PS("PSB", [128, 1024], F32).ap()
    PSC = PS("PSC", [128, 1024], F32).ap()
    PSD = PS("PSD", [128, 512], F32).ap()
    PSE = PS("PSE", [128, 512], F32).ap()
    assert nc.psum_banks_remaining >= 0
    pa = TB(PSA, "pa")
    ptr = TB(PSTR, "ptr")
    pb0, pb1 = TB(None, "pb0"), TB(None, "pb1")
    pc0, pc1 = TB(None, "pc0"), TB(None, "pc1")
    pd = TB(PSD, "pd")
    pe = TB(PSE, "pe")

    tb = {n: TB(None, n) for n in
          ["xt0", "xt1", "znt", "prb0", "prb1",
           "qtb", "qtb0", "qtb1", "otk", "otb", "krot", "qrot", "ubf",
           "ostg0", "ostg1", "gsc", "lnh", "y1bf", "rope"]}
    WAr = [TB(None, f"WAr{i}") for i in range(4)]
    W2wo = [TB(None, "W2wo0"), TB(None, "W2wo1")]
    W2g = [TB(None, "W2g0"), TB(None, "W2g1")]
    W2flat = W2.rearrange("p a b -> p (a b)")
    ubf_l = [W2flat[:, 0:2050].rearrange("p (a b) -> p a b", a=2),
             W2flat[:, 4096:4096 + 2050].rearrange("p (a b) -> p a b", a=2)]
    phase_toks = {"xnbf": None, "strm": None}

    def emit_init():
        Y.op("dma_start", out=id_f, in_=ident_in).inc(16, "yi")
        Y.op("dma_start", out=scl, in_=scales_in.rearrange(
            "p (l t d) -> p l t d", l=2, t=2)).inc(16, "yi")
        Y.op("dma_start", out=lnp, in_=lnp_in.rearrange(
            "p (l i c) -> p l i c", l=2, i=2)).inc(16, "yi")
        Y.op("dma_start", out=b2c, in_=b2c_in.rearrange("p (l c) -> p l c", l=2)).inc(16, "yi")
        Y.op("dma_start", out=b1a, in_=b1a_in.rearrange("p (l c) -> p l c", l=2)).inc(16, "yi")
        Y.op("dma_start", out=b1g, in_=b1g_in.rearrange("p (l c) -> p l c", l=2)).inc(16, "yi")
        Y.op("dma_start", out=bgv, in_=bgv_in).inc(16, "yi")
        Y.op("dma_start", out=gmc, in_=gmc_in).inc(16, "yi")
        wa_f32 = WA.rearrange("p a b -> p (a b)").bitcast(F32)
        mf_stage = wa_f32[:, 0:3200]
        ma_stage = wa_f32[:, 3200:3200 + 1152]
        Y.op("dma_start", out=mf_stage, in_=maskf_in).inc(16, "yi")
        tok_m = Y.op("dma_start", out=ma_stage, in_=maska_in).inc(16, "yi")
        V.wait(tok_m)
        V.op("tensor_copy", out=maskf.rearrange("p a b -> p (a b)"), in_=mf_stage)
        V.op("tensor_copy", out=maska.rearrange("p a b -> p (a b)"), in_=ma_stage)
        V.op("tensor_copy", out=id_b, in_=id_f)
        for ap in (kt1, kt2, kta):
            V.op("memset", ap, 0.0)
        for ap in (v1r, v2r, vat):
            v4 = ap.rearrange("p s (h e) -> p s h e", e=65)
            V.op("memset", v4[:, :, :, 0:64], 0.0)
            V.op("memset", v4[:, :, :, 64:65], 1.0)
        for ap in (xn_cr, y1b_cr, x1f_cr, y1f_cr, y1f_pv, inv32):
            V.op("memset", ap, 0.0)
        V.op("memset", eps5, 1e-5)
        V.op("memset", tiny, 1e-27)
        vtok = V.op("memset", sml, 0.0).inc()
        T.wait(vtok)
        S.wait(vtok)
        G.wait(vtok)
        ytok = ("yi", c.count["yi"])
        T.wait(ytok)
        S.wait(ytok)
        V.wait(ytok)
        for r in WAr:
            r.rd = [vtok]

    def load_wqkv_ffn(layer):
        bw(G, WAr[0], WAr[1])
        tok = G.op("dma_start", out=WA[:, :, 0:1536],
                   in_=w_fqkv[layer]).inc(16, "ga0")
        ew(tok, WAr[0], WAr[1])
        return tok

    def load_wo_ffn(layer):
        reg = W2wo[layer]
        bw(G, reg, W2g[0], W2g[1])
        dst = W2flat[:, layer * 4096:(layer + 1) * 4096].rearrange(
            "p (k n) -> p k n", k=4)
        tok = G.op("dma_start", out=dst,
                   in_=w_fo[layer]).inc(16, ("go0", "go1")[layer])
        ew(tok, reg)
        return tok

    def load_wgrp(layer, g):
        reg = WAr[g % 4]
        bw(G, reg)
        c0_ = 2 * g * 128
        cw = min(256, HID - c0_)
        base = WA[:, :, 768 * (g % 4):768 * (g % 4) + 768]
        tok1 = G.op("dma_start", out=base[:, :, 0:512],
                    in_=w_f1[layer, g]).inc(16, f"ga{g % 4}")
        ew(tok1, reg)
        w2reg = W2g[g % 2]
        bw(G, w2reg, W2wo[1 - layer])
        half0 = 4096 * (1 - layer)
        w2slot = W2flat[:, half0 + (g % 2) * 2048: half0 + (g % 2) * 2048 + 2048]
        w2slot = w2slot.rearrange("p (k n) -> p k n", k=2)
        tok2 = G.op("dma_start", out=w2slot,
                    in_=w_f2[layer, g]).inc(16, f"gw{g % 2}")
        ew(tok2, w2reg)
        return reg, w2reg, base, w2slot, cw

    def load_watt():
        bw(G, *WAr)
        tok = G.op("dma_start", out=WA, in_=w_aqkv).inc(16, "ga0")
        ew(tok, *WAr)
        tokg = G.op("dma_start", out=wag, in_=w_agat).inc(16, "ga1")
        return tok, tokg

    def load_wout():
        bw(G, W2wo[0], W2wo[1], W2g[0], W2g[1])
        dst = W2flat.rearrange("p (k n) -> p k n", k=8)
        tok = G.op("dma_start", out=dst, in_=w_aout).inc(16, "go0")
        ew(tok, W2wo[0], W2wo[1], W2g[0], W2g[1])
        return tok

    def emit_A(ch):
        ast = {}

        def a_load(i, r):
            xt = (xt0, xt1)[i % 2]
            xtb = tb[("xt0", "xt1")[i % 2]]
            bw(Y, xtb)
            ytok = Y.op("dma_start", out=xt,
                        in_=xin[r * 128:(r + 1) * 128, :]).inc(
                            16, ("yx0", "yx1")[i % 2])
            ew(ytok, xtb)
            ast[i] = {"ld": ytok}

        def a_stats1(i, r):
            xt = (xt0, xt1)[i % 2]
            sa, sb = ((0, 1), (2, 3))[i % 2]
            V.wait(ast[i]["ld"])
            V.op("bn_stats", out=sml[:, sa, 0:6], in_=xt[:, 0:512])
            vtok = V.op("bn_stats", out=sml[:, sa, 6:12], in_=xt[:, 512:1024]).inc()
            V.wait(vtok)
            vtok = V.op("bn_aggr", out=sml[:, sb, 0:2], in_=sml[:, sa, 0:12]).inc()
            S.wait(vtok)
            stok = S.op("activation", out=sml[:, sb, 2:3], in_=sml[:, sb, 1:2],
                        func=ACT.Ln, bias=eps5).inc()
            S.wait(stok)
            stok = S.op("activation", out=sml[:, sb, 3:4], in_=sml[:, sb, 2:3],
                        func=ACT.Exp, scale=-0.5).inc()
            V.wait(stok)
            vtok = V.op("tensor_scalar", out=sml[:, sb, 4:5], in0=sml[:, sb, 0:1],
                        scalar1=sml[:, sb, 3:4], scalar2=-1.0,
                        op0=ALU.mult, op1=ALU.mult).inc()
            ast[i]["nmr"] = vtok

        def a_affine(i, r):
            xt = (xt0, xt1)[i % 2]
            xtb = tb[("xt0", "xt1")[i % 2]]
            sb = (1, 3)[i % 2]
            bw(S, tb["znt"])
            S.wait(ast[i]["nmr"])
            stok = S.op("activation", out=znt, in_=xt, func=ACT.Identity,
                        bias=sml[:, sb, 4:5], scale=sml[:, sb, 3:4]).inc()
            ew(stok, tb["znt"])
            er(stok, xtb)
            ast[i]["aff"] = stok

        def a_stats2(i, r):
            sa, sb = ((4, 5), (8, 9))[i % 2]
            V.wait(ast[i]["aff"])
            V.op("bn_stats", out=sml[:, sa, 0:6], in_=znt[:, 0:512])
            vtok = V.op("bn_stats", out=sml[:, sa, 6:12], in_=znt[:, 512:1024]).inc()
            V.wait(vtok)
            vtok = V.op("bn_aggr", out=sml[:, sb, 0:2], in_=sml[:, sa, 0:12]).inc()
            er(vtok, tb["znt"])
            S.wait(vtok)
            stok = S.op("activation", out=sml[:, sb, 2:3], in_=sml[:, sb, 1:2],
                        func=ACT.Ln, bias=tiny).inc()
            S.wait(stok)
            stok = S.op("activation", out=inv32[:, r:r + 1], in_=sml[:, sb, 2:3],
                        func=ACT.Exp, scale=-0.5).inc()
            V.wait(stok)
            V.op("tensor_scalar_mul", out=ninv32[:, r:r + 1],
                 in0=inv32[:, r:r + 1], scalar1=-1.0)
            V.op("tensor_scalar_mul", out=inv32q[:, r:r + 1],
                 in0=inv32[:, r:r + 1], scalar1=0.125).inc()

        def a_trans(i, r):
            bw(T, pb0, pb1)
            T.wait(ast[i]["aff"])
            for kc in range(8):
                h = T.op("transpose", PSB[:, kc * 128:(kc + 1) * 128],
                         znt[:, kc * 128:(kc + 1) * 128], id_f)
            ttok = h.inc()
            ew(ttok, pb0, pb1)
            er(ttok, tb["znt"])
            pb8 = PSB.rearrange("p (a b) -> p a b", a=8)
            if r >= ch["xnbase"]:
                col = 1 + (r - ch["xnbase"]) * 128
                br(S, pb0)
                stok = S.op("copy", out=xn_bf[:, :, col:col + 128], in_=pb8).inc()
                er(stok, pb0, pb1)
            elif r == ch["xnbase"] - 1:
                br(S, pb0)
                stok = S.op("copy", out=xn_bf[:, :, 0:1],
                            in_=pb8[:, :, 127:128]).inc()
                er(stok, pb0, pb1)
            if r >= ch["sbase"]:
                scol = 1 + (r - ch["sbase"]) * 128
                br(V, pb0)
                V.wait(stok)
                vtok = V.op("tensor_copy", out=strm[:, :, scol:scol + 128],
                            in_=pb8).inc()
                er(vtok, pb0, pb1)

        tiles = ch["tiles"]
        a_load(0, tiles[0])
        a_load(1, tiles[1])
        a_stats1(0, tiles[0])
        for i, r in enumerate(tiles):
            a_affine(i, r)
            if i + 1 < len(tiles):
                a_stats1(i + 1, tiles[i + 1])
            if i + 2 < len(tiles):
                a_load(i + 2, tiles[i + 2])
            a_trans(i, r)
            a_stats2(i, r)
        if tiles[0] >= OWN0:
            S.op("copy", out=xn_bf[:, :, 0:1], in_=xn_cr.unsqueeze(2))
        last_col = 1 + (tiles[-1] - ch["xnbase"]) * 128 + 127
        S.wait(S.tok())
        stok = S.op("copy", out=xn_cr.unsqueeze(2),
                    in_=xn_bf[:, :, last_col:last_col + 1]).inc()
        phase_toks["xnbf"] = stok
        phase_toks["strm"] = c.V.tok()

    def emit_wave_ffn(ch, layer, wq_tok, wo_tok):
        kv_tiles = ch["b_kv"] if layer == 0 else ch["d_kv"]
        full_tiles = ch["b_full"] if layer == 0 else ch["d_full"]
        all_tiles = sorted(set(kv_tiles + full_tiles))
        if not all_tiles:
            return
        src_ = xn_bf if layer == 0 else y1a
        src_base = ch["xnbase"] if layer == 0 else ch["sbase"]
        src_tok = phase_toks["xnbf"] if layer == 0 else tb["y1bf"].wr
        ktr, vrr = (kt1, v1r) if layer == 0 else (kt2, v2r)
        kt3 = ktr.rearrange("p a (s n) -> p a s n", n=128)
        v4 = vrr.rearrange("p s (h e) -> p s h e", e=65)
        wq = WA[:, :, 0:1536]
        wo4 = W2flat[:, layer * 4096:(layer + 1) * 4096].rearrange(
            "p (k n) -> p k n", k=4)
        wave_slots = {r: j for j, r in enumerate(
            sorted(set(ch["b_kv"] + ch["b_full"])))}
        bw(Y, tb["rope"])
        rtok = Y.op("dma_start", out=rope.rearrange("p a b e -> p (a b e)"),
                    in_=rope_in[CHUNKS.index(ch)]).inc(16, "yr")
        ew(rtok, tb["rope"])
        T.wait(src_tok)

        ts_ = {r: {} for r in all_tiles}

        def prep_mm(r):
            d = ts_[r]
            full = r in full_tiles
            col = 1 + (r - src_base) * 128
            slot = r % RING
            psc5 = PSC[:, 0:512]
            bw(T, pa)
            T.wait(wq_tok)
            for kc in range(8):
                h = T.op("matmul", PSA, shift_view(src_, kc, col),
                         wq[:, kc, 512:1024], start=(kc == 0), stop=(kc == 7))
            tok = h.inc()
            ew(tok, pa)
            d["qkv_last"] = tok
            bw(S, tb["xt0"])
            br(S, pa)
            stok = S.op("copy", out=xt0[:, 0:512], in_=PSA).inc()
            er(stok, pa)
            ew(stok, tb["xt0"])
            bw(T, pc0)
            for kc in range(8):
                h = T.op("matmul", psc5, shift_view(src_, kc, col),
                         wq[:, kc, 1024:1536], start=(kc == 0), stop=(kc == 7))
            tok = h.inc()
            ew(tok, pc0)
            d["qkv_last"] = tok
            br(S, pc0)
            stok = S.op("copy", out=v4[:, slot, :, 0:64],
                        in_=psc5.rearrange("p (h e) -> p h e", e=64)).inc()
            er(stok, pc0)
            d["vring"] = stok
            if full:
                bw(T, pa)
                for kc in range(8):
                    h = T.op("matmul", PSA, shift_view(src_, kc, col),
                             wq[:, kc, 0:512], start=(kc == 0), stop=(kc == 7))
                tok = h.inc()
                ew(tok, pa)
                d["qkv_last"] = tok
                bw(S, tb["xt1"])
                br(S, pa)
                stok = S.op("copy", out=xt1[:, 0:512], in_=PSA).inc()
                er(stok, pa)
                ew(stok, tb["xt1"])

        def prep_vec(r):
            d = ts_[r]
            full = r in full_tiles
            slot = r % RING
            rj = wave_slots[r]
            ksl = 1 if (r - all_tiles[0]) % 2 == 0 else 6
            qsl = ksl + 1
            kf = xt0[:, 0:512]
            kf3 = kf.rearrange("p (h d) -> p h d", d=64)
            kf4 = kf.rearrange("p (h s d) -> p h s d", s=2, d=32)
            V.wait(tb["xt0"].wr)
            bw(V, tb["znt"])
            V.op("tensor_mul", out=znt[:, 0:512], in0=kf, in1=kf)
            vtok = V.op("reduce_sum", out=sml[:, ksl, 0:8],
                        in_=znt[:, 0:512].rearrange("p (h d) -> p h d", d=64),
                        axis=AX.X).inc()
            S.wait(vtok)
            stok = S.op("activation", out=sml[:, ksl, 8:16], in_=sml[:, ksl, 0:8],
                        func=ACT.Ln, bias=tiny).inc()
            S.wait(stok)
            stok = S.op("activation", out=sml[:, ksl, 8:16], in_=sml[:, ksl, 8:16],
                        func=ACT.Exp, scale=-0.5).inc()
            V.wait(stok)
            inv_b = sml[:, ksl, 8:16].unsqueeze(2).broadcast_to([128, 8, 64])
            V.op("tensor_tensor", out=kf3, in0=kf3, in1=inv_b, op=ALU.mult)
            ks_b = scl[:, layer, 1, :].unsqueeze(1).broadcast_to([128, 8, 64])
            vtok = V.op("tensor_tensor", out=kf3, in0=kf3, in1=ks_b,
                        op=ALU.mult).inc()
            V.wait(vtok)
            V.wait(tb["rope"].wr)
            sh = znt[:, 0:512].rearrange("p (h s d) -> p h s d", s=2, d=32)
            V.op("tensor_copy", out=sh[:, :, 0, :], in_=kf4[:, :, 1, :])
            V.op("tensor_copy", out=sh[:, :, 1, :], in_=kf4[:, :, 0, :])
            cs_b = rope[:, 2, rj, :].unsqueeze(1).unsqueeze(1).broadcast_to(
                [128, 8, 2, 32])
            sn_b = rope[:, 3, rj, :].unsqueeze(1).unsqueeze(1).broadcast_to(
                [128, 8, 2, 32])
            V.op("tensor_tensor", out=kf4, in0=kf4, in1=cs_b, op=ALU.mult)
            V.op("tensor_tensor", out=sh, in0=sh, in1=sn_b, op=ALU.mult)
            bw(V, tb["krot"])
            kr4 = krot.rearrange("p (h s d) -> p h s d", s=2, d=32)
            V.op("tensor_sub", out=kr4[:, :, 0, :], in0=kf4[:, :, 0, :],
                 in1=sh[:, :, 0, :])
            vtok = V.op("tensor_add", out=kr4[:, :, 1, :], in0=kf4[:, :, 1, :],
                        in1=sh[:, :, 1, :]).inc()
            ew(vtok, tb["krot"])
            ew(vtok, tb["znt"])
            er(vtok, tb["xt0"], tb["rope"])
            bw(T, ptr)
            T.wait(vtok)
            for j in range(4):
                h = T.op("transpose", PSTR[:, j * 128:(j + 1) * 128],
                         krot[:, j * 128:(j + 1) * 128], id_b)
            ttok = h.inc()
            ew(ttok, ptr)
            er(ttok, tb["krot"])
            br(S, ptr)
            stok = S.op("copy", out=kt3[:, :, slot, :],
                        in_=PSTR[:, 0:512].rearrange("p (a n) -> p a n",
                                                     n=128)).inc()
            er(stok, ptr)
            d["kring"] = stok
            if not full:
                return
            qpar = (r - all_tiles[0]) % 2
            qtb_tb = tb[("qtb0", "qtb1")[qpar]]
            qtv = qtb[:, 4 * qpar:4 * qpar + 4, :]
            d["qsl"] = qsl
            qf = xt1[:, 0:512]
            qf3 = qf.rearrange("p (h d) -> p h d", d=64)
            qf4 = qf.rearrange("p (h s d) -> p h s d", s=2, d=32)
            V.wait(tb["xt1"].wr)
            bw(V, tb["znt"])
            V.op("tensor_mul", out=znt[:, 0:512], in0=qf, in1=qf)
            vtok = V.op("reduce_sum", out=sml[:, qsl, 0:8],
                        in_=znt[:, 0:512].rearrange("p (h d) -> p h d", d=64),
                        axis=AX.X).inc()
            S.wait(vtok)
            stok = S.op("activation", out=sml[:, qsl, 8:16], in_=sml[:, qsl, 0:8],
                        func=ACT.Ln, scale=1.0 / 64.0, bias=tiny).inc()
            S.wait(stok)
            stok = S.op("activation", out=sml[:, qsl, 8:16],
                        in_=sml[:, qsl, 8:16], func=ACT.Exp, scale=-0.5).inc()
            V.wait(stok)
            invq_b = sml[:, qsl, 8:16].unsqueeze(2).broadcast_to([128, 8, 64])
            V.op("tensor_tensor", out=qf3, in0=qf3, in1=invq_b, op=ALU.mult)
            qs_b = scl[:, layer, 0, :].unsqueeze(1).broadcast_to([128, 8, 64])
            vtok = V.op("tensor_tensor", out=qf3, in0=qf3, in1=qs_b,
                        op=ALU.mult).inc()
            V.wait(vtok)
            sh2 = znt[:, 0:512].rearrange("p (h s d) -> p h s d", s=2, d=32)
            V.op("tensor_copy", out=sh2[:, :, 0, :], in_=qf4[:, :, 1, :])
            V.op("tensor_copy", out=sh2[:, :, 1, :], in_=qf4[:, :, 0, :])
            cs_q = rope[:, 0, rj, :].unsqueeze(1).unsqueeze(1).broadcast_to(
                [128, 8, 2, 32])
            sn_q = rope[:, 1, rj, :].unsqueeze(1).unsqueeze(1).broadcast_to(
                [128, 8, 2, 32])
            V.op("tensor_tensor", out=qf4, in0=qf4, in1=cs_q, op=ALU.mult)
            V.op("tensor_tensor", out=sh2, in0=sh2, in1=sn_q, op=ALU.mult)
            bw(V, tb["qrot"])
            qr4 = qrot.rearrange("p (h s d) -> p h s d", s=2, d=32)
            V.op("tensor_sub", out=qr4[:, :, 0, :], in0=qf4[:, :, 0, :],
                 in1=sh2[:, :, 0, :])
            vtok = V.op("tensor_add", out=qr4[:, :, 1, :], in0=qf4[:, :, 1, :],
                        in1=sh2[:, :, 1, :]).inc()
            ew(vtok, tb["qrot"])
            ew(vtok, tb["znt"])
            er(vtok, tb["xt1"], tb["rope"])
            bw(T, ptr)
            T.wait(vtok)
            for j in range(4):
                h = T.op("transpose", PSTR[:, 512 + j * 128:512 + (j + 1) * 128],
                         qrot[:, j * 128:(j + 1) * 128], id_b)
            ttok = h.inc()
            ew(ttok, ptr)
            er(ttok, tb["qrot"])
            bw(S, qtb_tb)
            br(S, ptr)
            stok = S.op("copy", out=qtv,
                        in_=PSTR[:, 512:1024].rearrange("p (a n) -> p a n",
                                                        n=128)).inc()
            er(stok, ptr)
            ew(stok, qtb_tb)
            d["qT"] = stok

        def heads(r):
            d = ts_[r]
            col = 1 + (r - src_base) * 128
            mvi = min(max(r - OWN0, 0), 4)
            qpar = (r - all_tiles[0]) % 2
            qtv = qtb[:, 4 * qpar:4 * qpar + 4, :]
            qtb_tb = tb[("qtb0", "qtb1")[qpar]]
            qsl = d["qsl"]
            ctx_tiles = list(range(r - 4, r + 1))
            hs = [dict() for _ in range(FH)]
            v_ = min(max(r - OWN0, 0), 4)
            mask_regions = [0, 4] + [j for j in (1, 2, 3) if j < 4 - v_]

            def _scp(pp):
                toks = []
                for par in (0, 1):
                    sc_tbs_ = ((pb0, pb1), (pc0, pc1))[par]
                    bw(T, *sc_tbs_)
                T.wait(d["qT"])
                T.wait(d["kring"])
                for j, cr in enumerate(ctx_tiles):
                    cslot = cr % RING
                    masked = j in mask_regions
                    for par in (0, 1):
                        sc_ap = (PSB, PSC)[par]
                        lo = 64 * par
                        T.op("matmul", sc_ap[:, j * 128:(j + 1) * 128],
                             qtv[lo:lo + 64, pp, :],
                             kt3[lo:lo + 64, pp, cslot, :],
                             start=True, stop=not masked)
                    if masked:
                        for par in (0, 1):
                            sc_ap = (PSB, PSC)[par]
                            h = T.op("matmul", sc_ap[:, j * 128:(j + 1) * 128],
                                     id_b, maskf[:, v_, j * 128:(j + 1) * 128],
                                     start=False, stop=True)
                tok = h.inc()
                hs[2 * pp]["sc"] = tok
                hs[2 * pp + 1]["sc"] = tok
                ew(tok, pb0, pb1, pc0, pc1)

            def _mx(hh):
                par = hh % 2
                sc_ap = (PSB, PSC)[par]
                sc_tbs_ = ((pb0, pb1), (pc0, pc1))[par]
                V.wait(hs[hh]["sc"])
                h = V.op("reduce_max", out=sml[:, 3, hh:hh + 1],
                         in_=sc_ap[:, 0:640], axis=AX.X, negate=True)
                hs[hh]["mx"] = h.inc()
                er(hs[hh]["mx"], *sc_tbs_)

            def _ex(hh):
                par = hh % 2
                sc_ap = (PSB, PSC)[par]
                sc_tbs_ = ((pb0, pb1), (pc0, pc1))[par]
                prb = (prb0, prb1)[par]
                prtb = tb[("prb0", "prb1")[par]]
                bw(S, prtb)
                S.wait(hs[hh]["mx"])
                h = S.op("activation", out=prb, in_=sc_ap[:, 0:640], func=ACT.Exp,
                         bias=sml[:, 3, hh:hh + 1])
                hs[hh]["ex"] = h.inc()
                er(hs[hh]["ex"], *sc_tbs_)
                ew(hs[hh]["ex"], prtb)

            def _pt(hh):
                par = hh % 2
                prb = (prb0, prb1)[par]
                prtb = tb[("prb0", "prb1")[par]]
                bw(T, ptr)
                T.wait(hs[hh]["ex"])
                for j in range(5):
                    h = T.op("transpose", PSTR[:, j * 128:(j + 1) * 128],
                             prb[:, j * 128:(j + 1) * 128], id_b)
                hs[hh]["pt"] = h.inc()
                ew(hs[hh]["pt"], ptr)
                er(hs[hh]["pt"], prtb)

            def _pe(hh):
                par = hh % 2
                ptbuf = (ptb0, ptb1)[par]
                prtb = tb[("prb0", "prb1")[par]]
                br(V, ptr)
                h = V.op("tensor_copy", out=ptbuf,
                         in_=PSTR[:, 0:640].rearrange("p (a n) -> p a n", n=128))
                hs[hh]["pe"] = h.inc()
                er(hs[hh]["pe"], ptr)
                ew(hs[hh]["pe"], prtb)

            def _av(hh):
                par = hh % 2
                ptbuf = (ptb0, ptb1)[par]
                prtb = tb[("prb0", "prb1")[par]]
                avtb = (pd, pe)[par]
                bw(T, avtb)
                T.wait(hs[hh]["pe"])
                T.wait(d["vring"])
                for j, cr in enumerate(ctx_tiles):
                    cslot = cr % RING
                    h = T.op("matmul", avtb.ap[:, 0:65], ptbuf[:, j, :],
                             vrr[:, cslot, hh * 65:hh * 65 + 65],
                             start=(j == 0), stop=(j == 4))
                hs[hh]["av"] = h.inc()
                ew(hs[hh]["av"], avtb)
                er(hs[hh]["av"], prtb)

            def _rc(hh):
                avtb = (pd, pe)[hh % 2]
                br(V, avtb)
                V.wait(hs[hh]["av"])
                h = V.op("reciprocal", out=sml[:, 5, hh:hh + 1],
                         in_=avtb.ap[:, 64:65])
                hs[hh]["rc"] = h.inc()
                er(hs[hh]["rc"], avtb)

            def _oe(hh):
                avtb = (pd, pe)[hh % 2]
                bw(S, tb["otk"])
                br(S, avtb)
                S.wait(hs[hh]["rc"])
                h = S.op("activation", out=otk[:, hh * 64:(hh + 1) * 64],
                         in_=avtb.ap[:, 0:64], func=ACT.Identity,
                         scale=sml[:, 5, hh:hh + 1])
                hs[hh]["oe"] = h.inc()
                er(hs[hh]["oe"], avtb)
                ew(hs[hh]["oe"], tb["otk"])

            _scp(0)
            _mx(0)
            _mx(1)
            for pp in range(FH // 2):
                h0, h1 = 2 * pp, 2 * pp + 1
                _ex(h0)
                _pt(h0)
                _ex(h1)
                if pp + 1 < FH // 2:
                    _scp(pp + 1)
                _pe(h0)
                _av(h0)
                if pp + 1 < FH // 2:
                    _mx(h0 + 2)
                _pt(h1)
                _pe(h1)
                _rc(h0)
                _av(h1)
                _oe(h0)
                if pp + 1 < FH // 2:
                    _mx(h1 + 2)
                _rc(h1)
                _oe(h1)
            er(c.T.tok(), qtb_tb)

        def tail(r):
            scol = 1 + (r - ch["sbase"]) * 128
            bw(T, ptr)
            T.wait(tb["otk"].wr)
            for j in range(4):
                h = T.op("transpose", PSTR[:, j * 128:(j + 1) * 128],
                         otk[:, j * 128:(j + 1) * 128], id_b)
            ttok = h.inc()
            ew(ttok, ptr)
            er(ttok, tb["otk"])
            bw(S, tb["otb"])
            br(S, ptr)
            stok = S.op("copy", out=otb[:, 0:4, :],
                        in_=PSTR[:, 0:512].rearrange("p (a n) -> p a n",
                                                     n=128)).inc()
            er(stok, ptr)
            ew(stok, tb["otb"])
            tok_oT = stok
            for rnd in range(2):
                bw(T, pe)
                T.wait(tok_oT)
                T.wait(wo_tok)
                for dmc in range(4):
                    dm = rnd * 4 + dmc
                    for ic in range(4):
                        h = T.op("matmul", PSE[:, dmc * 128:(dmc + 1) * 128],
                                 wo4[:, ic, dm * 128:(dm + 1) * 128],
                                 otb[:, ic, :], start=(ic == 0), stop=(ic == 3))
                tok_wo = h.inc()
                ew(tok_wo, pe)
                ts_[r]["wo"] = tok_wo
                br(V, pe)
                V.wait(tok_wo)
                for dmc in range(4):
                    dm = rnd * 4 + dmc
                    h = V.op("tensor_tensor", out=strm[:, dm, scol:scol + 128],
                             in0=PSE[:, dmc * 128:(dmc + 1) * 128],
                             in1=strm[:, dm, scol:scol + 128], op=ALU.add)
                vtok = h.inc()
                er(vtok, pe)
                er(vtok, tb["otb"])

        prep_mm(all_tiles[0])
        prep_vec(all_tiles[0])
        last_qkv = None
        last_wo = None
        for i, r in enumerate(all_tiles):
            if i + 1 < len(all_tiles):
                prep_mm(all_tiles[i + 1])
            if r in full_tiles:
                heads(r)
            if i + 1 < len(all_tiles):
                prep_vec(all_tiles[i + 1])
            if r in full_tiles:
                tail(r)
        for r in all_tiles:
            if "qkv_last" in ts_[r]:
                last_qkv = ts_[r]["qkv_last"]
            if "wo" in ts_[r]:
                last_wo = ts_[r]["wo"]
        er(last_qkv, WAr[0], WAr[1])
        if last_wo is not None:
            er(last_wo, W2wo[layer])
        phase_toks["strm"] = c.V.tok()


    def emit_ffn(ch, layer):
        tiles = ch["c_tiles"] if layer == 0 else ch["e_tiles"]
        if not tiles:
            return
        ncols = len(tiles) * 128
        cr_ = x1f_cr if layer == 0 else y1f_pv
        if layer == 0:
            h0 = V.op("tensor_copy", out=y1f_pv.unsqueeze(2),
                      in_=y1f_cr.unsqueeze(2))
            V.wait(h0.inc())
        if layer == 1:
            V.wait(tb["y1bf"].wr)
        h = V.op("tensor_copy", out=strm[:, :, 0:1],
                 in_=cr_.unsqueeze(2))
        last = 1 + (tiles[-1] - ch["sbase"]) * 128 + 127
        if layer == 0:
            h = V.op("tensor_copy", out=cr_.unsqueeze(2),
                     in_=strm[:, :, last:last + 1])
        crumb_tok = h.inc()
        cst = {r: {} for r in tiles}

        def c1_tr1(r):
            col = 1 + (r - ch["sbase"]) * 128
            bw(T, pb0, pb1)
            T.wait(crumb_tok)
            T.wait(phase_toks["strm"])
            for kc in range(8):
                h = T.op("transpose", PSB[:, kc * 128:(kc + 1) * 128],
                         shift_view(strm, kc, col), id_f)
            ttok = h.inc()
            ew(ttok, pb0, pb1)

        def c1_stats(r, i):
            xt = (xt0, xt1)[i % 2]
            xtb = tb[("xt0", "xt1")[i % 2]]
            sa = 6 + 2 * (i % 2)
            sb = sa + 1
            bw(S, xtb)
            br(S, pb0)
            stok = S.op("copy", out=xt, in_=PSB).inc()
            er(stok, pb0, pb1)
            ew(stok, xtb)
            V.wait(stok)
            V.op("bn_stats", out=sml[:, sa, 0:6], in_=xt[:, 0:512])
            vtok = V.op("bn_stats", out=sml[:, sa, 6:12], in_=xt[:, 512:1024]).inc()
            V.wait(vtok)
            vtok = V.op("bn_aggr", out=sml[:, sb, 0:2], in_=sml[:, sa, 0:12]).inc()
            S.wait(vtok)
            stok = S.op("activation", out=sml[:, sb, 2:3], in_=sml[:, sb, 1:2],
                        func=ACT.Ln, bias=eps5).inc()
            S.wait(stok)
            stok = S.op("activation", out=sml[:, sb, 3:4], in_=sml[:, sb, 2:3],
                        func=ACT.Exp, scale=-0.5).inc()
            V.wait(stok)
            vtok = V.op("tensor_scalar", out=sml[:, sb, 4:5], in0=sml[:, sb, 0:1],
                        scalar1=sml[:, sb, 3:4], scalar2=-1.0,
                        op0=ALU.mult, op1=ALU.mult).inc()
            bw(S, tb["znt"])
            S.wait(vtok)
            stok = S.op("activation", out=znt, in_=xt, func=ACT.Identity,
                        bias=sml[:, sb, 4:5], scale=sml[:, sb, 3:4]).inc()
            ew(stok, tb["znt"])
            er(stok, xtb)
            cst[r]["aff"] = stok

        def c1_tr2(r, i):
            lcol = (r - tiles[0]) * 128
            bw(T, pc0, pc1)
            T.wait(cst[r]["aff"])
            for kc in range(8):
                h = T.op("transpose", PSC[:, kc * 128:(kc + 1) * 128],
                         znt[:, kc * 128:(kc + 1) * 128], id_f)
            ttok = h.inc()
            ew(ttok, pc0, pc1)
            er(ttok, tb["znt"])
            br(S, pc0)
            for kc in range(8):
                h = S.op("activation", out=y1a[:, kc, 1 + lcol:1 + lcol + 128],
                         in_=PSC[:, kc * 128:(kc + 1) * 128], func=ACT.Identity,
                         scale=lnp[:, layer, 0, kc:kc + 1],
                         bias=lnp[:, layer, 1, kc:kc + 1])
            stok = h.inc()
            er(stok, pc0, pc1)
            cst[r]["lnh"] = stok

        c1_tr1(tiles[0])
        for i, r in enumerate(tiles):
            c1_stats(r, i)
            if i + 1 < len(tiles):
                c1_tr1(tiles[i + 1])
            c1_tr2(r, i)
        stok = cst[tiles[-1]]["lnh"]
        ew(stok, tb["lnh"])
        wins = []
        off, left = 1, ncols
        while left > 0:
            w = min(512, left)
            wins.append((off, w))
            off += w
            left -= w
        vtok = None
        ps_y_par = 0
        ubf = ubf_l[layer]
        tb["ubf"].rd = tb["ubf"].rd + list(W2wo[layer].rd)
        vtok_last_u = None
        for g in range(NGRP):
            reg, w2reg, base, w2slot, cw = load_wgrp(layer, g)
            nch = 2 if cw > 128 else 1
            for hc in range(nch):
                ccw = min(128, cw - hc * 128)
                for (wo_, W) in wins:
                    bw(T, pd, pe)
                    T.wait(reg.wr)
                    T.wait(tb["lnh"].wr)
                    for kc in range(8):
                        T.op("matmul", PSD[0:ccw, 0:W],
                             base[:, kc, hc * 128:hc * 128 + ccw],
                             y1a[:, kc, wo_:wo_ + W],
                             start=(kc == 0), stop=(kc == 7))
                    for kc in range(8):
                        h = T.op("matmul", PSE[0:ccw, 0:W],
                                 base[:, kc, 256 + hc * 128:256 + hc * 128 + ccw],
                                 y1a[:, kc, wo_:wo_ + W],
                                 start=(kc == 0), stop=(kc == 7))
                    tok_ag = h.inc()
                    ew(tok_ag, pd, pe)
                    er(tok_ag, reg)
                    bw(S, tb["znt"])
                    br(S, pe)
                    S.wait(tok_ag)
                    stok = S.op("activation", out=znt[0:ccw, 0:W],
                                in_=PSE[0:ccw, 0:W], func=ACT.Gelu,
                                bias=b1g[0:ccw, layer,
                                         2 * g + hc:2 * g + hc + 1]).inc()
                    er(stok, pe)
                    ew(stok, tb["znt"])
                    bw(V, tb["ubf"])
                    br(V, pd)
                    V.wait(stok)
                    vtok = V.op("scalar_tensor_tensor",
                                out=ubf[0:ccw, hc, wo_:wo_ + W],
                                in0=PSD[0:ccw, 0:W],
                                scalar=b1a[0:ccw, layer,
                                           2 * g + hc:2 * g + hc + 1],
                                in1=znt[0:ccw, 0:W], op0=ALU.add,
                                op1=ALU.mult).inc()
                    er(vtok, pd)
                    er(vtok, tb["znt"])
                    ew(vtok, tb["ubf"])
            for dmc in range(8):
                for (wo_, W) in wins:
                    pcb = (pc0, pc1)[ps_y_par]
                    pslice = PSC[:, ps_y_par * 512:ps_y_par * 512 + W]
                    bw(T, pcb)
                    T.wait(w2reg.wr)
                    T.wait(tb["ubf"].wr)
                    for hc in range(nch):
                        chw = min(128, cw - hc * 128)
                        h = T.op("matmul", pslice,
                                 w2slot[0:chw, hc, dmc * 128:(dmc + 1) * 128],
                                 ubf[0:chw, hc, wo_:wo_ + W],
                                 start=(hc == 0), stop=(hc == nch - 1))
                    tok_y = h.inc()
                    ew(tok_y, pcb)
                    er(tok_y, w2reg)
                    er(tok_y, tb["ubf"])
                    br(V, pcb)
                    V.wait(tok_y)
                    if g == 0:
                        h = V.op("scalar_tensor_tensor",
                                 out=strm[:, dmc, wo_:wo_ + W], in0=pslice,
                                 scalar=b2c[:, layer, dmc:dmc + 1],
                                 in1=strm[:, dmc, wo_:wo_ + W],
                                 op0=ALU.add, op1=ALU.add)
                    else:
                        h = V.op("tensor_tensor", out=strm[:, dmc, wo_:wo_ + W],
                                 in0=pslice, in1=strm[:, dmc, wo_:wo_ + W],
                                 op=ALU.add)
                    vtok = h.inc()
                    er(vtok, pcb)
                    ps_y_par ^= 1
        er(vtok, W2wo[layer])
        phase_toks["strm"] = vtok
        if layer == 0:
            S.wait(vtok)
            S.op("copy", out=y1a[:, :, 0:1], in_=y1b_cr.unsqueeze(2))
            for (wo_, W) in wins:
                h = S.op("copy", out=y1a[:, :, wo_:wo_ + W],
                         in_=strm[:, :, wo_:wo_ + W])
            S.wait(h.inc())
            S.op("copy", out=y1b_cr.unsqueeze(2),
                 in_=y1a[:, :, last:last + 1])
            stok = S.op("copy", out=y1f_cr.unsqueeze(2),
                        in_=strm[:, :, last:last + 1]).inc()
            ew(stok, tb["y1bf"])

    def emit_att(ch, wq_tok, wg_tok, wout_tok):
        kv_tiles, full_tiles = ch["f_kv"], ch["f_full"]
        all_tiles = sorted(set(kv_tiles + full_tiles))
        if not all_tiles:
            return
        kt3 = kta.rearrange("p a (s n) -> p a s n", n=128)
        va4 = vat.rearrange("p s (h e) -> p s h e", e=65)
        wo8 = W2flat.rearrange("p (k n) -> p k n", k=8)
        seedT, seedS = c.T.tok(), c.S.tok()
        for n_ in ("ostg0", "ostg1"):
            tb[n_].rd = tb[n_].rd + [seedT, seedS]
        V.wait(wq_tok)
        V.wait(wg_tok)
        gb = gmc.unsqueeze(2).broadcast_to([128, 8, 3072])
        V.op("tensor_tensor", out=WA, in0=WA, in1=gb, op=ALU.mult)
        gb2 = gmc.unsqueeze(2).broadcast_to([128, 8, 16])
        gtok = V.op("tensor_tensor", out=wag, in0=wag, in1=gb2, op=ALU.mult).inc()
        T.wait(gtok)
        T.wait(phase_toks["xnbf"])
        last_qkv_tok = None
        last_wout_tok = None
        for r in all_tiles:
            full = r in full_tiles
            slot = r % ACTX
            mvi = min(max(r - OWN0, 0), 2)
            col = 1 + (r - ch["xnbase"]) * 128
            for hf in range(2):
                pax = (pa, pc0)[hf]
                psx = (PSA, PSC[:, 0:512])[hf]
                bw(T, pax)
                for kc in range(8):
                    h = T.op("matmul", psx, xn_bf[:, kc, col:col + 128],
                             WA[:, kc, 1024 + hf * 512:1024 + (hf + 1) * 512],
                             start=(kc == 0), stop=(kc == 7))
                tok = h.inc()
                ew(tok, pax)
                last_qkv_tok = tok
                bw(S, tb["krot"])
                br(S, pax)
                stok = S.op("activation", out=krot, in_=psx, func=ACT.Identity,
                            scale=inv32[:, r:r + 1]).inc()
                er(stok, pax)
                ew(stok, tb["krot"])
                bw(T, ptr)
                T.wait(stok)
                for j in range(4):
                    h = T.op("transpose",
                             PSTR[:, hf * 512 + j * 128:hf * 512 + (j + 1) * 128],
                             krot[:, j * 128:(j + 1) * 128], id_b)
                ttok = h.inc()
                ew(ttok, ptr)
                er(ttok, tb["krot"])
                br(S, ptr)
                stok = S.op("copy", out=kt3[:, 4 * hf:4 * hf + 4, slot, :],
                            in_=PSTR[:, hf * 512:(hf + 1) * 512].rearrange(
                                "p (a n) -> p a n", n=128)).inc()
                er(stok, ptr)
            tok_kring = stok
            for hf in range(2):
                pax = (pa, pc0)[hf]
                psx = (PSA, PSC[:, 0:512])[hf]
                bw(T, pax)
                for kc in range(8):
                    h = T.op("matmul", psx, xn_bf[:, kc, col:col + 128],
                             WA[:, kc, 2048 + hf * 512:2048 + (hf + 1) * 512],
                             start=(kc == 0), stop=(kc == 7))
                tok = h.inc()
                ew(tok, pax)
                last_qkv_tok = tok
                br(S, pax)
                stok = S.op("activation",
                            out=va4[:, slot, 8 * hf:8 * hf + 8, 0:64],
                            in_=psx.rearrange("p (h e) -> p h e", e=64),
                            func=ACT.Identity, scale=inv32[:, r:r + 1]).inc()
                er(stok, pax)
            tok_vring = stok
            if not full:
                continue
            for hf in range(2):
                pax = (pa, pc0)[hf]
                psx = (PSA, PSC[:, 0:512])[hf]
                bw(T, pax)
                for kc in range(8):
                    h = T.op("matmul", psx, xn_bf[:, kc, col:col + 128],
                             WA[:, kc, hf * 512:(hf + 1) * 512],
                             start=(kc == 0), stop=(kc == 7))
                tok = h.inc()
                ew(tok, pax)
                last_qkv_tok = tok
                bw(S, tb["otk"])
                br(S, pax)
                stok = S.op("activation", out=otk[:, hf * 512:(hf + 1) * 512],
                            in_=psx, func=ACT.Identity,
                            scale=inv32q[:, r:r + 1]).inc()
                er(stok, pax)
                ew(stok, tb["otk"])
            bw(T, pd)
            for kc in range(8):
                h = T.op("matmul", PSD[:, 0:16], xn_bf[:, kc, col:col + 128],
                         wag[:, kc, :], start=(kc == 0), stop=(kc == 7))
            tok_g = h.inc()
            ew(tok_g, pd)
            last_qkv_tok = tok_g
            bw(S, tb["gsc"])
            br(S, pd)
            stok = S.op("activation", out=gsc[:, 0, :], in_=PSD[:, 0:16],
                        func=ACT.Identity, scale=ninv32[:, r:r + 1]).inc()
            er(stok, pd)
            V.wait(stok)
            vtok = V.op("tensor_sub", out=gsc[:, 0, :], in0=gsc[:, 0, :],
                        in1=bgv).inc()
            S.wait(vtok)
            stok = S.op("activation", out=gsc[:, 1, :], in_=gsc[:, 0, :],
                        func=ACT.Exp).inc()
            V.wait(stok)
            vtok = V.op("tensor_scalar_add", out=gsc[:, 1, :], in0=gsc[:, 1, :],
                        scalar1=1.0).inc()
            V.wait(vtok)
            vtok = V.op("reciprocal", out=gsc[:, 1, :], in_=gsc[:, 1, :]).inc()
            ew(vtok, tb["gsc"])
            bw(T, ptr)
            T.wait(tb["otk"].wr)
            for j in range(8):
                h = T.op("transpose", PSTR[:, j * 128:(j + 1) * 128],
                         otk[:, j * 128:(j + 1) * 128], id_b)
            ttok = h.inc()
            ew(ttok, ptr)
            er(ttok, tb["otk"])
            bw(S, tb["qtb"])
            br(S, ptr)
            stok = S.op("copy", out=qtb,
                        in_=PSTR.rearrange("p (a n) -> p a n", n=128)).inc()
            er(stok, ptr)
            ew(stok, tb["qtb"])
            tok_qT = stok
            ctx_tiles = list(range(r - 2, r + 1))
            hs = [dict() for _ in range(AH)]
            v_ = min(max(r - OWN0, 0), 2)
            mask_regions = [0, 2] + ([1] if v_ == 0 else [])

            def _scp(pp):
                for par in (0, 1):
                    bw(T, (pb0, pb1)[par])
                T.wait(tok_qT)
                T.wait(tok_kring)
                for j, cr_t in enumerate(ctx_tiles):
                    cslot = cr_t % ACTX
                    masked = j in mask_regions
                    for par in (0, 1):
                        lo = 64 * par
                        T.op("matmul",
                             PSB[:, par * 512 + j * 128:par * 512 + (j + 1) * 128],
                             qtb[lo:lo + 64, pp, :], kt3[lo:lo + 64, pp, cslot, :],
                             start=True, stop=not masked)
                    if masked:
                        for par in (0, 1):
                            h = T.op("matmul",
                                     PSB[:, par * 512 + j * 128:par * 512 + (j + 1) * 128],
                                     id_b, maska[:, v_, j * 128:(j + 1) * 128],
                                     start=False, stop=True)
                tok = h.inc()
                hs[2 * pp]["sc"] = tok
                hs[2 * pp + 1]["sc"] = tok
                ew(tok, pb0, pb1)

            def _mx(hh):
                par = hh % 2
                sc_tb = (pb0, pb1)[par]
                V.wait(hs[hh]["sc"])
                h = V.op("reduce_max", out=sml[:, 7, hh:hh + 1],
                         in_=PSB[:, par * 512:par * 512 + 384], axis=AX.X,
                         negate=True)
                hs[hh]["mx"] = h.inc()
                er(hs[hh]["mx"], sc_tb)

            def _ex(hh):
                par = hh % 2
                sc_tb = (pb0, pb1)[par]
                prb = (prb0, prb1)[par]
                prtb = tb[("prb0", "prb1")[par]]
                bw(S, prtb)
                S.wait(hs[hh]["mx"])
                h = S.op("activation", out=prb[:, 0:384],
                         in_=PSB[:, par * 512:par * 512 + 384], func=ACT.Exp,
                         bias=sml[:, 7, hh:hh + 1])
                hs[hh]["ex"] = h.inc()
                er(hs[hh]["ex"], sc_tb)
                ew(hs[hh]["ex"], prtb)

            def _pt(hh):
                par = hh % 2
                prb = (prb0, prb1)[par]
                prtb = tb[("prb0", "prb1")[par]]
                bw(T, ptr)
                T.wait(hs[hh]["ex"])
                for j in range(3):
                    h = T.op("transpose", PSTR[:, j * 128:(j + 1) * 128],
                             prb[:, j * 128:(j + 1) * 128], id_b)
                hs[hh]["pt"] = h.inc()
                ew(hs[hh]["pt"], ptr)
                er(hs[hh]["pt"], prtb)

            def _pe(hh):
                par = hh % 2
                ptbuf = (ptb0, ptb1)[par]
                prtb = tb[("prb0", "prb1")[par]]
                br(V, ptr)
                h = V.op("tensor_copy", out=ptbuf[:, 0:3, :],
                         in_=PSTR[:, 0:384].rearrange("p (a n) -> p a n", n=128))
                hs[hh]["pe"] = h.inc()
                er(hs[hh]["pe"], ptr)
                ew(hs[hh]["pe"], prtb)

            def _av(hh):
                par = hh % 2
                ptbuf = (ptb0, ptb1)[par]
                prtb = tb[("prb0", "prb1")[par]]
                avtb = (pd, pe)[par]
                bw(T, avtb)
                T.wait(hs[hh]["pe"])
                T.wait(tok_vring)
                for j, cr_t in enumerate(ctx_tiles):
                    cslot = cr_t % ACTX
                    h = T.op("matmul", avtb.ap[:, 0:65], ptbuf[:, j, :],
                             vat[:, cslot, hh * 65:hh * 65 + 65],
                             start=(j == 0), stop=(j == 2))
                hs[hh]["av"] = h.inc()
                ew(hs[hh]["av"], avtb)
                er(hs[hh]["av"], prtb)

            def _rc1(hh):
                avtb = (pd, pe)[hh % 2]
                br(V, avtb)
                V.wait(hs[hh]["av"])
                V.wait(tb["gsc"].wr)
                h = V.op("reciprocal", out=sml[:, 9, hh:hh + 1],
                         in_=avtb.ap[:, 64:65])
                hs[hh]["rc1"] = h.inc()
                er(hs[hh]["rc1"], avtb)

            def _rc2(hh):
                V.wait(hs[hh]["rc1"])
                h = V.op("tensor_mul", out=sml[:, 9, hh:hh + 1],
                         in0=sml[:, 9, hh:hh + 1], in1=gsc[:, 1, hh:hh + 1])
                hs[hh]["rc2"] = h.inc()

            def _oe(hh):
                avtb = (pd, pe)[hh % 2]
                bw(S, tb["otk"])
                br(S, avtb)
                S.wait(hs[hh]["rc2"])
                h = S.op("activation", out=otk[:, hh * 64:(hh + 1) * 64],
                         in_=avtb.ap[:, 0:64], func=ACT.Identity,
                         scale=sml[:, 9, hh:hh + 1])
                hs[hh]["oe"] = h.inc()
                er(hs[hh]["oe"], avtb)
                ew(hs[hh]["oe"], tb["otk"])

            _scp(0)
            _mx(0)
            _mx(1)
            for pp in range(AH // 2):
                h0, h1 = 2 * pp, 2 * pp + 1
                _ex(h0)
                _pt(h0)
                _ex(h1)
                if pp + 1 < AH // 2:
                    _scp(pp + 1)
                _pe(h0)
                _av(h0)
                if pp + 1 < AH // 2:
                    _mx(h0 + 2)
                _pt(h1)
                _pe(h1)
                _rc1(h0)
                _av(h1)
                _rc1(h1)
                if pp + 1 < AH // 2:
                    _mx(h1 + 2)
                _rc2(h0)
                _oe(h0)
                _rc2(h1)
                _oe(h1)
            bw(T, ptr)
            T.wait(tb["otk"].wr)
            for j in range(8):
                h = T.op("transpose", PSTR[:, j * 128:(j + 1) * 128],
                         otk[:, j * 128:(j + 1) * 128], id_b)
            ttok = h.inc()
            ew(ttok, ptr)
            er(ttok, tb["otk"])
            bw(S, tb["otb"])
            br(S, ptr)
            stok = S.op("copy", out=otb,
                        in_=PSTR.rearrange("p (a n) -> p a n", n=128)).inc()
            er(stok, ptr)
            ew(stok, tb["otb"])
            tok_oT = stok
            bw(T, pc0, pc1)
            T.wait(tok_oT)
            T.wait(wout_tok)
            for dmc in range(8):
                for ic in range(8):
                    h = T.op("matmul", PSC[:, dmc * 128:(dmc + 1) * 128],
                             wo8[:, ic, dmc * 128:(dmc + 1) * 128],
                             otb[:, ic, :], start=(ic == 0), stop=(ic == 7))
            tok_wout = h.inc()
            ew(tok_wout, pc0, pc1)
            er(tok_wout, tb["otb"])
            last_wout_tok = tok_wout
            scol = 1 + (r - ch["sbase"]) * 128
            bw(V, tb["znt"])
            br(V, pc0)
            V.wait(tok_wout)
            for dmc in range(8):
                h = V.op("tensor_tensor", out=znt[:, dmc * 128:(dmc + 1) * 128],
                         in0=PSC[:, dmc * 128:(dmc + 1) * 128],
                         in1=strm[:, dmc, scol:scol + 128], op=ALU.add)
            vtok = h.inc()
            er(vtok, pc0, pc1)
            ew(vtok, tb["znt"])
            bw(T, pb0, pb1)
            T.wait(vtok)
            for kc in range(8):
                h = T.op("transpose", PSB[:, kc * 128:(kc + 1) * 128],
                         znt[:, kc * 128:(kc + 1) * 128], id_f)
            ttok = h.inc()
            ew(ttok, pb0, pb1)
            er(ttok, tb["znt"])
            ib = r % 2
            stg = ostg[:, ib, :]
            stb = tb[("ostg0", "ostg1")[ib]]
            xtb = tb[("xt0", "xt1")[ib]]
            bw(Y, xtb)
            ytok = Y.op("dma_start", out=(xt0, xt1)[ib],
                        in_=xin[r * 128:(r + 1) * 128, :]).inc(
                            16, ("yx0", "yx1")[ib])
            ew(ytok, xtb)
            bw(V, stb)
            br(V, pb0)
            V.wait(ytok)
            vtok = V.op("tensor_add", out=stg, in0=PSB, in1=(xt0, xt1)[ib]).inc()
            er(vtok, pb0, pb1)
            er(vtok, xtb)
            ew(vtok, stb)
            Y.wait(vtok)
            orow = (r - OWN0) * 128
            ytok = Y.op("dma_start", out=yout[orow:orow + 128, :],
                        in_=stg).inc(16, ("yo0", "yo1")[ib])
            er(ytok, stb)
        er(last_qkv_tok, *WAr)
        if last_wout_tok is not None:
            er(last_wout_tok, W2wo[0], W2wo[1], W2g[0], W2g[1])

    def do_taps(label):
        here = dict(strm=strm, xn_bf=xn_bf, y1a=y1a, inv32=inv32, kt1=kt1,
                    v1r=v1r, kt2=kt2, v2r=v2r, kta=kta, vat=vat, otk=otk,
                    qtb=qtb, ubf0=ubf_l[0], ubf1=ubf_l[1], krot=krot, gsc=gsc, ninv32=ninv32,
                    qrot=qrot, sml=sml, rope=rope, WA=WA, W2=W2)
        for lab, bname in taps:
            if lab != label:
                continue
            ap = here[bname]
            tname = f"tap_{lab}_{bname}"
            shp = list(ap.shape)
            tout = nc.dram_tensor(tname, shp, F32, kind="ExternalOutput").ap()
            for e_ in (T, V, S):
                pass
            G.wait(("t", c.count["t"]))
            G.wait(("v", c.count["v"]))
            G.wait(("s", c.count["s"]))
            ttok = G.op("dma_start", out=tout, in_=ap).inc(16, "yi")
            for e_ in (T, V, S, Y):
                e_.wait(ttok)

    emit_init()
    for ch in CHUNKS:
        if nphase >= 2:
            wq0 = load_wqkv_ffn(0)
            wo0 = load_wo_ffn(0)
        ci = CHUNKS.index(ch)
        if nphase >= 1:
            emit_A(ch)
            do_taps(f"A{ci}")
        if nphase >= 2:
            emit_wave_ffn(ch, 0, wq0, wo0)
            do_taps(f"B{ci}")
        if nphase >= 3:
            emit_ffn(ch, 0)
            do_taps(f"C{ci}")
        if nphase >= 4:
            wq1 = load_wqkv_ffn(1)
            wo1 = load_wo_ffn(1)
            emit_wave_ffn(ch, 1, wq1, wo1)
            do_taps(f"D{ci}")
        if nphase >= 5:
            emit_ffn(ch, 1)
            do_taps(f"E{ci}")
        if nphase >= 6:
            watok, wgtok = load_watt()
            wouttok = load_wout()
            emit_att(ch, watok, wgtok, wouttok)
            do_taps(f"F{ci}")
    for s_ in ("yo0", "yo1"):
        Y.wait((s_, c.count[s_]))
        V.wait((s_, c.count[s_]))

    with contextlib.ExitStack() as st:
        for name in ("t", "v", "s") + DMA_SEMS:
            c.sem[name] = st.enter_context(nc.semaphore(f"sem_{name}"))
        block = st.enter_context(nc.Block())

        def mk_replay(rec):
            def f(eng):
                for item in rec.ops:
                    if item[0] == "w":
                        eng.wait_ge(c.sem[item[1]], item[2])
                    else:
                        m, a, kw, incs = item[1]
                        inst = getattr(eng, m)(*a, **kw)
                        for s, n in incs:
                            inst = inst.then_inc(c.sem[s], n)
            return f

        block.tensor(mk_replay(c.T))
        block.vector(mk_replay(c.V))
        block.scalar(mk_replay(c.S))
        block.sync(mk_replay(c.Y))
        block.gpsimd(mk_replay(c.G))
    return nc


def _lhs_chunks(w):
    w = np.ascontiguousarray(np.asarray(w, np.float32))
    K, N = w.shape
    return np.ascontiguousarray(w.reshape(K // 128, 128, N).transpose(1, 0, 2))


def _prep_weights(waqkv, wagat, waout, wfqkv, wfo, wf1, wf2):
    d = {}
    d["w_aqkv"] = _lhs_chunks(waqkv)
    d["w_agat"] = _lhs_chunks(wagat)
    d["w_aout"] = _lhs_chunks(waout)
    d["w_fqkv"] = np.stack([_lhs_chunks(np.asarray(wfqkv, np.float32)[l])
                            for l in range(2)])
    d["w_fo"] = np.stack([_lhs_chunks(np.asarray(wfo, np.float32)[l])
                          for l in range(2)])
    w1 = np.asarray(wf1, np.float32)
    w2 = np.asarray(wf2, np.float32)
    w1g = np.zeros((2, NGRP, 128, 8, 512), np.float32)
    w2g = np.zeros((2, NGRP, 128, 2, DM), np.float32)
    for l in range(2):
        w1c = _lhs_chunks(w1[l])
        for g in range(NGRP):
            c0 = 256 * g
            cw = min(256, HID - c0)
            w1g[l, g, :, :, 0:cw] = w1c[:, :, c0:c0 + cw]
            w1g[l, g, :, :, 256:256 + cw] = w1c[:, :, HID + c0:HID + c0 + cw]
            r0 = 2 * g * 128
            rw = min(256, HID - r0)
            blk = w2[l, r0:r0 + rw]
            pad = np.zeros((256, DM), np.float32)
            pad[:rw] = blk
            w2g[l, g] = pad.reshape(2, 128, DM).transpose(1, 0, 2)
    d["w_f1"] = np.ascontiguousarray(w1g)
    d["w_f2"] = np.ascontiguousarray(w2g)
    return d


def make_in_maps(x, consts, weights):
    rope_h = [_rope_tables_for_core(h) for h in (0, 1)]
    masks_h = [_masks_for_core(h) for h in (0, 1)]
    in_maps = []
    for core in range(8):
        seq, half = core // 2, core % 2
        if half == 0:
            xi = np.concatenate([np.zeros((HALO, DM), np.float32),
                                 x[seq, 0:P]], axis=0)
        else:
            xi = np.ascontiguousarray(x[seq, P - HALO:SEQ])
        mf, ma = masks_h[half]
        im = dict(xin=xi, rope_in=rope_h[half], maskf_in=mf, maska_in=ma,
                  **consts, **weights)
        in_maps.append(im)
    return in_maps


_NC_CACHE = {}


def _get_program():
    if "nc" not in _NC_CACHE:
        _NC_CACHE["nc"] = build_program()
    return _NC_CACHE["nc"]


def kernel(L0_attn_x, L0_attn_gamma, L0_attn_W_qkv, L0_attn_W_gates,
           L0_attn_b_gates, L0_attn_W_out, L0_ffn_wqkv, L0_ffn_q_scale,
           L0_ffn_k_scale, L0_ffn_wo, L0_ffn_ln_g, L0_ffn_ln_b, L0_ffn_w1,
           L0_ffn_b1, L0_ffn_w2, L0_ffn_b2):
    x = np.ascontiguousarray(np.asarray(L0_attn_x, dtype=np.float32))
    consts = _const_inputs(L0_ffn_q_scale, L0_ffn_k_scale, L0_ffn_ln_g,
                           L0_ffn_ln_b, L0_ffn_b1, L0_ffn_b2, L0_attn_b_gates,
                           L0_attn_gamma)
    weights = _prep_weights(L0_attn_W_qkv, L0_attn_W_gates, L0_attn_W_out,
                            L0_ffn_wqkv, L0_ffn_wo, L0_ffn_w1, L0_ffn_w2)
    in_maps = make_in_maps(x, consts, weights)
    nc = _get_program()
    res = run_bass_kernel_spmd(nc, in_maps, core_ids=list(range(8)))
    out = np.empty((NSEQ, SEQ, DM), dtype=np.float32)
    for core in range(8):
        seq, half = core // 2, core % 2
        out[seq, half * P:(half + 1) * P] = np.asarray(res.results[core]["yout"])
    return out


# version 10
# speedup vs baseline: 1.2747x; 1.0162x over previous
import contextlib
import math

import numpy as np

import concourse.bass as bass
import concourse.mybir as mybir
from concourse.bass_utils import run_bass_kernel_spmd

F32 = mybir.dt.float32
BF16 = mybir.dt.bfloat16
ACT = mybir.ActivationFunctionType
ALU = mybir.AluOpType
AX = mybir.AxisListType

DM = 1024
SEQ = 4096
NSEQ = 4
P = 2048
HALO = 1280
XROWS = HALO + P
NT = XROWS // 128
OWN0 = HALO // 128

FH = 8
AH = 16
DH = 64
HID = 2730
NHC = 22
NGRP = 11
FCTX = 5
RING = 6
ACTX = 3
MASKVAL = -1.0e30
LN32 = math.log(32.0)
LN8 = math.log(8.0)


def _chunks():
    c0 = dict(tiles=list(range(0, 10)), sbase=5, xnbase=1,
              b_kv=[1, 2, 3, 4], b_full=[5, 6, 7, 8, 9],
              c_tiles=[5, 6, 7, 8, 9],
              d_kv=[6, 7, 8, 9], d_full=[],
              e_tiles=[],
              f_kv=[8, 9], f_full=[])
    def own(t0):
        t = list(range(t0, t0 + 8))
        return dict(tiles=t, sbase=t0, xnbase=t0, b_kv=[], b_full=t,
                    c_tiles=t, d_kv=[], d_full=t, e_tiles=t, f_kv=[], f_full=t)
    return [c0, own(10), own(18)]


CHUNKS = _chunks()


def _rope_tables_for_core(half):
    n = SEQ
    hi = np.arange(0, DH, 2, dtype=np.float64)
    inv_freq = 10000.0 ** (-hi / DH)
    base = (hi + 0.4 * DH) / (1.4 * DH)
    out = np.zeros((3, 128, 1152), dtype=np.float64)
    for ci, ch in enumerate(CHUNKS):
        wave = sorted(set(ch["b_kv"] + ch["b_full"]))
        for sj, r in enumerate(wave):
            g0 = r * 128 - HALO + (0 if half == 0 else P)
            if half == 0 and g0 < 0:
                continue
            t = g0 + np.arange(128, dtype=np.float64)
            fr = t[:, None] * inv_freq[None, :]
            power = (t - n // 2) / 256.0
            scale = base[None, :] ** power[:, None]
            cs, sn = np.cos(fr), np.sin(fr)
            for ti, tab in enumerate((cs * scale, sn * scale,
                                      cs / scale, sn / scale)):
                out[ci, :, ti * 288 + sj * 32: ti * 288 + sj * 32 + 32] = tab
    return np.ascontiguousarray(out.astype(np.float32))


def _masks_for_core(half):
    qi = np.arange(128)[:, None]

    def mk(nblk, win, nvar):
        ctxw = nblk * 128
        j = np.arange(ctxw)[None, :]
        keep = (j >= qi) & (j <= qi + win)
        m = np.zeros((128, nvar * ctxw), dtype=np.float32)
        for v in range(nvar):
            mm = keep.copy()
            if half == 0 and (nvar - 1 - v) > 0:
                mm[:, :(nvar - 1 - v) * 128] = False
            m[:, v * ctxw:(v + 1) * ctxw] = np.where(mm, 0.0, MASKVAL)
        return np.ascontiguousarray(m)

    return mk(FCTX, 512, 5), mk(ACTX, 256, 3)


def _const_inputs(q_scale, k_scale, ln_g, ln_b, b1_, b2_, b_gates, gamma):
    d = {}
    sc = np.zeros((2, 2, 64), np.float32)
    for l in range(2):
        sc[l] = np.stack([np.asarray(q_scale[l], np.float32),
                          np.asarray(k_scale[l], np.float32)])
    d["scales"] = np.ascontiguousarray(
        np.broadcast_to(sc.reshape(1, 256), (128, 256)))
    lnp = np.zeros((128, 2, 2, 8), np.float32)
    for l in range(2):
        lnp[:, l, 0, :] = np.asarray(ln_g[l], np.float32).reshape(8, 128).T
        lnp[:, l, 1, :] = np.asarray(ln_b[l], np.float32).reshape(8, 128).T
    d["lnp"] = np.ascontiguousarray(lnp.reshape(128, 32))
    b2 = np.zeros((128, 2, 8), np.float32)
    for l in range(2):
        b2[:, l, :] = np.asarray(b2_[l], np.float32).reshape(8, 128).T
    d["b2c"] = np.ascontiguousarray(b2.reshape(128, 16))
    b1a = np.zeros((128, 2, NHC), np.float32)
    b1g = np.zeros((128, 2, NHC), np.float32)
    for l in range(2):
        b1 = np.asarray(b1_[l], np.float32)
        for arr, seg in ((b1a, b1[:HID]), (b1g, b1[HID:])):
            pad = np.zeros(NHC * 128, np.float32)
            pad[:HID] = seg
            arr[:, l, :] = pad.reshape(NHC, 128).T
    d["b1a"] = np.ascontiguousarray(b1a.reshape(128, 2 * NHC))
    d["b1g"] = np.ascontiguousarray(b1g.reshape(128, 2 * NHC))
    d["bgv"] = np.ascontiguousarray(
        np.broadcast_to(np.asarray(b_gates, np.float32).reshape(1, 16), (128, 16)))
    d["gmc"] = np.ascontiguousarray(np.asarray(gamma, np.float32).reshape(8, 128).T)
    d["ident"] = np.eye(128, dtype=np.float32)
    return d


class Rec:
    def __init__(self, ctx, sem_name):
        self.ctx = ctx
        self.sem = sem_name
        self.ops = []

    def wait(self, tok):
        if tok is None:
            return
        sem, val = tok
        if val > 0:
            self.ops.append(("w", sem, val))

    def waits(self, toks):
        best = {}
        for t in toks:
            if t is None:
                continue
            s, v = t
            best[s] = max(best.get(s, 0), v)
        for s, v in best.items():
            self.wait((s, v))

    def op(self, method, *a, **kw):
        r = [method, a, kw, []]
        self.ops.append(("o", r))
        return _OpH(self, r)

    def tok(self):
        return (self.sem, self.ctx.count[self.sem])


class _OpH:
    def __init__(self, rec, r):
        self.rec = rec
        self.r = r

    def inc(self, n=1, sem=None):
        sem = sem or self.rec.sem
        self.r[3].append((sem, n))
        self.rec.ctx.count[sem] += n
        return (sem, self.rec.ctx.count[sem])


class TB:

    def __init__(self, ap=None, name=""):
        self.ap = ap
        self.name = name
        self.wr = None
        self.rd = []


def bw(rec, *tbs):
    for t in tbs:
        rec.waits(t.rd)


def ew(tok, *tbs):
    for t in tbs:
        t.wr = tok
        t.rd = []


def br(rec, *tbs):
    for t in tbs:
        rec.wait(t.wr)


def er(tok, *tbs):
    for t in tbs:
        t.rd.append(tok)


DMA_SEMS = ("yi", "yx0", "yx1", "yr", "yo0", "yo1",
            "ga0", "ga1", "ga2", "ga3", "gw0", "gw1", "go0", "go1")


class Ctx:
    def __init__(self, nc):
        self.nc = nc
        self.count = {k: 0 for k in ("t", "v", "s", "y", "g") + DMA_SEMS}
        self.T = Rec(self, "t")
        self.V = Rec(self, "v")
        self.S = Rec(self, "s")
        self.Y = Rec(self, "y")
        self.G = Rec(self, "g")
        self.sem = {}


def shift_view(buf, kc, col, n=128):
    off = col - 1 if kc >= 4 else col
    return buf[:, kc, off:off + n]


def build_program(nphase=99, taps=()):
    nc = bass.Bass("TRN2", target_bir_lowering=False,
                   detect_race_conditions=False)
    c = Ctx(nc)
    T, V, S, Y, G = c.T, c.V, c.S, c.Y, c.G

    xin = nc.dram_tensor("xin", [XROWS, DM], F32, kind="ExternalInput").ap()
    yout = nc.dram_tensor("yout", [P, DM], F32, kind="ExternalOutput").ap()
    rope_in = nc.dram_tensor("rope_in", [3, 128, 1152], F32, kind="ExternalInput").ap()
    maskf_in = nc.dram_tensor("maskf_in", [128, 5 * 640], F32, kind="ExternalInput").ap()
    maska_in = nc.dram_tensor("maska_in", [128, 3 * 384], F32, kind="ExternalInput").ap()
    ident_in = nc.dram_tensor("ident", [128, 128], F32, kind="ExternalInput").ap()
    scales_in = nc.dram_tensor("scales", [128, 256], F32, kind="ExternalInput").ap()
    lnp_in = nc.dram_tensor("lnp", [128, 32], F32, kind="ExternalInput").ap()
    b2c_in = nc.dram_tensor("b2c", [128, 16], F32, kind="ExternalInput").ap()
    b1a_in = nc.dram_tensor("b1a", [128, 2 * NHC], F32, kind="ExternalInput").ap()
    b1g_in = nc.dram_tensor("b1g", [128, 2 * NHC], F32, kind="ExternalInput").ap()
    bgv_in = nc.dram_tensor("bgv", [128, 16], F32, kind="ExternalInput").ap()
    gmc_in = nc.dram_tensor("gmc", [128, 8], F32, kind="ExternalInput").ap()
    w_aqkv = nc.dram_tensor("w_aqkv", [128, 8, 3072], F32, kind="ExternalInput").ap()
    w_agat = nc.dram_tensor("w_agat", [128, 8, 16], F32, kind="ExternalInput").ap()
    w_aout = nc.dram_tensor("w_aout", [128, 8, DM], F32, kind="ExternalInput").ap()
    w_fqkv = nc.dram_tensor("w_fqkv", [2, 128, 8, 1536], F32, kind="ExternalInput").ap()
    w_fo = nc.dram_tensor("w_fo", [2, 128, 4, DM], F32, kind="ExternalInput").ap()
    w_f1 = nc.dram_tensor("w_f1", [2, NGRP, 128, 8, 512], F32, kind="ExternalInput").ap()
    w_f2 = nc.dram_tensor("w_f2", [2, NGRP, 128, 2, DM], F32, kind="ExternalInput").ap()

    A = nc.alloc_sbuf_tensor
    xn_bf = A("xn_bf", [128, 8, 9 * 128 + 1], BF16).ap()
    strm = A("strm", [128, 8, 8 * 128 + 1], F32).ap()
    y1arena = A("y1arena", [128, 4100], F32).ap()
    y1a = y1arena.bitcast(BF16).rearrange("p (a b) -> p a b", b=1025)
    kt1 = A("kt1", [128, 4, RING * 128], BF16).ap()
    v1r = A("v1r", [128, RING, FH * 65], BF16).ap()
    kt2 = A("kt2", [128, 4, RING * 128], BF16).ap()
    v2r = A("v2r", [128, RING, FH * 65], BF16).ap()
    kta = A("kta", [128, 8, ACTX * 128], BF16).ap()
    vat = A("vat", [128, ACTX, AH * 65], BF16).ap()
    rope = A("rope", [128, 4, 9, 32], F32).ap()
    maskf = A("maskf", [128, 5, 640], BF16).ap()
    maska = A("maska", [128, 3, 384], BF16).ap()
    id_f = A("id_f", [128, 128], F32).ap()
    id_b = A("id_b", [128, 128], BF16).ap()
    scl = A("scl", [128, 2, 2, 64], F32).ap()
    lnp = A("lnp_sb", [128, 2, 2, 8], F32).ap()
    b2c = A("b2c_sb", [128, 2, 8], F32).ap()
    b1a = A("b1a_sb", [128, 2, NHC], F32).ap()
    b1g = A("b1g_sb", [128, 2, NHC], F32).ap()
    bgv = A("bgv_sb", [128, 16], F32).ap()
    gmc = A("gmc_sb", [128, 8], F32).ap()
    wag = A("wag", [128, 8, 16], BF16).ap()
    inv32 = A("inv32", [128, NT], F32).ap()
    xn_cr = A("xn_cr", [128, 8], BF16).ap()
    y1b_cr = A("y1b_cr", [128, 8], BF16).ap()
    x1f_cr = A("x1f_cr", [128, 8], F32).ap()
    y1f_cr = A("y1f_cr", [128, 8], F32).ap()
    xt0 = A("xt0", [128, DM], F32).ap()
    xt1 = A("xt1", [128, DM], F32).ap()
    znt = A("znt", [128, DM], F32).ap()
    prb0 = A("prb0", [128, 640], BF16).ap()
    prb1 = A("prb1", [128, 640], BF16).ap()
    ptb0 = prb0.rearrange("p (a n) -> p a n", n=128)
    ptb1 = prb1.rearrange("p (a n) -> p a n", n=128)
    qtb = A("qtb", [128, 8, 128], BF16).ap()
    otk = A("otk", [128, DM], BF16).ap()
    otb = A("otb", [128, 8, 128], BF16).ap()
    krot = A("krot", [128, 512], BF16).ap()
    qrot = A("qrot", [128, 512], BF16).ap()
    sml = A("sml", [128, 10, 16], F32).ap()
    gsc = A("gsc", [128, 3, 16], F32).ap()
    eps5 = A("eps5", [128, 1], F32).ap()
    tiny = A("tiny", [128, 1], F32).ap()
    y1f_pv = A("y1f_pv", [128, 8], F32).ap()
    ninv32 = A("ninv32", [128, NT], F32).ap()
    inv32q = A("inv32q", [128, NT], F32).ap()
    WA = A("WA", [128, 8, 3072], BF16).ap()
    W2 = A("W2", [128, 8, DM], BF16).ap()
    ostg = y1arena[:, 0:2048].rearrange("p (a b) -> p a b", a=2)
    rem = nc.sbuf_bytes_remaining
    assert rem >= 0, f"SBUF over budget by {-rem} bytes"

    PS = nc.alloc_psum_tensor
    PSA = PS("PSA", [128, 512], F32).ap()
    PSTR = PS("PSTR", [128, 1024], BF16).ap()
    PSB = PS("PSB", [128, 1024], F32).ap()
    PSC = PS("PSC", [128, 1024], F32).ap()
    PSD = PS("PSD", [128, 512], F32).ap()
    PSE = PS("PSE", [128, 512], F32).ap()
    assert nc.psum_banks_remaining >= 0
    pa = TB(PSA, "pa")
    ptr = TB(PSTR, "ptr")
    pb0, pb1 = TB(None, "pb0"), TB(None, "pb1")
    pc0, pc1 = TB(None, "pc0"), TB(None, "pc1")
    pd = TB(PSD, "pd")
    pe = TB(PSE, "pe")

    tb = {n: TB(None, n) for n in
          ["xt0", "xt1", "znt", "prb0", "prb1",
           "qtb", "qtb0", "qtb1", "otk", "otb", "krot", "qrot", "ubf",
           "ostg0", "ostg1", "gsc", "lnh", "y1bf", "rope"]}
    WAr = [TB(None, f"WAr{i}") for i in range(4)]
    W2wo = [TB(None, "W2wo0"), TB(None, "W2wo1")]
    W2g = [TB(None, "W2g0"), TB(None, "W2g1")]
    W2flat = W2.rearrange("p a b -> p (a b)")
    ubf_l = [[W2flat[:, 0:2048].rearrange("p (a b) -> p a b", a=2),
              W2flat[:, 2048:4096].rearrange("p (a b) -> p a b", a=2)],
             [W2flat[:, 4096:6144].rearrange("p (a b) -> p a b", a=2),
              W2flat[:, 6144:8192].rearrange("p (a b) -> p a b", a=2)]]
    phase_toks = {"xnbf": None, "strm": None}

    def emit_init():
        Y.op("dma_start", out=id_f, in_=ident_in).inc(16, "yi")
        Y.op("dma_start", out=scl, in_=scales_in.rearrange(
            "p (l t d) -> p l t d", l=2, t=2)).inc(16, "yi")
        Y.op("dma_start", out=lnp, in_=lnp_in.rearrange(
            "p (l i c) -> p l i c", l=2, i=2)).inc(16, "yi")
        Y.op("dma_start", out=b2c, in_=b2c_in.rearrange("p (l c) -> p l c", l=2)).inc(16, "yi")
        Y.op("dma_start", out=b1a, in_=b1a_in.rearrange("p (l c) -> p l c", l=2)).inc(16, "yi")
        Y.op("dma_start", out=b1g, in_=b1g_in.rearrange("p (l c) -> p l c", l=2)).inc(16, "yi")
        Y.op("dma_start", out=bgv, in_=bgv_in).inc(16, "yi")
        Y.op("dma_start", out=gmc, in_=gmc_in).inc(16, "yi")
        wa_f32 = WA.rearrange("p a b -> p (a b)").bitcast(F32)
        mf_stage = wa_f32[:, 0:3200]
        ma_stage = wa_f32[:, 3200:3200 + 1152]
        Y.op("dma_start", out=mf_stage, in_=maskf_in).inc(16, "yi")
        tok_m = Y.op("dma_start", out=ma_stage, in_=maska_in).inc(16, "yi")
        V.wait(tok_m)
        V.op("tensor_copy", out=maskf.rearrange("p a b -> p (a b)"), in_=mf_stage)
        V.op("tensor_copy", out=maska.rearrange("p a b -> p (a b)"), in_=ma_stage)
        V.op("tensor_copy", out=id_b, in_=id_f)
        for ap in (kt1, kt2, kta):
            V.op("memset", ap, 0.0)
        for ap in (v1r, v2r, vat):
            v4 = ap.rearrange("p s (h e) -> p s h e", e=65)
            V.op("memset", v4[:, :, :, 0:64], 0.0)
            V.op("memset", v4[:, :, :, 64:65], 1.0)
        for ap in (xn_cr, y1b_cr, x1f_cr, y1f_cr, y1f_pv, inv32):
            V.op("memset", ap, 0.0)
        V.op("memset", eps5, 1e-5)
        V.op("memset", tiny, 1e-27)
        vtok = V.op("memset", sml, 0.0).inc()
        T.wait(vtok)
        S.wait(vtok)
        G.wait(vtok)
        ytok = ("yi", c.count["yi"])
        T.wait(ytok)
        S.wait(ytok)
        V.wait(ytok)
        for r in WAr:
            r.rd = [vtok]

    def load_wqkv_ffn(layer):
        bw(G, WAr[0], WAr[1])
        tok = G.op("dma_start", out=WA[:, :, 0:1536],
                   in_=w_fqkv[layer]).inc(16, "ga0")
        ew(tok, WAr[0], WAr[1])
        return tok

    def load_wo_ffn(layer):
        reg = W2wo[layer]
        bw(G, reg, W2g[0], W2g[1])
        dst = W2flat[:, layer * 4096:(layer + 1) * 4096].rearrange(
            "p (k n) -> p k n", k=4)
        tok = G.op("dma_start", out=dst,
                   in_=w_fo[layer]).inc(16, ("go0", "go1")[layer])
        ew(tok, reg)
        return tok

    def load_wgrp(layer, g):
        reg = WAr[g % 4]
        bw(G, reg)
        c0_ = 2 * g * 128
        cw = min(256, HID - c0_)
        base = WA[:, :, 768 * (g % 4):768 * (g % 4) + 768]
        tok1 = G.op("dma_start", out=base[:, :, 0:512],
                    in_=w_f1[layer, g]).inc(16, f"ga{g % 4}")
        ew(tok1, reg)
        w2reg = W2g[g % 2]
        bw(G, w2reg, W2wo[1 - layer])
        half0 = 4096 * (1 - layer)
        w2slot = W2flat[:, half0 + (g % 2) * 2048: half0 + (g % 2) * 2048 + 2048]
        w2slot = w2slot.rearrange("p (k n) -> p k n", k=2)
        tok2 = G.op("dma_start", out=w2slot,
                    in_=w_f2[layer, g]).inc(16, f"gw{g % 2}")
        ew(tok2, w2reg)
        return reg, w2reg, base, w2slot, cw

    def load_watt():
        bw(G, *WAr)
        tok = G.op("dma_start", out=WA, in_=w_aqkv).inc(16, "ga0")
        ew(tok, *WAr)
        tokg = G.op("dma_start", out=wag, in_=w_agat).inc(16, "ga1")
        return tok, tokg

    def load_wout():
        bw(G, W2wo[0], W2wo[1], W2g[0], W2g[1])
        dst = W2flat.rearrange("p (k n) -> p k n", k=8)
        tok = G.op("dma_start", out=dst, in_=w_aout).inc(16, "go0")
        ew(tok, W2wo[0], W2wo[1], W2g[0], W2g[1])
        return tok

    def emit_A(ch):
        ast = {}

        def a_load(i, r):
            xt = (xt0, xt1)[i % 2]
            xtb = tb[("xt0", "xt1")[i % 2]]
            bw(Y, xtb)
            ytok = Y.op("dma_start", out=xt,
                        in_=xin[r * 128:(r + 1) * 128, :]).inc(
                            16, ("yx0", "yx1")[i % 2])
            ew(ytok, xtb)
            ast[i] = {"ld": ytok}

        def a_stats1(i, r):
            xt = (xt0, xt1)[i % 2]
            sa, sb = ((0, 1), (2, 3))[i % 2]
            V.wait(ast[i]["ld"])
            V.op("bn_stats", out=sml[:, sa, 0:6], in_=xt[:, 0:512])
            vtok = V.op("bn_stats", out=sml[:, sa, 6:12], in_=xt[:, 512:1024]).inc()
            V.wait(vtok)
            vtok = V.op("bn_aggr", out=sml[:, sb, 0:2], in_=sml[:, sa, 0:12]).inc()
            S.wait(vtok)
            stok = S.op("activation", out=sml[:, sb, 2:3], in_=sml[:, sb, 1:2],
                        func=ACT.Ln, bias=eps5).inc()
            S.wait(stok)
            stok = S.op("activation", out=sml[:, sb, 3:4], in_=sml[:, sb, 2:3],
                        func=ACT.Exp, scale=-0.5).inc()
            V.wait(stok)
            vtok = V.op("tensor_scalar", out=sml[:, sb, 4:5], in0=sml[:, sb, 0:1],
                        scalar1=sml[:, sb, 3:4], scalar2=-1.0,
                        op0=ALU.mult, op1=ALU.mult).inc()
            ast[i]["nmr"] = vtok

        def a_affine(i, r):
            xt = (xt0, xt1)[i % 2]
            xtb = tb[("xt0", "xt1")[i % 2]]
            sb = (1, 3)[i % 2]
            bw(S, tb["znt"])
            S.wait(ast[i]["nmr"])
            stok = S.op("activation", out=znt, in_=xt, func=ACT.Identity,
                        bias=sml[:, sb, 4:5], scale=sml[:, sb, 3:4]).inc()
            ew(stok, tb["znt"])
            er(stok, xtb)
            ast[i]["aff"] = stok

        def a_stats2(i, r):
            sa, sb = ((4, 5), (8, 9))[i % 2]
            V.wait(ast[i]["aff"])
            V.op("bn_stats", out=sml[:, sa, 0:6], in_=znt[:, 0:512])
            vtok = V.op("bn_stats", out=sml[:, sa, 6:12], in_=znt[:, 512:1024]).inc()
            V.wait(vtok)
            vtok = V.op("bn_aggr", out=sml[:, sb, 0:2], in_=sml[:, sa, 0:12]).inc()
            er(vtok, tb["znt"])
            S.wait(vtok)
            stok = S.op("activation", out=sml[:, sb, 2:3], in_=sml[:, sb, 1:2],
                        func=ACT.Ln, bias=tiny).inc()
            S.wait(stok)
            stok = S.op("activation", out=inv32[:, r:r + 1], in_=sml[:, sb, 2:3],
                        func=ACT.Exp, scale=-0.5).inc()
            V.wait(stok)
            V.op("tensor_scalar_mul", out=ninv32[:, r:r + 1],
                 in0=inv32[:, r:r + 1], scalar1=-1.0)
            V.op("tensor_scalar_mul", out=inv32q[:, r:r + 1],
                 in0=inv32[:, r:r + 1], scalar1=0.125).inc()

        def a_trans(i, r):
            bw(T, pb0, pb1)
            T.wait(ast[i]["aff"])
            for kc in range(8):
                h = T.op("transpose", PSB[:, kc * 128:(kc + 1) * 128],
                         znt[:, kc * 128:(kc + 1) * 128], id_f)
            ttok = h.inc()
            ew(ttok, pb0, pb1)
            er(ttok, tb["znt"])
            pb8 = PSB.rearrange("p (a b) -> p a b", a=8)
            if r >= ch["xnbase"]:
                col = 1 + (r - ch["xnbase"]) * 128
                br(S, pb0)
                stok = S.op("copy", out=xn_bf[:, :, col:col + 128], in_=pb8).inc()
                er(stok, pb0, pb1)
            elif r == ch["xnbase"] - 1:
                br(S, pb0)
                stok = S.op("copy", out=xn_bf[:, :, 0:1],
                            in_=pb8[:, :, 127:128]).inc()
                er(stok, pb0, pb1)
            if r >= ch["sbase"]:
                scol = 1 + (r - ch["sbase"]) * 128
                br(V, pb0)
                V.wait(stok)
                vtok = V.op("tensor_copy", out=strm[:, :, scol:scol + 128],
                            in_=pb8).inc()
                er(vtok, pb0, pb1)

        tiles = ch["tiles"]
        a_load(0, tiles[0])
        a_load(1, tiles[1])
        a_stats1(0, tiles[0])
        for i, r in enumerate(tiles):
            a_affine(i, r)
            if i + 1 < len(tiles):
                a_stats1(i + 1, tiles[i + 1])
            if i + 2 < len(tiles):
                a_load(i + 2, tiles[i + 2])
            a_trans(i, r)
            a_stats2(i, r)
        if tiles[0] >= OWN0:
            S.op("copy", out=xn_bf[:, :, 0:1], in_=xn_cr.unsqueeze(2))
        last_col = 1 + (tiles[-1] - ch["xnbase"]) * 128 + 127
        S.wait(S.tok())
        stok = S.op("copy", out=xn_cr.unsqueeze(2),
                    in_=xn_bf[:, :, last_col:last_col + 1]).inc()
        phase_toks["xnbf"] = stok
        phase_toks["strm"] = c.V.tok()

    def emit_wave_ffn(ch, layer, wq_tok, wo_tok):
        kv_tiles = ch["b_kv"] if layer == 0 else ch["d_kv"]
        full_tiles = ch["b_full"] if layer == 0 else ch["d_full"]
        all_tiles = sorted(set(kv_tiles + full_tiles))
        if not all_tiles:
            return
        src_ = xn_bf if layer == 0 else y1a
        src_base = ch["xnbase"] if layer == 0 else ch["sbase"]
        src_tok = phase_toks["xnbf"] if layer == 0 else tb["y1bf"].wr
        ktr, vrr = (kt1, v1r) if layer == 0 else (kt2, v2r)
        kt3 = ktr.rearrange("p a (s n) -> p a s n", n=128)
        v4 = vrr.rearrange("p s (h e) -> p s h e", e=65)
        wq = WA[:, :, 0:1536]
        wo4 = W2flat[:, layer * 4096:(layer + 1) * 4096].rearrange(
            "p (k n) -> p k n", k=4)
        wave_slots = {r: j for j, r in enumerate(
            sorted(set(ch["b_kv"] + ch["b_full"])))}
        bw(Y, tb["rope"])
        rtok = Y.op("dma_start", out=rope.rearrange("p a b e -> p (a b e)"),
                    in_=rope_in[CHUNKS.index(ch)]).inc(16, "yr")
        ew(rtok, tb["rope"])
        T.wait(src_tok)

        ts_ = {r: {} for r in all_tiles}

        def prep_mm(r):
            d = ts_[r]
            full = r in full_tiles
            col = 1 + (r - src_base) * 128
            slot = r % RING
            psc5 = PSC[:, 0:512]
            bw(T, pa)
            T.wait(wq_tok)
            for kc in range(8):
                h = T.op("matmul", PSA, shift_view(src_, kc, col),
                         wq[:, kc, 512:1024], start=(kc == 0), stop=(kc == 7))
            tok = h.inc()
            ew(tok, pa)
            d["qkv_last"] = tok
            bw(S, tb["xt0"])
            br(S, pa)
            stok = S.op("copy", out=xt0[:, 0:512], in_=PSA).inc()
            er(stok, pa)
            ew(stok, tb["xt0"])
            bw(T, pc0)
            for kc in range(8):
                h = T.op("matmul", psc5, shift_view(src_, kc, col),
                         wq[:, kc, 1024:1536], start=(kc == 0), stop=(kc == 7))
            tok = h.inc()
            ew(tok, pc0)
            d["qkv_last"] = tok
            br(S, pc0)
            stok = S.op("copy", out=v4[:, slot, :, 0:64],
                        in_=psc5.rearrange("p (h e) -> p h e", e=64)).inc()
            er(stok, pc0)
            d["vring"] = stok
            if full:
                bw(T, pa)
                for kc in range(8):
                    h = T.op("matmul", PSA, shift_view(src_, kc, col),
                             wq[:, kc, 0:512], start=(kc == 0), stop=(kc == 7))
                tok = h.inc()
                ew(tok, pa)
                d["qkv_last"] = tok
                bw(S, tb["xt1"])
                br(S, pa)
                stok = S.op("copy", out=xt1[:, 0:512], in_=PSA).inc()
                er(stok, pa)
                ew(stok, tb["xt1"])

        def prep_vec(r):
            d = ts_[r]
            full = r in full_tiles
            slot = r % RING
            rj = wave_slots[r]
            ksl = 1 if (r - all_tiles[0]) % 2 == 0 else 6
            qsl = ksl + 1
            kf = xt0[:, 0:512]
            kf3 = kf.rearrange("p (h d) -> p h d", d=64)
            kf4 = kf.rearrange("p (h s d) -> p h s d", s=2, d=32)
            V.wait(tb["xt0"].wr)
            bw(V, tb["znt"])
            V.op("tensor_mul", out=znt[:, 0:512], in0=kf, in1=kf)
            vtok = V.op("reduce_sum", out=sml[:, ksl, 0:8],
                        in_=znt[:, 0:512].rearrange("p (h d) -> p h d", d=64),
                        axis=AX.X).inc()
            S.wait(vtok)
            stok = S.op("activation", out=sml[:, ksl, 8:16], in_=sml[:, ksl, 0:8],
                        func=ACT.Ln, bias=tiny).inc()
            S.wait(stok)
            stok = S.op("activation", out=sml[:, ksl, 8:16], in_=sml[:, ksl, 8:16],
                        func=ACT.Exp, scale=-0.5).inc()
            V.wait(stok)
            inv_b = sml[:, ksl, 8:16].unsqueeze(2).broadcast_to([128, 8, 64])
            V.op("tensor_tensor", out=kf3, in0=kf3, in1=inv_b, op=ALU.mult)
            ks_b = scl[:, layer, 1, :].unsqueeze(1).broadcast_to([128, 8, 64])
            vtok = V.op("tensor_tensor", out=kf3, in0=kf3, in1=ks_b,
                        op=ALU.mult).inc()
            V.wait(vtok)
            V.wait(tb["rope"].wr)
            sh = znt[:, 0:512].rearrange("p (h s d) -> p h s d", s=2, d=32)
            V.op("tensor_copy", out=sh[:, :, 0, :], in_=kf4[:, :, 1, :])
            V.op("tensor_copy", out=sh[:, :, 1, :], in_=kf4[:, :, 0, :])
            cs_b = rope[:, 2, rj, :].unsqueeze(1).unsqueeze(1).broadcast_to(
                [128, 8, 2, 32])
            sn_b = rope[:, 3, rj, :].unsqueeze(1).unsqueeze(1).broadcast_to(
                [128, 8, 2, 32])
            V.op("tensor_tensor", out=kf4, in0=kf4, in1=cs_b, op=ALU.mult)
            V.op("tensor_tensor", out=sh, in0=sh, in1=sn_b, op=ALU.mult)
            bw(V, tb["krot"])
            kr4 = krot.rearrange("p (h s d) -> p h s d", s=2, d=32)
            V.op("tensor_sub", out=kr4[:, :, 0, :], in0=kf4[:, :, 0, :],
                 in1=sh[:, :, 0, :])
            vtok = V.op("tensor_add", out=kr4[:, :, 1, :], in0=kf4[:, :, 1, :],
                        in1=sh[:, :, 1, :]).inc()
            ew(vtok, tb["krot"])
            ew(vtok, tb["znt"])
            er(vtok, tb["xt0"], tb["rope"])
            bw(T, ptr)
            T.wait(vtok)
            for j in range(4):
                h = T.op("transpose", PSTR[:, j * 128:(j + 1) * 128],
                         krot[:, j * 128:(j + 1) * 128], id_b)
            ttok = h.inc()
            ew(ttok, ptr)
            er(ttok, tb["krot"])
            br(S, ptr)
            stok = S.op("copy", out=kt3[:, :, slot, :],
                        in_=PSTR[:, 0:512].rearrange("p (a n) -> p a n",
                                                     n=128)).inc()
            er(stok, ptr)
            d["kring"] = stok
            if not full:
                return
            qpar = (r - all_tiles[0]) % 2
            qtb_tb = tb[("qtb0", "qtb1")[qpar]]
            qtv = qtb[:, 4 * qpar:4 * qpar + 4, :]
            d["qsl"] = qsl
            qf = xt1[:, 0:512]
            qf3 = qf.rearrange("p (h d) -> p h d", d=64)
            qf4 = qf.rearrange("p (h s d) -> p h s d", s=2, d=32)
            V.wait(tb["xt1"].wr)
            bw(V, tb["znt"])
            V.op("tensor_mul", out=znt[:, 0:512], in0=qf, in1=qf)
            vtok = V.op("reduce_sum", out=sml[:, qsl, 0:8],
                        in_=znt[:, 0:512].rearrange("p (h d) -> p h d", d=64),
                        axis=AX.X).inc()
            S.wait(vtok)
            stok = S.op("activation", out=sml[:, qsl, 8:16], in_=sml[:, qsl, 0:8],
                        func=ACT.Ln, scale=1.0 / 64.0, bias=tiny).inc()
            S.wait(stok)
            stok = S.op("activation", out=sml[:, qsl, 8:16],
                        in_=sml[:, qsl, 8:16], func=ACT.Exp, scale=-0.5).inc()
            V.wait(stok)
            invq_b = sml[:, qsl, 8:16].unsqueeze(2).broadcast_to([128, 8, 64])
            V.op("tensor_tensor", out=qf3, in0=qf3, in1=invq_b, op=ALU.mult)
            qs_b = scl[:, layer, 0, :].unsqueeze(1).broadcast_to([128, 8, 64])
            vtok = V.op("tensor_tensor", out=qf3, in0=qf3, in1=qs_b,
                        op=ALU.mult).inc()
            V.wait(vtok)
            sh2 = znt[:, 0:512].rearrange("p (h s d) -> p h s d", s=2, d=32)
            V.op("tensor_copy", out=sh2[:, :, 0, :], in_=qf4[:, :, 1, :])
            V.op("tensor_copy", out=sh2[:, :, 1, :], in_=qf4[:, :, 0, :])
            cs_q = rope[:, 0, rj, :].unsqueeze(1).unsqueeze(1).broadcast_to(
                [128, 8, 2, 32])
            sn_q = rope[:, 1, rj, :].unsqueeze(1).unsqueeze(1).broadcast_to(
                [128, 8, 2, 32])
            V.op("tensor_tensor", out=qf4, in0=qf4, in1=cs_q, op=ALU.mult)
            V.op("tensor_tensor", out=sh2, in0=sh2, in1=sn_q, op=ALU.mult)
            bw(V, tb["qrot"])
            qr4 = qrot.rearrange("p (h s d) -> p h s d", s=2, d=32)
            V.op("tensor_sub", out=qr4[:, :, 0, :], in0=qf4[:, :, 0, :],
                 in1=sh2[:, :, 0, :])
            vtok = V.op("tensor_add", out=qr4[:, :, 1, :], in0=qf4[:, :, 1, :],
                        in1=sh2[:, :, 1, :]).inc()
            ew(vtok, tb["qrot"])
            ew(vtok, tb["znt"])
            er(vtok, tb["xt1"], tb["rope"])
            bw(T, ptr)
            T.wait(vtok)
            for j in range(4):
                h = T.op("transpose", PSTR[:, 512 + j * 128:512 + (j + 1) * 128],
                         qrot[:, j * 128:(j + 1) * 128], id_b)
            ttok = h.inc()
            ew(ttok, ptr)
            er(ttok, tb["qrot"])
            bw(S, qtb_tb)
            br(S, ptr)
            stok = S.op("copy", out=qtv,
                        in_=PSTR[:, 512:1024].rearrange("p (a n) -> p a n",
                                                        n=128)).inc()
            er(stok, ptr)
            ew(stok, qtb_tb)
            d["qT"] = stok

        def heads(r):
            d = ts_[r]
            col = 1 + (r - src_base) * 128
            mvi = min(max(r - OWN0, 0), 4)
            qpar = (r - all_tiles[0]) % 2
            qtv = qtb[:, 4 * qpar:4 * qpar + 4, :]
            qtb_tb = tb[("qtb0", "qtb1")[qpar]]
            qsl = d["qsl"]
            ctx_tiles = list(range(r - 4, r + 1))
            hs = [dict() for _ in range(FH)]
            v_ = min(max(r - OWN0, 0), 4)
            mask_regions = [0, 4] + [j for j in (1, 2, 3) if j < 4 - v_]

            def _scp(pp):
                toks = []
                for par in (0, 1):
                    sc_tbs_ = ((pb0, pb1), (pc0, pc1))[par]
                    bw(T, *sc_tbs_)
                T.wait(d["qT"])
                T.wait(d["kring"])
                for j, cr in enumerate(ctx_tiles):
                    cslot = cr % RING
                    masked = j in mask_regions
                    for par in (0, 1):
                        sc_ap = (PSB, PSC)[par]
                        lo = 64 * par
                        T.op("matmul", sc_ap[:, j * 128:(j + 1) * 128],
                             qtv[lo:lo + 64, pp, :],
                             kt3[lo:lo + 64, pp, cslot, :],
                             start=True, stop=not masked)
                    if masked:
                        for par in (0, 1):
                            sc_ap = (PSB, PSC)[par]
                            h = T.op("matmul", sc_ap[:, j * 128:(j + 1) * 128],
                                     id_b, maskf[:, v_, j * 128:(j + 1) * 128],
                                     start=False, stop=True)
                tok = h.inc()
                hs[2 * pp]["sc"] = tok
                hs[2 * pp + 1]["sc"] = tok
                ew(tok, pb0, pb1, pc0, pc1)

            def _mx(hh):
                par = hh % 2
                sc_ap = (PSB, PSC)[par]
                sc_tbs_ = ((pb0, pb1), (pc0, pc1))[par]
                V.wait(hs[hh]["sc"])
                h = V.op("reduce_max", out=sml[:, 3, hh:hh + 1],
                         in_=sc_ap[:, 0:640], axis=AX.X, negate=True)
                hs[hh]["mx"] = h.inc()
                er(hs[hh]["mx"], *sc_tbs_)

            def _ex(hh):
                par = hh % 2
                sc_ap = (PSB, PSC)[par]
                sc_tbs_ = ((pb0, pb1), (pc0, pc1))[par]
                prb = (prb0, prb1)[par]
                prtb = tb[("prb0", "prb1")[par]]
                bw(S, prtb)
                S.wait(hs[hh]["mx"])
                h = S.op("activation", out=prb, in_=sc_ap[:, 0:640], func=ACT.Exp,
                         bias=sml[:, 3, hh:hh + 1])
                hs[hh]["ex"] = h.inc()
                er(hs[hh]["ex"], *sc_tbs_)
                ew(hs[hh]["ex"], prtb)

            def _pt(hh):
                par = hh % 2
                prb = (prb0, prb1)[par]
                prtb = tb[("prb0", "prb1")[par]]
                bw(T, ptr)
                T.wait(hs[hh]["ex"])
                for j in range(5):
                    h = T.op("transpose", PSTR[:, j * 128:(j + 1) * 128],
                             prb[:, j * 128:(j + 1) * 128], id_b)
                hs[hh]["pt"] = h.inc()
                ew(hs[hh]["pt"], ptr)
                er(hs[hh]["pt"], prtb)

            def _pe(hh):
                par = hh % 2
                ptbuf = (ptb0, ptb1)[par]
                prtb = tb[("prb0", "prb1")[par]]
                br(V, ptr)
                h = V.op("tensor_copy", out=ptbuf,
                         in_=PSTR[:, 0:640].rearrange("p (a n) -> p a n", n=128))
                hs[hh]["pe"] = h.inc()
                er(hs[hh]["pe"], ptr)
                ew(hs[hh]["pe"], prtb)

            def _av(hh):
                par = hh % 2
                ptbuf = (ptb0, ptb1)[par]
                prtb = tb[("prb0", "prb1")[par]]
                avtb = (pd, pe)[par]
                bw(T, avtb)
                T.wait(hs[hh]["pe"])
                T.wait(d["vring"])
                for j, cr in enumerate(ctx_tiles):
                    cslot = cr % RING
                    h = T.op("matmul", avtb.ap[:, 0:65], ptbuf[:, j, :],
                             vrr[:, cslot, hh * 65:hh * 65 + 65],
                             start=(j == 0), stop=(j == 4))
                hs[hh]["av"] = h.inc()
                ew(hs[hh]["av"], avtb)
                er(hs[hh]["av"], prtb)

            def _rc(hh):
                avtb = (pd, pe)[hh % 2]
                br(V, avtb)
                V.wait(hs[hh]["av"])
                h = V.op("reciprocal", out=sml[:, 5, hh:hh + 1],
                         in_=avtb.ap[:, 64:65])
                hs[hh]["rc"] = h.inc()
                er(hs[hh]["rc"], avtb)

            def _oe(hh):
                avtb = (pd, pe)[hh % 2]
                bw(S, tb["otk"])
                br(S, avtb)
                S.wait(hs[hh]["rc"])
                h = S.op("activation", out=otk[:, hh * 64:(hh + 1) * 64],
                         in_=avtb.ap[:, 0:64], func=ACT.Identity,
                         scale=sml[:, 5, hh:hh + 1])
                hs[hh]["oe"] = h.inc()
                er(hs[hh]["oe"], avtb)
                ew(hs[hh]["oe"], tb["otk"])

            _scp(0)
            _mx(0)
            _mx(1)
            for pp in range(FH // 2):
                h0, h1 = 2 * pp, 2 * pp + 1
                _ex(h0)
                _pt(h0)
                _ex(h1)
                if pp + 1 < FH // 2:
                    _scp(pp + 1)
                _pe(h0)
                _av(h0)
                if pp + 1 < FH // 2:
                    _mx(h0 + 2)
                _pt(h1)
                _pe(h1)
                _rc(h0)
                _av(h1)
                _oe(h0)
                if pp + 1 < FH // 2:
                    _mx(h1 + 2)
                _rc(h1)
                _oe(h1)
            er(c.T.tok(), qtb_tb)

        def tail(r):
            scol = 1 + (r - ch["sbase"]) * 128
            bw(T, ptr)
            T.wait(tb["otk"].wr)
            for j in range(4):
                h = T.op("transpose", PSTR[:, j * 128:(j + 1) * 128],
                         otk[:, j * 128:(j + 1) * 128], id_b)
            ttok = h.inc()
            ew(ttok, ptr)
            er(ttok, tb["otk"])
            bw(S, tb["otb"])
            br(S, ptr)
            stok = S.op("copy", out=otb[:, 0:4, :],
                        in_=PSTR[:, 0:512].rearrange("p (a n) -> p a n",
                                                     n=128)).inc()
            er(stok, ptr)
            ew(stok, tb["otb"])
            tok_oT = stok
            for rnd in range(2):
                bw(T, pe)
                T.wait(tok_oT)
                T.wait(wo_tok)
                for dmc in range(4):
                    dm = rnd * 4 + dmc
                    for ic in range(4):
                        h = T.op("matmul", PSE[:, dmc * 128:(dmc + 1) * 128],
                                 wo4[:, ic, dm * 128:(dm + 1) * 128],
                                 otb[:, ic, :], start=(ic == 0), stop=(ic == 3))
                tok_wo = h.inc()
                ew(tok_wo, pe)
                ts_[r]["wo"] = tok_wo
                br(V, pe)
                V.wait(tok_wo)
                for dmc in range(4):
                    dm = rnd * 4 + dmc
                    h = V.op("tensor_tensor", out=strm[:, dm, scol:scol + 128],
                             in0=PSE[:, dmc * 128:(dmc + 1) * 128],
                             in1=strm[:, dm, scol:scol + 128], op=ALU.add)
                vtok = h.inc()
                er(vtok, pe)
                er(vtok, tb["otb"])

        prep_mm(all_tiles[0])
        prep_vec(all_tiles[0])
        last_qkv = None
        last_wo = None
        for i, r in enumerate(all_tiles):
            if i + 1 < len(all_tiles):
                prep_mm(all_tiles[i + 1])
            if r in full_tiles:
                heads(r)
            if i + 1 < len(all_tiles):
                prep_vec(all_tiles[i + 1])
            if r in full_tiles:
                tail(r)
        for r in all_tiles:
            if "qkv_last" in ts_[r]:
                last_qkv = ts_[r]["qkv_last"]
            if "wo" in ts_[r]:
                last_wo = ts_[r]["wo"]
        er(last_qkv, WAr[0], WAr[1])
        if last_wo is not None:
            er(last_wo, W2wo[layer])
        phase_toks["strm"] = c.V.tok()


    def emit_ffn(ch, layer):
        tiles = ch["c_tiles"] if layer == 0 else ch["e_tiles"]
        if not tiles:
            return
        ncols = len(tiles) * 128
        cr_ = x1f_cr if layer == 0 else y1f_pv
        if layer == 0:
            h0 = V.op("tensor_copy", out=y1f_pv.unsqueeze(2),
                      in_=y1f_cr.unsqueeze(2))
            V.wait(h0.inc())
        if layer == 1:
            V.wait(tb["y1bf"].wr)
        h = V.op("tensor_copy", out=strm[:, :, 0:1],
                 in_=cr_.unsqueeze(2))
        last = 1 + (tiles[-1] - ch["sbase"]) * 128 + 127
        if layer == 0:
            h = V.op("tensor_copy", out=cr_.unsqueeze(2),
                     in_=strm[:, :, last:last + 1])
        crumb_tok = h.inc()
        cst = {r: {} for r in tiles}

        def c1_tr1(r):
            col = 1 + (r - ch["sbase"]) * 128
            bw(T, pb0, pb1)
            T.wait(crumb_tok)
            T.wait(phase_toks["strm"])
            for kc in range(8):
                h = T.op("transpose", PSB[:, kc * 128:(kc + 1) * 128],
                         shift_view(strm, kc, col), id_f)
            ttok = h.inc()
            ew(ttok, pb0, pb1)

        def c1_stats(r, i):
            xt = (xt0, xt1)[i % 2]
            xtb = tb[("xt0", "xt1")[i % 2]]
            sa = 6 + 2 * (i % 2)
            sb = sa + 1
            bw(S, xtb)
            br(S, pb0)
            stok = S.op("copy", out=xt, in_=PSB).inc()
            er(stok, pb0, pb1)
            ew(stok, xtb)
            V.wait(stok)
            V.op("bn_stats", out=sml[:, sa, 0:6], in_=xt[:, 0:512])
            vtok = V.op("bn_stats", out=sml[:, sa, 6:12], in_=xt[:, 512:1024]).inc()
            V.wait(vtok)
            vtok = V.op("bn_aggr", out=sml[:, sb, 0:2], in_=sml[:, sa, 0:12]).inc()
            S.wait(vtok)
            stok = S.op("activation", out=sml[:, sb, 2:3], in_=sml[:, sb, 1:2],
                        func=ACT.Ln, bias=eps5).inc()
            S.wait(stok)
            stok = S.op("activation", out=sml[:, sb, 3:4], in_=sml[:, sb, 2:3],
                        func=ACT.Exp, scale=-0.5).inc()
            V.wait(stok)
            vtok = V.op("tensor_scalar", out=sml[:, sb, 4:5], in0=sml[:, sb, 0:1],
                        scalar1=sml[:, sb, 3:4], scalar2=-1.0,
                        op0=ALU.mult, op1=ALU.mult).inc()
            bw(S, tb["znt"])
            S.wait(vtok)
            stok = S.op("activation", out=znt, in_=xt, func=ACT.Identity,
                        bias=sml[:, sb, 4:5], scale=sml[:, sb, 3:4]).inc()
            ew(stok, tb["znt"])
            er(stok, xtb)
            cst[r]["aff"] = stok

        def c1_tr2(r, i):
            lcol = (r - tiles[0]) * 128
            bw(T, pc0, pc1)
            T.wait(cst[r]["aff"])
            for kc in range(8):
                h = T.op("transpose", PSC[:, kc * 128:(kc + 1) * 128],
                         znt[:, kc * 128:(kc + 1) * 128], id_f)
            ttok = h.inc()
            ew(ttok, pc0, pc1)
            er(ttok, tb["znt"])
            br(S, pc0)
            for kc in range(8):
                h = S.op("activation", out=y1a[:, kc, 1 + lcol:1 + lcol + 128],
                         in_=PSC[:, kc * 128:(kc + 1) * 128], func=ACT.Identity,
                         scale=lnp[:, layer, 0, kc:kc + 1],
                         bias=lnp[:, layer, 1, kc:kc + 1])
            stok = h.inc()
            er(stok, pc0, pc1)
            cst[r]["lnh"] = stok

        c1_tr1(tiles[0])
        for i, r in enumerate(tiles):
            c1_stats(r, i)
            if i + 1 < len(tiles):
                c1_tr1(tiles[i + 1])
            c1_tr2(r, i)
        stok = cst[tiles[-1]]["lnh"]
        ew(stok, tb["lnh"])
        wins = []
        off, left = 1, ncols
        while left > 0:
            w = min(512, left)
            wins.append((off, w))
            off += w
            left -= w
        vtok = None
        ps_y_par = 0
        ub_tb = [TB(None, "ub0"), TB(None, "ub1")]
        ub_tb[0].rd = list(W2wo[layer].rd)
        ub_tb[1].rd = list(W2wo[layer].rd)
        grp = {}

        def c2_ag(g):
            reg, w2reg, base, w2slot, cw = load_wgrp(layer, g)
            nch = 2 if cw > 128 else 1
            ubf = ubf_l[layer][g % 2]
            utb = ub_tb[g % 2]
            uv = None
            for hc in range(nch):
                ccw = min(128, cw - hc * 128)
                for (wo_, W) in wins:
                    bw(T, pd, pe)
                    T.wait(reg.wr)
                    T.wait(tb["lnh"].wr)
                    for kc in range(8):
                        T.op("matmul", PSD[0:ccw, 0:W],
                             base[:, kc, hc * 128:hc * 128 + ccw],
                             y1a[:, kc, wo_:wo_ + W],
                             start=(kc == 0), stop=(kc == 7))
                    for kc in range(8):
                        h = T.op("matmul", PSE[0:ccw, 0:W],
                                 base[:, kc, 256 + hc * 128:256 + hc * 128 + ccw],
                                 y1a[:, kc, wo_:wo_ + W],
                                 start=(kc == 0), stop=(kc == 7))
                    tok_ag = h.inc()
                    ew(tok_ag, pd, pe)
                    er(tok_ag, reg)
                    bw(S, tb["znt"])
                    br(S, pe)
                    S.wait(tok_ag)
                    stok = S.op("activation", out=znt[0:ccw, 0:W],
                                in_=PSE[0:ccw, 0:W], func=ACT.Gelu,
                                bias=b1g[0:ccw, layer,
                                         2 * g + hc:2 * g + hc + 1]).inc()
                    er(stok, pe)
                    ew(stok, tb["znt"])
                    bw(V, utb)
                    br(V, pd)
                    V.wait(stok)
                    uv = V.op("scalar_tensor_tensor",
                              out=ubf[0:ccw, hc, wo_ - 1:wo_ - 1 + W],
                              in0=PSD[0:ccw, 0:W],
                              scalar=b1a[0:ccw, layer,
                                         2 * g + hc:2 * g + hc + 1],
                              in1=znt[0:ccw, 0:W], op0=ALU.add,
                              op1=ALU.mult).inc()
                    er(uv, pd)
                    er(uv, tb["znt"])
            ew(uv, utb)
            grp[g] = (w2reg, w2slot, cw, nch)

        def c2_y(g):
            nonlocal vtok, ps_y_par
            w2reg, w2slot, cw, nch = grp[g]
            ubf = ubf_l[layer][g % 2]
            utb = ub_tb[g % 2]
            for dmc in range(8):
                for (wo_, W) in wins:
                    pcb = (pc0, pc1)[ps_y_par]
                    pslice = PSC[:, ps_y_par * 512:ps_y_par * 512 + W]
                    bw(T, pcb)
                    T.wait(w2reg.wr)
                    T.wait(utb.wr)
                    for hc in range(nch):
                        chw = min(128, cw - hc * 128)
                        h = T.op("matmul", pslice,
                                 w2slot[0:chw, hc, dmc * 128:(dmc + 1) * 128],
                                 ubf[0:chw, hc, wo_ - 1:wo_ - 1 + W],
                                 start=(hc == 0), stop=(hc == nch - 1))
                    tok_y = h.inc()
                    ew(tok_y, pcb)
                    er(tok_y, w2reg)
                    er(tok_y, utb)
                    br(V, pcb)
                    V.wait(tok_y)
                    if g == 0:
                        h = V.op("scalar_tensor_tensor",
                                 out=strm[:, dmc, wo_:wo_ + W], in0=pslice,
                                 scalar=b2c[:, layer, dmc:dmc + 1],
                                 in1=strm[:, dmc, wo_:wo_ + W],
                                 op0=ALU.add, op1=ALU.add)
                    else:
                        h = V.op("tensor_tensor", out=strm[:, dmc, wo_:wo_ + W],
                                 in0=pslice, in1=strm[:, dmc, wo_:wo_ + W],
                                 op=ALU.add)
                    vtok = h.inc()
                    er(vtok, pcb)
                    ps_y_par ^= 1

        c2_ag(0)
        for g in range(NGRP):
            if g + 1 < NGRP:
                c2_ag(g + 1)
            c2_y(g)
        for utb in ub_tb:
            for t_ in utb.rd:
                er(t_, W2wo[layer])
        er(vtok, W2wo[layer])
        phase_toks["strm"] = vtok
        if layer == 0:
            S.wait(vtok)
            S.op("copy", out=y1a[:, :, 0:1], in_=y1b_cr.unsqueeze(2))
            for (wo_, W) in wins:
                h = S.op("copy", out=y1a[:, :, wo_:wo_ + W],
                         in_=strm[:, :, wo_:wo_ + W])
            S.wait(h.inc())
            S.op("copy", out=y1b_cr.unsqueeze(2),
                 in_=y1a[:, :, last:last + 1])
            stok = S.op("copy", out=y1f_cr.unsqueeze(2),
                        in_=strm[:, :, last:last + 1]).inc()
            ew(stok, tb["y1bf"])

    def emit_att(ch, wq_tok, wg_tok, wout_tok):
        kv_tiles, full_tiles = ch["f_kv"], ch["f_full"]
        all_tiles = sorted(set(kv_tiles + full_tiles))
        if not all_tiles:
            return
        kt3 = kta.rearrange("p a (s n) -> p a s n", n=128)
        va4 = vat.rearrange("p s (h e) -> p s h e", e=65)
        wo8 = W2flat.rearrange("p (k n) -> p k n", k=8)
        seedT, seedS = c.T.tok(), c.S.tok()
        for n_ in ("ostg0", "ostg1"):
            tb[n_].rd = tb[n_].rd + [seedT, seedS]
        V.wait(wq_tok)
        V.wait(wg_tok)
        gb = gmc.unsqueeze(2).broadcast_to([128, 8, 3072])
        V.op("tensor_tensor", out=WA, in0=WA, in1=gb, op=ALU.mult)
        gb2 = gmc.unsqueeze(2).broadcast_to([128, 8, 16])
        gtok = V.op("tensor_tensor", out=wag, in0=wag, in1=gb2, op=ALU.mult).inc()
        T.wait(gtok)
        T.wait(phase_toks["xnbf"])
        last_qkv_tok = None
        last_wout_tok = None
        for r in all_tiles:
            full = r in full_tiles
            slot = r % ACTX
            mvi = min(max(r - OWN0, 0), 2)
            col = 1 + (r - ch["xnbase"]) * 128
            for hf in range(2):
                pax = (pa, pc0)[hf]
                psx = (PSA, PSC[:, 0:512])[hf]
                bw(T, pax)
                for kc in range(8):
                    h = T.op("matmul", psx, xn_bf[:, kc, col:col + 128],
                             WA[:, kc, 1024 + hf * 512:1024 + (hf + 1) * 512],
                             start=(kc == 0), stop=(kc == 7))
                tok = h.inc()
                ew(tok, pax)
                last_qkv_tok = tok
                bw(S, tb["krot"])
                br(S, pax)
                stok = S.op("activation", out=krot, in_=psx, func=ACT.Identity,
                            scale=inv32[:, r:r + 1]).inc()
                er(stok, pax)
                ew(stok, tb["krot"])
                bw(T, ptr)
                T.wait(stok)
                for j in range(4):
                    h = T.op("transpose",
                             PSTR[:, hf * 512 + j * 128:hf * 512 + (j + 1) * 128],
                             krot[:, j * 128:(j + 1) * 128], id_b)
                ttok = h.inc()
                ew(ttok, ptr)
                er(ttok, tb["krot"])
                br(S, ptr)
                stok = S.op("copy", out=kt3[:, 4 * hf:4 * hf + 4, slot, :],
                            in_=PSTR[:, hf * 512:(hf + 1) * 512].rearrange(
                                "p (a n) -> p a n", n=128)).inc()
                er(stok, ptr)
            tok_kring = stok
            for hf in range(2):
                pax = (pa, pc0)[hf]
                psx = (PSA, PSC[:, 0:512])[hf]
                bw(T, pax)
                for kc in range(8):
                    h = T.op("matmul", psx, xn_bf[:, kc, col:col + 128],
                             WA[:, kc, 2048 + hf * 512:2048 + (hf + 1) * 512],
                             start=(kc == 0), stop=(kc == 7))
                tok = h.inc()
                ew(tok, pax)
                last_qkv_tok = tok
                br(S, pax)
                stok = S.op("activation",
                            out=va4[:, slot, 8 * hf:8 * hf + 8, 0:64],
                            in_=psx.rearrange("p (h e) -> p h e", e=64),
                            func=ACT.Identity, scale=inv32[:, r:r + 1]).inc()
                er(stok, pax)
            tok_vring = stok
            if not full:
                continue
            for hf in range(2):
                pax = (pa, pc0)[hf]
                psx = (PSA, PSC[:, 0:512])[hf]
                bw(T, pax)
                for kc in range(8):
                    h = T.op("matmul", psx, xn_bf[:, kc, col:col + 128],
                             WA[:, kc, hf * 512:(hf + 1) * 512],
                             start=(kc == 0), stop=(kc == 7))
                tok = h.inc()
                ew(tok, pax)
                last_qkv_tok = tok
                bw(S, tb["otk"])
                br(S, pax)
                stok = S.op("activation", out=otk[:, hf * 512:(hf + 1) * 512],
                            in_=psx, func=ACT.Identity,
                            scale=inv32q[:, r:r + 1]).inc()
                er(stok, pax)
                ew(stok, tb["otk"])
            bw(T, pd)
            for kc in range(8):
                h = T.op("matmul", PSD[:, 0:16], xn_bf[:, kc, col:col + 128],
                         wag[:, kc, :], start=(kc == 0), stop=(kc == 7))
            tok_g = h.inc()
            ew(tok_g, pd)
            last_qkv_tok = tok_g
            bw(S, tb["gsc"])
            br(S, pd)
            stok = S.op("activation", out=gsc[:, 0, :], in_=PSD[:, 0:16],
                        func=ACT.Identity, scale=ninv32[:, r:r + 1]).inc()
            er(stok, pd)
            V.wait(stok)
            vtok = V.op("tensor_sub", out=gsc[:, 0, :], in0=gsc[:, 0, :],
                        in1=bgv).inc()
            S.wait(vtok)
            stok = S.op("activation", out=gsc[:, 1, :], in_=gsc[:, 0, :],
                        func=ACT.Exp).inc()
            V.wait(stok)
            vtok = V.op("tensor_scalar_add", out=gsc[:, 1, :], in0=gsc[:, 1, :],
                        scalar1=1.0).inc()
            V.wait(vtok)
            vtok = V.op("reciprocal", out=gsc[:, 1, :], in_=gsc[:, 1, :]).inc()
            ew(vtok, tb["gsc"])
            bw(T, ptr)
            T.wait(tb["otk"].wr)
            for j in range(8):
                h = T.op("transpose", PSTR[:, j * 128:(j + 1) * 128],
                         otk[:, j * 128:(j + 1) * 128], id_b)
            ttok = h.inc()
            ew(ttok, ptr)
            er(ttok, tb["otk"])
            bw(S, tb["qtb"])
            br(S, ptr)
            stok = S.op("copy", out=qtb,
                        in_=PSTR.rearrange("p (a n) -> p a n", n=128)).inc()
            er(stok, ptr)
            ew(stok, tb["qtb"])
            tok_qT = stok
            ctx_tiles = list(range(r - 2, r + 1))
            hs = [dict() for _ in range(AH)]
            v_ = min(max(r - OWN0, 0), 2)
            mask_regions = [0, 2] + ([1] if v_ == 0 else [])

            def _scp(pp):
                for par in (0, 1):
                    bw(T, (pb0, pb1)[par])
                T.wait(tok_qT)
                T.wait(tok_kring)
                for j, cr_t in enumerate(ctx_tiles):
                    cslot = cr_t % ACTX
                    masked = j in mask_regions
                    for par in (0, 1):
                        lo = 64 * par
                        T.op("matmul",
                             PSB[:, par * 512 + j * 128:par * 512 + (j + 1) * 128],
                             qtb[lo:lo + 64, pp, :], kt3[lo:lo + 64, pp, cslot, :],
                             start=True, stop=not masked)
                    if masked:
                        for par in (0, 1):
                            h = T.op("matmul",
                                     PSB[:, par * 512 + j * 128:par * 512 + (j + 1) * 128],
                                     id_b, maska[:, v_, j * 128:(j + 1) * 128],
                                     start=False, stop=True)
                tok = h.inc()
                hs[2 * pp]["sc"] = tok
                hs[2 * pp + 1]["sc"] = tok
                ew(tok, pb0, pb1)

            def _mx(hh):
                par = hh % 2
                sc_tb = (pb0, pb1)[par]
                V.wait(hs[hh]["sc"])
                h = V.op("reduce_max", out=sml[:, 7, hh:hh + 1],
                         in_=PSB[:, par * 512:par * 512 + 384], axis=AX.X,
                         negate=True)
                hs[hh]["mx"] = h.inc()
                er(hs[hh]["mx"], sc_tb)

            def _ex(hh):
                par = hh % 2
                sc_tb = (pb0, pb1)[par]
                prb = (prb0, prb1)[par]
                prtb = tb[("prb0", "prb1")[par]]
                bw(S, prtb)
                S.wait(hs[hh]["mx"])
                h = S.op("activation", out=prb[:, 0:384],
                         in_=PSB[:, par * 512:par * 512 + 384], func=ACT.Exp,
                         bias=sml[:, 7, hh:hh + 1])
                hs[hh]["ex"] = h.inc()
                er(hs[hh]["ex"], sc_tb)
                ew(hs[hh]["ex"], prtb)

            def _pt(hh):
                par = hh % 2
                prb = (prb0, prb1)[par]
                prtb = tb[("prb0", "prb1")[par]]
                bw(T, ptr)
                T.wait(hs[hh]["ex"])
                for j in range(3):
                    h = T.op("transpose", PSTR[:, j * 128:(j + 1) * 128],
                             prb[:, j * 128:(j + 1) * 128], id_b)
                hs[hh]["pt"] = h.inc()
                ew(hs[hh]["pt"], ptr)
                er(hs[hh]["pt"], prtb)

            def _pe(hh):
                par = hh % 2
                ptbuf = (ptb0, ptb1)[par]
                prtb = tb[("prb0", "prb1")[par]]
                br(V, ptr)
                h = V.op("tensor_copy", out=ptbuf[:, 0:3, :],
                         in_=PSTR[:, 0:384].rearrange("p (a n) -> p a n", n=128))
                hs[hh]["pe"] = h.inc()
                er(hs[hh]["pe"], ptr)
                ew(hs[hh]["pe"], prtb)

            def _av(hh):
                par = hh % 2
                ptbuf = (ptb0, ptb1)[par]
                prtb = tb[("prb0", "prb1")[par]]
                avtb = (pd, pe)[par]
                bw(T, avtb)
                T.wait(hs[hh]["pe"])
                T.wait(tok_vring)
                for j, cr_t in enumerate(ctx_tiles):
                    cslot = cr_t % ACTX
                    h = T.op("matmul", avtb.ap[:, 0:65], ptbuf[:, j, :],
                             vat[:, cslot, hh * 65:hh * 65 + 65],
                             start=(j == 0), stop=(j == 2))
                hs[hh]["av"] = h.inc()
                ew(hs[hh]["av"], avtb)
                er(hs[hh]["av"], prtb)

            def _rc1(hh):
                avtb = (pd, pe)[hh % 2]
                br(V, avtb)
                V.wait(hs[hh]["av"])
                V.wait(tb["gsc"].wr)
                h = V.op("reciprocal", out=sml[:, 9, hh:hh + 1],
                         in_=avtb.ap[:, 64:65])
                hs[hh]["rc1"] = h.inc()
                er(hs[hh]["rc1"], avtb)

            def _rc2(hh):
                V.wait(hs[hh]["rc1"])
                h = V.op("tensor_mul", out=sml[:, 9, hh:hh + 1],
                         in0=sml[:, 9, hh:hh + 1], in1=gsc[:, 1, hh:hh + 1])
                hs[hh]["rc2"] = h.inc()

            def _oe(hh):
                avtb = (pd, pe)[hh % 2]
                bw(S, tb["otk"])
                br(S, avtb)
                S.wait(hs[hh]["rc2"])
                h = S.op("activation", out=otk[:, hh * 64:(hh + 1) * 64],
                         in_=avtb.ap[:, 0:64], func=ACT.Identity,
                         scale=sml[:, 9, hh:hh + 1])
                hs[hh]["oe"] = h.inc()
                er(hs[hh]["oe"], avtb)
                ew(hs[hh]["oe"], tb["otk"])

            _scp(0)
            _mx(0)
            _mx(1)
            for pp in range(AH // 2):
                h0, h1 = 2 * pp, 2 * pp + 1
                _ex(h0)
                _pt(h0)
                _ex(h1)
                if pp + 1 < AH // 2:
                    _scp(pp + 1)
                _pe(h0)
                _av(h0)
                if pp + 1 < AH // 2:
                    _mx(h0 + 2)
                _pt(h1)
                _pe(h1)
                _rc1(h0)
                _av(h1)
                _rc1(h1)
                if pp + 1 < AH // 2:
                    _mx(h1 + 2)
                _rc2(h0)
                _oe(h0)
                _rc2(h1)
                _oe(h1)
            bw(T, ptr)
            T.wait(tb["otk"].wr)
            for j in range(8):
                h = T.op("transpose", PSTR[:, j * 128:(j + 1) * 128],
                         otk[:, j * 128:(j + 1) * 128], id_b)
            ttok = h.inc()
            ew(ttok, ptr)
            er(ttok, tb["otk"])
            bw(S, tb["otb"])
            br(S, ptr)
            stok = S.op("copy", out=otb,
                        in_=PSTR.rearrange("p (a n) -> p a n", n=128)).inc()
            er(stok, ptr)
            ew(stok, tb["otb"])
            tok_oT = stok
            bw(T, pc0, pc1)
            T.wait(tok_oT)
            T.wait(wout_tok)
            for dmc in range(8):
                for ic in range(8):
                    h = T.op("matmul", PSC[:, dmc * 128:(dmc + 1) * 128],
                             wo8[:, ic, dmc * 128:(dmc + 1) * 128],
                             otb[:, ic, :], start=(ic == 0), stop=(ic == 7))
            tok_wout = h.inc()
            ew(tok_wout, pc0, pc1)
            er(tok_wout, tb["otb"])
            last_wout_tok = tok_wout
            scol = 1 + (r - ch["sbase"]) * 128
            bw(V, tb["znt"])
            br(V, pc0)
            V.wait(tok_wout)
            for dmc in range(8):
                h = V.op("tensor_tensor", out=znt[:, dmc * 128:(dmc + 1) * 128],
                         in0=PSC[:, dmc * 128:(dmc + 1) * 128],
                         in1=strm[:, dmc, scol:scol + 128], op=ALU.add)
            vtok = h.inc()
            er(vtok, pc0, pc1)
            ew(vtok, tb["znt"])
            bw(T, pb0, pb1)
            T.wait(vtok)
            for kc in range(8):
                h = T.op("transpose", PSB[:, kc * 128:(kc + 1) * 128],
                         znt[:, kc * 128:(kc + 1) * 128], id_f)
            ttok = h.inc()
            ew(ttok, pb0, pb1)
            er(ttok, tb["znt"])
            ib = r % 2
            stg = ostg[:, ib, :]
            stb = tb[("ostg0", "ostg1")[ib]]
            xtb = tb[("xt0", "xt1")[ib]]
            bw(Y, xtb)
            ytok = Y.op("dma_start", out=(xt0, xt1)[ib],
                        in_=xin[r * 128:(r + 1) * 128, :]).inc(
                            16, ("yx0", "yx1")[ib])
            ew(ytok, xtb)
            bw(V, stb)
            br(V, pb0)
            V.wait(ytok)
            vtok = V.op("tensor_add", out=stg, in0=PSB, in1=(xt0, xt1)[ib]).inc()
            er(vtok, pb0, pb1)
            er(vtok, xtb)
            ew(vtok, stb)
            Y.wait(vtok)
            orow = (r - OWN0) * 128
            ytok = Y.op("dma_start", out=yout[orow:orow + 128, :],
                        in_=stg).inc(16, ("yo0", "yo1")[ib])
            er(ytok, stb)
        er(last_qkv_tok, *WAr)
        if last_wout_tok is not None:
            er(last_wout_tok, W2wo[0], W2wo[1], W2g[0], W2g[1])

    def do_taps(label):
        here = dict(strm=strm, xn_bf=xn_bf, y1a=y1a, inv32=inv32, kt1=kt1,
                    v1r=v1r, kt2=kt2, v2r=v2r, kta=kta, vat=vat, otk=otk,
                    qtb=qtb, ubf0=ubf_l[0], ubf1=ubf_l[1], krot=krot, gsc=gsc, ninv32=ninv32,
                    qrot=qrot, sml=sml, rope=rope, WA=WA, W2=W2)
        for lab, bname in taps:
            if lab != label:
                continue
            ap = here[bname]
            tname = f"tap_{lab}_{bname}"
            shp = list(ap.shape)
            tout = nc.dram_tensor(tname, shp, F32, kind="ExternalOutput").ap()
            for e_ in (T, V, S):
                pass
            G.wait(("t", c.count["t"]))
            G.wait(("v", c.count["v"]))
            G.wait(("s", c.count["s"]))
            ttok = G.op("dma_start", out=tout, in_=ap).inc(16, "yi")
            for e_ in (T, V, S, Y):
                e_.wait(ttok)

    emit_init()
    for ch in CHUNKS:
        if nphase >= 2:
            wq0 = load_wqkv_ffn(0)
            wo0 = load_wo_ffn(0)
        ci = CHUNKS.index(ch)
        if nphase >= 1:
            emit_A(ch)
            do_taps(f"A{ci}")
        if nphase >= 2:
            emit_wave_ffn(ch, 0, wq0, wo0)
            do_taps(f"B{ci}")
        if nphase >= 3:
            emit_ffn(ch, 0)
            do_taps(f"C{ci}")
        if nphase >= 4:
            wq1 = load_wqkv_ffn(1)
            wo1 = load_wo_ffn(1)
            emit_wave_ffn(ch, 1, wq1, wo1)
            do_taps(f"D{ci}")
        if nphase >= 5:
            emit_ffn(ch, 1)
            do_taps(f"E{ci}")
        if nphase >= 6:
            watok, wgtok = load_watt()
            wouttok = load_wout()
            emit_att(ch, watok, wgtok, wouttok)
            do_taps(f"F{ci}")
    for s_ in ("yo0", "yo1"):
        Y.wait((s_, c.count[s_]))
        V.wait((s_, c.count[s_]))

    with contextlib.ExitStack() as st:
        for name in ("t", "v", "s") + DMA_SEMS:
            c.sem[name] = st.enter_context(nc.semaphore(f"sem_{name}"))
        block = st.enter_context(nc.Block())

        def mk_replay(rec):
            def f(eng):
                for item in rec.ops:
                    if item[0] == "w":
                        eng.wait_ge(c.sem[item[1]], item[2])
                    else:
                        m, a, kw, incs = item[1]
                        inst = getattr(eng, m)(*a, **kw)
                        for s, n in incs:
                            inst = inst.then_inc(c.sem[s], n)
            return f

        block.tensor(mk_replay(c.T))
        block.vector(mk_replay(c.V))
        block.scalar(mk_replay(c.S))
        block.sync(mk_replay(c.Y))
        block.gpsimd(mk_replay(c.G))
    return nc


def _lhs_chunks(w):
    w = np.ascontiguousarray(np.asarray(w, np.float32))
    K, N = w.shape
    return np.ascontiguousarray(w.reshape(K // 128, 128, N).transpose(1, 0, 2))


def _prep_weights(waqkv, wagat, waout, wfqkv, wfo, wf1, wf2):
    d = {}
    d["w_aqkv"] = _lhs_chunks(waqkv)
    d["w_agat"] = _lhs_chunks(wagat)
    d["w_aout"] = _lhs_chunks(waout)
    d["w_fqkv"] = np.stack([_lhs_chunks(np.asarray(wfqkv, np.float32)[l])
                            for l in range(2)])
    d["w_fo"] = np.stack([_lhs_chunks(np.asarray(wfo, np.float32)[l])
                          for l in range(2)])
    w1 = np.asarray(wf1, np.float32)
    w2 = np.asarray(wf2, np.float32)
    w1g = np.zeros((2, NGRP, 128, 8, 512), np.float32)
    w2g = np.zeros((2, NGRP, 128, 2, DM), np.float32)
    for l in range(2):
        w1c = _lhs_chunks(w1[l])
        for g in range(NGRP):
            c0 = 256 * g
            cw = min(256, HID - c0)
            w1g[l, g, :, :, 0:cw] = w1c[:, :, c0:c0 + cw]
            w1g[l, g, :, :, 256:256 + cw] = w1c[:, :, HID + c0:HID + c0 + cw]
            r0 = 2 * g * 128
            rw = min(256, HID - r0)
            blk = w2[l, r0:r0 + rw]
            pad = np.zeros((256, DM), np.float32)
            pad[:rw] = blk
            w2g[l, g] = pad.reshape(2, 128, DM).transpose(1, 0, 2)
    d["w_f1"] = np.ascontiguousarray(w1g)
    d["w_f2"] = np.ascontiguousarray(w2g)
    return d


def make_in_maps(x, consts, weights):
    rope_h = [_rope_tables_for_core(h) for h in (0, 1)]
    masks_h = [_masks_for_core(h) for h in (0, 1)]
    in_maps = []
    for core in range(8):
        seq, half = core // 2, core % 2
        if half == 0:
            xi = np.concatenate([np.zeros((HALO, DM), np.float32),
                                 x[seq, 0:P]], axis=0)
        else:
            xi = np.ascontiguousarray(x[seq, P - HALO:SEQ])
        mf, ma = masks_h[half]
        im = dict(xin=xi, rope_in=rope_h[half], maskf_in=mf, maska_in=ma,
                  **consts, **weights)
        in_maps.append(im)
    return in_maps


_NC_CACHE = {}


def _get_program():
    if "nc" not in _NC_CACHE:
        _NC_CACHE["nc"] = build_program()
    return _NC_CACHE["nc"]


def kernel(L0_attn_x, L0_attn_gamma, L0_attn_W_qkv, L0_attn_W_gates,
           L0_attn_b_gates, L0_attn_W_out, L0_ffn_wqkv, L0_ffn_q_scale,
           L0_ffn_k_scale, L0_ffn_wo, L0_ffn_ln_g, L0_ffn_ln_b, L0_ffn_w1,
           L0_ffn_b1, L0_ffn_w2, L0_ffn_b2):
    x = np.ascontiguousarray(np.asarray(L0_attn_x, dtype=np.float32))
    consts = _const_inputs(L0_ffn_q_scale, L0_ffn_k_scale, L0_ffn_ln_g,
                           L0_ffn_ln_b, L0_ffn_b1, L0_ffn_b2, L0_attn_b_gates,
                           L0_attn_gamma)
    weights = _prep_weights(L0_attn_W_qkv, L0_attn_W_gates, L0_attn_W_out,
                            L0_ffn_wqkv, L0_ffn_wo, L0_ffn_w1, L0_ffn_w2)
    in_maps = make_in_maps(x, consts, weights)
    nc = _get_program()
    res = run_bass_kernel_spmd(nc, in_maps, core_ids=list(range(8)))
    out = np.empty((NSEQ, SEQ, DM), dtype=np.float32)
    for core in range(8):
        seq, half = core // 2, core % 2
        out[seq, half * P:(half + 1) * P] = np.asarray(res.results[core]["yout"])
    return out


# version 11
# speedup vs baseline: 1.2825x; 1.0061x over previous
import contextlib
import math

import numpy as np

import concourse.bass as bass
import concourse.mybir as mybir
from concourse.bass_utils import run_bass_kernel_spmd

F32 = mybir.dt.float32
BF16 = mybir.dt.bfloat16
ACT = mybir.ActivationFunctionType
ALU = mybir.AluOpType
AX = mybir.AxisListType

DM = 1024
SEQ = 4096
NSEQ = 4
P = 2048
HALO = 1280
XROWS = HALO + P
NT = XROWS // 128
OWN0 = HALO // 128

FH = 8
AH = 16
DH = 64
HID = 2730
NHC = 22
NGRP = 11
FCTX = 5
RING = 6
ACTX = 3
MASKVAL = -1.0e30
LN32 = math.log(32.0)
LN8 = math.log(8.0)


def _chunks():
    c0 = dict(tiles=list(range(0, 10)), sbase=5, xnbase=1,
              b_kv=[1, 2, 3, 4], b_full=[5, 6, 7, 8, 9],
              c_tiles=[5, 6, 7, 8, 9],
              d_kv=[6, 7, 8, 9], d_full=[],
              e_tiles=[],
              f_kv=[8, 9], f_full=[])
    def own(t0):
        t = list(range(t0, t0 + 8))
        return dict(tiles=t, sbase=t0, xnbase=t0, b_kv=[], b_full=t,
                    c_tiles=t, d_kv=[], d_full=t, e_tiles=t, f_kv=[], f_full=t)
    return [c0, own(10), own(18)]


CHUNKS = _chunks()


def _rope_tables_for_core(half):
    n = SEQ
    hi = np.arange(0, DH, 2, dtype=np.float64)
    inv_freq = 10000.0 ** (-hi / DH)
    base = (hi + 0.4 * DH) / (1.4 * DH)
    out = np.zeros((3, 128, 1152), dtype=np.float64)
    for ci, ch in enumerate(CHUNKS):
        wave = sorted(set(ch["b_kv"] + ch["b_full"]))
        for sj, r in enumerate(wave):
            g0 = r * 128 - HALO + (0 if half == 0 else P)
            if half == 0 and g0 < 0:
                continue
            t = g0 + np.arange(128, dtype=np.float64)
            fr = t[:, None] * inv_freq[None, :]
            power = (t - n // 2) / 256.0
            scale = base[None, :] ** power[:, None]
            cs, sn = np.cos(fr), np.sin(fr)
            for ti, tab in enumerate((cs * scale, sn * scale,
                                      cs / scale, sn / scale)):
                out[ci, :, ti * 288 + sj * 32: ti * 288 + sj * 32 + 32] = tab
    return np.ascontiguousarray(out.astype(np.float32))


def _masks_for_core(half):
    qi = np.arange(128)[:, None]

    def mk(nblk, win, nvar):
        ctxw = nblk * 128
        j = np.arange(ctxw)[None, :]
        keep = (j >= qi) & (j <= qi + win)
        m = np.zeros((128, nvar * ctxw), dtype=np.float32)
        for v in range(nvar):
            mm = keep.copy()
            if half == 0 and (nvar - 1 - v) > 0:
                mm[:, :(nvar - 1 - v) * 128] = False
            m[:, v * ctxw:(v + 1) * ctxw] = np.where(mm, 0.0, MASKVAL)
        return np.ascontiguousarray(m)

    return mk(FCTX, 512, 5), mk(ACTX, 256, 3)


def _const_inputs(q_scale, k_scale, ln_g, ln_b, b1_, b2_, b_gates, gamma):
    d = {}
    sc = np.zeros((2, 2, 64), np.float32)
    for l in range(2):
        sc[l] = np.stack([np.asarray(q_scale[l], np.float32),
                          np.asarray(k_scale[l], np.float32)])
    d["scales"] = np.ascontiguousarray(
        np.broadcast_to(sc.reshape(1, 256), (128, 256)))
    lnp = np.zeros((128, 2, 2, 8), np.float32)
    for l in range(2):
        lnp[:, l, 0, :] = np.asarray(ln_g[l], np.float32).reshape(8, 128).T
        lnp[:, l, 1, :] = np.asarray(ln_b[l], np.float32).reshape(8, 128).T
    d["lnp"] = np.ascontiguousarray(lnp.reshape(128, 32))
    b2 = np.zeros((128, 2, 8), np.float32)
    for l in range(2):
        b2[:, l, :] = np.asarray(b2_[l], np.float32).reshape(8, 128).T
    d["b2c"] = np.ascontiguousarray(b2.reshape(128, 16))
    b1a = np.zeros((128, 2, NHC), np.float32)
    b1g = np.zeros((128, 2, NHC), np.float32)
    for l in range(2):
        b1 = np.asarray(b1_[l], np.float32)
        for arr, seg in ((b1a, b1[:HID]), (b1g, b1[HID:])):
            pad = np.zeros(NHC * 128, np.float32)
            pad[:HID] = seg
            arr[:, l, :] = pad.reshape(NHC, 128).T
    d["b1a"] = np.ascontiguousarray(b1a.reshape(128, 2 * NHC))
    d["b1g"] = np.ascontiguousarray(b1g.reshape(128, 2 * NHC))
    d["bgv"] = np.ascontiguousarray(
        np.broadcast_to(np.asarray(b_gates, np.float32).reshape(1, 16), (128, 16)))
    d["gmc"] = np.ascontiguousarray(np.asarray(gamma, np.float32).reshape(8, 128).T)
    d["ident"] = np.eye(128, dtype=np.float32)
    return d


class Rec:
    def __init__(self, ctx, sem_name):
        self.ctx = ctx
        self.sem = sem_name
        self.ops = []

    def wait(self, tok):
        if tok is None:
            return
        sem, val = tok
        if val > 0:
            self.ops.append(("w", sem, val))

    def waits(self, toks):
        best = {}
        for t in toks:
            if t is None:
                continue
            s, v = t
            best[s] = max(best.get(s, 0), v)
        for s, v in best.items():
            self.wait((s, v))

    def op(self, method, *a, **kw):
        r = [method, a, kw, []]
        self.ops.append(("o", r))
        return _OpH(self, r)

    def tok(self):
        return (self.sem, self.ctx.count[self.sem])


class _OpH:
    def __init__(self, rec, r):
        self.rec = rec
        self.r = r

    def inc(self, n=1, sem=None):
        sem = sem or self.rec.sem
        self.r[3].append((sem, n))
        self.rec.ctx.count[sem] += n
        return (sem, self.rec.ctx.count[sem])


class TB:

    def __init__(self, ap=None, name=""):
        self.ap = ap
        self.name = name
        self.wr = None
        self.rd = []


def bw(rec, *tbs):
    for t in tbs:
        rec.waits(t.rd)


def ew(tok, *tbs):
    for t in tbs:
        t.wr = tok
        t.rd = []


def br(rec, *tbs):
    for t in tbs:
        rec.wait(t.wr)


def er(tok, *tbs):
    for t in tbs:
        t.rd.append(tok)


DMA_SEMS = ("yi", "yx0", "yx1", "yr", "yo0", "yo1",
            "ga0", "ga1", "ga2", "ga3", "gw0", "gw1", "go0", "go1")


class Ctx:
    def __init__(self, nc):
        self.nc = nc
        self.count = {k: 0 for k in ("t", "v", "s", "y", "g") + DMA_SEMS}
        self.T = Rec(self, "t")
        self.V = Rec(self, "v")
        self.S = Rec(self, "s")
        self.Y = Rec(self, "y")
        self.G = Rec(self, "g")
        self.sem = {}


def shift_view(buf, kc, col, n=128):
    off = col - 1 if kc >= 4 else col
    return buf[:, kc, off:off + n]


def build_program(nphase=99, taps=()):
    nc = bass.Bass("TRN2", target_bir_lowering=False,
                   detect_race_conditions=False)
    c = Ctx(nc)
    T, V, S, Y, G = c.T, c.V, c.S, c.Y, c.G

    xin = nc.dram_tensor("xin", [XROWS, DM], F32, kind="ExternalInput").ap()
    yout = nc.dram_tensor("yout", [P, DM], F32, kind="ExternalOutput").ap()
    rope_in = nc.dram_tensor("rope_in", [3, 128, 1152], F32, kind="ExternalInput").ap()
    maskf_in = nc.dram_tensor("maskf_in", [128, 5 * 640], F32, kind="ExternalInput").ap()
    maska_in = nc.dram_tensor("maska_in", [128, 3 * 384], F32, kind="ExternalInput").ap()
    ident_in = nc.dram_tensor("ident", [128, 128], F32, kind="ExternalInput").ap()
    scales_in = nc.dram_tensor("scales", [128, 256], F32, kind="ExternalInput").ap()
    lnp_in = nc.dram_tensor("lnp", [128, 32], F32, kind="ExternalInput").ap()
    b2c_in = nc.dram_tensor("b2c", [128, 16], F32, kind="ExternalInput").ap()
    b1a_in = nc.dram_tensor("b1a", [128, 2 * NHC], F32, kind="ExternalInput").ap()
    b1g_in = nc.dram_tensor("b1g", [128, 2 * NHC], F32, kind="ExternalInput").ap()
    bgv_in = nc.dram_tensor("bgv", [128, 16], F32, kind="ExternalInput").ap()
    gmc_in = nc.dram_tensor("gmc", [128, 8], F32, kind="ExternalInput").ap()
    w_aqkv = nc.dram_tensor("w_aqkv", [128, 8, 3072], F32, kind="ExternalInput").ap()
    w_agat = nc.dram_tensor("w_agat", [128, 8, 16], F32, kind="ExternalInput").ap()
    w_aout = nc.dram_tensor("w_aout", [128, 8, DM], F32, kind="ExternalInput").ap()
    w_fqkv = nc.dram_tensor("w_fqkv", [2, 128, 8, 1536], F32, kind="ExternalInput").ap()
    w_fo = nc.dram_tensor("w_fo", [2, 128, 4, DM], F32, kind="ExternalInput").ap()
    w_f1 = nc.dram_tensor("w_f1", [2, NGRP, 128, 8, 512], F32, kind="ExternalInput").ap()
    w_f2 = nc.dram_tensor("w_f2", [2, NGRP, 128, 2, DM], F32, kind="ExternalInput").ap()

    A = nc.alloc_sbuf_tensor
    xn_bf = A("xn_bf", [128, 8, 9 * 128 + 1], BF16).ap()
    strm = A("strm", [128, 8, 8 * 128 + 1], F32).ap()
    y1arena = A("y1arena", [128, 4100], F32).ap()
    y1a = y1arena.bitcast(BF16).rearrange("p (a b) -> p a b", b=1025)
    kt1 = A("kt1", [128, 4, RING * 128], BF16).ap()
    v1r = A("v1r", [128, RING, FH * 65], BF16).ap()
    kt2 = A("kt2", [128, 4, RING * 128], BF16).ap()
    v2r = A("v2r", [128, RING, FH * 65], BF16).ap()
    kta = A("kta", [128, 8, ACTX * 128], BF16).ap()
    vat = A("vat", [128, ACTX, AH * 65], BF16).ap()
    rope = A("rope", [128, 4, 9, 32], F32).ap()
    maskf = A("maskf", [128, 5, 640], BF16).ap()
    maska = A("maska", [128, 3, 384], BF16).ap()
    id_f = A("id_f", [128, 128], F32).ap()
    id_b = A("id_b", [128, 128], BF16).ap()
    scl = A("scl", [128, 2, 2, 64], F32).ap()
    lnp = A("lnp_sb", [128, 2, 2, 8], F32).ap()
    b2c = A("b2c_sb", [128, 2, 8], F32).ap()
    b1a = A("b1a_sb", [128, 2, NHC], F32).ap()
    b1g = A("b1g_sb", [128, 2, NHC], F32).ap()
    bgv = A("bgv_sb", [128, 16], F32).ap()
    gmc = A("gmc_sb", [128, 8], F32).ap()
    wag = A("wag", [128, 8, 16], BF16).ap()
    inv32 = A("inv32", [128, NT], F32).ap()
    xn_cr = A("xn_cr", [128, 8], BF16).ap()
    y1b_cr = A("y1b_cr", [128, 8], BF16).ap()
    x1f_cr = A("x1f_cr", [128, 8], F32).ap()
    y1f_cr = A("y1f_cr", [128, 8], F32).ap()
    xt0 = A("xt0", [128, DM], F32).ap()
    xt1 = A("xt1", [128, DM], F32).ap()
    znt = A("znt", [128, DM], F32).ap()
    prb0 = A("prb0", [128, 640], BF16).ap()
    prb1 = A("prb1", [128, 640], BF16).ap()
    ptb0 = prb0.rearrange("p (a n) -> p a n", n=128)
    ptb1 = prb1.rearrange("p (a n) -> p a n", n=128)
    qtb = A("qtb", [128, 8, 128], BF16).ap()
    otk = A("otk", [128, DM], BF16).ap()
    otb = A("otb", [128, 8, 128], BF16).ap()
    krot = A("krot", [128, 512], BF16).ap()
    qrot = A("qrot", [128, 512], BF16).ap()
    sml = A("sml", [128, 10, 16], F32).ap()
    gsc = A("gsc", [128, 3, 16], F32).ap()
    eps5 = A("eps5", [128, 1], F32).ap()
    tiny = A("tiny", [128, 1], F32).ap()
    y1f_pv = A("y1f_pv", [128, 8], F32).ap()
    ninv32 = A("ninv32", [128, NT], F32).ap()
    inv32q = A("inv32q", [128, NT], F32).ap()
    WA = A("WA", [128, 8, 3072], BF16).ap()
    W2 = A("W2", [128, 8, DM], BF16).ap()
    ostg = y1arena[:, 0:2048].rearrange("p (a b) -> p a b", a=2)
    rem = nc.sbuf_bytes_remaining
    assert rem >= 0, f"SBUF over budget by {-rem} bytes"

    PS = nc.alloc_psum_tensor
    PSA = PS("PSA", [128, 512], F32).ap()
    PSTR = PS("PSTR", [128, 1024], BF16).ap()
    PSB = PS("PSB", [128, 1024], F32).ap()
    PSC = PS("PSC", [128, 1024], F32).ap()
    PSD = PS("PSD", [128, 512], F32).ap()
    PSE = PS("PSE", [128, 512], F32).ap()
    assert nc.psum_banks_remaining >= 0
    pa = TB(PSA, "pa")
    ptr = TB(PSTR, "ptr")
    pb0, pb1 = TB(None, "pb0"), TB(None, "pb1")
    pc0, pc1 = TB(None, "pc0"), TB(None, "pc1")
    pd = TB(PSD, "pd")
    pe = TB(PSE, "pe")

    tb = {n: TB(None, n) for n in
          ["xt0", "xt1", "znt", "prb0", "prb1",
           "qtb", "qtb0", "qtb1", "otk", "otb", "krot", "qrot", "ubf",
           "ostg0", "ostg1", "gsc", "lnh", "y1bf", "rope"]}
    WAr = [TB(None, f"WAr{i}") for i in range(4)]
    W2wo = [TB(None, "W2wo0"), TB(None, "W2wo1")]
    W2g = [TB(None, "W2g0"), TB(None, "W2g1")]
    W2flat = W2.rearrange("p a b -> p (a b)")
    ubf_l = [[W2flat[:, 0:2048].rearrange("p (a b) -> p a b", a=2),
              W2flat[:, 2048:4096].rearrange("p (a b) -> p a b", a=2)],
             [W2flat[:, 4096:6144].rearrange("p (a b) -> p a b", a=2),
              W2flat[:, 6144:8192].rearrange("p (a b) -> p a b", a=2)]]
    phase_toks = {"xnbf": None, "strm": None}

    def emit_init():
        Y.op("dma_start", out=id_f, in_=ident_in).inc(16, "yi")
        Y.op("dma_start", out=scl, in_=scales_in.rearrange(
            "p (l t d) -> p l t d", l=2, t=2)).inc(16, "yi")
        Y.op("dma_start", out=lnp, in_=lnp_in.rearrange(
            "p (l i c) -> p l i c", l=2, i=2)).inc(16, "yi")
        Y.op("dma_start", out=b2c, in_=b2c_in.rearrange("p (l c) -> p l c", l=2)).inc(16, "yi")
        Y.op("dma_start", out=b1a, in_=b1a_in.rearrange("p (l c) -> p l c", l=2)).inc(16, "yi")
        Y.op("dma_start", out=b1g, in_=b1g_in.rearrange("p (l c) -> p l c", l=2)).inc(16, "yi")
        Y.op("dma_start", out=bgv, in_=bgv_in).inc(16, "yi")
        Y.op("dma_start", out=gmc, in_=gmc_in).inc(16, "yi")
        wa_f32 = WA.rearrange("p a b -> p (a b)").bitcast(F32)
        mf_stage = wa_f32[:, 0:3200]
        ma_stage = wa_f32[:, 3200:3200 + 1152]
        Y.op("dma_start", out=mf_stage, in_=maskf_in).inc(16, "yi")
        tok_m = Y.op("dma_start", out=ma_stage, in_=maska_in).inc(16, "yi")
        V.wait(tok_m)
        V.op("tensor_copy", out=maskf.rearrange("p a b -> p (a b)"), in_=mf_stage)
        V.op("tensor_copy", out=maska.rearrange("p a b -> p (a b)"), in_=ma_stage)
        V.op("tensor_copy", out=id_b, in_=id_f)
        for ap in (kt1, kt2, kta):
            V.op("memset", ap, 0.0)
        for ap in (v1r, v2r, vat):
            v4 = ap.rearrange("p s (h e) -> p s h e", e=65)
            V.op("memset", v4[:, :, :, 0:64], 0.0)
            V.op("memset", v4[:, :, :, 64:65], 1.0)
        for ap in (xn_cr, y1b_cr, x1f_cr, y1f_cr, y1f_pv, inv32):
            V.op("memset", ap, 0.0)
        V.op("memset", eps5, 1e-5)
        V.op("memset", tiny, 1e-27)
        vtok = V.op("memset", sml, 0.0).inc()
        T.wait(vtok)
        S.wait(vtok)
        G.wait(vtok)
        ytok = ("yi", c.count["yi"])
        T.wait(ytok)
        S.wait(ytok)
        V.wait(ytok)
        for r in WAr:
            r.rd = [vtok]

    def load_wqkv_ffn(layer):
        bw(G, WAr[0], WAr[1])
        tok = G.op("dma_start", out=WA[:, :, 0:1536],
                   in_=w_fqkv[layer]).inc(16, "ga0")
        ew(tok, WAr[0], WAr[1])
        return tok

    def load_wo_ffn(layer):
        reg = W2wo[layer]
        bw(G, reg, W2g[0], W2g[1])
        dst = W2flat[:, layer * 4096:(layer + 1) * 4096].rearrange(
            "p (k n) -> p k n", k=4)
        tok = G.op("dma_start", out=dst,
                   in_=w_fo[layer]).inc(16, ("go0", "go1")[layer])
        ew(tok, reg)
        return tok

    def load_wgrp(layer, g):
        reg = WAr[g % 4]
        bw(G, reg)
        c0_ = 2 * g * 128
        cw = min(256, HID - c0_)
        base = WA[:, :, 768 * (g % 4):768 * (g % 4) + 768]
        tok1 = G.op("dma_start", out=base[:, :, 0:512],
                    in_=w_f1[layer, g]).inc(16, f"ga{g % 4}")
        ew(tok1, reg)
        w2reg = W2g[g % 2]
        bw(G, w2reg, W2wo[1 - layer])
        half0 = 4096 * (1 - layer)
        w2slot = W2flat[:, half0 + (g % 2) * 2048: half0 + (g % 2) * 2048 + 2048]
        w2slot = w2slot.rearrange("p (k n) -> p k n", k=2)
        tok2 = G.op("dma_start", out=w2slot,
                    in_=w_f2[layer, g]).inc(16, f"gw{g % 2}")
        ew(tok2, w2reg)
        return reg, w2reg, base, w2slot, cw

    def load_watt():
        bw(G, *WAr)
        tok = G.op("dma_start", out=WA, in_=w_aqkv).inc(16, "ga0")
        ew(tok, *WAr)
        tokg = G.op("dma_start", out=wag, in_=w_agat).inc(16, "ga1")
        return tok, tokg

    def load_wout():
        bw(G, W2wo[0], W2wo[1], W2g[0], W2g[1])
        dst = W2flat.rearrange("p (k n) -> p k n", k=8)
        tok = G.op("dma_start", out=dst, in_=w_aout).inc(16, "go0")
        ew(tok, W2wo[0], W2wo[1], W2g[0], W2g[1])
        return tok

    def emit_A(ch):
        ast = {}

        def a_load(i, r):
            xt = (xt0, xt1)[i % 2]
            xtb = tb[("xt0", "xt1")[i % 2]]
            bw(Y, xtb)
            ytok = Y.op("dma_start", out=xt,
                        in_=xin[r * 128:(r + 1) * 128, :]).inc(
                            16, ("yx0", "yx1")[i % 2])
            ew(ytok, xtb)
            ast[i] = {"ld": ytok}

        def a_stats1(i, r):
            xt = (xt0, xt1)[i % 2]
            sa, sb = ((0, 1), (2, 3))[i % 2]
            V.wait(ast[i]["ld"])
            V.op("bn_stats", out=sml[:, sa, 0:6], in_=xt[:, 0:512])
            vtok = V.op("bn_stats", out=sml[:, sa, 6:12], in_=xt[:, 512:1024]).inc()
            V.wait(vtok)
            vtok = V.op("bn_aggr", out=sml[:, sb, 0:2], in_=sml[:, sa, 0:12]).inc()
            S.wait(vtok)
            stok = S.op("activation", out=sml[:, sb, 2:3], in_=sml[:, sb, 1:2],
                        func=ACT.Ln, bias=eps5).inc()
            S.wait(stok)
            stok = S.op("activation", out=sml[:, sb, 3:4], in_=sml[:, sb, 2:3],
                        func=ACT.Exp, scale=-0.5).inc()
            V.wait(stok)
            vtok = V.op("tensor_scalar", out=sml[:, sb, 4:5], in0=sml[:, sb, 0:1],
                        scalar1=sml[:, sb, 3:4], scalar2=-1.0,
                        op0=ALU.mult, op1=ALU.mult).inc()
            ast[i]["nmr"] = vtok

        def a_affine(i, r):
            xt = (xt0, xt1)[i % 2]
            xtb = tb[("xt0", "xt1")[i % 2]]
            sb = (1, 3)[i % 2]
            bw(S, tb["znt"])
            S.wait(ast[i]["nmr"])
            stok = S.op("activation", out=znt, in_=xt, func=ACT.Identity,
                        bias=sml[:, sb, 4:5], scale=sml[:, sb, 3:4]).inc()
            ew(stok, tb["znt"])
            er(stok, xtb)
            ast[i]["aff"] = stok

        def a_stats2(i, r):
            sa, sb = ((4, 5), (8, 9))[i % 2]
            V.wait(ast[i]["aff"])
            V.op("bn_stats", out=sml[:, sa, 0:6], in_=znt[:, 0:512])
            vtok = V.op("bn_stats", out=sml[:, sa, 6:12], in_=znt[:, 512:1024]).inc()
            V.wait(vtok)
            vtok = V.op("bn_aggr", out=sml[:, sb, 0:2], in_=sml[:, sa, 0:12]).inc()
            er(vtok, tb["znt"])
            S.wait(vtok)
            stok = S.op("activation", out=sml[:, sb, 2:3], in_=sml[:, sb, 1:2],
                        func=ACT.Ln, bias=tiny).inc()
            S.wait(stok)
            stok = S.op("activation", out=inv32[:, r:r + 1], in_=sml[:, sb, 2:3],
                        func=ACT.Exp, scale=-0.5).inc()
            V.wait(stok)
            V.op("tensor_scalar_mul", out=ninv32[:, r:r + 1],
                 in0=inv32[:, r:r + 1], scalar1=-1.0)
            V.op("tensor_scalar_mul", out=inv32q[:, r:r + 1],
                 in0=inv32[:, r:r + 1], scalar1=0.125).inc()

        def a_trans(i, r):
            bw(T, pb0, pb1)
            T.wait(ast[i]["aff"])
            for kc in range(8):
                h = T.op("transpose", PSB[:, kc * 128:(kc + 1) * 128],
                         znt[:, kc * 128:(kc + 1) * 128], id_f)
            ttok = h.inc()
            ew(ttok, pb0, pb1)
            er(ttok, tb["znt"])
            pb8 = PSB.rearrange("p (a b) -> p a b", a=8)
            if r >= ch["xnbase"]:
                col = 1 + (r - ch["xnbase"]) * 128
                br(S, pb0)
                stok = S.op("copy", out=xn_bf[:, :, col:col + 128], in_=pb8).inc()
                er(stok, pb0, pb1)
            elif r == ch["xnbase"] - 1:
                br(S, pb0)
                stok = S.op("copy", out=xn_bf[:, :, 0:1],
                            in_=pb8[:, :, 127:128]).inc()
                er(stok, pb0, pb1)
            if r >= ch["sbase"]:
                scol = 1 + (r - ch["sbase"]) * 128
                br(V, pb0)
                V.wait(stok)
                vtok = V.op("tensor_copy", out=strm[:, :, scol:scol + 128],
                            in_=pb8).inc()
                er(vtok, pb0, pb1)

        tiles = ch["tiles"]
        a_load(0, tiles[0])
        a_load(1, tiles[1])
        a_stats1(0, tiles[0])
        for i, r in enumerate(tiles):
            a_affine(i, r)
            if i + 1 < len(tiles):
                a_stats1(i + 1, tiles[i + 1])
            if i + 2 < len(tiles):
                a_load(i + 2, tiles[i + 2])
            a_trans(i, r)
            a_stats2(i, r)
        if tiles[0] >= OWN0:
            S.op("copy", out=xn_bf[:, :, 0:1], in_=xn_cr.unsqueeze(2))
        last_col = 1 + (tiles[-1] - ch["xnbase"]) * 128 + 127
        S.wait(S.tok())
        stok = S.op("copy", out=xn_cr.unsqueeze(2),
                    in_=xn_bf[:, :, last_col:last_col + 1]).inc()
        phase_toks["xnbf"] = stok
        phase_toks["strm"] = c.V.tok()

    def emit_wave_ffn(ch, layer, wq_tok, wo_tok):
        kv_tiles = ch["b_kv"] if layer == 0 else ch["d_kv"]
        full_tiles = ch["b_full"] if layer == 0 else ch["d_full"]
        all_tiles = sorted(set(kv_tiles + full_tiles))
        if not all_tiles:
            return
        src_ = xn_bf if layer == 0 else y1a
        src_base = ch["xnbase"] if layer == 0 else ch["sbase"]
        src_tok = phase_toks["xnbf"] if layer == 0 else tb["y1bf"].wr
        ktr, vrr = (kt1, v1r) if layer == 0 else (kt2, v2r)
        kt3 = ktr.rearrange("p a (s n) -> p a s n", n=128)
        v4 = vrr.rearrange("p s (h e) -> p s h e", e=65)
        wq = WA[:, :, 0:1536]
        wo4 = W2flat[:, layer * 4096:(layer + 1) * 4096].rearrange(
            "p (k n) -> p k n", k=4)
        wave_slots = {r: j for j, r in enumerate(
            sorted(set(ch["b_kv"] + ch["b_full"])))}
        bw(Y, tb["rope"])
        rtok = Y.op("dma_start", out=rope.rearrange("p a b e -> p (a b e)"),
                    in_=rope_in[CHUNKS.index(ch)]).inc(16, "yr")
        ew(rtok, tb["rope"])
        T.wait(src_tok)

        ts_ = {r: {} for r in all_tiles}

        def prep_mm(r):
            d = ts_[r]
            full = r in full_tiles
            col = 1 + (r - src_base) * 128
            slot = r % RING
            psc5 = PSC[:, 0:512]
            bw(T, pa)
            T.wait(wq_tok)
            for kc in range(8):
                h = T.op("matmul", PSA, shift_view(src_, kc, col),
                         wq[:, kc, 512:1024], start=(kc == 0), stop=(kc == 7))
            tok = h.inc()
            ew(tok, pa)
            d["qkv_last"] = tok
            bw(S, tb["xt0"])
            br(S, pa)
            stok = S.op("copy", out=xt0[:, 0:512], in_=PSA).inc()
            er(stok, pa)
            ew(stok, tb["xt0"])
            bw(T, pc0)
            for kc in range(8):
                h = T.op("matmul", psc5, shift_view(src_, kc, col),
                         wq[:, kc, 1024:1536], start=(kc == 0), stop=(kc == 7))
            tok = h.inc()
            ew(tok, pc0)
            d["qkv_last"] = tok
            br(S, pc0)
            stok = S.op("copy", out=v4[:, slot, :, 0:64],
                        in_=psc5.rearrange("p (h e) -> p h e", e=64)).inc()
            er(stok, pc0)
            d["vring"] = stok
            if full:
                bw(T, pa)
                for kc in range(8):
                    h = T.op("matmul", PSA, shift_view(src_, kc, col),
                             wq[:, kc, 0:512], start=(kc == 0), stop=(kc == 7))
                tok = h.inc()
                ew(tok, pa)
                d["qkv_last"] = tok
                bw(S, tb["xt1"])
                br(S, pa)
                stok = S.op("copy", out=xt1[:, 0:512], in_=PSA).inc()
                er(stok, pa)
                ew(stok, tb["xt1"])

        def prep_vec(r):
            d = ts_[r]
            full = r in full_tiles
            slot = r % RING
            rj = wave_slots[r]
            ksl = 1 if (r - all_tiles[0]) % 2 == 0 else 6
            qsl = ksl + 1
            kf = xt0[:, 0:512]
            kf3 = kf.rearrange("p (h d) -> p h d", d=64)
            kf4 = kf.rearrange("p (h s d) -> p h s d", s=2, d=32)
            V.wait(tb["xt0"].wr)
            bw(V, tb["znt"])
            V.op("tensor_mul", out=znt[:, 0:512], in0=kf, in1=kf)
            vtok = V.op("reduce_sum", out=sml[:, ksl, 0:8],
                        in_=znt[:, 0:512].rearrange("p (h d) -> p h d", d=64),
                        axis=AX.X).inc()
            S.wait(vtok)
            stok = S.op("activation", out=sml[:, ksl, 8:16], in_=sml[:, ksl, 0:8],
                        func=ACT.Ln, bias=tiny).inc()
            S.wait(stok)
            stok = S.op("activation", out=sml[:, ksl, 8:16], in_=sml[:, ksl, 8:16],
                        func=ACT.Exp, scale=-0.5).inc()
            V.wait(stok)
            inv_b = sml[:, ksl, 8:16].unsqueeze(2).broadcast_to([128, 8, 64])
            V.op("tensor_tensor", out=kf3, in0=kf3, in1=inv_b, op=ALU.mult)
            ks_b = scl[:, layer, 1, :].unsqueeze(1).broadcast_to([128, 8, 64])
            vtok = V.op("tensor_tensor", out=kf3, in0=kf3, in1=ks_b,
                        op=ALU.mult).inc()
            V.wait(vtok)
            V.wait(tb["rope"].wr)
            sh = znt[:, 0:512].rearrange("p (h s d) -> p h s d", s=2, d=32)
            V.op("tensor_copy", out=sh[:, :, 0, :], in_=kf4[:, :, 1, :])
            V.op("tensor_copy", out=sh[:, :, 1, :], in_=kf4[:, :, 0, :])
            cs_b = rope[:, 2, rj, :].unsqueeze(1).unsqueeze(1).broadcast_to(
                [128, 8, 2, 32])
            sn_b = rope[:, 3, rj, :].unsqueeze(1).unsqueeze(1).broadcast_to(
                [128, 8, 2, 32])
            V.op("tensor_tensor", out=kf4, in0=kf4, in1=cs_b, op=ALU.mult)
            V.op("tensor_tensor", out=sh, in0=sh, in1=sn_b, op=ALU.mult)
            bw(V, tb["krot"])
            kr4 = krot.rearrange("p (h s d) -> p h s d", s=2, d=32)
            V.op("tensor_sub", out=kr4[:, :, 0, :], in0=kf4[:, :, 0, :],
                 in1=sh[:, :, 0, :])
            vtok = V.op("tensor_add", out=kr4[:, :, 1, :], in0=kf4[:, :, 1, :],
                        in1=sh[:, :, 1, :]).inc()
            ew(vtok, tb["krot"])
            ew(vtok, tb["znt"])
            er(vtok, tb["xt0"], tb["rope"])
            bw(T, ptr)
            T.wait(vtok)
            for j in range(4):
                h = T.op("transpose", PSTR[:, j * 128:(j + 1) * 128],
                         krot[:, j * 128:(j + 1) * 128], id_b)
            ttok = h.inc()
            ew(ttok, ptr)
            er(ttok, tb["krot"])
            br(S, ptr)
            stok = S.op("copy", out=kt3[:, :, slot, :],
                        in_=PSTR[:, 0:512].rearrange("p (a n) -> p a n",
                                                     n=128)).inc()
            er(stok, ptr)
            d["kring"] = stok
            if not full:
                return
            qpar = (r - all_tiles[0]) % 2
            qtb_tb = tb[("qtb0", "qtb1")[qpar]]
            qtv = qtb[:, 4 * qpar:4 * qpar + 4, :]
            d["qsl"] = qsl
            qf = xt1[:, 0:512]
            qf3 = qf.rearrange("p (h d) -> p h d", d=64)
            qf4 = qf.rearrange("p (h s d) -> p h s d", s=2, d=32)
            V.wait(tb["xt1"].wr)
            bw(V, tb["znt"])
            V.op("tensor_mul", out=znt[:, 0:512], in0=qf, in1=qf)
            vtok = V.op("reduce_sum", out=sml[:, qsl, 0:8],
                        in_=znt[:, 0:512].rearrange("p (h d) -> p h d", d=64),
                        axis=AX.X).inc()
            S.wait(vtok)
            stok = S.op("activation", out=sml[:, qsl, 8:16], in_=sml[:, qsl, 0:8],
                        func=ACT.Ln, scale=1.0 / 64.0, bias=tiny).inc()
            S.wait(stok)
            stok = S.op("activation", out=sml[:, qsl, 8:16],
                        in_=sml[:, qsl, 8:16], func=ACT.Exp, scale=-0.5).inc()
            V.wait(stok)
            invq_b = sml[:, qsl, 8:16].unsqueeze(2).broadcast_to([128, 8, 64])
            V.op("tensor_tensor", out=qf3, in0=qf3, in1=invq_b, op=ALU.mult)
            qs_b = scl[:, layer, 0, :].unsqueeze(1).broadcast_to([128, 8, 64])
            vtok = V.op("tensor_tensor", out=qf3, in0=qf3, in1=qs_b,
                        op=ALU.mult).inc()
            V.wait(vtok)
            sh2 = znt[:, 0:512].rearrange("p (h s d) -> p h s d", s=2, d=32)
            V.op("tensor_copy", out=sh2[:, :, 0, :], in_=qf4[:, :, 1, :])
            V.op("tensor_copy", out=sh2[:, :, 1, :], in_=qf4[:, :, 0, :])
            cs_q = rope[:, 0, rj, :].unsqueeze(1).unsqueeze(1).broadcast_to(
                [128, 8, 2, 32])
            sn_q = rope[:, 1, rj, :].unsqueeze(1).unsqueeze(1).broadcast_to(
                [128, 8, 2, 32])
            V.op("tensor_tensor", out=qf4, in0=qf4, in1=cs_q, op=ALU.mult)
            V.op("tensor_tensor", out=sh2, in0=sh2, in1=sn_q, op=ALU.mult)
            bw(V, tb["qrot"])
            qr4 = qrot.rearrange("p (h s d) -> p h s d", s=2, d=32)
            V.op("tensor_sub", out=qr4[:, :, 0, :], in0=qf4[:, :, 0, :],
                 in1=sh2[:, :, 0, :])
            vtok = V.op("tensor_add", out=qr4[:, :, 1, :], in0=qf4[:, :, 1, :],
                        in1=sh2[:, :, 1, :]).inc()
            ew(vtok, tb["qrot"])
            ew(vtok, tb["znt"])
            er(vtok, tb["xt1"], tb["rope"])
            bw(T, ptr)
            T.wait(vtok)
            for j in range(4):
                h = T.op("transpose", PSTR[:, 512 + j * 128:512 + (j + 1) * 128],
                         qrot[:, j * 128:(j + 1) * 128], id_b)
            ttok = h.inc()
            ew(ttok, ptr)
            er(ttok, tb["qrot"])
            bw(S, qtb_tb)
            br(S, ptr)
            stok = S.op("copy", out=qtv,
                        in_=PSTR[:, 512:1024].rearrange("p (a n) -> p a n",
                                                        n=128)).inc()
            er(stok, ptr)
            ew(stok, qtb_tb)
            d["qT"] = stok

        def heads(r):
            d = ts_[r]
            col = 1 + (r - src_base) * 128
            mvi = min(max(r - OWN0, 0), 4)
            qpar = (r - all_tiles[0]) % 2
            qtv = qtb[:, 4 * qpar:4 * qpar + 4, :]
            qtb_tb = tb[("qtb0", "qtb1")[qpar]]
            qsl = d["qsl"]
            ctx_tiles = list(range(r - 4, r + 1))
            hs = [dict() for _ in range(FH)]
            v_ = min(max(r - OWN0, 0), 4)
            mask_regions = [0, 4] + [j for j in (1, 2, 3) if j < 4 - v_]

            def _scp(pp):
                toks = []
                for par in (0, 1):
                    sc_tbs_ = ((pb0, pb1), (pc0, pc1))[par]
                    bw(T, *sc_tbs_)
                T.wait(d["qT"])
                T.wait(d["kring"])
                for j, cr in enumerate(ctx_tiles):
                    cslot = cr % RING
                    masked = j in mask_regions
                    for par in (0, 1):
                        sc_ap = (PSB, PSC)[par]
                        lo = 64 * par
                        T.op("matmul", sc_ap[:, j * 128:(j + 1) * 128],
                             qtv[lo:lo + 64, pp, :],
                             kt3[lo:lo + 64, pp, cslot, :],
                             start=True, stop=not masked)
                    if masked:
                        for par in (0, 1):
                            sc_ap = (PSB, PSC)[par]
                            h = T.op("matmul", sc_ap[:, j * 128:(j + 1) * 128],
                                     id_b, maskf[:, v_, j * 128:(j + 1) * 128],
                                     start=False, stop=True)
                tok = h.inc()
                hs[2 * pp]["sc"] = tok
                hs[2 * pp + 1]["sc"] = tok
                ew(tok, pb0, pb1, pc0, pc1)

            def _mx(hh):
                par = hh % 2
                sc_ap = (PSB, PSC)[par]
                sc_tbs_ = ((pb0, pb1), (pc0, pc1))[par]
                V.wait(hs[hh]["sc"])
                h = V.op("reduce_max", out=sml[:, 3, hh:hh + 1],
                         in_=sc_ap[:, 0:640], axis=AX.X, negate=True)
                hs[hh]["mx"] = h.inc()
                er(hs[hh]["mx"], *sc_tbs_)

            def _ex(hh):
                par = hh % 2
                sc_ap = (PSB, PSC)[par]
                sc_tbs_ = ((pb0, pb1), (pc0, pc1))[par]
                prb = (prb0, prb1)[par]
                prtb = tb[("prb0", "prb1")[par]]
                bw(S, prtb)
                S.wait(hs[hh]["mx"])
                h = S.op("activation", out=prb, in_=sc_ap[:, 0:640], func=ACT.Exp,
                         bias=sml[:, 3, hh:hh + 1])
                hs[hh]["ex"] = h.inc()
                er(hs[hh]["ex"], *sc_tbs_)
                ew(hs[hh]["ex"], prtb)

            def _pt(hh):
                par = hh % 2
                prb = (prb0, prb1)[par]
                prtb = tb[("prb0", "prb1")[par]]
                bw(T, ptr)
                T.wait(hs[hh]["ex"])
                for j in range(5):
                    h = T.op("transpose", PSTR[:, j * 128:(j + 1) * 128],
                             prb[:, j * 128:(j + 1) * 128], id_b)
                hs[hh]["pt"] = h.inc()
                ew(hs[hh]["pt"], ptr)
                er(hs[hh]["pt"], prtb)

            def _pe(hh):
                par = hh % 2
                ptbuf = (ptb0, ptb1)[par]
                prtb = tb[("prb0", "prb1")[par]]
                br(V, ptr)
                h = V.op("tensor_copy", out=ptbuf,
                         in_=PSTR[:, 0:640].rearrange("p (a n) -> p a n", n=128))
                hs[hh]["pe"] = h.inc()
                er(hs[hh]["pe"], ptr)
                ew(hs[hh]["pe"], prtb)

            def _av(hh):
                par = hh % 2
                ptbuf = (ptb0, ptb1)[par]
                prtb = tb[("prb0", "prb1")[par]]
                avtb = (pd, pe)[par]
                bw(T, avtb)
                T.wait(hs[hh]["pe"])
                T.wait(d["vring"])
                for j, cr in enumerate(ctx_tiles):
                    cslot = cr % RING
                    h = T.op("matmul", avtb.ap[:, 0:65], ptbuf[:, j, :],
                             vrr[:, cslot, hh * 65:hh * 65 + 65],
                             start=(j == 0), stop=(j == 4))
                hs[hh]["av"] = h.inc()
                ew(hs[hh]["av"], avtb)
                er(hs[hh]["av"], prtb)

            def _rc(hh):
                avtb = (pd, pe)[hh % 2]
                br(V, avtb)
                V.wait(hs[hh]["av"])
                h = V.op("reciprocal", out=sml[:, 5, hh:hh + 1],
                         in_=avtb.ap[:, 64:65])
                hs[hh]["rc"] = h.inc()
                er(hs[hh]["rc"], avtb)

            def _oe(hh):
                avtb = (pd, pe)[hh % 2]
                bw(S, tb["otk"])
                br(S, avtb)
                S.wait(hs[hh]["rc"])
                h = S.op("activation", out=otk[:, hh * 64:(hh + 1) * 64],
                         in_=avtb.ap[:, 0:64], func=ACT.Identity,
                         scale=sml[:, 5, hh:hh + 1])
                hs[hh]["oe"] = h.inc()
                er(hs[hh]["oe"], avtb)
                ew(hs[hh]["oe"], tb["otk"])

            _scp(0)
            _mx(0)
            _mx(1)
            for pp in range(FH // 2):
                h0, h1 = 2 * pp, 2 * pp + 1
                _ex(h0)
                _pt(h0)
                _ex(h1)
                if pp + 1 < FH // 2:
                    _scp(pp + 1)
                _pe(h0)
                _av(h0)
                if pp + 1 < FH // 2:
                    _mx(h0 + 2)
                _pt(h1)
                _pe(h1)
                _rc(h0)
                _av(h1)
                _oe(h0)
                if pp + 1 < FH // 2:
                    _mx(h1 + 2)
                _rc(h1)
                _oe(h1)
            er(c.T.tok(), qtb_tb)

        def tail(r):
            scol = 1 + (r - ch["sbase"]) * 128
            bw(T, ptr)
            T.wait(tb["otk"].wr)
            for j in range(4):
                h = T.op("transpose", PSTR[:, j * 128:(j + 1) * 128],
                         otk[:, j * 128:(j + 1) * 128], id_b)
            ttok = h.inc()
            ew(ttok, ptr)
            er(ttok, tb["otk"])
            bw(S, tb["otb"])
            br(S, ptr)
            stok = S.op("copy", out=otb[:, 0:4, :],
                        in_=PSTR[:, 0:512].rearrange("p (a n) -> p a n",
                                                     n=128)).inc()
            er(stok, ptr)
            ew(stok, tb["otb"])
            tok_oT = stok
            for rnd in range(2):
                bw(T, pe)
                T.wait(tok_oT)
                T.wait(wo_tok)
                for dmc in range(4):
                    dm = rnd * 4 + dmc
                    for ic in range(4):
                        h = T.op("matmul", PSE[:, dmc * 128:(dmc + 1) * 128],
                                 wo4[:, ic, dm * 128:(dm + 1) * 128],
                                 otb[:, ic, :], start=(ic == 0), stop=(ic == 3))
                tok_wo = h.inc()
                ew(tok_wo, pe)
                ts_[r]["wo"] = tok_wo
                br(V, pe)
                V.wait(tok_wo)
                for dmc in range(4):
                    dm = rnd * 4 + dmc
                    h = V.op("tensor_tensor", out=strm[:, dm, scol:scol + 128],
                             in0=PSE[:, dmc * 128:(dmc + 1) * 128],
                             in1=strm[:, dm, scol:scol + 128], op=ALU.add)
                vtok = h.inc()
                er(vtok, pe)
                er(vtok, tb["otb"])

        prep_mm(all_tiles[0])
        prep_vec(all_tiles[0])
        last_qkv = None
        last_wo = None
        for i, r in enumerate(all_tiles):
            if i + 1 < len(all_tiles):
                prep_mm(all_tiles[i + 1])
            if r in full_tiles:
                heads(r)
            if i + 1 < len(all_tiles):
                prep_vec(all_tiles[i + 1])
            if r in full_tiles:
                tail(r)
        for r in all_tiles:
            if "qkv_last" in ts_[r]:
                last_qkv = ts_[r]["qkv_last"]
            if "wo" in ts_[r]:
                last_wo = ts_[r]["wo"]
        er(last_qkv, WAr[0], WAr[1])
        if last_wo is not None:
            er(last_wo, W2wo[layer])
        phase_toks["strm"] = c.V.tok()


    def emit_ffn(ch, layer):
        tiles = ch["c_tiles"] if layer == 0 else ch["e_tiles"]
        if not tiles:
            return
        ncols = len(tiles) * 128
        cr_ = x1f_cr if layer == 0 else y1f_pv
        if layer == 0:
            h0 = V.op("tensor_copy", out=y1f_pv.unsqueeze(2),
                      in_=y1f_cr.unsqueeze(2))
            V.wait(h0.inc())
        if layer == 1:
            V.wait(tb["y1bf"].wr)
        h = V.op("tensor_copy", out=strm[:, :, 0:1],
                 in_=cr_.unsqueeze(2))
        last = 1 + (tiles[-1] - ch["sbase"]) * 128 + 127
        if layer == 0:
            h = V.op("tensor_copy", out=cr_.unsqueeze(2),
                     in_=strm[:, :, last:last + 1])
        crumb_tok = h.inc()
        cst = {r: {} for r in tiles}

        def c1_tr1(r):
            col = 1 + (r - ch["sbase"]) * 128
            bw(T, pb0, pb1)
            T.wait(crumb_tok)
            T.wait(phase_toks["strm"])
            for kc in range(8):
                h = T.op("transpose", PSB[:, kc * 128:(kc + 1) * 128],
                         shift_view(strm, kc, col), id_f)
            ttok = h.inc()
            ew(ttok, pb0, pb1)

        def c1_stats(r, i):
            xt = (xt0, xt1)[i % 2]
            xtb = tb[("xt0", "xt1")[i % 2]]
            sa = 6 + 2 * (i % 2)
            sb = sa + 1
            bw(S, xtb)
            br(S, pb0)
            stok = S.op("copy", out=xt, in_=PSB).inc()
            er(stok, pb0, pb1)
            ew(stok, xtb)
            V.wait(stok)
            V.op("bn_stats", out=sml[:, sa, 0:6], in_=xt[:, 0:512])
            vtok = V.op("bn_stats", out=sml[:, sa, 6:12], in_=xt[:, 512:1024]).inc()
            V.wait(vtok)
            vtok = V.op("bn_aggr", out=sml[:, sb, 0:2], in_=sml[:, sa, 0:12]).inc()
            S.wait(vtok)
            stok = S.op("activation", out=sml[:, sb, 2:3], in_=sml[:, sb, 1:2],
                        func=ACT.Ln, bias=eps5).inc()
            S.wait(stok)
            stok = S.op("activation", out=sml[:, sb, 3:4], in_=sml[:, sb, 2:3],
                        func=ACT.Exp, scale=-0.5).inc()
            V.wait(stok)
            vtok = V.op("tensor_scalar", out=sml[:, sb, 4:5], in0=sml[:, sb, 0:1],
                        scalar1=sml[:, sb, 3:4], scalar2=-1.0,
                        op0=ALU.mult, op1=ALU.mult).inc()
            bw(S, tb["znt"])
            S.wait(vtok)
            stok = S.op("activation", out=znt, in_=xt, func=ACT.Identity,
                        bias=sml[:, sb, 4:5], scale=sml[:, sb, 3:4]).inc()
            ew(stok, tb["znt"])
            er(stok, xtb)
            cst[r]["aff"] = stok

        def c1_tr2(r, i):
            lcol = (r - tiles[0]) * 128
            bw(T, pc0, pc1)
            T.wait(cst[r]["aff"])
            for kc in range(8):
                h = T.op("transpose", PSC[:, kc * 128:(kc + 1) * 128],
                         znt[:, kc * 128:(kc + 1) * 128], id_f)
            ttok = h.inc()
            ew(ttok, pc0, pc1)
            er(ttok, tb["znt"])
            br(S, pc0)
            for kc in range(8):
                h = S.op("activation", out=y1a[:, kc, 1 + lcol:1 + lcol + 128],
                         in_=PSC[:, kc * 128:(kc + 1) * 128], func=ACT.Identity,
                         scale=lnp[:, layer, 0, kc:kc + 1],
                         bias=lnp[:, layer, 1, kc:kc + 1])
            stok = h.inc()
            er(stok, pc0, pc1)
            cst[r]["lnh"] = stok

        c1_tr1(tiles[0])
        for i, r in enumerate(tiles):
            c1_stats(r, i)
            if i + 1 < len(tiles):
                c1_tr1(tiles[i + 1])
            c1_tr2(r, i)
        stok = cst[tiles[-1]]["lnh"]
        ew(stok, tb["lnh"])
        wins = []
        off, left = 1, ncols
        while left > 0:
            w = min(512, left)
            wins.append((off, w))
            off += w
            left -= w
        vtok = None
        ps_y_par = 0
        ub_tb = [TB(None, "ub0"), TB(None, "ub1")]
        ub_tb[0].rd = list(W2wo[layer].rd)
        ub_tb[1].rd = list(W2wo[layer].rd)
        grp = {}

        def c2_ag(g):
            reg, w2reg, base, w2slot, cw = load_wgrp(layer, g)
            nch = 2 if cw > 128 else 1
            ubf = ubf_l[layer][g % 2]
            utb = ub_tb[g % 2]
            uv = None
            for hc in range(nch):
                ccw = min(128, cw - hc * 128)
                for (wo_, W) in wins:
                    bw(T, pd, pe)
                    T.wait(reg.wr)
                    T.wait(tb["lnh"].wr)
                    for kc in range(8):
                        T.op("matmul", PSD[0:ccw, 0:W],
                             base[:, kc, hc * 128:hc * 128 + ccw],
                             y1a[:, kc, wo_:wo_ + W],
                             start=(kc == 0), stop=(kc == 7))
                    for kc in range(8):
                        h = T.op("matmul", PSE[0:ccw, 0:W],
                                 base[:, kc, 256 + hc * 128:256 + hc * 128 + ccw],
                                 y1a[:, kc, wo_:wo_ + W],
                                 start=(kc == 0), stop=(kc == 7))
                    tok_ag = h.inc()
                    ew(tok_ag, pd, pe)
                    er(tok_ag, reg)
                    bw(S, tb["znt"])
                    br(S, pe)
                    S.wait(tok_ag)
                    stok = S.op("activation", out=znt[0:ccw, 0:W],
                                in_=PSE[0:ccw, 0:W], func=ACT.Gelu,
                                bias=b1g[0:ccw, layer,
                                         2 * g + hc:2 * g + hc + 1]).inc()
                    er(stok, pe)
                    ew(stok, tb["znt"])
                    bw(V, utb)
                    br(V, pd)
                    V.wait(stok)
                    uv = V.op("scalar_tensor_tensor",
                              out=ubf[0:ccw, hc, wo_ - 1:wo_ - 1 + W],
                              in0=PSD[0:ccw, 0:W],
                              scalar=b1a[0:ccw, layer,
                                         2 * g + hc:2 * g + hc + 1],
                              in1=znt[0:ccw, 0:W], op0=ALU.add,
                              op1=ALU.mult).inc()
                    er(uv, pd)
                    er(uv, tb["znt"])
            ew(uv, utb)
            grp[g] = (w2reg, w2slot, cw, nch)

        def c2_y(g):
            nonlocal vtok, ps_y_par
            w2reg, w2slot, cw, nch = grp[g]
            ubf = ubf_l[layer][g % 2]
            utb = ub_tb[g % 2]
            for dmc in range(8):
                for (wo_, W) in wins:
                    pcb = (pc0, pc1)[ps_y_par]
                    pslice = PSC[:, ps_y_par * 512:ps_y_par * 512 + W]
                    bw(T, pcb)
                    T.wait(w2reg.wr)
                    T.wait(utb.wr)
                    for hc in range(nch):
                        chw = min(128, cw - hc * 128)
                        h = T.op("matmul", pslice,
                                 w2slot[0:chw, hc, dmc * 128:(dmc + 1) * 128],
                                 ubf[0:chw, hc, wo_ - 1:wo_ - 1 + W],
                                 start=(hc == 0), stop=(hc == nch - 1))
                    tok_y = h.inc()
                    ew(tok_y, pcb)
                    er(tok_y, w2reg)
                    er(tok_y, utb)
                    br(V, pcb)
                    V.wait(tok_y)
                    if g == 0:
                        h = V.op("scalar_tensor_tensor",
                                 out=strm[:, dmc, wo_:wo_ + W], in0=pslice,
                                 scalar=b2c[:, layer, dmc:dmc + 1],
                                 in1=strm[:, dmc, wo_:wo_ + W],
                                 op0=ALU.add, op1=ALU.add)
                    else:
                        h = V.op("tensor_tensor", out=strm[:, dmc, wo_:wo_ + W],
                                 in0=pslice, in1=strm[:, dmc, wo_:wo_ + W],
                                 op=ALU.add)
                    vtok = h.inc()
                    er(vtok, pcb)
                    ps_y_par ^= 1

        c2_ag(0)
        for g in range(NGRP):
            if g + 1 < NGRP:
                c2_ag(g + 1)
            c2_y(g)
        for utb in ub_tb:
            for t_ in utb.rd:
                er(t_, W2wo[layer])
        er(vtok, W2wo[layer])
        phase_toks["strm"] = vtok
        if layer == 0:
            S.wait(vtok)
            S.op("copy", out=y1a[:, :, 0:1], in_=y1b_cr.unsqueeze(2))
            for (wo_, W) in wins:
                h = S.op("copy", out=y1a[:, :, wo_:wo_ + W],
                         in_=strm[:, :, wo_:wo_ + W])
            S.wait(h.inc())
            S.op("copy", out=y1b_cr.unsqueeze(2),
                 in_=y1a[:, :, last:last + 1])
            stok = S.op("copy", out=y1f_cr.unsqueeze(2),
                        in_=strm[:, :, last:last + 1]).inc()
            ew(stok, tb["y1bf"])

    def emit_att(ch, wq_tok, wg_tok, wout_tok):
        kv_tiles, full_tiles = ch["f_kv"], ch["f_full"]
        all_tiles = sorted(set(kv_tiles + full_tiles))
        if not all_tiles:
            return
        kt3 = kta.rearrange("p a (s n) -> p a s n", n=128)
        va4 = vat.rearrange("p s (h e) -> p s h e", e=65)
        wo8 = W2flat.rearrange("p (k n) -> p k n", k=8)
        seedT, seedS = c.T.tok(), c.S.tok()
        for n_ in ("ostg0", "ostg1"):
            tb[n_].rd = tb[n_].rd + [seedT, seedS]
        V.wait(wq_tok)
        V.wait(wg_tok)
        gb = gmc.unsqueeze(2).broadcast_to([128, 8, 3072])
        V.op("tensor_tensor", out=WA, in0=WA, in1=gb, op=ALU.mult)
        gb2 = gmc.unsqueeze(2).broadcast_to([128, 8, 16])
        gtok = V.op("tensor_tensor", out=wag, in0=wag, in1=gb2, op=ALU.mult).inc()
        T.wait(gtok)
        T.wait(phase_toks["xnbf"])
        last_qkv_tok = None
        last_wout_tok = None
        pend = None
        for r in all_tiles:
            full = r in full_tiles
            slot = r % ACTX
            mvi = min(max(r - OWN0, 0), 2)
            col = 1 + (r - ch["xnbase"]) * 128
            for hf in range(2):
                pax = (pa, pc0)[hf]
                psx = (PSA, PSC[:, 0:512])[hf]
                bw(T, pax)
                for kc in range(8):
                    h = T.op("matmul", psx, xn_bf[:, kc, col:col + 128],
                             WA[:, kc, 1024 + hf * 512:1024 + (hf + 1) * 512],
                             start=(kc == 0), stop=(kc == 7))
                tok = h.inc()
                ew(tok, pax)
                last_qkv_tok = tok
                bw(S, tb["krot"])
                br(S, pax)
                stok = S.op("activation", out=krot, in_=psx, func=ACT.Identity,
                            scale=inv32[:, r:r + 1]).inc()
                er(stok, pax)
                ew(stok, tb["krot"])
                bw(T, ptr)
                T.wait(stok)
                for j in range(4):
                    h = T.op("transpose",
                             PSTR[:, hf * 512 + j * 128:hf * 512 + (j + 1) * 128],
                             krot[:, j * 128:(j + 1) * 128], id_b)
                ttok = h.inc()
                ew(ttok, ptr)
                er(ttok, tb["krot"])
                br(S, ptr)
                stok = S.op("copy", out=kt3[:, 4 * hf:4 * hf + 4, slot, :],
                            in_=PSTR[:, hf * 512:(hf + 1) * 512].rearrange(
                                "p (a n) -> p a n", n=128)).inc()
                er(stok, ptr)
            tok_kring = stok
            if full:
                bw(T, pd)
                for kc in range(8):
                    h = T.op("matmul", PSD[:, 0:16], xn_bf[:, kc, col:col + 128],
                             wag[:, kc, :], start=(kc == 0), stop=(kc == 7))
                tok_g = h.inc()
                ew(tok_g, pd)
                last_qkv_tok = tok_g
                bw(S, tb["gsc"])
                br(S, pd)
                stok = S.op("activation", out=gsc[:, 0, :], in_=PSD[:, 0:16],
                            func=ACT.Identity, scale=ninv32[:, r:r + 1]).inc()
                er(stok, pd)
                V.wait(stok)
                vtok = V.op("tensor_sub", out=gsc[:, 0, :], in0=gsc[:, 0, :],
                            in1=bgv).inc()
                S.wait(vtok)
                stok = S.op("activation", out=gsc[:, 1, :], in_=gsc[:, 0, :],
                            func=ACT.Exp).inc()
                V.wait(stok)
                vtok = V.op("tensor_scalar_add", out=gsc[:, 1, :],
                            in0=gsc[:, 1, :], scalar1=1.0).inc()
                V.wait(vtok)
                vtok = V.op("reciprocal", out=gsc[:, 1, :],
                            in_=gsc[:, 1, :]).inc()
                ew(vtok, tb["gsc"])
            for hf in range(2):
                pax = (pa, pc0)[hf]
                psx = (PSA, PSC[:, 0:512])[hf]
                bw(T, pax)
                for kc in range(8):
                    h = T.op("matmul", psx, xn_bf[:, kc, col:col + 128],
                             WA[:, kc, 2048 + hf * 512:2048 + (hf + 1) * 512],
                             start=(kc == 0), stop=(kc == 7))
                tok = h.inc()
                ew(tok, pax)
                last_qkv_tok = tok
                br(S, pax)
                stok = S.op("activation",
                            out=va4[:, slot, 8 * hf:8 * hf + 8, 0:64],
                            in_=psx.rearrange("p (h e) -> p h e", e=64),
                            func=ACT.Identity, scale=inv32[:, r:r + 1]).inc()
                er(stok, pax)
            tok_vring = stok
            if pend is not None:
                pend()
                pend = None
            if not full:
                continue
            for hf in range(2):
                pax = (pa, pc0)[hf]
                psx = (PSA, PSC[:, 0:512])[hf]
                bw(T, pax)
                for kc in range(8):
                    h = T.op("matmul", psx, xn_bf[:, kc, col:col + 128],
                             WA[:, kc, hf * 512:(hf + 1) * 512],
                             start=(kc == 0), stop=(kc == 7))
                tok = h.inc()
                ew(tok, pax)
                last_qkv_tok = tok
                bw(S, tb["otk"])
                br(S, pax)
                stok = S.op("activation", out=otk[:, hf * 512:(hf + 1) * 512],
                            in_=psx, func=ACT.Identity,
                            scale=inv32q[:, r:r + 1]).inc()
                er(stok, pax)
                ew(stok, tb["otk"])
            bw(T, ptr)
            T.wait(tb["otk"].wr)
            for j in range(8):
                h = T.op("transpose", PSTR[:, j * 128:(j + 1) * 128],
                         otk[:, j * 128:(j + 1) * 128], id_b)
            ttok = h.inc()
            ew(ttok, ptr)
            er(ttok, tb["otk"])
            bw(S, tb["qtb"])
            br(S, ptr)
            stok = S.op("copy", out=qtb,
                        in_=PSTR.rearrange("p (a n) -> p a n", n=128)).inc()
            er(stok, ptr)
            ew(stok, tb["qtb"])
            tok_qT = stok
            ctx_tiles = list(range(r - 2, r + 1))
            hs = [dict() for _ in range(AH)]
            v_ = min(max(r - OWN0, 0), 2)
            mask_regions = [0, 2] + ([1] if v_ == 0 else [])

            def _scp(pp):
                for par in (0, 1):
                    bw(T, (pb0, pb1)[par])
                T.wait(tok_qT)
                T.wait(tok_kring)
                for j, cr_t in enumerate(ctx_tiles):
                    cslot = cr_t % ACTX
                    masked = j in mask_regions
                    for par in (0, 1):
                        lo = 64 * par
                        T.op("matmul",
                             PSB[:, par * 512 + j * 128:par * 512 + (j + 1) * 128],
                             qtb[lo:lo + 64, pp, :], kt3[lo:lo + 64, pp, cslot, :],
                             start=True, stop=not masked)
                    if masked:
                        for par in (0, 1):
                            h = T.op("matmul",
                                     PSB[:, par * 512 + j * 128:par * 512 + (j + 1) * 128],
                                     id_b, maska[:, v_, j * 128:(j + 1) * 128],
                                     start=False, stop=True)
                tok = h.inc()
                hs[2 * pp]["sc"] = tok
                hs[2 * pp + 1]["sc"] = tok
                ew(tok, pb0, pb1)

            def _mx(hh):
                par = hh % 2
                sc_tb = (pb0, pb1)[par]
                V.wait(hs[hh]["sc"])
                h = V.op("reduce_max", out=sml[:, 7, hh:hh + 1],
                         in_=PSB[:, par * 512:par * 512 + 384], axis=AX.X,
                         negate=True)
                hs[hh]["mx"] = h.inc()
                er(hs[hh]["mx"], sc_tb)

            def _ex(hh):
                par = hh % 2
                sc_tb = (pb0, pb1)[par]
                prb = (prb0, prb1)[par]
                prtb = tb[("prb0", "prb1")[par]]
                bw(S, prtb)
                S.wait(hs[hh]["mx"])
                h = S.op("activation", out=prb[:, 0:384],
                         in_=PSB[:, par * 512:par * 512 + 384], func=ACT.Exp,
                         bias=sml[:, 7, hh:hh + 1])
                hs[hh]["ex"] = h.inc()
                er(hs[hh]["ex"], sc_tb)
                ew(hs[hh]["ex"], prtb)

            def _pt(hh):
                par = hh % 2
                prb = (prb0, prb1)[par]
                prtb = tb[("prb0", "prb1")[par]]
                bw(T, ptr)
                T.wait(hs[hh]["ex"])
                for j in range(3):
                    h = T.op("transpose", PSTR[:, j * 128:(j + 1) * 128],
                             prb[:, j * 128:(j + 1) * 128], id_b)
                hs[hh]["pt"] = h.inc()
                ew(hs[hh]["pt"], ptr)
                er(hs[hh]["pt"], prtb)

            def _pe(hh):
                par = hh % 2
                ptbuf = (ptb0, ptb1)[par]
                prtb = tb[("prb0", "prb1")[par]]
                br(V, ptr)
                h = V.op("tensor_copy", out=ptbuf[:, 0:3, :],
                         in_=PSTR[:, 0:384].rearrange("p (a n) -> p a n", n=128))
                hs[hh]["pe"] = h.inc()
                er(hs[hh]["pe"], ptr)
                ew(hs[hh]["pe"], prtb)

            def _av(hh):
                par = hh % 2
                ptbuf = (ptb0, ptb1)[par]
                prtb = tb[("prb0", "prb1")[par]]
                avtb = (pd, pe)[par]
                bw(T, avtb)
                T.wait(hs[hh]["pe"])
                T.wait(tok_vring)
                for j, cr_t in enumerate(ctx_tiles):
                    cslot = cr_t % ACTX
                    h = T.op("matmul", avtb.ap[:, 0:65], ptbuf[:, j, :],
                             vat[:, cslot, hh * 65:hh * 65 + 65],
                             start=(j == 0), stop=(j == 2))
                hs[hh]["av"] = h.inc()
                ew(hs[hh]["av"], avtb)
                er(hs[hh]["av"], prtb)

            def _rc1(hh):
                avtb = (pd, pe)[hh % 2]
                br(V, avtb)
                V.wait(hs[hh]["av"])
                V.wait(tb["gsc"].wr)
                h = V.op("reciprocal", out=sml[:, 9, hh:hh + 1],
                         in_=avtb.ap[:, 64:65])
                hs[hh]["rc1"] = h.inc()
                er(hs[hh]["rc1"], avtb)

            def _rc2(hh):
                V.wait(hs[hh]["rc1"])
                h = V.op("tensor_mul", out=sml[:, 9, hh:hh + 1],
                         in0=sml[:, 9, hh:hh + 1], in1=gsc[:, 1, hh:hh + 1])
                hs[hh]["rc2"] = h.inc()

            def _oe(hh):
                avtb = (pd, pe)[hh % 2]
                bw(S, tb["otk"])
                br(S, avtb)
                S.wait(hs[hh]["rc2"])
                h = S.op("activation", out=otk[:, hh * 64:(hh + 1) * 64],
                         in_=avtb.ap[:, 0:64], func=ACT.Identity,
                         scale=sml[:, 9, hh:hh + 1])
                hs[hh]["oe"] = h.inc()
                er(hs[hh]["oe"], avtb)
                ew(hs[hh]["oe"], tb["otk"])

            _scp(0)
            _mx(0)
            _mx(1)
            for pp in range(AH // 2):
                h0, h1 = 2 * pp, 2 * pp + 1
                _ex(h0)
                _pt(h0)
                _ex(h1)
                if pp + 1 < AH // 2:
                    _scp(pp + 1)
                _pe(h0)
                _av(h0)
                if pp + 1 < AH // 2:
                    _mx(h0 + 2)
                _pt(h1)
                _pe(h1)
                _rc1(h0)
                _av(h1)
                _rc1(h1)
                if pp + 1 < AH // 2:
                    _mx(h1 + 2)
                _rc2(h0)
                _oe(h0)
                _rc2(h1)
                _oe(h1)
            bw(T, ptr)
            T.wait(tb["otk"].wr)
            for j in range(8):
                h = T.op("transpose", PSTR[:, j * 128:(j + 1) * 128],
                         otk[:, j * 128:(j + 1) * 128], id_b)
            ttok = h.inc()
            ew(ttok, ptr)
            er(ttok, tb["otk"])
            bw(S, tb["otb"])
            br(S, ptr)
            stok = S.op("copy", out=otb,
                        in_=PSTR.rearrange("p (a n) -> p a n", n=128)).inc()
            er(stok, ptr)
            ew(stok, tb["otb"])
            tok_oT = stok
            bw(T, pc0, pc1)
            T.wait(tok_oT)
            T.wait(wout_tok)
            for dmc in range(8):
                for ic in range(8):
                    h = T.op("matmul", PSC[:, dmc * 128:(dmc + 1) * 128],
                             wo8[:, ic, dmc * 128:(dmc + 1) * 128],
                             otb[:, ic, :], start=(ic == 0), stop=(ic == 7))
            tok_wout = h.inc()
            ew(tok_wout, pc0, pc1)
            er(tok_wout, tb["otb"])
            last_wout_tok = tok_wout
            scol = 1 + (r - ch["sbase"]) * 128
            ib = r % 2
            xtb = tb[("xt0", "xt1")[ib]]
            bw(Y, xtb)
            ytok_x = Y.op("dma_start", out=(xt0, xt1)[ib],
                          in_=xin[r * 128:(r + 1) * 128, :]).inc(
                              16, ("yx0", "yx1")[ib])
            ew(ytok_x, xtb)
            bw(V, tb["znt"])
            br(V, pc0)
            V.wait(tok_wout)
            for dmc in range(8):
                h = V.op("tensor_tensor", out=znt[:, dmc * 128:(dmc + 1) * 128],
                         in0=PSC[:, dmc * 128:(dmc + 1) * 128],
                         in1=strm[:, dmc, scol:scol + 128], op=ALU.add)
            vtok = h.inc()
            er(vtok, pc0, pc1)
            ew(vtok, tb["znt"])

            def _epi(r=r, ib=ib, vtok=vtok, ytok_x=ytok_x):
                bw(T, pb0, pb1)
                T.wait(vtok)
                for kc in range(8):
                    h = T.op("transpose", PSB[:, kc * 128:(kc + 1) * 128],
                             znt[:, kc * 128:(kc + 1) * 128], id_f)
                ttok = h.inc()
                ew(ttok, pb0, pb1)
                er(ttok, tb["znt"])
                stg = ostg[:, ib, :]
                stb = tb[("ostg0", "ostg1")[ib]]
                xtb = tb[("xt0", "xt1")[ib]]
                bw(V, stb)
                br(V, pb0)
                V.wait(ytok_x)
                vt = V.op("tensor_add", out=stg, in0=PSB,
                          in1=(xt0, xt1)[ib]).inc()
                er(vt, pb0, pb1)
                er(vt, xtb)
                ew(vt, stb)
                Y.wait(vt)
                orow = (r - OWN0) * 128
                yt = Y.op("dma_start", out=yout[orow:orow + 128, :],
                          in_=stg).inc(16, ("yo0", "yo1")[ib])
                er(yt, stb)
            pend = _epi
        if pend is not None:
            pend()
        er(last_qkv_tok, *WAr)
        if last_wout_tok is not None:
            er(last_wout_tok, W2wo[0], W2wo[1], W2g[0], W2g[1])

    def do_taps(label):
        here = dict(strm=strm, xn_bf=xn_bf, y1a=y1a, inv32=inv32, kt1=kt1,
                    v1r=v1r, kt2=kt2, v2r=v2r, kta=kta, vat=vat, otk=otk,
                    qtb=qtb, ubf0=ubf_l[0], ubf1=ubf_l[1], krot=krot, gsc=gsc, ninv32=ninv32,
                    qrot=qrot, sml=sml, rope=rope, WA=WA, W2=W2)
        for lab, bname in taps:
            if lab != label:
                continue
            ap = here[bname]
            tname = f"tap_{lab}_{bname}"
            shp = list(ap.shape)
            tout = nc.dram_tensor(tname, shp, F32, kind="ExternalOutput").ap()
            for e_ in (T, V, S):
                pass
            G.wait(("t", c.count["t"]))
            G.wait(("v", c.count["v"]))
            G.wait(("s", c.count["s"]))
            ttok = G.op("dma_start", out=tout, in_=ap).inc(16, "yi")
            for e_ in (T, V, S, Y):
                e_.wait(ttok)

    emit_init()
    for ch in CHUNKS:
        if nphase >= 2:
            wq0 = load_wqkv_ffn(0)
            wo0 = load_wo_ffn(0)
        ci = CHUNKS.index(ch)
        if nphase >= 1:
            emit_A(ch)
            do_taps(f"A{ci}")
        if nphase >= 2:
            emit_wave_ffn(ch, 0, wq0, wo0)
            do_taps(f"B{ci}")
        if nphase >= 3:
            emit_ffn(ch, 0)
            do_taps(f"C{ci}")
        if nphase >= 4:
            wq1 = load_wqkv_ffn(1)
            wo1 = load_wo_ffn(1)
            emit_wave_ffn(ch, 1, wq1, wo1)
            do_taps(f"D{ci}")
        if nphase >= 5:
            emit_ffn(ch, 1)
            do_taps(f"E{ci}")
        if nphase >= 6:
            watok, wgtok = load_watt()
            wouttok = load_wout()
            emit_att(ch, watok, wgtok, wouttok)
            do_taps(f"F{ci}")
    for s_ in ("yo0", "yo1"):
        Y.wait((s_, c.count[s_]))
        V.wait((s_, c.count[s_]))

    with contextlib.ExitStack() as st:
        for name in ("t", "v", "s") + DMA_SEMS:
            c.sem[name] = st.enter_context(nc.semaphore(f"sem_{name}"))
        block = st.enter_context(nc.Block())

        def mk_replay(rec):
            def f(eng):
                for item in rec.ops:
                    if item[0] == "w":
                        eng.wait_ge(c.sem[item[1]], item[2])
                    else:
                        m, a, kw, incs = item[1]
                        inst = getattr(eng, m)(*a, **kw)
                        for s, n in incs:
                            inst = inst.then_inc(c.sem[s], n)
            return f

        block.tensor(mk_replay(c.T))
        block.vector(mk_replay(c.V))
        block.scalar(mk_replay(c.S))
        block.sync(mk_replay(c.Y))
        block.gpsimd(mk_replay(c.G))
    return nc


def _lhs_chunks(w):
    w = np.ascontiguousarray(np.asarray(w, np.float32))
    K, N = w.shape
    return np.ascontiguousarray(w.reshape(K // 128, 128, N).transpose(1, 0, 2))


def _prep_weights(waqkv, wagat, waout, wfqkv, wfo, wf1, wf2):
    d = {}
    d["w_aqkv"] = _lhs_chunks(waqkv)
    d["w_agat"] = _lhs_chunks(wagat)
    d["w_aout"] = _lhs_chunks(waout)
    d["w_fqkv"] = np.stack([_lhs_chunks(np.asarray(wfqkv, np.float32)[l])
                            for l in range(2)])
    d["w_fo"] = np.stack([_lhs_chunks(np.asarray(wfo, np.float32)[l])
                          for l in range(2)])
    w1 = np.asarray(wf1, np.float32)
    w2 = np.asarray(wf2, np.float32)
    w1g = np.zeros((2, NGRP, 128, 8, 512), np.float32)
    w2g = np.zeros((2, NGRP, 128, 2, DM), np.float32)
    for l in range(2):
        w1c = _lhs_chunks(w1[l])
        for g in range(NGRP):
            c0 = 256 * g
            cw = min(256, HID - c0)
            w1g[l, g, :, :, 0:cw] = w1c[:, :, c0:c0 + cw]
            w1g[l, g, :, :, 256:256 + cw] = w1c[:, :, HID + c0:HID + c0 + cw]
            r0 = 2 * g * 128
            rw = min(256, HID - r0)
            blk = w2[l, r0:r0 + rw]
            pad = np.zeros((256, DM), np.float32)
            pad[:rw] = blk
            w2g[l, g] = pad.reshape(2, 128, DM).transpose(1, 0, 2)
    d["w_f1"] = np.ascontiguousarray(w1g)
    d["w_f2"] = np.ascontiguousarray(w2g)
    return d


def make_in_maps(x, consts, weights):
    rope_h = [_rope_tables_for_core(h) for h in (0, 1)]
    masks_h = [_masks_for_core(h) for h in (0, 1)]
    in_maps = []
    for core in range(8):
        seq, half = core // 2, core % 2
        if half == 0:
            xi = np.concatenate([np.zeros((HALO, DM), np.float32),
                                 x[seq, 0:P]], axis=0)
        else:
            xi = np.ascontiguousarray(x[seq, P - HALO:SEQ])
        mf, ma = masks_h[half]
        im = dict(xin=xi, rope_in=rope_h[half], maskf_in=mf, maska_in=ma,
                  **consts, **weights)
        in_maps.append(im)
    return in_maps


_NC_CACHE = {}


def _get_program():
    if "nc" not in _NC_CACHE:
        _NC_CACHE["nc"] = build_program()
    return _NC_CACHE["nc"]


def kernel(L0_attn_x, L0_attn_gamma, L0_attn_W_qkv, L0_attn_W_gates,
           L0_attn_b_gates, L0_attn_W_out, L0_ffn_wqkv, L0_ffn_q_scale,
           L0_ffn_k_scale, L0_ffn_wo, L0_ffn_ln_g, L0_ffn_ln_b, L0_ffn_w1,
           L0_ffn_b1, L0_ffn_w2, L0_ffn_b2):
    x = np.ascontiguousarray(np.asarray(L0_attn_x, dtype=np.float32))
    consts = _const_inputs(L0_ffn_q_scale, L0_ffn_k_scale, L0_ffn_ln_g,
                           L0_ffn_ln_b, L0_ffn_b1, L0_ffn_b2, L0_attn_b_gates,
                           L0_attn_gamma)
    weights = _prep_weights(L0_attn_W_qkv, L0_attn_W_gates, L0_attn_W_out,
                            L0_ffn_wqkv, L0_ffn_wo, L0_ffn_w1, L0_ffn_w2)
    in_maps = make_in_maps(x, consts, weights)
    nc = _get_program()
    res = run_bass_kernel_spmd(nc, in_maps, core_ids=list(range(8)))
    out = np.empty((NSEQ, SEQ, DM), dtype=np.float32)
    for core in range(8):
        seq, half = core // 2, core % 2
        out[seq, half * P:(half + 1) * P] = np.asarray(res.results[core]["yout"])
    return out
